# version 10
# baseline (speedup 1.0000x reference)
import numpy as np
import ml_dtypes
import concourse.bass as bass
import concourse.mybir as mybir
from concourse.bass_utils import run_bass_kernel_spmd
from contextlib import ExitStack

F32 = mybir.dt.float32
BF16 = mybir.dt.bfloat16
AF = mybir.ActivationFunctionType
ALU = mybir.AluOpType
AX = mybir.AxisListType

D = 1024
NPAIR = 8
EPS = 1e-5

ENGINES = ("sync", "gpsimd", "tensor", "scalar", "vector")
ESEM = {"sync": "eS", "gpsimd": "eG", "tensor": "eT", "scalar": "eA", "vector": "eV"}
import os as _os_se
SIM_EDGES = _os_se.environ.get("SIM_EDGES", "") == "1"
NDMASEM = 20
DMAPOOL = {"sync": [f"dS{i}" for i in range(NDMASEM)],
           "gpsimd": [f"dG{i}" for i in range(NDMASEM)]}


class Planner:
    def __init__(self):
        self.stream = {e: [] for e in ENGINES}
        self.count = {}
        self.snap = {}
        self.W = {e: {} for e in ENGINES}
        self.nwaits = 0
        self.dma_rr = {}

    def _merge_into(self, w, tok):
        s, v = tok
        snap = self.snap.get((s, v))
        if snap is None:
            if w.get(s, -1) < v:
                w[s] = v
            return
        for k2, v2 in snap.items():
            if w.get(k2, -1) < v2:
                w[k2] = v2

    def op(self, eng, fn, deps=(), dma=False, tag=""):
        w = self.W[eng]
        deps = list(deps)
        if dma:
            pool = DMAPOOL[eng]
            i = self.dma_rr.get(eng, 0)
            self.dma_rr[eng] = i + 1
            sem = pool[i % NDMASEM]
            prevc = self.count.get(sem, 0)
            if prevc > 0:
                deps.append((sem, prevc))
            amt = 16
        else:
            sem = ESEM[eng]
            amt = 1
            prev = self.count.get(sem, 0)
            if prev > 0:
                deps.insert(0, (sem, prev))
        waits = []
        for tok in deps:
            if tok is None:
                continue
            s, v = tok
            if w.get(s, -1) >= v:
                continue
            waits.append((s, v))
            self._merge_into(w, tok)
        c = self.count.get(sem, 0) + amt
        self.count[sem] = c
        snapd = dict(w)
        snapd[sem] = max(snapd.get(sem, 0), c)
        self.snap[(sem, c)] = snapd
        self.stream[eng].append((fn, waits, sem, amt, tag))
        self.nwaits += len(waits)
        return (sem, c)


def build_program(S, skip=()):
    NT = S // 128
    CH = S // 512
    import os as _os0
    _scr = 16384 if _os0.environ.get("DEVDBG", "") == "1" else 2048
    nc = bass.Bass("TRN2", target_bir_lowering=False, dynamic_dma_scratch_size=_scr)
    P = Planner()

    def din(name, shape):
        return nc.dram_tensor(name, list(shape), F32, kind="ExternalInput").ap()

    x0_d = din("x0", (S, D))
    wqT_d = din("wqT", (D, D)); wkT_d = din("wkT", (D, D)); wvT_d = din("wvT", (D, D))
    bqT_d = din("bqT", (128, 8)); bkT_d = din("bkT", (128, 8)); bvT_d = din("bvT", (128, 8))
    gam_d = din("gamma", (1, D)); bet_d = din("beta", (1, D))
    ngam_d = din("ngamma", (1, D)); nbet_d = din("nbeta", (1, D))
    sigT_d = din("sigT", (D, 64))
    knots_d = din("knots", (1, 8)); temp_d = din("temp", (1, 1)); oscale_d = din("oscale", (1, 1))
    cw1_d = din("cw1", (D, 256)); cb1T_d = din("cb1T", (128, 2))
    cw2_d = din("cw2r", (128, 4)); cb2_d = din("cb2", (2, 1))
    spl_d = din("spl", (64, 65)); dirs_d = din("dirs", (64, D))
    out_d = nc.dram_tensor("out", [S, D], F32, kind="ExternalOutput").ap()
    import os as _os
    _dbg = _os.environ.get("DEVDBG", "") in ("1", "2")
    _dbg1 = _os.environ.get("DEVDBG", "") == "1"
    _ik = "ExternalOutput" if _dbg else "Internal"
    kts_d = nc.dram_tensor("kts", [NPAIR, 128, S], BF16, kind=_ik).ap()
    qts_d = nc.dram_tensor("qts", [NPAIR, 128, S], BF16, kind="Internal").ap()
    aTs_d = nc.dram_tensor("aTs", [8, 128, S], BF16, kind="Internal").ap()
    vs_d = nc.dram_tensor("vs", [NPAIR, NT, 128, 130], BF16, kind=_ik).ap()
    x1s_d = nc.dram_tensor("x1s", [S, D], F32, kind=_ik).ap()
    zs_d = nc.dram_tensor("zs", [S, D], F32, kind=_ik).ap()
    dbg_d = nc.dram_tensor("dbg", [8, 128, D], F32, kind="ExternalOutput").ap() if _dbg1 else None

    es = ExitStack()

    def sb(name, shape, dtype):
        return es.enter_context(nc.sbuf_tensor(name, list(shape), dtype))

    ident = sb("ident", (128, 128), F32)
    diagmask = sb("diagmask", (128, 128), BF16)
    dgf = sb("dgf", (128, 128), F32)
    onescol_b = sb("onescol_b", (128, 1), BF16)
    onesrow = sb("onesrow", (1, 128), F32)
    iota64 = sb("iota64", (64, 1), F32)
    iotaj = sb("iotaj", (128, 72), F32)
    bigmi = sb("bigmi", (128, 8), F32)
    scal = sb("scal", (128, 16), F32)
    scrow = sb("scrow", (1, 16), F32)
    knots_s = sb("knots_s", (1, 8), F32)
    temp_s = sb("temp_s", (1, 4), F32)
    gam_t = sb("gam_t", (128, D), F32); bet_t = sb("bet_t", (128, D), F32)
    ngam_t = sb("ngam_t", (128, D), F32); nbet_t = sb("nbet_t", (128, D), F32)
    bqT = sb("bqT_s", (128, 8), F32); bkT = sb("bkT_s", (128, 8), F32); bvT = sb("bvT_s", (128, 8), F32)
    cb1T = sb("cb1T_s", (128, 2), F32)
    cb2_s = sb("cb2_s", (2, 1), F32)
    mu1 = sb("mu1", (128, NT), F32); r1 = sb("r1", (128, NT), F32)
    mvt = sb("mvt", (128, 2), F32)
    st6 = sb("st6", (128, 12), F32)
    sgt = sb("sgt", (128, 8 * 72), F32)
    x0c = [sb(f"x0c{i}", (128, D), F32) for i in range(8)]
    ACC = [sb(f"ACC{i}", (128, D), F32) for i in range(8)]
    aT = sb("aT", (128, 8 * 512), BF16)
    qT = sb("qT", (128, 8 * 512), BF16)
    kbuf = [sb(f"kbuf{i}", (128, S), BF16) for i in range(2)]
    vbuf = [sb(f"vbuf{i}", (128, NT * 130), BF16) for i in range(2)]
    pTb = [sb(f"pTb{i}", (128, 1536), BF16) for i in range(3)]
    zsb = sb("zsb", (128, 512), F32)
    zsb2 = sb("zsb2", (128, 512), F32)
    dnm = sb("dnm", (128, 64), F32)
    dnr = sb("dnr", (128, 12), F32)
    zst = [sb(f"zst{i}", (128, 64), F32) for i in range(4)]
    wstage = sb("wstage", (128, D), F32)
    zrb = [sb(f"zrb{i}", (128, D), F32) for i in range(2)]
    ybuf = [sb(f"ybuf{i}", (128, D), F32) for i in range(4)]
    x1rb = [sb("x1rb0", (128, D), F32)]
    gates = sb("gates", (128, 4 * 72), F32)
    calv = sb("calv", (128, 72), F32)
    calt = sb("calt", (128, 72), F32)
    cal2 = sb("cal2", (128, 72), F32)
    idxs = sb("idxs", (128, 16), F32)
    tidxr = sb("tidxr", (1, 512), F32)
    ohbufs = [sb(f"ohT{i}", (64, 128), BF16) for i in range(4)]
    compt = sb("compt", (128, 8), F32)
    ctmp = sb("ctmp", (2, 512), F32)
    splg = sb("splg", (128, 65), F32)
    hhs = sb("hhs", (128, 48), F32)
    mags = sb("mags", (128, 4), F32)
    mv2 = sb("mv2", (128, 8), F32)
    mv3 = sb("mv3", (128, 8), F32)
    mvy = sb("mvy", (128, 8), F32)
    r2c = sb("r2c", (128, 4), F32); r3c = sb("r3c", (128, 4), F32); ryc = sb("ryc", (128, 4), F32)
    cw2_b = sb("cw2_b", (128, 4), BF16)
    dirs_b = sb("dirs_b", (64, D), BF16)
    spl_b = sb("spl_b", (64, 65), BF16)

    nsc = sb("nsc", (128, 3 * NT), F32)
    hpre = sb("hpre", (128, 512), F32)
    ps_ring = es.enter_context(nc.psum_tensor("ps_ring", [128, 3072], F32))
    ps_z = es.enter_context(nc.psum_tensor("ps_z", [128, 1024], F32))

    semh = {}
    for sname in (["eS", "eG", "eT", "eA", "eV"] + DMAPOOL["sync"] + DMAPOOL["gpsimd"]):
        semh[sname] = es.enter_context(nc.semaphore(sname))

    def mm(out_ap, lhsT_ap, rhs_ap, deps, start=True, stop=True, tag="mm", tpos=None):
        def f(t, o=out_ap, l=lhsT_ap, r=rhs_ap, s1=start, s2=stop, tp=tpos):
            return t.matmul(o, l, r, start=s1, stop=s2, tile_position=tp)
        return P.op("tensor", f, deps, tag=tag)

    def tp_(out_ap, in_ap, deps, tag="tp", k=128):
        def f(t, o=out_ap, i=in_ap, k=k):
            return t.transpose(o, i, ident[0:k, 0:k])
        return P.op("tensor", f, deps, tag=tag)

    def vop(fn, deps, tag="v"):
        return P.op("vector", fn, deps, tag=tag)

    def sop(fn, deps, tag="s"):
        return P.op("scalar", fn, deps, tag=tag)

    def vcopy(out_ap, in_ap, deps, tag="vcopy"):
        return vop(lambda v, o=out_ap, i=in_ap: v.tensor_copy(o, i), deps, tag=tag)

    def dma(queue, out_ap, in_ap, deps, tag="dma"):
        return P.op(queue, lambda e, o=out_ap, i=in_ap: e.dma_start(out=o, in_=i),
                    deps, dma=True, tag=tag)

    ring_use = [None] * 6
    ring_fill = [None] * 6
    rp = [0]

    def ring_alloc(n=1):
        if (rp[0] % 6) + n > 6:
            rp[0] += 6 - (rp[0] % 6)
        slots = [(rp[0] + i) % 6 for i in range(n)]
        rp[0] += n
        deps = [ring_use[s] for s in slots if ring_use[s] is not None]
        deps += [ring_fill[s] for s in slots if ring_fill[s] is not None]
        return slots, deps

    def ring_filled(slots, tok):
        for s in slots:
            ring_fill[s] = tok

    def ring_used(slots, tok):
        for s in slots:
            ring_use[s] = tok

    def rs(slot, cols=512, off=0):
        return ps_ring[:, 512 * slot + off: 512 * slot + off + cols]

    SC_NINV, SC_K0, SC_D0, SC_OS = 0, 1, 2, 9

    def scap(i):
        return scal[:, i:i + 1]

    def newton_rsqrt(dst_ap, t_ap, w, deps, tag):
        T = nsc[:, 0:w]; Sx = nsc[:, NT:NT + w]; U = nsc[:, 2 * NT:2 * NT + w]
        tk = vcopy(T, t_ap, list(deps) + [newton_busy[0]], tag=tag + "cp")
        tk = vop(lambda v: v.reciprocal(Sx, T), [tk], tag=tag + "rc")
        tk = vop(lambda v: v.tensor_single_scalar(out=Sx, in_=Sx, scalar=1.0, op=ALU.min),
                 [tk], tag=tag + "mn")
        for it_ in range(16):
            tk = vop(lambda v: v.tensor_mul(U, Sx, Sx), [tk], tag=tag + "n1")
            tk = vop(lambda v: v.tensor_mul(U, U, T), [tk], tag=tag + "n2")
            tk = vop(lambda v: v.tensor_scalar(out=U, in0=U, scalar1=-0.5, scalar2=1.5,
                                               op0=ALU.mult, op1=ALU.add), [tk], tag=tag + "n3")
            tk = vop(lambda v: v.tensor_mul(Sx, Sx, U), [tk], tag=tag + "n4")
        tk = vcopy(dst_ap, Sx, [tk], tag=tag + "out")
        newton_busy[0] = tk
        return tk

    newton_busy = [None]
    gelu_busy = [None]

    t_id_i = P.op("gpsimd", lambda g: g.iota(dgf[:, :], pattern=[[1, 128]], base=0,
                                             channel_multiplier=-1,
                                             allow_small_or_imprecise_dtypes=True), (), tag="iota")
    t_ident = vop(lambda v: v.tensor_single_scalar(out=ident[:, :], in_=dgf[:, :],
                                                   scalar=0.0, op=ALU.is_equal), [t_id_i], tag="ident")
    t_dgm = vop(lambda v: v.tensor_single_scalar(out=diagmask[:, :], in_=dgf[:, :],
                                                 scalar=0.0, op=ALU.is_le), [t_id_i], tag="dgm")
    t_io64 = P.op("gpsimd", lambda g: g.iota(iota64[:, :], pattern=[[0, 1]], base=0,
                                             channel_multiplier=1,
                                             allow_small_or_imprecise_dtypes=True),
                  [t_id_i], tag="io64")
    t_ioj = P.op("gpsimd", lambda g: g.iota(iotaj[:, :], pattern=[[1, 72]], base=0,
                                            channel_multiplier=0,
                                            allow_small_or_imprecise_dtypes=True),
                 [t_io64], tag="ioj")
    t_bigmi_i = P.op("gpsimd", lambda g: g.iota(bigmi[:, :], pattern=[[-1, 8]], base=999,
                                                channel_multiplier=0,
                                                allow_small_or_imprecise_dtypes=True),
                     [t_ioj], tag="iobm")
    t_ones_b = vop(lambda v: v.memset(onescol_b[:, :], 1.0), (), tag="onesb")
    t_ones_r = vop(lambda v: v.memset(onesrow[:, :], 1.0), (), tag="onesr")
    t_zsb0 = vop(lambda v: v.memset(zsb[64:128, :], 0.0), (), tag="zsb0")
    t_zsb1 = vop(lambda v: v.memset(zsb2[64:128, :], 0.0), (), tag="zsb1")
    t_zsbp = [t_zsb0, t_zsb1]


    rowbuf_read = [None]

    def bcast_row(dst, src_d, tag):
        tok_l = dma("sync", wstage[0:1, 0:D], src_d[:, :], [rowbuf_read[0]], tag="ld_" + tag)
        outs = []
        for j in range(2):
            slots, rdeps = ring_alloc(1)
            sl = slots[0]
            tmm = mm(rs(sl), onesrow[:, :], wstage[0:1, 512 * j: 512 * j + 512],
                     [tok_l, t_ones_r] + rdeps, tag="bc_" + tag)
            rowbuf_read[0] = tmm
            ring_filled(slots, tmm)
            tcp = vcopy(dst[:, 512 * j: 512 * j + 512], rs(sl), [tmm], tag="bcc_" + tag)
            ring_used(slots, tcp)
            outs.append(tcp)
        return outs, tok_l

    gam_toks, l_g = bcast_row(gam_t, gam_d, "gam")
    bet_toks, l_b = bcast_row(bet_t, bet_d, "bet")
    ngam_toks, l_n = bcast_row(ngam_t, ngam_d, "ngam")
    nbet_toks, l_nb = bcast_row(nbet_t, nbet_d, "nbet")

    tok_bq = dma("sync", bqT[:, :], bqT_d[:, :], (), tag="bq")
    tok_bk = dma("sync", bkT[:, :], bkT_d[:, :], (), tag="bk")
    tok_bv = dma("sync", bvT[:, :], bvT_d[:, :], (), tag="bv")
    tok_cb1 = dma("sync", cb1T[:, :], cb1T_d[:, :], (), tag="cb1")
    tok_cb2 = dma("sync", cb2_s[:, :], cb2_d[:, :], (), tag="cb2")
    tok_kn = dma("sync", knots_s[:, :], knots_d[:, :], (), tag="kn")
    tok_tm = dma("sync", temp_s[:, 0:1], temp_d[:, :], (), tag="tm")
    tok_os = dma("sync", temp_s[:, 3:4], oscale_d[:, :], (), tag="os")
    t_t1 = vop(lambda v: v.tensor_single_scalar(out=temp_s[:, 1:2], in_=temp_s[:, 0:1],
                                                scalar=1e-6, op=ALU.add), [tok_tm], tag="t1")
    t_t2 = vop(lambda v: v.reciprocal(temp_s[:, 2:3], temp_s[:, 1:2]), [t_t1], tag="t2")
    t_t3 = vop(lambda v: v.tensor_single_scalar(out=temp_s[:, 2:3], in_=temp_s[:, 2:3],
                                                scalar=-1.0, op=ALU.mult), [t_t2], tag="t3")
    t_sz = vop(lambda v: v.memset(scrow[:, :], 0.0), (), tag="sz")
    t_s0 = vcopy(scrow[:, 0:1], temp_s[:, 2:3], [t_t3, t_sz], tag="s0")
    t_s1 = vcopy(scrow[:, 1:2], knots_s[:, 0:1], [tok_kn], tag="s1")
    t_s2 = vop(lambda v: v.tensor_sub(scrow[:, 2:9], knots_s[:, 1:8], knots_s[:, 0:7]),
               [tok_kn], tag="s2")
    t_s3 = vcopy(scrow[:, 9:10], temp_s[:, 3:4], [tok_os], tag="s3")
    slots, rdeps = ring_alloc(1)
    sl0 = slots[0]
    t_scm = mm(rs(sl0, 16), onesrow[:, :], scrow[:, 0:16],
               [t_s0, t_s1, t_s2, t_s3, t_ones_r] + rdeps, tag="scm")
    ring_filled(slots, t_scm)
    t_scal = vcopy(scal[:, :], rs(sl0, 16), [t_scm], tag="scal")
    ring_used(slots, t_scal)

    for kk in range(8):
        a_ap = sgt[:, 72 * kk: 72 * kk + 64]
        cs_ap = sgt[:, 72 * kk + 64: 72 * kk + 72]
        tok_l = dma("sync", a_ap, sigT_d[128 * kk:128 * kk + 128, :], (), tag="sig")
        t_a = vop(lambda v, a=a_ap: v.tensor_scalar(out=calv[:, 0:64], in0=a, scalar1=0.3,
                                                    scalar2=None, op0=ALU.is_gt), [tok_l], tag="siga")
        t_b = vop(lambda v, a=a_ap: v.tensor_scalar(out=calt[:, 0:64], in0=a, scalar1=-0.3,
                                                    scalar2=None, op0=ALU.is_lt), [tok_l], tag="sigb")
        t_c = vop(lambda v, a=a_ap: v.tensor_sub(a, calv[:, 0:64], calt[:, 0:64]),
                  [t_a, t_b], tag="sigc")
        t_d = vop(lambda v, kk=kk: v.reduce_sum(
            sgt[:, 72 * kk + 64: 72 * kk + 72],
            sgt[:, 72 * kk: 72 * kk + 64].rearrange("p (c t) -> p c t", t=8), axis=AX.X),
            [t_c], tag="sigd")
        t_e = vop(lambda v, a=cs_ap: v.tensor_scalar(out=calv[:, 0:8], in0=a, scalar1=0.0,
                                                     scalar2=None, op0=ALU.is_gt), [t_d], tag="sige")
        t_f = vop(lambda v, a=cs_ap: v.tensor_scalar(out=calt[:, 0:8], in0=a, scalar1=0.0,
                                                     scalar2=None, op0=ALU.is_lt), [t_d], tag="sigf")
        vop(lambda v, a=cs_ap: v.tensor_sub(a, calv[:, 0:8], calt[:, 0:8]), [t_e, t_f], tag="sigg")

    tok_cw2l = dma("sync", wstage[:, 0:4], cw2_d[:, :], [rowbuf_read[0]], tag="cw2")
    tok_cw2 = vcopy(cw2_b[:, :], wstage[:, 0:4], [tok_cw2l], tag="cw2c")
    tok_dirs0 = dma("sync", wstage[0:64, :], dirs_d[:, :], [tok_cw2], tag="dirs")
    tok_dirs = vcopy(dirs_b[:, :], wstage[0:64, :], [tok_dirs0], tag="dirsc")
    tok_spl0 = dma("sync", wstage[0:64, 0:65], spl_d[:, :], [tok_dirs], tag="spl")
    tok_spl = vcopy(spl_b[:, :], wstage[0:64, 0:65], [tok_spl0], tag="splc")

    x0_free = [None] * 8
    var_toks = []
    for t in range(NT):
        buf = x0c[t % 8]
        tok_l = dma("sync", buf[:, :], x0_d[128 * t:128 * t + 128, :],
                    [x0_free[t % 8]], tag="x0s")
        t1 = vop(lambda v, b=buf: v.bn_stats(st6[:, 0:6], b[:, 0:512]), [tok_l], tag="bns1")
        t2 = vop(lambda v, b=buf: v.bn_stats(st6[:, 6:12], b[:, 512:1024]), [tok_l], tag="bns2")
        t3 = vop(lambda v: v.bn_aggr(mvt[:, :], st6[:, :]), [t1, t2], tag="bnag")
        x0_free[t % 8] = t3
        t4 = vop(lambda v, t=t: v.tensor_single_scalar(out=r1[:, t:t + 1], in_=mvt[:, 1:2],
                                                       scalar=EPS, op=ALU.add), [t3], tag="veps")
        t7 = vcopy(mu1[:, t:t + 1], mvt[:, 0:1], [t3], tag="mu1")
        var_toks.append((t4, t7))
    t_r1 = newton_rsqrt(r1[:, 0:NT], r1[:, 0:NT], NT, [tk for pair in var_toks for tk in pair],
                        "ln1n")
    ln1_tok = [(t_r1, var_toks[t][1]) for t in range(NT)]

    es_kv = ExitStack()
    wkv_b = es_kv.enter_context(nc.sbuf_tensor("wkv_b", [128, 8 * 1024], BF16))
    kstage = es_kv.enter_context(nc.sbuf_tensor("kstage", [128, 512], BF16))
    vstage_f = es_kv.enter_context(nc.sbuf_tensor("vstage_f", [128, 512], F32))
    vstage = es_kv.enter_context(nc.sbuf_tensor("vstage", [128, 4 * 130], BF16))

    acc_tok = [None] * 8
    acc_read = [None] * 8
    aT_read = [None] * 8
    aT_tok = [None] * 8
    aT_store = [None] * 8
    kv_dma_toks = []
    last_wkv_mm = [None]

    def load_wkv(w_d, dep0):
        prev = dep0
        for kk in range(8):
            tok_l = dma("sync", wstage[:, :], w_d[128 * kk:128 * kk + 128, :],
                        [prev], tag="wkv")
            prev = vcopy(wkv_b[:, 1024 * kk: 1024 * kk + 1024], wstage[:, :],
                         [tok_l, last_wkv_mm[0]], tag="wkvc")
        return prev

    kst_free = [None]
    vstf_free = [None]
    vst_free = [None]

    def kv_pass(proj, w_ready, bT, bias_tok, kdst=None, build_aT=False):
        for n in range(CH):
            if build_aT:
                a_toks = []
                for tt in range(4):
                    t = 4 * n + tt
                    buf = x0c[t % 8]
                    tok_l = dma("sync", buf[:, :], x0_d[128 * t:128 * t + 128, :],
                                [x0_free[t % 8]], tag="x0kv")
                    ta = vop(lambda v, b=buf, t=t, tt=tt: v.tensor_scalar(
                        out=ACC[tt][:, :], in0=b[:, :], scalar1=mu1[:, t:t + 1],
                        scalar2=r1[:, t:t + 1], op0=ALU.subtract, op1=ALU.mult),
                        [tok_l, ln1_tok[t][0], ln1_tok[t][1], acc_read[tt], acc_tok[tt]],
                        tag="akv")
                    x0_free[t % 8] = ta
                    acc_tok[tt] = ta
                    a_toks.append(ta)
                for kk in range(8):
                    slots, rdeps = ring_alloc(1)
                    sl = slots[0]
                    last_t = None
                    for tt in range(4):
                        deps = [a_toks[tt], t_ident] + (rdeps if tt == 0 else [])
                        last_t = tp_(rs(sl, 128, 128 * tt),
                                     ACC[tt][:, 128 * kk:128 * kk + 128],
                                     deps, tag="aTkv")
                        acc_read[tt] = last_t
                    ring_filled(slots, last_t)
                    tc = vop(lambda v, kk=kk, sl=sl: v.tensor_copy(
                        aT[:, 512 * kk: 512 * kk + 512], rs(sl)),
                        [last_t, aT_tok[kk], aT_read[kk], aT_store[kk]], tag="aTcp")
                    ring_used(slots, tc)
                    aT_tok[kk] = tc
                    td = dma("sync", aTs_d[kk, :, 512 * n: 512 * n + 512],
                             aT[:, 512 * kk: 512 * kk + 512], [tc], tag="aTst")
                    aT_store[kk] = td
            else:
                for kk in range(8):
                    tok = dma("sync", aT[:, 512 * kk: 512 * kk + 512],
                              aTs_d[kk, :, 512 * n: 512 * n + 512],
                              [aT_tok[kk], aT_read[kk], aT_store[kk]], tag="aTld")
                    aT_tok[kk] = tok
            for p in range(NPAIR):
                slots, rdeps = ring_alloc(1)
                sl = slots[0]
                mmt = None
                for kk in range(8):
                    deps = [aT_tok[kk], w_ready] + (rdeps if kk == 0 else [])
                    mmt = mm(rs(sl), wkv_b[:, 1024 * kk + 128 * p: 1024 * kk + 128 * p + 128],
                             aT[:, 512 * kk: 512 * kk + 512], deps,
                             start=(kk == 0), stop=(kk == 7), tag=proj + "mm")
                    aT_read[kk] = mmt
                last_wkv_mm[0] = mmt
                ring_filled(slots, mmt)
                if proj in ("k", "q"):
                    tev = vop(lambda v, sl=sl, p=p: v.tensor_scalar(
                        out=kstage[:, :], in0=rs(sl), scalar1=bT[:, p:p + 1], scalar2=None,
                        op0=ALU.add), [mmt, bias_tok, kst_free[0]], tag="kev")
                    ring_used(slots, tev)
                    tdma = dma("sync", kdst[p, :, 512 * n: 512 * n + 512], kstage[:, :],
                               [tev], tag="kst")
                    kst_free[0] = tdma
                    kv_dma_toks.append(tdma)
                else:
                    tevs = []
                    for j in range(4):
                        tev = vop(lambda v, sl=sl, p=p, j=j: v.tensor_scalar(
                            out=vstage_f[:, 128 * j: 128 * j + 128],
                            in0=rs(sl, 128, 128 * j), scalar1=bT[:, p:p + 1], scalar2=None,
                            op0=ALU.add), [mmt, bias_tok, vstf_free[0]], tag="vev")
                        tevs.append(tev)
                    ring_used(slots, tevs[-1])
                    slots2, rdeps2 = ring_alloc(1)
                    sl2 = slots2[0]
                    last_t = None
                    for j in range(4):
                        deps = [tevs[j]] + (rdeps2 if j == 0 else [])
                        last_t = tp_(rs(sl2, 128, 128 * j), vstage_f[:, 128 * j: 128 * j + 128],
                                     deps, tag="vT")
                    vstf_free[0] = last_t
                    ring_filled(slots2, last_t)
                    tcp = vop(lambda v, sl2=sl2: v.tensor_copy(
                        vstage[:, :].rearrange("p (t s c) -> p t s c", t=4, s=2)[:, :, :, 0:64],
                        rs(sl2, 512).rearrange("p (t s c) -> p t s c", t=4, s=2)),
                        [last_t, vst_free[0]], tag="vcp")
                    ring_used(slots2, tcp)
                    tms = vop(lambda v: v.memset(
                        vstage[:, :].rearrange("p (t c) -> p t c", t=4)[:, :, 64:130:65], 1.0),
                        [tcp], tag="vones")
                    tdma = dma("sync", vs_d[p, 4 * n: 4 * n + 4].rearrange("t p c -> p t c"),
                               vstage[:, :].rearrange("p (t c) -> p t c", t=4), [tms], tag="vst")
                    vst_free[0] = tdma
                    kv_dma_toks.append(tdma)

    wq_ready = load_wkv(wqT_d, tok_spl)
    kv_pass("q", wq_ready, bqT, tok_bq, kdst=qts_d, build_aT=True)
    wk_ready = load_wkv(wkT_d, wq_ready)
    kv_pass("k", wk_ready, bkT, tok_bk, kdst=kts_d)
    wv_ready = load_wkv(wvT_d, wk_ready)
    kv_pass("v", wv_ready, bvT, tok_bv)
    zmov = wstage[0:1, 0:512]
    zrow = wstage[0:1, 512:577]
    t_zmov = vop(lambda v: v.memset(wstage[0:1, 0:577], 0.0), [wv_ready], tag="zmov")
    t_zrow = t_zmov
    es_kv.close()
    kv_all = P.op("gpsimd", lambda g: g.nop(), kv_dma_toks, tag="kvbar")

    cw1_f = sb("cw1_f", (128, 8 * 256), F32)
    x2T_f = sb("x2T_f", (128, 8 * 512), F32)
    hTb = sb("hTb", (128, 2 * 512), BF16)
    cw1_toks = []
    for kk in range(8):
        tok_l = dma("sync", cw1_f[:, 256 * kk: 256 * kk + 256],
                    cw1_d[128 * kk: 128 * kk + 128, :], [last_wkv_mm[0]], tag="cw1")
        cw1_toks.append(tok_l)

    kbuf_tok = [None, None]; kbuf_read = [None, None]
    vbuf_tok = [None, None]; vbuf_read = [None, None]
    pT_tok = [None, None, None]; pT_read = [None, None, None]
    zs_tok = [None, None]; zs_read = [None, None]
    ucnt = [0]
    psz_read = [None, None]; zs_close = [None, None]
    qbuf = [qT, aT]
    qbuf_tok = [None, None]
    qbuf_read = [[], [tk for tk in aT_tok if tk] + [tk for tk in aT_read if tk] +
                 [tk for tk in aT_store if tk]]
    dnm_tok = {}; dnm_read = [None, None]
    ybuf_tok = [None] * 4; ybuf_read = [None] * 4
    x1rb_tok = [None, None]; x1rb_read = [None, None]
    x1dma_tok = [None] * 8
    outdma_toks = []
    gates_tok = [None] * 4
    gexp_tok = [None] * 4
    cal_state = {}
    mags_tok = [None] * 4
    compt_tok = [None] * 4
    oh_tok = [None] * 4; oh_read = [None] * 4
    tidxr_read = [None] * 4; idx_read = [None] * 4
    gather_tok = {}
    x2T_tok = [None] * 8; x2T_read = [None] * 8
    hT_tok = [None] * 2; hT_read = [None] * 2
    mv2_tok = [None] * 4; mv3_tok = [None] * 4; mvy_tok = [None] * 4
    ctmp_read = [None]
    ctv4_hold = [None]
    r2_tok = [None] * 4; r3_tok = [None] * 4; ry_tok = [None] * 4
    ysts_read = [None] * 4
    spl_ps = {}

    def a_init(c, free_x0=False):
        toks = []
        for tt in range(4):
            t = 4 * c + tt
            ai = 4 * (c % 2) + tt
            buf = x0c[(c % 2) * 4 + tt]
            tok_l = dma("sync", buf[:, :], x0_d[128 * t:128 * t + 128, :],
                        [x0_free[(c % 2) * 4 + tt]], tag=f"x0c{c}")
            ta = vop(lambda v, b=buf, t=t, ai=ai: v.tensor_scalar(
                out=ACC[ai][:, :], in0=b[:, :], scalar1=mu1[:, t:t + 1],
                scalar2=r1[:, t:t + 1], op0=ALU.subtract, op1=ALU.mult),
                [tok_l, ln1_tok[t][0], ln1_tok[t][1], acc_read[ai], acc_tok[ai],
                 x1dma_tok[ai]], tag=f"a{c}")
            acc_tok[ai] = ta
            if free_x0:
                x0_free[(c % 2) * 4 + tt] = ta
            toks.append(ta)
        return toks

    def kvq_load(p):
        pb = p % 2
        deps = [kbuf_read[pb], kv_all]
        tk = dma("sync", kbuf[pb][:, :], kts_d[p, :, :], deps, tag=f"kld{p}")
        deps = [vbuf_read[pb], kv_all]
        tvs = []
        for half in range(2):
            tv = dma("sync",
                     vbuf[pb][:, half * (NT // 2) * 130:(half + 1) * (NT // 2) * 130].rearrange(
                         "p (t c) -> p t c", c=130),
                     vs_d[p, half * (NT // 2):(half + 1) * (NT // 2)].rearrange(
                         "t p c -> p t c"), deps, tag=f"vld{p}")
            tvs.append(tv)
        tq = dma("sync", qbuf[pb][:, 0:S], qts_d[p, :, :],
                 list(qbuf_read[pb]) + [kv_all], tag=f"qld{p}")
        kbuf_tok[pb] = tk
        vbuf_tok[pb] = tvs
        qbuf_tok[pb] = tq
        return tk, tvs

    def tile_schedule(c):
        groups = []
        if "sside" in skip:
            CHsk = True
        else:
            CHsk = False
        for hh in range(2):
            if CHsk:
                break
            for it in range(4):
                i = 4 * c + it
                chunks = []
                rem = 128 * (i + 1)
                j = 0
                while rem > 0:
                    nj = min(512, rem)
                    chunks.append((j, nj))
                    rem -= nj
                    j += 1
                g = []
                for (j, nj) in chunks:
                    if nj < 512:
                        if g:
                            groups.append(("S", g))
                            g = []
                        groups.append(("S", [(hh, it, j, nj)]))
                        continue
                    g.append((hh, it, j, nj))
                    if len(g) == 3:
                        groups.append(("S", g))
                        g = []
                if g:
                    groups.append(("S", g))
        stg = []
        for kp in range(4 * c + 4, NT):
            for hh in range(2):
                stg.append((hh, kp, 512))
                if len(stg) == 3:
                    groups.append(("T", stg))
                    stg = []
        if stg:
            groups.append(("T", stg))
        for kp in range(4 * c, min(4 * c + 4, NT)):
            jj = kp - 4 * c
            n = 128 * (jj + 1)
            for hh in range(2):
                groups.append(("T", [(hh, kp, n)]))
        return groups

    def emit_pair(c, p, sched):
        pb = p % 2
        ktok = kbuf_tok[pb]
        vtoks = vbuf_tok[pb]
        qb = qbuf[pb]
        qtok = qbuf_tok[pb]
        ucnt[0] += 1
        dpar = ucnt[0] % 2
        dbase = 32 * dpar
        pv_last = [None, None]
        zopen = [None, None]
        have_full = 4 * c + 4 < NT
        if not have_full:
            for hh in range(2):
                zopen[hh] = mm(ps_z[0:65, 512 * hh: 512 * hh + 512], zrow[0:1, 0:65],
                               zmov[0:1, :],
                               [t_zrow, t_zmov, psz_read[hh], zs_close[hh]], start=True,
                               stop=False, tag=f"zop{c}_{p}")
                pv_last[hh] = zopen[hh]
        first_pv = [have_full, have_full]
        pT_parity = [0]
        pend_consume = []
        kread = []
        vread = []
        qread = []
        sg_idx = {}
        for (kind, g) in sched:
            L = len(g)
            slots, rdeps = ring_alloc(L)
            qk_toks = []
            for ii, item in enumerate(g):
                sl = slots[ii]
                if "qk" in skip:
                    continue
                if "qk1" in skip and not (p == 0 and c == 0 and ii == 0 and kind == "T"):
                    continue
                if kind == "T":
                    hh, kp, n = item
                    lhs = kbuf[pb][64 * hh:64 * hh + 64, 128 * kp:128 * kp + 128]
                    rhsap = qb[64 * hh:64 * hh + 64, 512 * c: 512 * c + n]
                    deps = [ktok, qtok] + (rdeps if ii == 0 else [])
                    tokm = mm(rs(sl, n), lhs, rhsap, deps, tag=f"qkT{c}_{p}")
                else:
                    hh, it, j, n = item
                    lhs = qb[64 * hh:64 * hh + 64, 512 * c + 128 * it: 512 * c + 128 * it + 128]
                    rhsap = kbuf[pb][64 * hh:64 * hh + 64, 512 * j: 512 * j + n]
                    deps = [ktok, qtok] + (rdeps if ii == 0 else [])
                    tokm = mm(rs(sl, n), lhs, rhsap, deps, tag=f"qkS{c}_{p}")
                qk_toks.append(tokm)
                ring_fill[sl] = tokm
                kread.append(tokm)
                qread.append(tokm)
            pp = pT_parity[0]
            pT_parity[0] = (pp + 1) % 3
            ns = [item[-1] for item in g]
            sameN = all(x == ns[0] for x in ns)
            exp_toks = []
            if "exp" in skip:
                for ii2, sl_ in enumerate(slots):
                    if ii2 < len(qk_toks):
                        ring_used([sl_], qk_toks[ii2])
                pT_parity[0] = pp
                continue
            if kind == "S":
                hh, it, _, _ = g[0]
                base_col = dbase + 4 * (4 * hh + it)
                gi = sg_idx.get((hh, it), 0)
                sg_idx[(hh, it)] = gi + 1
                totN = sum(ns)
                acap = dnm[:, base_col + gi: base_col + gi + 1]
                din_ap = rs(slots[0], totN) if (sameN and ns[0] == 512) or L == 1 else None
                if din_ap is None:
                    for ii, item in enumerate(g):
                        _, _, _, n = item
                        gi2 = sg_idx[(hh, it)] - 1 + ii
                        pass
                    raise AssertionError("S group must be uniform or single")
                texp = sop(lambda s, i_=din_ap, o=pTb[pp][:, 0:totN], a=acap:
                           s.activation(out=o, in_=i_, func=AF.Exp, scale=1.0 / 64.0,
                                        accum_out=a),
                           qk_toks + [pT_read[pp], pT_tok[pp], dnm_read[dpar]],
                           tag=f"expS{c}_{p}")
                dnm_tok[(hh, it, gi)] = texp
                pT_tok[pp] = texp
                exp_toks.append(texp)
                ring_used(slots, texp)
            else:
                if sameN:
                    totN = ns[0] * L
                    texp = sop(lambda s, i_=rs(slots[0], totN), o=pTb[pp][:, 0:totN]:
                               s.activation(out=o, in_=i_, func=AF.Exp, scale=1.0 / 64.0),
                               qk_toks + [pT_read[pp], pT_tok[pp]], tag=f"expT{c}_{p}")
                    exp_toks = [texp] * L
                    pT_tok[pp] = texp
                    ring_used(slots, texp)
                else:
                    off = 0
                    for ii, item in enumerate(g):
                        n = item[-1]
                        texp = sop(lambda s, i_=rs(slots[ii], n), o=pTb[pp][:, off:off + n]:
                                   s.activation(out=o, in_=i_, func=AF.Exp, scale=1.0 / 64.0),
                                   [qk_toks[ii], pT_read[pp], pT_tok[pp]], tag=f"expT{c}_{p}")
                        exp_toks.append(texp)
                        pT_tok[pp] = texp
                        ring_used([slots[ii]], texp)
                        off += n
            if kind == "S":
                continue
            if dbg_d is not None and c == 0 and p == 0 and not dbg_state.get(5):
                dbg_state[5] = True
                _dtok = dbg_dump(5, pTb[pp][:, 0:512], [exp_toks[0]], w=512)
                pT_tok[pp] = _dtok
                _d6 = dbg_dump(6, qb[:, 0:512], [qtok], w=512)
                qbuf_read[pb] = [_d6]
                _d7 = dbg_dump(7, kbuf[pb][:, 0:512], [ktok], w=512)
                kbuf_read[pb] = _d7

            def consume(g=g, pp=pp, exp_toks=exp_toks, sameN=sameN):
                off = 0
                creads = []
                for ii, item in enumerate(g):
                    hh, kp, n = item
                    jj = kp - 4 * c
                    et = exp_toks[ii] if not sameN else exp_toks[0]
                    pslice = lambda a, b_: pTb[pp][:, off + a: off + b_]
                    zoff = 512 * hh
                    if jj >= 4:
                        if "pvones" not in skip:
                            opener = first_pv[hh]
                            deps = [et] + vtoks
                            if opener:
                                deps += [psz_read[hh], zs_close[hh]]
                            elif zopen[hh] is not None:
                                deps.append(zopen[hh])
                            st = mm(ps_z[0:65, zoff:zoff + 512],
                                    vbuf[pb][:, 130 * kp + 65 * hh: 130 * kp + 65 * hh + 65],
                                    pslice(0, 512), deps,
                                    start=opener, stop=False, tag=f"pv{c}_{p}")
                            if opener:
                                first_pv[hh] = False
                                zopen[hh] = st
                            pv_last[hh] = st
                            creads.append(st)
                            vread.append(st)
                    else:
                        if jj > 0 and "pvones" not in skip:
                            st = mm(ps_z[0:65, zoff:zoff + 128 * jj],
                                    vbuf[pb][:, 130 * kp + 65 * hh: 130 * kp + 65 * hh + 65],
                                    pslice(0, 128 * jj),
                                    [et, zopen[hh]] + vtoks,
                                    start=False, stop=False, tag=f"pvp{c}_{p}")
                            pv_last[hh] = st
                            creads.append(st)
                            vread.append(st)
                        if "dpv" not in skip:
                            tmk = vop(lambda v, a=pslice(128 * jj, 128 * jj + 128):
                                      v.tensor_mul(a, a, diagmask[:, :]), [et, t_dgm],
                                      tag=f"dmask{c}_{p}")
                            st = mm(ps_z[0:64, zoff + 128 * jj: zoff + 128 * jj + 128],
                                    vbuf[pb][:, 130 * kp + 65 * hh: 130 * kp + 65 * hh + 64],
                                    pslice(128 * jj, 128 * jj + 128),
                                    [tmk, zopen[hh]] + vtoks,
                                    start=False, stop=False, tag=f"dpv{c}_{p}")
                            pv_last[hh] = st
                            creads.append(st)
                            vread.append(st)
                    off += n
                pT_read[pp] = creads[-1] if creads else pT_read[pp]

            pend_consume.append(consume)
            if len(pend_consume) > 2:
                pend_consume.pop(0)()
        while pend_consume:
            pend_consume.pop(0)()
        kbuf_read[pb] = kread[-1] if kread else kbuf_read[pb]
        vbuf_read[pb] = vread[-1] if vread else vbuf_read[pb]
        if qread:
            qbuf_read[pb] = [qread[-1]]
        return pv_last, sg_idx

    zst_read = [None] * 8
    zst_idx = [0]
    zdma_toks = []
    zdma_by_chunk = [[] for _ in range(CH)]
    RH = S // 16

    def z_evac(c, p, pv_last, sg_idx):
        toks = []
        dpar = ucnt[0] % 2
        dbase = 32 * dpar
        for hh in range(2):
            zoff = 512 * hh
            zsbh = zsb if hh == 0 else zsb2
            tclose = mm(ps_z[0:65, zoff: zoff + 1], zrow[0:1, 0:65], zmov[0:1, 0:1],
                        [pv_last[hh], t_zrow], start=False, stop=True, tag=f"zcl{c}_{p}")
            zs_close[hh] = tclose
            tcps = []
            for j in range(4):
                tcp = vcopy(zsbh[0:65, 128 * j: 128 * j + 128],
                            ps_z[0:65, zoff + 128 * j: zoff + 128 * j + 128],
                            [tclose, zs_tok[hh], zs_read[hh], t_zsbp[hh]], tag=f"zc{c}_{p}")
                tcps.append(tcp)
            zs_tok[hh] = tcps[-1]
            psz_read[hh] = tcps[-1]
            slots, rdeps = ring_alloc(1)
            sl = slots[0]
            tts = []
            for j in range(4):
                deps = [tcps[j], t_ident] + (rdeps if j == 0 else [])
                tt_ = tp_(rs(sl, 128, 128 * j), zsbh[:, 128 * j: 128 * j + 128],
                          deps, tag=f"zT{c}_{p}")
                tts.append(tt_)
            zs_read[hh] = tts[-1]
            ring_filled(slots, tts[-1])
            h_glob = 2 * p + hh
            tBs = []
            for j in range(4):
                m = sg_idx.get((hh, j), 0)
                if m > 0:
                    bc = dbase + 4 * (4 * hh + j)
                    tA = vop(lambda v, bc=bc, m=m, j=j: v.reduce_sum(
                        dnr[:, j:j + 1], dnm[:, bc:bc + m], axis=AX.X),
                        [dnm_tok[(hh, j, gg)] for gg in range(m)], tag=f"dn{c}_{p}")
                    tB = vop(lambda v, sl=sl, j=j: v.tensor_add(
                        dnr[:, 4 + j:5 + j], dnr[:, j:j + 1], rs(sl, 1, 128 * j + 64)),
                        [tA, tts[-1]], tag=f"dt{c}_{p}")
                else:
                    tB = vcopy(dnr[:, 4 + j:5 + j], rs(sl, 1, 128 * j + 64), [tts[-1]],
                               tag=f"dt0{c}_{p}")
                tBs.append(tB)
            trc = vop(lambda v: v.reciprocal(dnr[:, 8:12], dnr[:, 4:8]), tBs,
                      tag=f"dr{c}_{p}")
            vlast = None
            for j in range(4):
                zi = zst_idx[0] % 4
                zst_idx[0] += 1
                zb = zst[zi]
                tz = vop(lambda v, sl=sl, j=j, zb=zb: v.tensor_scalar(
                    out=zb[:, :], in0=rs(sl, 64, 128 * j), scalar1=dnr[:, 8 + j:9 + j],
                    scalar2=None, op0=ALU.mult),
                    [trc, zst_read[zi], tts[-1]], tag=f"za{c}_{p}")
                r0 = RH * h_glob + 32 * c + 8 * j
                td = dma("sync",
                         zs_d[r0:r0 + 8, :].rearrange("r (t h) -> (r t) h", h=64),
                         zb[:, :], [tz], tag=f"zd{c}_{p}")
                zst_read[zi] = td
                zdma_toks.append(td)
                zdma_by_chunk[(RH * h_glob) // 512].append(td)
                vlast = tz
            ring_used(slots, vlast)
            dnm_read[dpar] = vlast
            toks.append(vlast)
        return toks

    def ln2_x1(c, z_toks):
        stat_toks = []
        for tt in range(4):
            ai = 4 * (c % 2) + tt
            t1 = vop(lambda v, ai=ai: v.bn_stats(st6[:, 0:6], ACC[ai][:, 0:512]),
                     [acc_tok[ai]] + z_toks, tag=f"l2a{c}")
            t2 = vop(lambda v, ai=ai: v.bn_stats(st6[:, 6:12], ACC[ai][:, 512:1024]),
                     [acc_tok[ai]], tag=f"l2b{c}")
            t3 = vop(lambda v, tt=tt: v.bn_aggr(mv2[:, 2 * tt:2 * tt + 2], st6[:, :]),
                     [t1, t2, mv2_tok[tt]], tag=f"l2c{c}")
            mv2_tok[tt] = t3
            stat_toks.append(t3)
        return stat_toks

    def rstd_visit(c, stat_toks, mvx, rxc, rx_tok, tagp):
        deps = [tk for tk in stat_toks if tk] + [tk for tk in rx_tok if tk]
        t4 = vop(lambda v, mvx=mvx: v.tensor_scalar(
            out=rxc[:, 0:4],
            in0=mvx[:, 0:8].rearrange("p (t c) -> p t c", c=2)[:, :, 1],
            scalar1=EPS, scalar2=None, op0=ALU.add), deps, tag=f"{tagp}e{c}")
        t6 = newton_rsqrt(rxc[:, 0:4], rxc[:, 0:4], 4, [t4], f"{tagp}n{c}")
        for tt in range(4):
            rx_tok[tt] = t6
        return [t6] * 4

    def x1_compute(c, r2_toks):
        toks = []
        for tt in range(4):
            ai = 4 * (c % 2) + tt
            buf = x0c[(c % 2) * 4 + tt]
            t1 = vop(lambda v, ai=ai, tt=tt: v.tensor_scalar(
                out=ACC[ai][:, :], in0=ACC[ai][:, :], scalar1=mv2[:, 2 * tt:2 * tt + 1],
                scalar2=r2c[:, tt:tt + 1], op0=ALU.subtract, op1=ALU.mult),
                [acc_tok[ai], r2_toks[tt], acc_read[ai]], tag=f"x1n{c}")
            t2 = vop(lambda v, ai=ai: v.tensor_mul(ACC[ai][:, :], ACC[ai][:, :], gam_t[:, :]),
                     [t1] + gam_toks, tag=f"x1g{c}")
            t3 = vop(lambda v, ai=ai, b=buf: v.tensor_add(ACC[ai][:, :], ACC[ai][:, :], b[:, :]),
                     [t2], tag=f"x1x{c}")
            t4 = vop(lambda v, ai=ai: v.tensor_add(ACC[ai][:, :], ACC[ai][:, :], bet_t[:, :]),
                     [t3] + bet_toks, tag=f"x1f{c}")
            acc_tok[ai] = t4
            x0_free[(c % 2) * 4 + tt] = t3
            td = dma("sync", x1s_d[128 * (4 * c + tt):128 * (4 * c + tt) + 128, :],
                     ACC[ai][:, :], [t4], tag=f"x1d{c}")
            x1dma_tok[ai] = td
            toks.append(t4)
        return toks

    def ffn_T1(c, x1_toks, r3_toks):
        xn_toks = []
        for tt in range(4):
            ai = 4 * (c % 2) + tt
            buf = x0c[(c % 2) * 4 + tt]
            t1 = vop(lambda v, ai=ai, tt=tt, b=buf: v.tensor_scalar(
                out=b[:, :], in0=ACC[ai][:, :], scalar1=mv3[:, 2 * tt:2 * tt + 1],
                scalar2=r3c[:, tt:tt + 1], op0=ALU.subtract, op1=ALU.mult),
                [acc_tok[ai], r3_toks[tt], x0_free[(c % 2) * 4 + tt]], tag=f"xn{c}")
            x0_free[(c % 2) * 4 + tt] = t1
            xn_toks.append(t1)
        for kk in range(8):
            slots, rdeps = ring_alloc(1)
            sl = slots[0]
            last_t = None
            for tt in range(4):
                buf = x0c[(c % 2) * 4 + tt]
                deps = [xn_toks[tt], t_ident, x2T_read[kk], x2T_tok[kk]] + (rdeps if tt == 0 else [])
                last_t = tp_(rs(sl, 128, 128 * tt), buf[:, 128 * kk:128 * kk + 128],
                             deps, tag=f"x2T{c}")
                x0_free[(c % 2) * 4 + tt] = last_t
            ring_filled(slots, last_t)
            tc = vcopy(x2T_f[:, 512 * kk: 512 * kk + 512], rs(sl), [last_t], tag=f"x2Tc{c}")
            ring_used(slots, tc)
            x2T_tok[kk] = tc
        gelu_in = []
        for jo in range(2):
            slots, rdeps = ring_alloc(1)
            sl = slots[0]
            mmt = None
            for kk in range(8):
                deps = [x2T_tok[kk], cw1_toks[kk]] + (rdeps if kk == 0 else [])
                mmt = mm(rs(sl), cw1_f[:, 256 * kk + 128 * jo: 256 * kk + 128 * jo + 128],
                         x2T_f[:, 512 * kk: 512 * kk + 512], deps,
                         start=(kk == 0), stop=(kk == 7), tag=f"hmm{c}")
                x2T_read[kk] = mmt
            ring_filled(slots, mmt)
            gelu_in.append((sl, mmt, slots))
        A1, A2, A3, A4, A5 = (0.254829592, -0.284496736, 1.421413741,
                              -1.453152027, 1.061405429)
        PP = 0.3275911
        ISQ2 = 0.7071067811865476
        hT_toks = []
        for jo, (sl, mmt, slots) in enumerate(gelu_in):
            hp = hpre[:, :]
            tb = vop(lambda v, sl=sl, jo=jo: v.tensor_scalar(
                out=hp, in0=rs(sl), scalar1=cb1T[:, jo:jo + 1], scalar2=None,
                op0=ALU.add), [mmt, tok_cb1, gelu_busy[0]], tag=f"gb{c}")
            ring_used(slots, tb)
            Z = zsb[:, 0:512]
            tz = vop(lambda v: v.tensor_scalar(out=Z, in0=hp, scalar1=-1.0, scalar2=None,
                                               op0=ALU.mult),
                     [tb, zs_tok[0], zs_read[0]], tag=f"gz{c}")
            tz = vop(lambda v: v.tensor_tensor(out=Z, in0=Z, in1=hp, op=ALU.max),
                     [tz], tag=f"gz2{c}")
            tz = vop(lambda v: v.tensor_scalar(out=Z, in0=Z, scalar1=ISQ2, scalar2=None,
                                               op0=ALU.mult), [tz], tag=f"gz3{c}")
            DN = x1rb[0][:, 0:512]
            td = vop(lambda v: v.tensor_scalar(out=DN, in0=Z, scalar1=PP, scalar2=1.0,
                                               op0=ALU.mult, op1=ALU.add),
                     [tz, x1rb_read[0], x1rb_tok[0]], tag=f"gd{c}")
            td = vop(lambda v: v.reciprocal(DN, DN), [td], tag=f"gd2{c}")
            E = x1rb[0][:, 512:1024]
            te = vop(lambda v: v.tensor_mul(E, Z, Z), [tz, td], tag=f"gsq{c}")
            te = sop(lambda s: s.activation(out=E, in_=E, func=AF.Exp, scale=-1.0),
                     [te], tag=f"gex{c}")
            tq = vop(lambda v: v.tensor_scalar(out=Z, in0=DN, scalar1=0.5 * A5,
                                               scalar2=0.5 * A4, op0=ALU.mult, op1=ALU.add),
                     [te], tag=f"gp1{c}")
            for aa in (A3, A2, A1):
                tq = vop(lambda v: v.tensor_mul(Z, Z, DN), [tq], tag=f"gp2{c}")
                tq = vop(lambda v, aa=aa: v.tensor_single_scalar(out=Z, in_=Z,
                                                                 scalar=0.5 * aa, op=ALU.add),
                         [tq], tag=f"gp3{c}")
            tq = vop(lambda v: v.tensor_mul(Z, Z, DN), [tq], tag=f"gp4{c}")
            tq = vop(lambda v: v.tensor_mul(Z, Z, E), [tq], tag=f"gp5{c}")
            tm = vop(lambda v: v.tensor_scalar(out=DN, in0=hp, scalar1=0.0, scalar2=None,
                                               op0=ALU.is_ge), [tq], tag=f"gm{c}")
            tm2 = vop(lambda v: v.tensor_scalar(out=E, in0=DN, scalar1=-2.0, scalar2=1.0,
                                                op0=ALU.mult, op1=ALU.add), [tm], tag=f"gm2{c}")
            tphi = vop(lambda v: v.tensor_mul(Z, Z, E), [tm2], tag=f"gf1{c}")
            tphi = vop(lambda v: v.tensor_add(Z, Z, DN), [tphi], tag=f"gf2{c}")
            tg = vop(lambda v, jo=jo: v.tensor_mul(hTb[:, 512 * jo: 512 * jo + 512], hp, Z),
                     [tphi, hT_tok[jo], hT_read[jo]], tag=f"gelu{c}")
            gelu_busy[0] = tg
            zs_tok[0] = tg
            x1rb_tok[0] = tg
            x1rb_read[0] = tg
            hT_tok[jo] = tg
            hT_toks.append(tg)
        return xn_toks, hT_toks

    def ffn_T2(c, hT_toks):
        slots, rdeps = ring_alloc(1)
        slc = slots[0]
        mmt = None
        for kk in range(2):
            deps = [hT_toks[kk], tok_cw2] + (rdeps if kk == 0 else [])
            mmt = mm(ps_ring[0:2, 512 * slc: 512 * slc + 512],
                     cw2_b[:, 2 * kk: 2 * kk + 2], hTb[:, 512 * kk: 512 * kk + 512],
                     deps, start=(kk == 0), stop=(kk == 1), tag=f"cmm{c}")
            hT_read[kk] = mmt
        ring_filled(slots, mmt)
        tv1 = vop(lambda v: v.tensor_scalar(out=ctmp[:, :], in0=ps_ring[0:2, 512 * slc: 512 * slc + 512],
                                            scalar1=cb2_s[:, 0:1], scalar2=None, op0=ALU.add),
                  [mmt, tok_cb2, ctmp_read[0]], tag=f"cb{c}")
        ring_used(slots, tv1)
        te = sop(lambda s: s.activation(out=ctmp[:, :], in_=ctmp[:, :], func=AF.Exp, scale=2.0),
                 [tv1], tag=f"ctanh{c}")
        tv2 = vop(lambda v: v.tensor_single_scalar(out=ctmp[:, :], in_=ctmp[:, :], scalar=1.0,
                                                   op=ALU.add), [te], tag=f"ct1{c}")
        tv3 = vop(lambda v: v.reciprocal(ctmp[:, :], ctmp[:, :]), [tv2], tag=f"ct2{c}")
        tv4 = vop(lambda v: v.tensor_scalar(out=ctmp[:, :], in0=ctmp[:, :], scalar1=-2.0,
                                            scalar2=1.0, op0=ALU.mult, op1=ALU.add),
                  [tv3], tag=f"ct3{c}")
        ctv4_hold[0] = tv4
        for tt in range(4):
            slots, rdeps = ring_alloc(1)
            sl = slots[0]
            mmt = None
            for kk in range(8):
                deps = [x2T_tok[kk]] + (rdeps if kk == 0 else [])
                mmt = mm(rs(sl, 72), x2T_f[:, 512 * kk + 128 * tt: 512 * kk + 128 * tt + 128],
                         sgt[:, 72 * kk: 72 * kk + 72], deps,
                         start=(kk == 0), stop=(kk == 7), tag=f"gmm{c}")
                x2T_read[kk] = mmt
            ring_filled(slots, mmt)
            tev = vcopy(gates[:, 72 * tt: 72 * tt + 72], rs(sl, 72), [mmt, gates_tok[tt]],
                        tag=f"gev{c}")
            ring_used(slots, tev)
            gates_tok[tt] = tev

    def compt_stage(c):
        tv4 = ctv4_hold[0]
        slots, rdeps = ring_alloc(1)
        slt = slots[0]
        ctoks = []
        for tt in range(4):
            deps = [tv4, t_ident] + (rdeps if tt == 0 else [])
            ttk = tp_(rs(slt, 2, 128 * tt), ctmp[:, 128 * tt: 128 * tt + 128], deps,
                      tag=f"cT{c}", k=2)
            ctoks.append(ttk)
            ctmp_read[0] = ttk
        ring_filled(slots, ctoks[-1])
        cc = vop(lambda v, slt=slt: v.tensor_copy(
            compt[:, :].rearrange("p (t c) -> p t c", t=4),
            rs(slt, 512).rearrange("p (t c) -> p t c", c=128)[:, :, 0:2]),
            [ctoks[-1]] + [compt_tok[t] for t in range(4)], tag=f"ccp{c}")
        ring_used(slots, cc)
        for tt in range(4):
            compt_tok[tt] = cc

    def gates_exp(c):
        toks = []
        for tt in range(4):
            te = sop(lambda s, tt=tt: s.activation(
                out=gates[:, 72 * tt: 72 * tt + 72], in_=gates[:, 72 * tt: 72 * tt + 72],
                func=AF.Exp, scale=scap(SC_NINV)), [gates_tok[tt], t_scal], tag=f"gex{c}")
            gates_tok[tt] = te
            toks.append(te)
        return toks

    def calibrate_argmax(c, ge_toks):
        tidx_toks = []
        for tt in range(4):
            g_ap = gates[:, 72 * tt: 72 * tt + 72]
            t1 = vop(lambda v, g=g_ap: v.tensor_single_scalar(out=calv[:, :], in_=g,
                                                              scalar=1.0, op=ALU.add),
                     [ge_toks[tt]], tag=f"ca1{c}")
            t2 = vop(lambda v: v.reciprocal(calv[:, :], calv[:, :]), [t1], tag=f"ca2{c}")
            t3 = vop(lambda v: v.tensor_scalar(out=calv[:, :], in0=calv[:, :], scalar1=7.0,
                                               scalar2=None, op0=ALU.mult), [t2], tag=f"ca3{c}")
            tacc = vop(lambda v: v.memset(cal2[:, :], 0.0), [t3], tag=f"ca4{c}")
            for j in range(7):
                ta = vop(lambda v, j=j: v.tensor_scalar(out=calt[:, :], in0=calv[:, :],
                                                        scalar1=float(j), scalar2=0.0,
                                                        op0=ALU.subtract, op1=ALU.max),
                         [tacc if j == 0 else tb2], tag=f"cj1{c}")
                tb = vop(lambda v, j=j: v.tensor_scalar(out=calt[:, :], in0=calt[:, :],
                                                        scalar1=1.0, scalar2=scap(SC_D0 + j),
                                                        op0=ALU.min, op1=ALU.mult),
                         [ta, t_scal], tag=f"cj2{c}")
                tb2 = vop(lambda v: v.tensor_add(cal2[:, :], cal2[:, :], calt[:, :]),
                          [tb], tag=f"cj3{c}")
            tk0 = vop(lambda v: v.tensor_scalar(out=cal2[:, :], in0=cal2[:, :],
                                                scalar1=scap(SC_K0), scalar2=None, op0=ALU.add),
                      [tb2, t_scal], tag=f"ck0{c}")
            m1 = vop(lambda v: v.reduce_max(idxs[:, 0:1], cal2[:, 64:72], axis=AX.X),
                     [tk0], tag=f"am1{c}")
            m2 = vop(lambda v: v.tensor_scalar(out=calt[:, 0:8], in0=cal2[:, 64:72],
                                               scalar1=idxs[:, 0:1], scalar2=None,
                                               op0=ALU.is_ge), [m1], tag=f"am2{c}")
            m3 = vop(lambda v: v.tensor_mul(calt[:, 8:16], calt[:, 0:8], bigmi[:, :]),
                     [m2, t_bigmi_i], tag=f"am3{c}")
            m4 = vop(lambda v: v.tensor_scalar(out=calt[:, 8:16], in0=calt[:, 8:16],
                                               scalar1=-1.0, scalar2=999.0,
                                               op0=ALU.mult, op1=ALU.add), [m3], tag=f"am4{c}")
            m5 = vop(lambda v: v.tensor_reduce(idxs[:, 1:2], calt[:, 8:16], axis=AX.X,
                                               op=ALU.min), [m4], tag=f"am5{c}")
            w1 = vop(lambda v: v.reduce_max(calt[:, 0:8],
                                            cal2[:, 0:64].rearrange("p (c t) -> p c t", t=8),
                                            axis=AX.X), [tk0], tag=f"aw1{c}")
            w2 = vop(lambda v: v.tensor_tensor(
                out=calv[:, 0:64].rearrange("p (c t) -> p c t", t=8),
                in0=cal2[:, 0:64].rearrange("p (c t) -> p c t", t=8),
                in1=calt[:, 0:8].rearrange("p c -> p c 1" if False else "p (c o) -> p c o", o=1).to_broadcast([128, 8, 8]),
                op=ALU.is_ge), [w1], tag=f"aw2{c}")
            w3 = vop(lambda v: v.tensor_tensor(
                out=calv[:, 0:64].rearrange("p (c t) -> p c t", t=8),
                in0=calv[:, 0:64].rearrange("p (c t) -> p c t", t=8),
                in1=bigmi[:, :].rearrange("p (c o) -> p c o" if False else "p t -> p 1 t" if False else "p (o t) -> p o t", o=1).to_broadcast([128, 8, 8]),
                op=ALU.mult), [w2, t_bigmi_i], tag=f"aw3{c}")
            w4 = vop(lambda v: v.tensor_scalar(out=calv[:, 0:64], in0=calv[:, 0:64],
                                               scalar1=-1.0, scalar2=999.0,
                                               op0=ALU.mult, op1=ALU.add), [w3], tag=f"aw4{c}")
            w5 = vop(lambda v: v.tensor_reduce(calt[:, 16:24],
                                               calv[:, 0:64].rearrange("p (c t) -> p c t", t=8),
                                               axis=AX.X, op=ALU.min), [w4], tag=f"aw5{c}")
            s1 = vop(lambda v: v.tensor_scalar(out=calt[:, 24:32], in0=iotaj[:, 0:8],
                                               scalar1=idxs[:, 1:2], scalar2=None,
                                               op0=ALU.is_equal), [m5, t_ioj], tag=f"as1{c}")
            s2 = vop(lambda v: v.tensor_mul(calt[:, 24:32], calt[:, 24:32], calt[:, 16:24]),
                     [s1, w5], tag=f"as2{c}")
            s3 = vop(lambda v: v.reduce_sum(idxs[:, 2:3], calt[:, 24:32], axis=AX.X),
                     [s2], tag=f"as3{c}")
            s4 = vop(lambda v, tt=tt: v.tensor_scalar(out=idxs[:, 3 + tt:4 + tt], in0=idxs[:, 1:2],
                                                      scalar1=8.0, scalar2=idxs[:, 2:3],
                                                      op0=ALU.mult, op1=ALU.add),
                     [s3, idx_read[tt]], tag=f"as4{c}")
            tidx_toks.append(s4)
        return tidx_toks

    def gathers(c, tidx_toks):
        t2s = []
        for tt in range(4):
            slots, rdeps = ring_alloc(1)
            sl = slots[0]
            t1 = tp_(ps_ring[0:1, 512 * sl: 512 * sl + 128], idxs[:, 3 + tt:4 + tt],
                     [tidx_toks[tt], t_ident] + rdeps, tag=f"tiT{c}")
            idx_read[tt] = t1
            ring_filled([sl], t1)
            t2 = vcopy(tidxr[:, 128 * tt: 128 * tt + 128],
                       ps_ring[0:1, 512 * sl: 512 * sl + 128],
                       [t1, tidxr_read[tt]], tag=f"tir{c}")
            ring_used([sl], t2)
            t2s.append(t2)
        t4s = []
        for tt in range(4):
            ohb = ohbufs[tt]
            slots2, rdeps2 = ring_alloc(1)
            sl2 = slots2[0]
            t3 = mm(ps_ring[0:64, 512 * sl2: 512 * sl2 + 128], onesrow[:, 0:64],
                    tidxr[:, 128 * tt: 128 * tt + 128],
                    [t2s[tt], t_ones_r] + rdeps2, tag=f"tibc{c}")
            tidxr_read[tt] = t3
            ring_filled(slots2, t3)
            t4 = vop(lambda v, sl2=sl2, ohb=ohb: v.tensor_scalar(
                out=ohb[:, :], in0=ps_ring[0:64, 512 * sl2: 512 * sl2 + 128],
                scalar1=iota64[:, :], scalar2=None, op0=ALU.is_equal),
                [t3, t_io64, oh_tok[tt], oh_read[tt]], tag=f"ioh{c}")
            ring_used(slots2, t4)
            oh_tok[tt] = t4
            t4s.append(t4)

        def emit_tile(tt):
            ohb = ohbufs[tt]
            slots3, rdeps3 = ring_alloc(3)
            g1 = mm(rs(slots3[0], 65), ohb[:, :], spl_b[:, :], [t4s[tt], tok_spl] + rdeps3,
                    tag=f"gsp{c}")
            g2 = mm(rs(slots3[1]), ohb[:, :], dirs_b[:, 0:512], [t4s[tt], tok_dirs],
                    tag=f"gd1{c}")
            g3 = mm(rs(slots3[2]), ohb[:, :], dirs_b[:, 512:1024], [t4s[tt], tok_dirs],
                    tag=f"gd2{c}")
            ring_filled(slots3, g3)
            oh_read[tt] = g3
            return (slots3, g1, g2, g3)
        return emit_tile

    def spline_y(c, gth_fn):
        ysts = []
        for tt in range(4):
            slots3, g1, g2, g3 = gth_fn(tt)
            ai = 4 * (c % 2) + tt
            e1 = vcopy(splg[:, :], rs(slots3[0], 65), [g1], tag=f"spc{c}")
            h1 = vop(lambda v, tt=tt: v.tensor_scalar(out=hhs[:, 0:16], in0=splg[:, 0:16],
                                               scalar1=compt[:, 2 * tt: 2 * tt + 1],
                                               scalar2=None, op0=ALU.mult),
                     [e1, compt_tok[tt]], tag=f"sp1{c}")
            h2 = vop(lambda v, tt=tt: v.scalar_tensor_tensor(out=hhs[:, 16:32], in0=splg[:, 16:32],
                                                      scalar=compt[:, 2 * tt + 1: 2 * tt + 2],
                                                      in1=hhs[:, 0:16], op0=ALU.mult,
                                                      op1=ALU.add), [h1], tag=f"sp2{c}")
            h3 = vop(lambda v: v.tensor_add(hhs[:, 32:48], hhs[:, 16:32], splg[:, 32:48]),
                     [h2], tag=f"sp3{c}")
            h4 = vop(lambda v: v.tensor_scalar(out=hhs[:, 32:48], in0=hhs[:, 32:48],
                                               scalar1=0.0, scalar2=None, op0=ALU.max),
                     [h3], tag=f"sp4{c}")
            h5 = vop(lambda v: v.tensor_mul(hhs[:, 0:16], hhs[:, 32:48], splg[:, 48:64]),
                     [h4], tag=f"sp5{c}")
            h6 = vop(lambda v, tt=tt: v.reduce_sum(mags[:, tt:tt + 1], hhs[:, 0:16], axis=AX.X),
                     [h5, mags_tok[tt]], tag=f"sp6{c}")
            h7 = vop(lambda v, tt=tt: v.tensor_add(mags[:, tt:tt + 1], mags[:, tt:tt + 1],
                                            splg[:, 64:65]), [h6], tag=f"sp7{c}")
            h8 = vop(lambda v, tt=tt: v.tensor_scalar(out=mags[:, tt:tt + 1], in0=mags[:, tt:tt + 1],
                                               scalar1=scap(SC_OS), scalar2=None,
                                               op0=ALU.mult), [h7, t_scal], tag=f"sp8{c}")
            mags_tok[tt] = h8
            y1 = vop(lambda v, ai=ai, tt=tt: v.tensor_scalar(
                out=ybuf[tt][:, :], in0=ACC[ai][:, :], scalar1=mv3[:, 2 * tt:2 * tt + 1],
                scalar2=r3c[:, tt:tt + 1], op0=ALU.subtract, op1=ALU.mult),
                [acc_tok[ai], p2[c]["r3"][tt], ybuf_read[tt], ybuf_tok[tt]], tag=f"y1{c}")
            y2 = vop(lambda v, tt=tt, s=slots3[1]: v.scalar_tensor_tensor(
                out=ybuf[tt][:, 0:512], in0=rs(s), scalar=mags[:, tt:tt + 1],
                in1=ybuf[tt][:, 0:512], op0=ALU.mult, op1=ALU.add), [y1, h8, g2], tag=f"y2{c}")
            y3 = vop(lambda v, tt=tt, s=slots3[2]: v.scalar_tensor_tensor(
                out=ybuf[tt][:, 512:1024], in0=rs(s), scalar=mags[:, tt:tt + 1],
                in1=ybuf[tt][:, 512:1024], op0=ALU.mult, op1=ALU.add), [y2, g3], tag=f"y3{c}")
            ring_used(slots3, y3)
            ybuf_tok[tt] = y3
            acc_read[ai] = y1
            s1 = vop(lambda v, tt=tt: v.bn_stats(st6[:, 0:6], ybuf[tt][:, 0:512]), [y3],
                     tag=f"ys1{c}")
            s2 = vop(lambda v, tt=tt: v.bn_stats(st6[:, 6:12], ybuf[tt][:, 512:1024]), [y3],
                     tag=f"ys2{c}")
            s3 = vop(lambda v, tt=tt: v.bn_aggr(mvy[:, 2 * tt:2 * tt + 2], st6[:, :]),
                     [s1, s2, mvy_tok[tt]], tag=f"ys3{c}")
            mvy_tok[tt] = s3
            ysts.append(s3)
        return ysts

    def out_assembly(c, ry_toks):
        for tt in range(4):
            row = 128 * (4 * c + tt)
            tok_x1 = dma("sync", x1rb[0][:, :], x1s_d[row:row + 128, :],
                         [x1rb_read[0], x1dma_tok[4 * (c % 2) + tt]], tag=f"x1r{c}")
            x1rb_tok[0] = tok_x1
            o1 = vop(lambda v, tt=tt: v.tensor_scalar(
                out=ybuf[tt][:, :], in0=ybuf[tt][:, :], scalar1=mvy[:, 2 * tt:2 * tt + 1],
                scalar2=ryc[:, tt:tt + 1], op0=ALU.subtract, op1=ALU.mult),
                [ybuf_tok[tt], ry_toks[tt]], tag=f"o1{c}")
            o2 = vop(lambda v, tt=tt: v.tensor_mul(ybuf[tt][:, :], ybuf[tt][:, :], ngam_t[:, :]),
                     [o1] + ngam_toks, tag=f"o2{c}")
            o3 = vop(lambda v, tt=tt: v.tensor_add(ybuf[tt][:, :], ybuf[tt][:, :],
                                                   x1rb[0][:, :]), [o2, tok_x1], tag=f"o3{c}")
            x1rb_read[0] = o3
            o4 = vop(lambda v, tt=tt: v.tensor_add(ybuf[tt][:, :], ybuf[tt][:, :], nbet_t[:, :]),
                     [o3] + nbet_toks, tag=f"o4{c}")
            ybuf_tok[tt] = o4
            od = dma("sync", out_d[row:row + 128, :], ybuf[tt][:, :], [o4], tag=f"od{c}")
            ybuf_read[tt] = od
            outdma_toks.append(od)

    dbg_state = {}

    def dbg_dump(i, ap, deps, w=D):
        if dbg_d is None:
            return None
        return dma("gpsimd", dbg_d[i, :, 0:w], ap, deps, tag=f"dbg{i}")

    zrb_read = [None, None]
    p2 = {}

    def p2_m0(ch):
        st = p2.setdefault(ch, {})
        a_toks = a_init(ch)
        ztoks = []
        for tt in range(4):
            ai = 4 * (ch % 2) + tt
            zi = tt % 2
            tl = dma("sync", zrb[zi][:, :],
                     zs_d[512 * ch + 128 * tt: 512 * ch + 128 * tt + 128, :],
                     [zrb_read[zi]] + zdma_by_chunk[ch], tag=f"zl{ch}")
            tadd = vop(lambda v, ai=ai, zi=zi: v.tensor_add(
                ACC[ai][:, :], ACC[ai][:, :], zrb[zi][:, :]),
                [tl, a_toks[tt]], tag=f"zadd{ch}")
            zrb_read[zi] = tadd
            acc_tok[ai] = tadd
            ztoks.append(tadd)
        st["ztoks"] = ztoks

    def p2_m1(ch):
        st = p2[ch]
        stat2 = ln2_x1(ch, st["ztoks"])
        st["r2"] = rstd_visit(ch, stat2, mv2, r2c, r2_tok, "r2")

    def p2_m2(ch):
        st = p2[ch]
        x1_toks = x1_compute(ch, st["r2"])
        stat3 = []
        for tt in range(4):
            ai = 4 * (ch % 2) + tt
            t1 = vop(lambda v, ai=ai: v.bn_stats(st6[:, 0:6], ACC[ai][:, 0:512]),
                     [x1_toks[tt]], tag=f"l3a{ch}")
            t2 = vop(lambda v, ai=ai: v.bn_stats(st6[:, 6:12], ACC[ai][:, 512:1024]),
                     [x1_toks[tt]], tag=f"l3b{ch}")
            t3 = vop(lambda v, tt=tt: v.bn_aggr(mv3[:, 2 * tt:2 * tt + 2], st6[:, :]),
                     [t1, t2, mv3_tok[tt]], tag=f"l3c{ch}")
            mv3_tok[tt] = t3
            stat3.append(t3)
        st["r3"] = rstd_visit(ch, stat3, mv3, r3c, r3_tok, "r3")
        st["x1_toks"] = x1_toks

    def p2_m3(ch):
        st = p2[ch]
        _, st["hT_toks"] = ffn_T1(ch, st["x1_toks"], st["r3"])

    def p2_m3b(ch):
        st = p2[ch]
        ffn_T2(ch, st["hT_toks"])

    def p2_m4(ch):
        st = p2[ch]
        if "ffntail" not in skip:
            compt_stage(ch)
            st["tidx"] = calibrate_argmax(ch, gates_exp(ch))
        else:
            st["tidx"] = [None] * 4

    def p2_m5(ch):
        st = p2[ch]
        if "ffntail" not in skip:
            gth = gathers(ch, st["tidx"])
            st["ysts"] = spline_y(ch, gth)
        else:
            st["ysts"] = [None] * 4

    def p2_m6(ch):
        st = p2[ch]
        st["ry"] = rstd_visit(ch, st["ysts"], mvy, ryc, ry_tok, "ry")

    def p2_m7(ch):
        st = p2[ch]
        out_assembly(ch, st["ry"])
        del p2[ch]

    phase_queue = []

    def enqueue_chunk(ch):
        q = [lambda ch=ch: p2_m0(ch), lambda ch=ch: p2_m1(ch)]
        if ch >= 1:
            q.append(lambda ch=ch: p2_m5(ch - 1))
        q.append(lambda ch=ch: p2_m2(ch))
        if ch >= 1:
            q.append(lambda ch=ch: p2_m6(ch - 1))
        q.append(lambda ch=ch: p2_m3(ch))
        if ch >= 1:
            q.append(lambda ch=ch: p2_m7(ch - 1))
        q.append(lambda ch=ch: p2_m3b(ch))
        q.append(lambda ch=ch: p2_m4(ch))
        phase_queue.extend(q)

    ppc = NPAIR // CH if CH <= NPAIR else 1

    sched_by_c = [tile_schedule(c) for c in range(CH)]
    for p in range(NPAIR):
        if p == 0:
            kvq_load(0)
        for ci, c in enumerate(range(CH)):
            if "pairs" not in skip:
                pv_last, sg_idx = emit_pair(c, p, sched_by_c[c])
                if ci == 0 and p + 1 < NPAIR:
                    kvq_load(p + 1)
                if phase_queue:
                    phase_queue.pop(0)()
                    if ci == CH - 1 and len(phase_queue) > 1:
                        phase_queue.pop(0)()
                zt = z_evac(c, p, pv_last, sg_idx)
                if c == 0 and p == 0:
                    dbg_dump(1, zsb[:, 0:512], [zs_tok[0]], w=512)
            else:
                if phase_queue:
                    phase_queue.pop(0)()
        if (p + 1) % ppc == 0:
            enqueue_chunk((p + 1) // ppc - 1)

    while phase_queue:
        phase_queue.pop(0)()
    p2_m5(CH - 1)
    p2_m6(CH - 1)
    p2_m7(CH - 1)

    fin_deps = [outdma_toks[-1]]
    for sname in DMAPOOL["sync"] + DMAPOOL["gpsimd"]:
        cc = P.count.get(sname, 0)
        if cc:
            fin_deps.append((sname, cc))
    P.op("gpsimd", lambda g: g.nop(), fin_deps, tag="fin")

    import os
    nopify = set(os.environ.get("NOPIFY", "").split(",")) - {""}
    noptags = set(os.environ.get("NOPTAGS", "").split(",")) - {""}
    refd = {}
    for e2 in ENGINES:
        for (fn, waits, sem, amt, tag) in P.stream[e2]:
            for (s, v) in waits:
                refd.setdefault(s, set()).add(v)
    remap = {}
    for e2 in ENGINES:
        if os.environ.get("NO_SEMCOMP", "") == "1":
            break
        s = ESEM[e2]
        total = P.count.get(s, 0)
        if not total:
            continue
        rset = refd.get(s, set())
        m = {}
        nci = 0
        for k in range(1, total + 1):
            if k in rset or k == total:
                nci += 1
                m[k] = (nci, True)
            else:
                m[k] = (nci, False)
        remap[s] = m
    with nc.Block() as block:
        def emit(eng_name):
            def body(eng):
                own = ESEM[eng_name]
                for (fn, waits, sem, amt, tag) in P.stream[eng_name]:
                    for (s, v) in waits:
                        if s in remap:
                            v = remap[s][v][0]
                        eng.wait_ge(semh[s], v)
                    tagnop = any(tag.startswith(tp) for tp in noptags) if noptags else False
                    if (eng_name in nopify or tagnop) and amt == 1:
                        inst = eng.nop()
                    else:
                        inst = fn(eng)
                    if sem in remap:
                        cnt, do_inc = remap[sem][P2_cursor[sem]]
                        P2_cursor[sem] += 1
                        if do_inc:
                            inst.then_inc(semh[sem], amt)
                    else:
                        inst.then_inc(semh[sem], amt)
            return body
        P2_cursor = {s: 1 for s in remap}
        block.sync(emit("sync"))
        block.gpsimd(emit("gpsimd"))
        block.tensor(emit("tensor"))
        block.scalar(emit("scalar"))
        block.vector(emit("vector"))

    es.close()
    return nc


_prog_cache = {}


def _get_prog(S):
    if S not in _prog_cache:
        _prog_cache[S] = build_program(S)
    return _prog_cache[S]


def make_in_map(core_inputs):
    return core_inputs


def kernel(**inputs):
    emb = np.ascontiguousarray(np.asarray(inputs["L0_attn_embedded"], np.float32))
    B, S, _ = emb.shape
    f = lambda k: np.ascontiguousarray(np.asarray(inputs[k], np.float32))
    wqT = np.ascontiguousarray(f("L0_attn_Wq").T)
    wkT = np.ascontiguousarray(f("L0_attn_Wk").T)
    wvT = np.ascontiguousarray(f("L0_attn_Wv").T)
    bqT = np.ascontiguousarray(f("L0_attn_bq").reshape(8, 128).T)
    bkT = np.ascontiguousarray(f("L0_attn_bk").reshape(8, 128).T)
    bvT = np.ascontiguousarray(f("L0_attn_bv").reshape(8, 128).T)
    gamma = f("L0_attn_gamma").reshape(1, D)
    beta = f("L0_attn_beta").reshape(1, D)
    ngamma = f("L0_ffn_norm_gamma").reshape(1, D)
    nbeta = f("L0_ffn_norm_beta").reshape(1, D)
    sigT = np.ascontiguousarray(f("L0_ffn_signatures_raw").T)
    knots = f("L0_ffn_knot_values").reshape(1, 8)
    temp = f("L0_ffn_temperature").reshape(1, 1)
    oscale = f("L0_ffn_output_scale").reshape(1, 1)
    cw1 = f("L0_ffn_compress_w1")
    cb1T = np.ascontiguousarray(f("L0_ffn_compress_b1").reshape(2, 128).T)
    cw2 = np.ascontiguousarray(
        f("L0_ffn_compress_w2").reshape(2, 128, 2).transpose(1, 0, 2).reshape(128, 4))
    cb2 = f("L0_ffn_compress_b2").reshape(2, 1)
    spl = np.ascontiguousarray(np.concatenate([
        f("L0_ffn_spline_w1").reshape(64, 32),
        f("L0_ffn_spline_b1").reshape(64, 16),
        f("L0_ffn_spline_w2").reshape(64, 16),
        f("L0_ffn_spline_b2").reshape(64, 1)], axis=1))
    dirs = f("L0_ffn_directions")

    shared = dict(wqT=wqT, wkT=wkT, wvT=wvT, bqT=bqT, bkT=bkT, bvT=bvT,
                  gamma=gamma, beta=beta, ngamma=ngamma, nbeta=nbeta, sigT=sigT,
                  knots=knots, temp=temp, oscale=oscale, cw1=cw1, cb1T=cb1T,
                  cw2r=cw2, cb2=cb2, spl=spl, dirs=dirs)
    in_maps = []
    for b in range(B):
        m = dict(shared)
        m["x0"] = np.ascontiguousarray(emb[b])
        in_maps.append(m)
    nc = _get_prog(S)
    res = run_bass_kernel_spmd(nc, in_maps, core_ids=list(range(B)))
    out = np.stack([np.asarray(r["out"], np.float32) for r in res.results], axis=0)
    return out


# version 11
# speedup vs baseline: 1.1125x; 1.1125x over previous
import numpy as np
import ml_dtypes
import concourse.bass as bass
import concourse.mybir as mybir
from concourse.bass_utils import run_bass_kernel_spmd
from contextlib import ExitStack

F32 = mybir.dt.float32
BF16 = mybir.dt.bfloat16
AF = mybir.ActivationFunctionType
ALU = mybir.AluOpType
AX = mybir.AxisListType

D = 1024
NPAIR = 8
EPS = 1e-5

ENGINES = ("sync", "gpsimd", "tensor", "scalar", "vector")
ESEM = {"sync": "eS", "gpsimd": "eG", "tensor": "eT", "scalar": "eA", "vector": "eV"}
import os as _os_se
SIM_EDGES = _os_se.environ.get("SIM_EDGES", "") == "1"
NDMASEM = 20
DMAPOOL = {"sync": [f"dS{i}" for i in range(NDMASEM)],
           "gpsimd": [f"dG{i}" for i in range(NDMASEM)]}


class Planner:
    def __init__(self):
        self.stream = {e: [] for e in ENGINES}
        self.count = {}
        self.snap = {}
        self.W = {e: {} for e in ENGINES}
        self.nwaits = 0
        self.dma_rr = {}

    def _merge_into(self, w, tok):
        s, v = tok
        snap = self.snap.get((s, v))
        if snap is None:
            if w.get(s, -1) < v:
                w[s] = v
            return
        for k2, v2 in snap.items():
            if w.get(k2, -1) < v2:
                w[k2] = v2

    def op(self, eng, fn, deps=(), dma=False, tag=""):
        w = self.W[eng]
        deps = list(deps)
        if dma:
            pool = DMAPOOL[eng]
            i = self.dma_rr.get(eng, 0)
            self.dma_rr[eng] = i + 1
            sem = pool[i % NDMASEM]
            prevc = self.count.get(sem, 0)
            if prevc > 0:
                deps.append((sem, prevc))
            amt = 16
        else:
            sem = ESEM[eng]
            amt = 1
            prev = self.count.get(sem, 0)
            if prev > 0:
                deps.insert(0, (sem, prev))
        waits = []
        for tok in deps:
            if tok is None:
                continue
            s, v = tok
            if w.get(s, -1) >= v:
                continue
            waits.append((s, v))
            self._merge_into(w, tok)
        c = self.count.get(sem, 0) + amt
        self.count[sem] = c
        snapd = dict(w)
        snapd[sem] = max(snapd.get(sem, 0), c)
        self.snap[(sem, c)] = snapd
        self.stream[eng].append((fn, waits, sem, amt, tag))
        self.nwaits += len(waits)
        return (sem, c)


def build_program(S, skip=()):
    NT = S // 128
    CH = S // 512
    import os as _os0
    _scr = 16384 if _os0.environ.get("DEVDBG", "") == "1" else 2048
    nc = bass.Bass("TRN2", target_bir_lowering=False, dynamic_dma_scratch_size=_scr)
    P = Planner()

    def din(name, shape):
        return nc.dram_tensor(name, list(shape), F32, kind="ExternalInput").ap()

    x0_d = din("x0", (S, D))
    wqT_d = din("wqT", (D, D)); wkT_d = din("wkT", (D, D)); wvT_d = din("wvT", (D, D))
    bqT_d = din("bqT", (128, 8)); bkT_d = din("bkT", (128, 8)); bvT_d = din("bvT", (128, 8))
    gam_d = din("gamma", (1, D)); bet_d = din("beta", (1, D))
    ngam_d = din("ngamma", (1, D)); nbet_d = din("nbeta", (1, D))
    sigT_d = din("sigT", (D, 64))
    knots_d = din("knots", (1, 8)); temp_d = din("temp", (1, 1)); oscale_d = din("oscale", (1, 1))
    cw1_d = din("cw1", (D, 256)); cb1T_d = din("cb1T", (128, 2))
    cw2_d = din("cw2r", (128, 4)); cb2_d = din("cb2", (2, 1))
    spl_d = din("spl", (64, 65)); dirs_d = din("dirs", (64, D))
    out_d = nc.dram_tensor("out", [S, D], F32, kind="ExternalOutput").ap()
    import os as _os
    _dbg = _os.environ.get("DEVDBG", "") in ("1", "2")
    _dbg1 = _os.environ.get("DEVDBG", "") == "1"
    _ik = "ExternalOutput" if _dbg else "Internal"
    kts_d = nc.dram_tensor("kts", [NPAIR, 128, S], BF16, kind=_ik).ap()
    qts_d = nc.dram_tensor("qts", [NPAIR, 128, S], BF16, kind="Internal").ap()
    aTs_d = nc.dram_tensor("aTs", [8, 128, S], BF16, kind="Internal").ap()
    vs_d = nc.dram_tensor("vs", [NPAIR, NT, 128, 130], BF16, kind=_ik).ap()
    x1s_d = nc.dram_tensor("x1s", [S, D], F32, kind=_ik).ap()
    zs_d = nc.dram_tensor("zs", [S, D], F32, kind=_ik).ap()
    dbg_d = nc.dram_tensor("dbg", [8, 128, D], F32, kind="ExternalOutput").ap() if _dbg1 else None

    es = ExitStack()

    def sb(name, shape, dtype):
        return es.enter_context(nc.sbuf_tensor(name, list(shape), dtype))

    ident = sb("ident", (128, 128), F32)
    diagmask = sb("diagmask", (128, 128), BF16)
    dgf = sb("dgf", (128, 128), F32)
    onescol_b = sb("onescol_b", (128, 1), BF16)
    onesrow = sb("onesrow", (1, 128), F32)
    iota64 = sb("iota64", (64, 1), F32)
    iotaj = sb("iotaj", (128, 72), F32)
    bigmi = sb("bigmi", (128, 8), F32)
    scal = sb("scal", (128, 16), F32)
    scrow = sb("scrow", (1, 16), F32)
    knots_s = sb("knots_s", (1, 8), F32)
    temp_s = sb("temp_s", (1, 4), F32)
    gam_t = sb("gam_t", (128, D), F32); bet_t = sb("bet_t", (128, D), F32)
    ngam_t = sb("ngam_t", (128, D), F32); nbet_t = sb("nbet_t", (128, D), F32)
    bqT = sb("bqT_s", (128, 8), F32); bkT = sb("bkT_s", (128, 8), F32); bvT = sb("bvT_s", (128, 8), F32)
    cb1T = sb("cb1T_s", (128, 2), F32)
    cb2_s = sb("cb2_s", (2, 1), F32)
    mu1 = sb("mu1", (128, NT), F32); r1 = sb("r1", (128, NT), F32)
    mvt = sb("mvt", (128, 2), F32)
    st6 = sb("st6", (128, 12), F32)
    sgt = sb("sgt", (128, 8 * 72), F32)
    x0c = [sb(f"x0c{i}", (128, D), F32) for i in range(8)]
    ACC = [sb(f"ACC{i}", (128, D), F32) for i in range(8)]
    aT = sb("aT", (128, 8 * 512), BF16)
    qT = sb("qT", (128, 8 * 512), BF16)
    kbuf = [sb(f"kbuf{i}", (128, S), BF16) for i in range(2)]
    vbuf = [sb(f"vbuf{i}", (128, NT * 130), BF16) for i in range(2)]
    pTb = [sb(f"pTb{i}", (128, 1536), BF16) for i in range(4)]
    zsb = sb("zsb", (128, 512), F32)
    zsb2 = sb("zsb2", (128, 512), F32)
    dnm = sb("dnm", (128, 64), F32)
    dnr = sb("dnr", (128, 12), F32)
    zst = [sb(f"zst{i}", (128, 64), F32) for i in range(4)]
    wstage = sb("wstage", (128, D), F32)
    zrb = [sb(f"zrb{i}", (128, D), F32) for i in range(2)]
    ybuf = [sb(f"ybuf{i}", (128, D), F32) for i in range(4)]
    x1rb = [sb("x1rb0", (128, D), F32)]
    gates = sb("gates", (128, 4 * 72), F32)
    calv = sb("calv", (128, 72), F32)
    calt = sb("calt", (128, 72), F32)
    cal2 = sb("cal2", (128, 72), F32)
    idxs = sb("idxs", (128, 16), F32)
    tidxr = sb("tidxr", (1, 512), F32)
    ohbufs = [sb(f"ohT{i}", (64, 128), BF16) for i in range(4)]
    compt = sb("compt", (128, 8), F32)
    ctmp = sb("ctmp", (2, 512), F32)
    splg = sb("splg", (128, 65), F32)
    hhs = sb("hhs", (128, 48), F32)
    mags = sb("mags", (128, 4), F32)
    mv2 = sb("mv2", (128, 8), F32)
    mv3 = sb("mv3", (128, 8), F32)
    mvy = sb("mvy", (128, 8), F32)
    r2c = sb("r2c", (128, 4), F32); r3c = sb("r3c", (128, 4), F32); ryc = sb("ryc", (128, 4), F32)
    cw2_b = sb("cw2_b", (128, 4), BF16)
    dirs_b = sb("dirs_b", (64, D), BF16)
    spl_b = sb("spl_b", (64, 65), BF16)

    nsc = sb("nsc", (128, 3 * NT), F32)
    hpre = sb("hpre", (128, 512), F32)
    ps_ring = es.enter_context(nc.psum_tensor("ps_ring", [128, 3072], F32))
    ps_z = es.enter_context(nc.psum_tensor("ps_z", [128, 1024], F32))

    semh = {}
    for sname in (["eS", "eG", "eT", "eA", "eV"] + DMAPOOL["sync"] + DMAPOOL["gpsimd"]):
        semh[sname] = es.enter_context(nc.semaphore(sname))

    def mm(out_ap, lhsT_ap, rhs_ap, deps, start=True, stop=True, tag="mm", tpos=None):
        def f(t, o=out_ap, l=lhsT_ap, r=rhs_ap, s1=start, s2=stop, tp=tpos):
            return t.matmul(o, l, r, start=s1, stop=s2, tile_position=tp)
        return P.op("tensor", f, deps, tag=tag)

    def tp_(out_ap, in_ap, deps, tag="tp", k=128):
        def f(t, o=out_ap, i=in_ap, k=k):
            return t.transpose(o, i, ident[0:k, 0:k])
        return P.op("tensor", f, deps, tag=tag)

    def vop(fn, deps, tag="v"):
        return P.op("vector", fn, deps, tag=tag)

    def sop(fn, deps, tag="s"):
        return P.op("scalar", fn, deps, tag=tag)

    def vcopy(out_ap, in_ap, deps, tag="vcopy"):
        return vop(lambda v, o=out_ap, i=in_ap: v.tensor_copy(o, i), deps, tag=tag)

    def dma(queue, out_ap, in_ap, deps, tag="dma"):
        return P.op(queue, lambda e, o=out_ap, i=in_ap: e.dma_start(out=o, in_=i),
                    deps, dma=True, tag=tag)

    ring_use = [None] * 6
    ring_fill = [None] * 6
    rp = [0]

    def ring_alloc(n=1):
        if (rp[0] % 6) + n > 6:
            rp[0] += 6 - (rp[0] % 6)
        slots = [(rp[0] + i) % 6 for i in range(n)]
        rp[0] += n
        deps = [ring_use[s] for s in slots if ring_use[s] is not None]
        deps += [ring_fill[s] for s in slots if ring_fill[s] is not None]
        return slots, deps

    def ring_filled(slots, tok):
        for s in slots:
            ring_fill[s] = tok

    def ring_used(slots, tok):
        for s in slots:
            ring_use[s] = tok

    def rs(slot, cols=512, off=0):
        return ps_ring[:, 512 * slot + off: 512 * slot + off + cols]

    SC_NINV, SC_K0, SC_D0, SC_OS = 0, 1, 2, 9

    def scap(i):
        return scal[:, i:i + 1]

    def newton_rsqrt(dst_ap, t_ap, w, deps, tag):
        T = nsc[:, 0:w]; Sx = nsc[:, NT:NT + w]; U = nsc[:, 2 * NT:2 * NT + w]
        tk = vcopy(T, t_ap, list(deps) + [newton_busy[0]], tag=tag + "cp")
        tk = vop(lambda v: v.reciprocal(Sx, T), [tk], tag=tag + "rc")
        tk = vop(lambda v: v.tensor_single_scalar(out=Sx, in_=Sx, scalar=1.0, op=ALU.min),
                 [tk], tag=tag + "mn")
        for it_ in range(16):
            tk = vop(lambda v: v.tensor_mul(U, Sx, Sx), [tk], tag=tag + "n1")
            tk = vop(lambda v: v.tensor_mul(U, U, T), [tk], tag=tag + "n2")
            tk = vop(lambda v: v.tensor_scalar(out=U, in0=U, scalar1=-0.5, scalar2=1.5,
                                               op0=ALU.mult, op1=ALU.add), [tk], tag=tag + "n3")
            tk = vop(lambda v: v.tensor_mul(Sx, Sx, U), [tk], tag=tag + "n4")
        tk = vcopy(dst_ap, Sx, [tk], tag=tag + "out")
        newton_busy[0] = tk
        return tk

    newton_busy = [None]
    gelu_busy = [None]

    t_id_i = P.op("gpsimd", lambda g: g.iota(dgf[:, :], pattern=[[1, 128]], base=0,
                                             channel_multiplier=-1,
                                             allow_small_or_imprecise_dtypes=True), (), tag="iota")
    t_ident = vop(lambda v: v.tensor_single_scalar(out=ident[:, :], in_=dgf[:, :],
                                                   scalar=0.0, op=ALU.is_equal), [t_id_i], tag="ident")
    t_dgm = vop(lambda v: v.tensor_single_scalar(out=diagmask[:, :], in_=dgf[:, :],
                                                 scalar=0.0, op=ALU.is_le), [t_id_i], tag="dgm")
    t_io64 = P.op("gpsimd", lambda g: g.iota(iota64[:, :], pattern=[[0, 1]], base=0,
                                             channel_multiplier=1,
                                             allow_small_or_imprecise_dtypes=True),
                  [t_id_i], tag="io64")
    t_ioj = P.op("gpsimd", lambda g: g.iota(iotaj[:, :], pattern=[[1, 72]], base=0,
                                            channel_multiplier=0,
                                            allow_small_or_imprecise_dtypes=True),
                 [t_io64], tag="ioj")
    t_bigmi_i = P.op("gpsimd", lambda g: g.iota(bigmi[:, :], pattern=[[-1, 8]], base=999,
                                                channel_multiplier=0,
                                                allow_small_or_imprecise_dtypes=True),
                     [t_ioj], tag="iobm")
    t_ones_b = vop(lambda v: v.memset(onescol_b[:, :], 1.0), (), tag="onesb")
    t_ones_r = vop(lambda v: v.memset(onesrow[:, :], 1.0), (), tag="onesr")
    t_zsb0 = vop(lambda v: v.memset(zsb[64:128, :], 0.0), (), tag="zsb0")
    t_zsb1 = vop(lambda v: v.memset(zsb2[64:128, :], 0.0), (), tag="zsb1")
    t_zsbp = [t_zsb0, t_zsb1]


    rowbuf_read = [None]

    def bcast_row(dst, src_d, tag):
        tok_l = dma("sync", wstage[0:1, 0:D], src_d[:, :], [rowbuf_read[0]], tag="ld_" + tag)
        outs = []
        for j in range(2):
            slots, rdeps = ring_alloc(1)
            sl = slots[0]
            tmm = mm(rs(sl), onesrow[:, :], wstage[0:1, 512 * j: 512 * j + 512],
                     [tok_l, t_ones_r] + rdeps, tag="bc_" + tag)
            rowbuf_read[0] = tmm
            ring_filled(slots, tmm)
            tcp = vcopy(dst[:, 512 * j: 512 * j + 512], rs(sl), [tmm], tag="bcc_" + tag)
            ring_used(slots, tcp)
            outs.append(tcp)
        return outs, tok_l

    gam_toks, l_g = bcast_row(gam_t, gam_d, "gam")
    bet_toks, l_b = bcast_row(bet_t, bet_d, "bet")
    ngam_toks, l_n = bcast_row(ngam_t, ngam_d, "ngam")
    nbet_toks, l_nb = bcast_row(nbet_t, nbet_d, "nbet")

    tok_bq = dma("sync", bqT[:, :], bqT_d[:, :], (), tag="bq")
    tok_bk = dma("sync", bkT[:, :], bkT_d[:, :], (), tag="bk")
    tok_bv = dma("sync", bvT[:, :], bvT_d[:, :], (), tag="bv")
    tok_cb1 = dma("sync", cb1T[:, :], cb1T_d[:, :], (), tag="cb1")
    tok_cb2 = dma("sync", cb2_s[:, :], cb2_d[:, :], (), tag="cb2")
    tok_kn = dma("sync", knots_s[:, :], knots_d[:, :], (), tag="kn")
    tok_tm = dma("sync", temp_s[:, 0:1], temp_d[:, :], (), tag="tm")
    tok_os = dma("sync", temp_s[:, 3:4], oscale_d[:, :], (), tag="os")
    t_t1 = vop(lambda v: v.tensor_single_scalar(out=temp_s[:, 1:2], in_=temp_s[:, 0:1],
                                                scalar=1e-6, op=ALU.add), [tok_tm], tag="t1")
    t_t2 = vop(lambda v: v.reciprocal(temp_s[:, 2:3], temp_s[:, 1:2]), [t_t1], tag="t2")
    t_t3 = vop(lambda v: v.tensor_single_scalar(out=temp_s[:, 2:3], in_=temp_s[:, 2:3],
                                                scalar=-1.0, op=ALU.mult), [t_t2], tag="t3")
    t_sz = vop(lambda v: v.memset(scrow[:, :], 0.0), (), tag="sz")
    t_s0 = vcopy(scrow[:, 0:1], temp_s[:, 2:3], [t_t3, t_sz], tag="s0")
    t_s1 = vcopy(scrow[:, 1:2], knots_s[:, 0:1], [tok_kn], tag="s1")
    t_s2 = vop(lambda v: v.tensor_sub(scrow[:, 2:9], knots_s[:, 1:8], knots_s[:, 0:7]),
               [tok_kn], tag="s2")
    t_s3 = vcopy(scrow[:, 9:10], temp_s[:, 3:4], [tok_os], tag="s3")
    slots, rdeps = ring_alloc(1)
    sl0 = slots[0]
    t_scm = mm(rs(sl0, 16), onesrow[:, :], scrow[:, 0:16],
               [t_s0, t_s1, t_s2, t_s3, t_ones_r] + rdeps, tag="scm")
    ring_filled(slots, t_scm)
    t_scal = vcopy(scal[:, :], rs(sl0, 16), [t_scm], tag="scal")
    ring_used(slots, t_scal)

    for kk in range(8):
        a_ap = sgt[:, 72 * kk: 72 * kk + 64]
        cs_ap = sgt[:, 72 * kk + 64: 72 * kk + 72]
        tok_l = dma("sync", a_ap, sigT_d[128 * kk:128 * kk + 128, :], (), tag="sig")
        t_a = vop(lambda v, a=a_ap: v.tensor_scalar(out=calv[:, 0:64], in0=a, scalar1=0.3,
                                                    scalar2=None, op0=ALU.is_gt), [tok_l], tag="siga")
        t_b = vop(lambda v, a=a_ap: v.tensor_scalar(out=calt[:, 0:64], in0=a, scalar1=-0.3,
                                                    scalar2=None, op0=ALU.is_lt), [tok_l], tag="sigb")
        t_c = vop(lambda v, a=a_ap: v.tensor_sub(a, calv[:, 0:64], calt[:, 0:64]),
                  [t_a, t_b], tag="sigc")
        t_d = vop(lambda v, kk=kk: v.reduce_sum(
            sgt[:, 72 * kk + 64: 72 * kk + 72],
            sgt[:, 72 * kk: 72 * kk + 64].rearrange("p (c t) -> p c t", t=8), axis=AX.X),
            [t_c], tag="sigd")
        t_e = vop(lambda v, a=cs_ap: v.tensor_scalar(out=calv[:, 0:8], in0=a, scalar1=0.0,
                                                     scalar2=None, op0=ALU.is_gt), [t_d], tag="sige")
        t_f = vop(lambda v, a=cs_ap: v.tensor_scalar(out=calt[:, 0:8], in0=a, scalar1=0.0,
                                                     scalar2=None, op0=ALU.is_lt), [t_d], tag="sigf")
        vop(lambda v, a=cs_ap: v.tensor_sub(a, calv[:, 0:8], calt[:, 0:8]), [t_e, t_f], tag="sigg")

    tok_cw2l = dma("sync", wstage[:, 0:4], cw2_d[:, :], [rowbuf_read[0]], tag="cw2")
    tok_cw2 = vcopy(cw2_b[:, :], wstage[:, 0:4], [tok_cw2l], tag="cw2c")
    tok_dirs0 = dma("sync", wstage[0:64, :], dirs_d[:, :], [tok_cw2], tag="dirs")
    tok_dirs = vcopy(dirs_b[:, :], wstage[0:64, :], [tok_dirs0], tag="dirsc")
    tok_spl0 = dma("sync", wstage[0:64, 0:65], spl_d[:, :], [tok_dirs], tag="spl")
    tok_spl = vcopy(spl_b[:, :], wstage[0:64, 0:65], [tok_spl0], tag="splc")

    x0_free = [None] * 8
    var_toks = []
    for t in range(NT):
        buf = x0c[t % 8]
        tok_l = dma("sync", buf[:, :], x0_d[128 * t:128 * t + 128, :],
                    [x0_free[t % 8]], tag="x0s")
        t1 = vop(lambda v, b=buf: v.bn_stats(st6[:, 0:6], b[:, 0:512]), [tok_l], tag="bns1")
        t2 = vop(lambda v, b=buf: v.bn_stats(st6[:, 6:12], b[:, 512:1024]), [tok_l], tag="bns2")
        t3 = vop(lambda v: v.bn_aggr(mvt[:, :], st6[:, :]), [t1, t2], tag="bnag")
        x0_free[t % 8] = t3
        t4 = vop(lambda v, t=t: v.tensor_single_scalar(out=r1[:, t:t + 1], in_=mvt[:, 1:2],
                                                       scalar=EPS, op=ALU.add), [t3], tag="veps")
        t7 = vcopy(mu1[:, t:t + 1], mvt[:, 0:1], [t3], tag="mu1")
        var_toks.append((t4, t7))
    t_r1 = newton_rsqrt(r1[:, 0:NT], r1[:, 0:NT], NT, [tk for pair in var_toks for tk in pair],
                        "ln1n")
    ln1_tok = [(t_r1, var_toks[t][1]) for t in range(NT)]

    es_kv = ExitStack()
    wkv_b = es_kv.enter_context(nc.sbuf_tensor("wkv_b", [128, 8 * 1024], BF16))
    kstage = es_kv.enter_context(nc.sbuf_tensor("kstage", [128, 512], BF16))
    vstage_f = es_kv.enter_context(nc.sbuf_tensor("vstage_f", [128, 512], F32))
    vstage = es_kv.enter_context(nc.sbuf_tensor("vstage", [128, 4 * 130], BF16))

    acc_tok = [None] * 8
    acc_read = [None] * 8
    aT_read = [None] * 8
    aT_tok = [None] * 8
    aT_store = [None] * 8
    kv_dma_toks = []
    last_wkv_mm = [None]

    def load_wkv(w_d, dep0):
        prev = dep0
        for kk in range(8):
            tok_l = dma("sync", wstage[:, :], w_d[128 * kk:128 * kk + 128, :],
                        [prev], tag="wkv")
            prev = vcopy(wkv_b[:, 1024 * kk: 1024 * kk + 1024], wstage[:, :],
                         [tok_l, last_wkv_mm[0]], tag="wkvc")
        return prev

    kst_free = [None]
    vstf_free = [None]
    vst_free = [None]

    def kv_pass(proj, w_ready, bT, bias_tok, kdst=None, build_aT=False):
        for n in range(CH):
            if build_aT:
                a_toks = []
                for tt in range(4):
                    t = 4 * n + tt
                    buf = x0c[t % 8]
                    tok_l = dma("sync", buf[:, :], x0_d[128 * t:128 * t + 128, :],
                                [x0_free[t % 8]], tag="x0kv")
                    ta = vop(lambda v, b=buf, t=t, tt=tt: v.tensor_scalar(
                        out=ACC[tt][:, :], in0=b[:, :], scalar1=mu1[:, t:t + 1],
                        scalar2=r1[:, t:t + 1], op0=ALU.subtract, op1=ALU.mult),
                        [tok_l, ln1_tok[t][0], ln1_tok[t][1], acc_read[tt], acc_tok[tt]],
                        tag="akv")
                    x0_free[t % 8] = ta
                    acc_tok[tt] = ta
                    a_toks.append(ta)
                for kk in range(8):
                    slots, rdeps = ring_alloc(1)
                    sl = slots[0]
                    last_t = None
                    for tt in range(4):
                        deps = [a_toks[tt], t_ident] + (rdeps if tt == 0 else [])
                        last_t = tp_(rs(sl, 128, 128 * tt),
                                     ACC[tt][:, 128 * kk:128 * kk + 128],
                                     deps, tag="aTkv")
                        acc_read[tt] = last_t
                    ring_filled(slots, last_t)
                    tc = vop(lambda v, kk=kk, sl=sl: v.tensor_copy(
                        aT[:, 512 * kk: 512 * kk + 512], rs(sl)),
                        [last_t, aT_tok[kk], aT_read[kk], aT_store[kk]], tag="aTcp")
                    ring_used(slots, tc)
                    aT_tok[kk] = tc
                    td = dma("sync", aTs_d[kk, :, 512 * n: 512 * n + 512],
                             aT[:, 512 * kk: 512 * kk + 512], [tc], tag="aTst")
                    aT_store[kk] = td
            else:
                for kk in range(8):
                    tok = dma("sync", aT[:, 512 * kk: 512 * kk + 512],
                              aTs_d[kk, :, 512 * n: 512 * n + 512],
                              [aT_tok[kk], aT_read[kk], aT_store[kk]], tag="aTld")
                    aT_tok[kk] = tok
            for p in range(NPAIR):
                slots, rdeps = ring_alloc(1)
                sl = slots[0]
                mmt = None
                for kk in range(8):
                    deps = [aT_tok[kk], w_ready] + (rdeps if kk == 0 else [])
                    mmt = mm(rs(sl), wkv_b[:, 1024 * kk + 128 * p: 1024 * kk + 128 * p + 128],
                             aT[:, 512 * kk: 512 * kk + 512], deps,
                             start=(kk == 0), stop=(kk == 7), tag=proj + "mm")
                    aT_read[kk] = mmt
                last_wkv_mm[0] = mmt
                ring_filled(slots, mmt)
                if proj in ("k", "q"):
                    tev = vop(lambda v, sl=sl, p=p: v.tensor_scalar(
                        out=kstage[:, :], in0=rs(sl), scalar1=bT[:, p:p + 1], scalar2=None,
                        op0=ALU.add), [mmt, bias_tok, kst_free[0]], tag="kev")
                    ring_used(slots, tev)
                    tdma = dma("sync", kdst[p, :, 512 * n: 512 * n + 512], kstage[:, :],
                               [tev], tag="kst")
                    kst_free[0] = tdma
                    kv_dma_toks.append(tdma)
                else:
                    tevs = []
                    for j in range(4):
                        tev = vop(lambda v, sl=sl, p=p, j=j: v.tensor_scalar(
                            out=vstage_f[:, 128 * j: 128 * j + 128],
                            in0=rs(sl, 128, 128 * j), scalar1=bT[:, p:p + 1], scalar2=None,
                            op0=ALU.add), [mmt, bias_tok, vstf_free[0]], tag="vev")
                        tevs.append(tev)
                    ring_used(slots, tevs[-1])
                    slots2, rdeps2 = ring_alloc(1)
                    sl2 = slots2[0]
                    last_t = None
                    for j in range(4):
                        deps = [tevs[j]] + (rdeps2 if j == 0 else [])
                        last_t = tp_(rs(sl2, 128, 128 * j), vstage_f[:, 128 * j: 128 * j + 128],
                                     deps, tag="vT")
                    vstf_free[0] = last_t
                    ring_filled(slots2, last_t)
                    tcp = vop(lambda v, sl2=sl2: v.tensor_copy(
                        vstage[:, :].rearrange("p (t s c) -> p t s c", t=4, s=2)[:, :, :, 0:64],
                        rs(sl2, 512).rearrange("p (t s c) -> p t s c", t=4, s=2)),
                        [last_t, vst_free[0]], tag="vcp")
                    ring_used(slots2, tcp)
                    tms = vop(lambda v: v.memset(
                        vstage[:, :].rearrange("p (t c) -> p t c", t=4)[:, :, 64:130:65], 1.0),
                        [tcp], tag="vones")
                    tdma = dma("sync", vs_d[p, 4 * n: 4 * n + 4].rearrange("t p c -> p t c"),
                               vstage[:, :].rearrange("p (t c) -> p t c", t=4), [tms], tag="vst")
                    vst_free[0] = tdma
                    kv_dma_toks.append(tdma)

    wq_ready = load_wkv(wqT_d, tok_spl)
    kv_pass("q", wq_ready, bqT, tok_bq, kdst=qts_d, build_aT=True)
    wk_ready = load_wkv(wkT_d, wq_ready)
    kv_pass("k", wk_ready, bkT, tok_bk, kdst=kts_d)
    wv_ready = load_wkv(wvT_d, wk_ready)
    kv_pass("v", wv_ready, bvT, tok_bv)
    zmov = wstage[0:1, 0:512]
    zrow = wstage[0:1, 512:577]
    t_zmov = vop(lambda v: v.memset(wstage[0:1, 0:577], 0.0), [wv_ready], tag="zmov")
    t_zrow = t_zmov
    es_kv.close()
    kv_all = P.op("gpsimd", lambda g: g.nop(), kv_dma_toks, tag="kvbar")

    cw1_f = sb("cw1_f", (128, 8 * 256), F32)
    x2T_f = sb("x2T_f", (128, 8 * 512), F32)
    hTb = sb("hTb", (128, 2 * 512), BF16)
    cw1_toks = []
    for kk in range(8):
        tok_l = dma("sync", cw1_f[:, 256 * kk: 256 * kk + 256],
                    cw1_d[128 * kk: 128 * kk + 128, :], [last_wkv_mm[0]], tag="cw1")
        cw1_toks.append(tok_l)

    kbuf_tok = [None, None]; kbuf_read = [None, None]
    vbuf_tok = [None, None]; vbuf_read = [None, None]
    pT_tok = [None] * 4; pT_read = [None] * 4
    zs_tok = [None, None]; zs_read = [None, None]
    ucnt = [0]
    psz_read = [None, None]; zs_close = [None, None]
    qbuf = [qT, aT]
    qbuf_tok = [None, None]
    qbuf_read = [[], [tk for tk in aT_tok if tk] + [tk for tk in aT_read if tk] +
                 [tk for tk in aT_store if tk]]
    dnm_tok = {}; dnm_read = [None, None]
    ybuf_tok = [None] * 4; ybuf_read = [None] * 4
    x1rb_tok = [None, None]; x1rb_read = [None, None]
    x1dma_tok = [None] * 8
    outdma_toks = []
    gates_tok = [None] * 4
    gexp_tok = [None] * 4
    cal_state = {}
    mags_tok = [None] * 4
    compt_tok = [None] * 4
    oh_tok = [None] * 4; oh_read = [None] * 4
    tidxr_read = [None] * 4; idx_read = [None] * 4
    gather_tok = {}
    x2T_tok = [None] * 8; x2T_read = [None] * 8
    hT_tok = [None] * 2; hT_read = [None] * 2
    mv2_tok = [None] * 4; mv3_tok = [None] * 4; mvy_tok = [None] * 4
    ctmp_read = [None]
    ctv4_hold = [None]
    r2_tok = [None] * 4; r3_tok = [None] * 4; ry_tok = [None] * 4
    ysts_read = [None] * 4
    spl_ps = {}

    def a_init(c, free_x0=False):
        toks = []
        for tt in range(4):
            t = 4 * c + tt
            ai = 4 * (c % 2) + tt
            buf = x0c[(c % 2) * 4 + tt]
            tok_l = dma("sync", buf[:, :], x0_d[128 * t:128 * t + 128, :],
                        [x0_free[(c % 2) * 4 + tt]], tag=f"x0c{c}")
            ta = vop(lambda v, b=buf, t=t, ai=ai: v.tensor_scalar(
                out=ACC[ai][:, :], in0=b[:, :], scalar1=mu1[:, t:t + 1],
                scalar2=r1[:, t:t + 1], op0=ALU.subtract, op1=ALU.mult),
                [tok_l, ln1_tok[t][0], ln1_tok[t][1], acc_read[ai], acc_tok[ai],
                 x1dma_tok[ai]], tag=f"a{c}")
            acc_tok[ai] = ta
            if free_x0:
                x0_free[(c % 2) * 4 + tt] = ta
            toks.append(ta)
        return toks

    def kvq_load(p):
        pb = p % 2
        deps = [kbuf_read[pb], kv_all]
        tk = dma("sync", kbuf[pb][:, :], kts_d[p, :, :], deps, tag=f"kld{p}")
        deps = [vbuf_read[pb], kv_all]
        tvs = []
        for half in range(2):
            tv = dma("sync",
                     vbuf[pb][:, half * (NT // 2) * 130:(half + 1) * (NT // 2) * 130].rearrange(
                         "p (t c) -> p t c", c=130),
                     vs_d[p, half * (NT // 2):(half + 1) * (NT // 2)].rearrange(
                         "t p c -> p t c"), deps, tag=f"vld{p}")
            tvs.append(tv)
        tq = dma("sync", qbuf[pb][:, 0:S], qts_d[p, :, :],
                 list(qbuf_read[pb]) + [kv_all], tag=f"qld{p}")
        kbuf_tok[pb] = tk
        vbuf_tok[pb] = tvs
        qbuf_tok[pb] = tq
        return tk, tvs

    def tile_schedule(c):
        groups = []
        if "sside" in skip:
            CHsk = True
        else:
            CHsk = False
        for hh in range(2):
            if CHsk:
                break
            for it in range(4):
                i = 4 * c + it
                chunks = []
                rem = 128 * (i + 1)
                j = 0
                while rem > 0:
                    nj = min(512, rem)
                    chunks.append((j, nj))
                    rem -= nj
                    j += 1
                g = []
                for (j, nj) in chunks:
                    if nj < 512:
                        if g:
                            groups.append(("S", g))
                            g = []
                        groups.append(("S", [(hh, it, j, nj)]))
                        continue
                    g.append((hh, it, j, nj))
                    if len(g) == 3:
                        groups.append(("S", g))
                        g = []
                if g:
                    groups.append(("S", g))
        stg = []
        for kp in range(4 * c + 4, NT):
            for hh in range(2):
                stg.append((hh, kp, 512))
                if len(stg) == 2:
                    groups.append(("T", stg))
                    stg = []
        if stg:
            groups.append(("T", stg))
        for kp in range(4 * c, min(4 * c + 4, NT)):
            jj = kp - 4 * c
            n = 128 * (jj + 1)
            for hh in range(2):
                groups.append(("T", [(hh, kp, n)]))
        return groups

    def emit_pair(c, p, sched):
        pb = p % 2
        ktok = kbuf_tok[pb]
        vtoks = vbuf_tok[pb]
        qb = qbuf[pb]
        qtok = qbuf_tok[pb]
        ucnt[0] += 1
        dpar = ucnt[0] % 2
        dbase = 32 * dpar
        pv_last = [None, None]
        zopen = [None, None]
        have_full = 4 * c + 4 < NT
        if not have_full:
            for hh in range(2):
                zopen[hh] = mm(ps_z[0:65, 512 * hh: 512 * hh + 512], zrow[0:1, 0:65],
                               zmov[0:1, :],
                               [t_zrow, t_zmov, psz_read[hh], zs_close[hh]], start=True,
                               stop=False, tag=f"zop{c}_{p}")
                pv_last[hh] = zopen[hh]
        first_pv = [have_full, have_full]
        pT_parity = [0]
        pend_consume = []
        kread = []
        vread = []
        qread = []
        sg_idx = {}
        for (kind, g) in sched:
            L = len(g)
            slots, rdeps = ring_alloc(L)
            qk_toks = []
            for ii, item in enumerate(g):
                sl = slots[ii]
                if "qk" in skip:
                    continue
                if "qk1" in skip and not (p == 0 and c == 0 and ii == 0 and kind == "T"):
                    continue
                if kind == "T":
                    hh, kp, n = item
                    lhs = kbuf[pb][64 * hh:64 * hh + 64, 128 * kp:128 * kp + 128]
                    rhsap = qb[64 * hh:64 * hh + 64, 512 * c: 512 * c + n]
                    deps = [ktok, qtok] + (rdeps if ii == 0 else [])
                    tokm = mm(rs(sl, n), lhs, rhsap, deps, tag=f"qkT{c}_{p}")
                else:
                    hh, it, j, n = item
                    lhs = qb[64 * hh:64 * hh + 64, 512 * c + 128 * it: 512 * c + 128 * it + 128]
                    rhsap = kbuf[pb][64 * hh:64 * hh + 64, 512 * j: 512 * j + n]
                    deps = [ktok, qtok] + (rdeps if ii == 0 else [])
                    tokm = mm(rs(sl, n), lhs, rhsap, deps, tag=f"qkS{c}_{p}")
                qk_toks.append(tokm)
                ring_fill[sl] = tokm
                kread.append(tokm)
                qread.append(tokm)
            pp = pT_parity[0]
            pT_parity[0] = (pp + 1) % 4
            ns = [item[-1] for item in g]
            sameN = all(x == ns[0] for x in ns)
            exp_toks = []
            if "exp" in skip:
                for ii2, sl_ in enumerate(slots):
                    if ii2 < len(qk_toks):
                        ring_used([sl_], qk_toks[ii2])
                pT_parity[0] = pp
                continue
            if kind == "S":
                hh, it, _, _ = g[0]
                base_col = dbase + 4 * (4 * hh + it)
                gi = sg_idx.get((hh, it), 0)
                sg_idx[(hh, it)] = gi + 1
                totN = sum(ns)
                acap = dnm[:, base_col + gi: base_col + gi + 1]
                din_ap = rs(slots[0], totN) if (sameN and ns[0] == 512) or L == 1 else None
                if din_ap is None:
                    for ii, item in enumerate(g):
                        _, _, _, n = item
                        gi2 = sg_idx[(hh, it)] - 1 + ii
                        pass
                    raise AssertionError("S group must be uniform or single")
                texp = sop(lambda s, i_=din_ap, o=pTb[pp][:, 0:totN], a=acap:
                           s.activation(out=o, in_=i_, func=AF.Exp, scale=1.0 / 64.0,
                                        accum_out=a),
                           qk_toks + [pT_read[pp], pT_tok[pp], dnm_read[dpar]],
                           tag=f"expS{c}_{p}")
                dnm_tok[(hh, it, gi)] = texp
                pT_tok[pp] = texp
                exp_toks.append(texp)
                ring_used(slots, texp)
            else:
                if sameN:
                    totN = ns[0] * L
                    texp = sop(lambda s, i_=rs(slots[0], totN), o=pTb[pp][:, 0:totN]:
                               s.activation(out=o, in_=i_, func=AF.Exp, scale=1.0 / 64.0),
                               qk_toks + [pT_read[pp], pT_tok[pp]], tag=f"expT{c}_{p}")
                    exp_toks = [texp] * L
                    pT_tok[pp] = texp
                    ring_used(slots, texp)
                else:
                    off = 0
                    for ii, item in enumerate(g):
                        n = item[-1]
                        texp = sop(lambda s, i_=rs(slots[ii], n), o=pTb[pp][:, off:off + n]:
                                   s.activation(out=o, in_=i_, func=AF.Exp, scale=1.0 / 64.0),
                                   [qk_toks[ii], pT_read[pp], pT_tok[pp]], tag=f"expT{c}_{p}")
                        exp_toks.append(texp)
                        pT_tok[pp] = texp
                        ring_used([slots[ii]], texp)
                        off += n
            if kind == "S":
                continue
            if dbg_d is not None and c == 0 and p == 0 and not dbg_state.get(5):
                dbg_state[5] = True
                _dtok = dbg_dump(5, pTb[pp][:, 0:512], [exp_toks[0]], w=512)
                pT_tok[pp] = _dtok
                _d6 = dbg_dump(6, qb[:, 0:512], [qtok], w=512)
                qbuf_read[pb] = [_d6]
                _d7 = dbg_dump(7, kbuf[pb][:, 0:512], [ktok], w=512)
                kbuf_read[pb] = _d7

            def consume(g=g, pp=pp, exp_toks=exp_toks, sameN=sameN):
                off = 0
                creads = []
                for ii, item in enumerate(g):
                    hh, kp, n = item
                    jj = kp - 4 * c
                    et = exp_toks[ii] if not sameN else exp_toks[0]
                    pslice = lambda a, b_: pTb[pp][:, off + a: off + b_]
                    zoff = 512 * hh
                    if jj >= 4:
                        if "pvones" not in skip:
                            opener = first_pv[hh]
                            deps = [et] + vtoks
                            if opener:
                                deps += [psz_read[hh], zs_close[hh]]
                            elif zopen[hh] is not None:
                                deps.append(zopen[hh])
                            st = mm(ps_z[0:65, zoff:zoff + 512],
                                    vbuf[pb][:, 130 * kp + 65 * hh: 130 * kp + 65 * hh + 65],
                                    pslice(0, 512), deps,
                                    start=opener, stop=False, tag=f"pv{c}_{p}")
                            if opener:
                                first_pv[hh] = False
                                zopen[hh] = st
                            pv_last[hh] = st
                            creads.append(st)
                            vread.append(st)
                    else:
                        if jj > 0 and "pvones" not in skip:
                            st = mm(ps_z[0:65, zoff:zoff + 128 * jj],
                                    vbuf[pb][:, 130 * kp + 65 * hh: 130 * kp + 65 * hh + 65],
                                    pslice(0, 128 * jj),
                                    [et, zopen[hh]] + vtoks,
                                    start=False, stop=False, tag=f"pvp{c}_{p}")
                            pv_last[hh] = st
                            creads.append(st)
                            vread.append(st)
                        if "dpv" not in skip:
                            tmk = vop(lambda v, a=pslice(128 * jj, 128 * jj + 128):
                                      v.tensor_mul(a, a, diagmask[:, :]), [et, t_dgm],
                                      tag=f"dmask{c}_{p}")
                            st = mm(ps_z[0:64, zoff + 128 * jj: zoff + 128 * jj + 128],
                                    vbuf[pb][:, 130 * kp + 65 * hh: 130 * kp + 65 * hh + 64],
                                    pslice(128 * jj, 128 * jj + 128),
                                    [tmk, zopen[hh]] + vtoks,
                                    start=False, stop=False, tag=f"dpv{c}_{p}")
                            pv_last[hh] = st
                            creads.append(st)
                            vread.append(st)
                    off += n
                pT_read[pp] = creads[-1] if creads else pT_read[pp]

            pend_consume.append(consume)
            if len(pend_consume) > 3:
                pend_consume.pop(0)()
        while pend_consume:
            pend_consume.pop(0)()
        kbuf_read[pb] = kread[-1] if kread else kbuf_read[pb]
        vbuf_read[pb] = vread[-1] if vread else vbuf_read[pb]
        if qread:
            qbuf_read[pb] = [qread[-1]]
        return pv_last, sg_idx

    zst_read = [None] * 8
    zst_idx = [0]
    zdma_toks = []
    zdma_by_chunk = [[] for _ in range(CH)]
    RH = S // 16

    def z_evac(c, p, pv_last, sg_idx):
        toks = []
        dpar = ucnt[0] % 2
        dbase = 32 * dpar
        for hh in range(2):
            zoff = 512 * hh
            zsbh = zsb if hh == 0 else zsb2
            tclose = mm(ps_z[0:65, zoff: zoff + 1], zrow[0:1, 0:65], zmov[0:1, 0:1],
                        [pv_last[hh], t_zrow], start=False, stop=True, tag=f"zcl{c}_{p}")
            zs_close[hh] = tclose
            tcps = []
            for j in range(4):
                tcp = vcopy(zsbh[0:65, 128 * j: 128 * j + 128],
                            ps_z[0:65, zoff + 128 * j: zoff + 128 * j + 128],
                            [tclose, zs_tok[hh], zs_read[hh], t_zsbp[hh]], tag=f"zc{c}_{p}")
                tcps.append(tcp)
            zs_tok[hh] = tcps[-1]
            psz_read[hh] = tcps[-1]
            slots, rdeps = ring_alloc(1)
            sl = slots[0]
            tts = []
            for j in range(4):
                deps = [tcps[j], t_ident] + (rdeps if j == 0 else [])
                tt_ = tp_(rs(sl, 128, 128 * j), zsbh[:, 128 * j: 128 * j + 128],
                          deps, tag=f"zT{c}_{p}")
                tts.append(tt_)
            zs_read[hh] = tts[-1]
            ring_filled(slots, tts[-1])
            h_glob = 2 * p + hh
            tBs = []
            for j in range(4):
                m = sg_idx.get((hh, j), 0)
                if m > 0:
                    bc = dbase + 4 * (4 * hh + j)
                    tA = vop(lambda v, bc=bc, m=m, j=j: v.reduce_sum(
                        dnr[:, j:j + 1], dnm[:, bc:bc + m], axis=AX.X),
                        [dnm_tok[(hh, j, gg)] for gg in range(m)], tag=f"dn{c}_{p}")
                    tB = vop(lambda v, sl=sl, j=j: v.tensor_add(
                        dnr[:, 4 + j:5 + j], dnr[:, j:j + 1], rs(sl, 1, 128 * j + 64)),
                        [tA, tts[-1]], tag=f"dt{c}_{p}")
                else:
                    tB = vcopy(dnr[:, 4 + j:5 + j], rs(sl, 1, 128 * j + 64), [tts[-1]],
                               tag=f"dt0{c}_{p}")
                tBs.append(tB)
            trc = vop(lambda v: v.reciprocal(dnr[:, 8:12], dnr[:, 4:8]), tBs,
                      tag=f"dr{c}_{p}")
            vlast = None
            for j in range(4):
                zi = zst_idx[0] % 4
                zst_idx[0] += 1
                zb = zst[zi]
                tz = vop(lambda v, sl=sl, j=j, zb=zb: v.tensor_scalar(
                    out=zb[:, :], in0=rs(sl, 64, 128 * j), scalar1=dnr[:, 8 + j:9 + j],
                    scalar2=None, op0=ALU.mult),
                    [trc, zst_read[zi], tts[-1]], tag=f"za{c}_{p}")
                r0 = RH * h_glob + 32 * c + 8 * j
                td = dma("sync",
                         zs_d[r0:r0 + 8, :].rearrange("r (t h) -> (r t) h", h=64),
                         zb[:, :], [tz], tag=f"zd{c}_{p}")
                zst_read[zi] = td
                zdma_toks.append(td)
                zdma_by_chunk[(RH * h_glob) // 512].append(td)
                vlast = tz
            ring_used(slots, vlast)
            dnm_read[dpar] = vlast
            toks.append(vlast)
        return toks

    def ln2_x1(c, z_toks):
        stat_toks = []
        for tt in range(4):
            ai = 4 * (c % 2) + tt
            t1 = vop(lambda v, ai=ai: v.bn_stats(st6[:, 0:6], ACC[ai][:, 0:512]),
                     [acc_tok[ai]] + z_toks, tag=f"l2a{c}")
            t2 = vop(lambda v, ai=ai: v.bn_stats(st6[:, 6:12], ACC[ai][:, 512:1024]),
                     [acc_tok[ai]], tag=f"l2b{c}")
            t3 = vop(lambda v, tt=tt: v.bn_aggr(mv2[:, 2 * tt:2 * tt + 2], st6[:, :]),
                     [t1, t2, mv2_tok[tt]], tag=f"l2c{c}")
            mv2_tok[tt] = t3
            stat_toks.append(t3)
        return stat_toks

    def rstd_visit(c, stat_toks, mvx, rxc, rx_tok, tagp):
        deps = [tk for tk in stat_toks if tk] + [tk for tk in rx_tok if tk]
        t4 = vop(lambda v, mvx=mvx: v.tensor_scalar(
            out=rxc[:, 0:4],
            in0=mvx[:, 0:8].rearrange("p (t c) -> p t c", c=2)[:, :, 1],
            scalar1=EPS, scalar2=None, op0=ALU.add), deps, tag=f"{tagp}e{c}")
        t6 = newton_rsqrt(rxc[:, 0:4], rxc[:, 0:4], 4, [t4], f"{tagp}n{c}")
        for tt in range(4):
            rx_tok[tt] = t6
        return [t6] * 4

    def x1_compute(c, r2_toks):
        toks = []
        for tt in range(4):
            ai = 4 * (c % 2) + tt
            buf = x0c[(c % 2) * 4 + tt]
            t1 = vop(lambda v, ai=ai, tt=tt: v.tensor_scalar(
                out=ACC[ai][:, :], in0=ACC[ai][:, :], scalar1=mv2[:, 2 * tt:2 * tt + 1],
                scalar2=r2c[:, tt:tt + 1], op0=ALU.subtract, op1=ALU.mult),
                [acc_tok[ai], r2_toks[tt], acc_read[ai]], tag=f"x1n{c}")
            t2 = vop(lambda v, ai=ai: v.tensor_mul(ACC[ai][:, :], ACC[ai][:, :], gam_t[:, :]),
                     [t1] + gam_toks, tag=f"x1g{c}")
            t3 = vop(lambda v, ai=ai, b=buf: v.tensor_add(ACC[ai][:, :], ACC[ai][:, :], b[:, :]),
                     [t2], tag=f"x1x{c}")
            t4 = vop(lambda v, ai=ai: v.tensor_add(ACC[ai][:, :], ACC[ai][:, :], bet_t[:, :]),
                     [t3] + bet_toks, tag=f"x1f{c}")
            acc_tok[ai] = t4
            x0_free[(c % 2) * 4 + tt] = t3
            td = dma("sync", x1s_d[128 * (4 * c + tt):128 * (4 * c + tt) + 128, :],
                     ACC[ai][:, :], [t4], tag=f"x1d{c}")
            x1dma_tok[ai] = td
            toks.append(t4)
        return toks

    def ffn_T1(c, x1_toks, r3_toks):
        xn_toks = []
        for tt in range(4):
            ai = 4 * (c % 2) + tt
            buf = x0c[(c % 2) * 4 + tt]
            t1 = vop(lambda v, ai=ai, tt=tt, b=buf: v.tensor_scalar(
                out=b[:, :], in0=ACC[ai][:, :], scalar1=mv3[:, 2 * tt:2 * tt + 1],
                scalar2=r3c[:, tt:tt + 1], op0=ALU.subtract, op1=ALU.mult),
                [acc_tok[ai], r3_toks[tt], x0_free[(c % 2) * 4 + tt]], tag=f"xn{c}")
            x0_free[(c % 2) * 4 + tt] = t1
            xn_toks.append(t1)
        for kk in range(8):
            slots, rdeps = ring_alloc(1)
            sl = slots[0]
            last_t = None
            for tt in range(4):
                buf = x0c[(c % 2) * 4 + tt]
                deps = [xn_toks[tt], t_ident, x2T_read[kk], x2T_tok[kk]] + (rdeps if tt == 0 else [])
                last_t = tp_(rs(sl, 128, 128 * tt), buf[:, 128 * kk:128 * kk + 128],
                             deps, tag=f"x2T{c}")
                x0_free[(c % 2) * 4 + tt] = last_t
            ring_filled(slots, last_t)
            tc = vcopy(x2T_f[:, 512 * kk: 512 * kk + 512], rs(sl), [last_t], tag=f"x2Tc{c}")
            ring_used(slots, tc)
            x2T_tok[kk] = tc
        gelu_in = []
        for jo in range(2):
            slots, rdeps = ring_alloc(1)
            sl = slots[0]
            mmt = None
            for kk in range(8):
                deps = [x2T_tok[kk], cw1_toks[kk]] + (rdeps if kk == 0 else [])
                mmt = mm(rs(sl), cw1_f[:, 256 * kk + 128 * jo: 256 * kk + 128 * jo + 128],
                         x2T_f[:, 512 * kk: 512 * kk + 512], deps,
                         start=(kk == 0), stop=(kk == 7), tag=f"hmm{c}")
                x2T_read[kk] = mmt
            ring_filled(slots, mmt)
            gelu_in.append((sl, mmt, slots))
        A1, A2, A3, A4, A5 = (0.254829592, -0.284496736, 1.421413741,
                              -1.453152027, 1.061405429)
        PP = 0.3275911
        ISQ2 = 0.7071067811865476
        hT_toks = []
        for jo, (sl, mmt, slots) in enumerate(gelu_in):
            hp = hpre[:, :]
            tb = vop(lambda v, sl=sl, jo=jo: v.tensor_scalar(
                out=hp, in0=rs(sl), scalar1=cb1T[:, jo:jo + 1], scalar2=None,
                op0=ALU.add), [mmt, tok_cb1, gelu_busy[0]], tag=f"gb{c}")
            ring_used(slots, tb)
            Z = zsb[:, 0:512]
            tz = vop(lambda v: v.tensor_scalar(out=Z, in0=hp, scalar1=-1.0, scalar2=None,
                                               op0=ALU.mult),
                     [tb, zs_tok[0], zs_read[0]], tag=f"gz{c}")
            tz = vop(lambda v: v.tensor_tensor(out=Z, in0=Z, in1=hp, op=ALU.max),
                     [tz], tag=f"gz2{c}")
            tz = vop(lambda v: v.tensor_scalar(out=Z, in0=Z, scalar1=ISQ2, scalar2=None,
                                               op0=ALU.mult), [tz], tag=f"gz3{c}")
            DN = x1rb[0][:, 0:512]
            td = vop(lambda v: v.tensor_scalar(out=DN, in0=Z, scalar1=PP, scalar2=1.0,
                                               op0=ALU.mult, op1=ALU.add),
                     [tz, x1rb_read[0], x1rb_tok[0]], tag=f"gd{c}")
            td = vop(lambda v: v.reciprocal(DN, DN), [td], tag=f"gd2{c}")
            E = x1rb[0][:, 512:1024]
            te = vop(lambda v: v.tensor_mul(E, Z, Z), [tz, td], tag=f"gsq{c}")
            te = sop(lambda s: s.activation(out=E, in_=E, func=AF.Exp, scale=-1.0),
                     [te], tag=f"gex{c}")
            tq = vop(lambda v: v.tensor_scalar(out=Z, in0=DN, scalar1=0.5 * A5,
                                               scalar2=0.5 * A4, op0=ALU.mult, op1=ALU.add),
                     [te], tag=f"gp1{c}")
            for aa in (A3, A2, A1):
                tq = vop(lambda v: v.tensor_mul(Z, Z, DN), [tq], tag=f"gp2{c}")
                tq = vop(lambda v, aa=aa: v.tensor_single_scalar(out=Z, in_=Z,
                                                                 scalar=0.5 * aa, op=ALU.add),
                         [tq], tag=f"gp3{c}")
            tq = vop(lambda v: v.tensor_mul(Z, Z, DN), [tq], tag=f"gp4{c}")
            tq = vop(lambda v: v.tensor_mul(Z, Z, E), [tq], tag=f"gp5{c}")
            tm = vop(lambda v: v.tensor_scalar(out=DN, in0=hp, scalar1=0.0, scalar2=None,
                                               op0=ALU.is_ge), [tq], tag=f"gm{c}")
            tm2 = vop(lambda v: v.tensor_scalar(out=E, in0=DN, scalar1=-2.0, scalar2=1.0,
                                                op0=ALU.mult, op1=ALU.add), [tm], tag=f"gm2{c}")
            tphi = vop(lambda v: v.tensor_mul(Z, Z, E), [tm2], tag=f"gf1{c}")
            tphi = vop(lambda v: v.tensor_add(Z, Z, DN), [tphi], tag=f"gf2{c}")
            tg = vop(lambda v, jo=jo: v.tensor_mul(hTb[:, 512 * jo: 512 * jo + 512], hp, Z),
                     [tphi, hT_tok[jo], hT_read[jo]], tag=f"gelu{c}")
            gelu_busy[0] = tg
            zs_tok[0] = tg
            x1rb_tok[0] = tg
            x1rb_read[0] = tg
            hT_tok[jo] = tg
            hT_toks.append(tg)
        return xn_toks, hT_toks

    def ffn_T2(c, hT_toks):
        slots, rdeps = ring_alloc(1)
        slc = slots[0]
        mmt = None
        for kk in range(2):
            deps = [hT_toks[kk], tok_cw2] + (rdeps if kk == 0 else [])
            mmt = mm(ps_ring[0:2, 512 * slc: 512 * slc + 512],
                     cw2_b[:, 2 * kk: 2 * kk + 2], hTb[:, 512 * kk: 512 * kk + 512],
                     deps, start=(kk == 0), stop=(kk == 1), tag=f"cmm{c}")
            hT_read[kk] = mmt
        ring_filled(slots, mmt)
        tv1 = vop(lambda v: v.tensor_scalar(out=ctmp[:, :], in0=ps_ring[0:2, 512 * slc: 512 * slc + 512],
                                            scalar1=cb2_s[:, 0:1], scalar2=None, op0=ALU.add),
                  [mmt, tok_cb2, ctmp_read[0]], tag=f"cb{c}")
        ring_used(slots, tv1)
        te = sop(lambda s: s.activation(out=ctmp[:, :], in_=ctmp[:, :], func=AF.Exp, scale=2.0),
                 [tv1], tag=f"ctanh{c}")
        tv2 = vop(lambda v: v.tensor_single_scalar(out=ctmp[:, :], in_=ctmp[:, :], scalar=1.0,
                                                   op=ALU.add), [te], tag=f"ct1{c}")
        tv3 = vop(lambda v: v.reciprocal(ctmp[:, :], ctmp[:, :]), [tv2], tag=f"ct2{c}")
        tv4 = vop(lambda v: v.tensor_scalar(out=ctmp[:, :], in0=ctmp[:, :], scalar1=-2.0,
                                            scalar2=1.0, op0=ALU.mult, op1=ALU.add),
                  [tv3], tag=f"ct3{c}")
        ctv4_hold[0] = tv4
        for tt in range(4):
            slots, rdeps = ring_alloc(1)
            sl = slots[0]
            mmt = None
            for kk in range(8):
                deps = [x2T_tok[kk]] + (rdeps if kk == 0 else [])
                mmt = mm(rs(sl, 72), x2T_f[:, 512 * kk + 128 * tt: 512 * kk + 128 * tt + 128],
                         sgt[:, 72 * kk: 72 * kk + 72], deps,
                         start=(kk == 0), stop=(kk == 7), tag=f"gmm{c}")
                x2T_read[kk] = mmt
            ring_filled(slots, mmt)
            tev = vcopy(gates[:, 72 * tt: 72 * tt + 72], rs(sl, 72), [mmt, gates_tok[tt]],
                        tag=f"gev{c}")
            ring_used(slots, tev)
            gates_tok[tt] = tev

    def compt_stage(c):
        tv4 = ctv4_hold[0]
        slots, rdeps = ring_alloc(1)
        slt = slots[0]
        ctoks = []
        for tt in range(4):
            deps = [tv4, t_ident] + (rdeps if tt == 0 else [])
            ttk = tp_(rs(slt, 2, 128 * tt), ctmp[:, 128 * tt: 128 * tt + 128], deps,
                      tag=f"cT{c}", k=2)
            ctoks.append(ttk)
            ctmp_read[0] = ttk
        ring_filled(slots, ctoks[-1])
        cc = vop(lambda v, slt=slt: v.tensor_copy(
            compt[:, :].rearrange("p (t c) -> p t c", t=4),
            rs(slt, 512).rearrange("p (t c) -> p t c", c=128)[:, :, 0:2]),
            [ctoks[-1]] + [compt_tok[t] for t in range(4)], tag=f"ccp{c}")
        ring_used(slots, cc)
        for tt in range(4):
            compt_tok[tt] = cc

    def gates_exp(c):
        toks = []
        for tt in range(4):
            te = sop(lambda s, tt=tt: s.activation(
                out=gates[:, 72 * tt: 72 * tt + 72], in_=gates[:, 72 * tt: 72 * tt + 72],
                func=AF.Exp, scale=scap(SC_NINV)), [gates_tok[tt], t_scal], tag=f"gex{c}")
            gates_tok[tt] = te
            toks.append(te)
        return toks

    def calibrate_argmax(c, ge_toks):
        tidx_toks = []
        for tt in range(4):
            g_ap = gates[:, 72 * tt: 72 * tt + 72]
            t1 = vop(lambda v, g=g_ap: v.tensor_single_scalar(out=calv[:, :], in_=g,
                                                              scalar=1.0, op=ALU.add),
                     [ge_toks[tt]], tag=f"ca1{c}")
            t2 = vop(lambda v: v.reciprocal(calv[:, :], calv[:, :]), [t1], tag=f"ca2{c}")
            t3 = vop(lambda v: v.tensor_scalar(out=calv[:, :], in0=calv[:, :], scalar1=7.0,
                                               scalar2=None, op0=ALU.mult), [t2], tag=f"ca3{c}")
            tacc = vop(lambda v: v.memset(cal2[:, :], 0.0), [t3], tag=f"ca4{c}")
            for j in range(7):
                ta = vop(lambda v, j=j: v.tensor_scalar(out=calt[:, :], in0=calv[:, :],
                                                        scalar1=float(j), scalar2=0.0,
                                                        op0=ALU.subtract, op1=ALU.max),
                         [tacc if j == 0 else tb2], tag=f"cj1{c}")
                tb = vop(lambda v, j=j: v.tensor_scalar(out=calt[:, :], in0=calt[:, :],
                                                        scalar1=1.0, scalar2=scap(SC_D0 + j),
                                                        op0=ALU.min, op1=ALU.mult),
                         [ta, t_scal], tag=f"cj2{c}")
                tb2 = vop(lambda v: v.tensor_add(cal2[:, :], cal2[:, :], calt[:, :]),
                          [tb], tag=f"cj3{c}")
            tk0 = vop(lambda v: v.tensor_scalar(out=cal2[:, :], in0=cal2[:, :],
                                                scalar1=scap(SC_K0), scalar2=None, op0=ALU.add),
                      [tb2, t_scal], tag=f"ck0{c}")
            m1 = vop(lambda v: v.reduce_max(idxs[:, 0:1], cal2[:, 64:72], axis=AX.X),
                     [tk0], tag=f"am1{c}")
            m2 = vop(lambda v: v.tensor_scalar(out=calt[:, 0:8], in0=cal2[:, 64:72],
                                               scalar1=idxs[:, 0:1], scalar2=None,
                                               op0=ALU.is_ge), [m1], tag=f"am2{c}")
            m3 = vop(lambda v: v.tensor_mul(calt[:, 8:16], calt[:, 0:8], bigmi[:, :]),
                     [m2, t_bigmi_i], tag=f"am3{c}")
            m4 = vop(lambda v: v.tensor_scalar(out=calt[:, 8:16], in0=calt[:, 8:16],
                                               scalar1=-1.0, scalar2=999.0,
                                               op0=ALU.mult, op1=ALU.add), [m3], tag=f"am4{c}")
            m5 = vop(lambda v: v.tensor_reduce(idxs[:, 1:2], calt[:, 8:16], axis=AX.X,
                                               op=ALU.min), [m4], tag=f"am5{c}")
            w1 = vop(lambda v: v.reduce_max(calt[:, 0:8],
                                            cal2[:, 0:64].rearrange("p (c t) -> p c t", t=8),
                                            axis=AX.X), [tk0], tag=f"aw1{c}")
            w2 = vop(lambda v: v.tensor_tensor(
                out=calv[:, 0:64].rearrange("p (c t) -> p c t", t=8),
                in0=cal2[:, 0:64].rearrange("p (c t) -> p c t", t=8),
                in1=calt[:, 0:8].rearrange("p c -> p c 1" if False else "p (c o) -> p c o", o=1).to_broadcast([128, 8, 8]),
                op=ALU.is_ge), [w1], tag=f"aw2{c}")
            w3 = vop(lambda v: v.tensor_tensor(
                out=calv[:, 0:64].rearrange("p (c t) -> p c t", t=8),
                in0=calv[:, 0:64].rearrange("p (c t) -> p c t", t=8),
                in1=bigmi[:, :].rearrange("p (c o) -> p c o" if False else "p t -> p 1 t" if False else "p (o t) -> p o t", o=1).to_broadcast([128, 8, 8]),
                op=ALU.mult), [w2, t_bigmi_i], tag=f"aw3{c}")
            w4 = vop(lambda v: v.tensor_scalar(out=calv[:, 0:64], in0=calv[:, 0:64],
                                               scalar1=-1.0, scalar2=999.0,
                                               op0=ALU.mult, op1=ALU.add), [w3], tag=f"aw4{c}")
            w5 = vop(lambda v: v.tensor_reduce(calt[:, 16:24],
                                               calv[:, 0:64].rearrange("p (c t) -> p c t", t=8),
                                               axis=AX.X, op=ALU.min), [w4], tag=f"aw5{c}")
            s1 = vop(lambda v: v.tensor_scalar(out=calt[:, 24:32], in0=iotaj[:, 0:8],
                                               scalar1=idxs[:, 1:2], scalar2=None,
                                               op0=ALU.is_equal), [m5, t_ioj], tag=f"as1{c}")
            s2 = vop(lambda v: v.tensor_mul(calt[:, 24:32], calt[:, 24:32], calt[:, 16:24]),
                     [s1, w5], tag=f"as2{c}")
            s3 = vop(lambda v: v.reduce_sum(idxs[:, 2:3], calt[:, 24:32], axis=AX.X),
                     [s2], tag=f"as3{c}")
            s4 = vop(lambda v, tt=tt: v.tensor_scalar(out=idxs[:, 3 + tt:4 + tt], in0=idxs[:, 1:2],
                                                      scalar1=8.0, scalar2=idxs[:, 2:3],
                                                      op0=ALU.mult, op1=ALU.add),
                     [s3, idx_read[tt]], tag=f"as4{c}")
            tidx_toks.append(s4)
        return tidx_toks

    def gathers(c, tidx_toks):
        t2s = []
        for tt in range(4):
            slots, rdeps = ring_alloc(1)
            sl = slots[0]
            t1 = tp_(ps_ring[0:1, 512 * sl: 512 * sl + 128], idxs[:, 3 + tt:4 + tt],
                     [tidx_toks[tt], t_ident] + rdeps, tag=f"tiT{c}")
            idx_read[tt] = t1
            ring_filled([sl], t1)
            t2 = vcopy(tidxr[:, 128 * tt: 128 * tt + 128],
                       ps_ring[0:1, 512 * sl: 512 * sl + 128],
                       [t1, tidxr_read[tt]], tag=f"tir{c}")
            ring_used([sl], t2)
            t2s.append(t2)
        t4s = []
        for tt in range(4):
            ohb = ohbufs[tt]
            slots2, rdeps2 = ring_alloc(1)
            sl2 = slots2[0]
            t3 = mm(ps_ring[0:64, 512 * sl2: 512 * sl2 + 128], onesrow[:, 0:64],
                    tidxr[:, 128 * tt: 128 * tt + 128],
                    [t2s[tt], t_ones_r] + rdeps2, tag=f"tibc{c}")
            tidxr_read[tt] = t3
            ring_filled(slots2, t3)
            t4 = vop(lambda v, sl2=sl2, ohb=ohb: v.tensor_scalar(
                out=ohb[:, :], in0=ps_ring[0:64, 512 * sl2: 512 * sl2 + 128],
                scalar1=iota64[:, :], scalar2=None, op0=ALU.is_equal),
                [t3, t_io64, oh_tok[tt], oh_read[tt]], tag=f"ioh{c}")
            ring_used(slots2, t4)
            oh_tok[tt] = t4
            t4s.append(t4)

        def emit_tile(tt):
            ohb = ohbufs[tt]
            slots3, rdeps3 = ring_alloc(3)
            g1 = mm(rs(slots3[0], 65), ohb[:, :], spl_b[:, :], [t4s[tt], tok_spl] + rdeps3,
                    tag=f"gsp{c}")
            g2 = mm(rs(slots3[1]), ohb[:, :], dirs_b[:, 0:512], [t4s[tt], tok_dirs],
                    tag=f"gd1{c}")
            g3 = mm(rs(slots3[2]), ohb[:, :], dirs_b[:, 512:1024], [t4s[tt], tok_dirs],
                    tag=f"gd2{c}")
            ring_filled(slots3, g3)
            oh_read[tt] = g3
            return (slots3, g1, g2, g3)
        return emit_tile

    def spline_y(c, gth_fn):
        ysts = []
        for tt in range(4):
            slots3, g1, g2, g3 = gth_fn(tt)
            ai = 4 * (c % 2) + tt
            e1 = vcopy(splg[:, :], rs(slots3[0], 65), [g1], tag=f"spc{c}")
            h1 = vop(lambda v, tt=tt: v.tensor_scalar(out=hhs[:, 0:16], in0=splg[:, 0:16],
                                               scalar1=compt[:, 2 * tt: 2 * tt + 1],
                                               scalar2=None, op0=ALU.mult),
                     [e1, compt_tok[tt]], tag=f"sp1{c}")
            h2 = vop(lambda v, tt=tt: v.scalar_tensor_tensor(out=hhs[:, 16:32], in0=splg[:, 16:32],
                                                      scalar=compt[:, 2 * tt + 1: 2 * tt + 2],
                                                      in1=hhs[:, 0:16], op0=ALU.mult,
                                                      op1=ALU.add), [h1], tag=f"sp2{c}")
            h3 = vop(lambda v: v.tensor_add(hhs[:, 32:48], hhs[:, 16:32], splg[:, 32:48]),
                     [h2], tag=f"sp3{c}")
            h4 = vop(lambda v: v.tensor_scalar(out=hhs[:, 32:48], in0=hhs[:, 32:48],
                                               scalar1=0.0, scalar2=None, op0=ALU.max),
                     [h3], tag=f"sp4{c}")
            h5 = vop(lambda v: v.tensor_mul(hhs[:, 0:16], hhs[:, 32:48], splg[:, 48:64]),
                     [h4], tag=f"sp5{c}")
            h6 = vop(lambda v, tt=tt: v.reduce_sum(mags[:, tt:tt + 1], hhs[:, 0:16], axis=AX.X),
                     [h5, mags_tok[tt]], tag=f"sp6{c}")
            h7 = vop(lambda v, tt=tt: v.tensor_add(mags[:, tt:tt + 1], mags[:, tt:tt + 1],
                                            splg[:, 64:65]), [h6], tag=f"sp7{c}")
            h8 = vop(lambda v, tt=tt: v.tensor_scalar(out=mags[:, tt:tt + 1], in0=mags[:, tt:tt + 1],
                                               scalar1=scap(SC_OS), scalar2=None,
                                               op0=ALU.mult), [h7, t_scal], tag=f"sp8{c}")
            mags_tok[tt] = h8
            y1 = vop(lambda v, ai=ai, tt=tt: v.tensor_scalar(
                out=ybuf[tt][:, :], in0=ACC[ai][:, :], scalar1=mv3[:, 2 * tt:2 * tt + 1],
                scalar2=r3c[:, tt:tt + 1], op0=ALU.subtract, op1=ALU.mult),
                [acc_tok[ai], p2[c]["r3"][tt], ybuf_read[tt], ybuf_tok[tt]], tag=f"y1{c}")
            y2 = vop(lambda v, tt=tt, s=slots3[1]: v.scalar_tensor_tensor(
                out=ybuf[tt][:, 0:512], in0=rs(s), scalar=mags[:, tt:tt + 1],
                in1=ybuf[tt][:, 0:512], op0=ALU.mult, op1=ALU.add), [y1, h8, g2], tag=f"y2{c}")
            y3 = vop(lambda v, tt=tt, s=slots3[2]: v.scalar_tensor_tensor(
                out=ybuf[tt][:, 512:1024], in0=rs(s), scalar=mags[:, tt:tt + 1],
                in1=ybuf[tt][:, 512:1024], op0=ALU.mult, op1=ALU.add), [y2, g3], tag=f"y3{c}")
            ring_used(slots3, y3)
            ybuf_tok[tt] = y3
            acc_read[ai] = y1
            s1 = vop(lambda v, tt=tt: v.bn_stats(st6[:, 0:6], ybuf[tt][:, 0:512]), [y3],
                     tag=f"ys1{c}")
            s2 = vop(lambda v, tt=tt: v.bn_stats(st6[:, 6:12], ybuf[tt][:, 512:1024]), [y3],
                     tag=f"ys2{c}")
            s3 = vop(lambda v, tt=tt: v.bn_aggr(mvy[:, 2 * tt:2 * tt + 2], st6[:, :]),
                     [s1, s2, mvy_tok[tt]], tag=f"ys3{c}")
            mvy_tok[tt] = s3
            ysts.append(s3)
        return ysts

    def out_assembly(c, ry_toks):
        for tt in range(4):
            row = 128 * (4 * c + tt)
            tok_x1 = dma("sync", x1rb[0][:, :], x1s_d[row:row + 128, :],
                         [x1rb_read[0], x1dma_tok[4 * (c % 2) + tt]], tag=f"x1r{c}")
            x1rb_tok[0] = tok_x1
            o1 = vop(lambda v, tt=tt: v.tensor_scalar(
                out=ybuf[tt][:, :], in0=ybuf[tt][:, :], scalar1=mvy[:, 2 * tt:2 * tt + 1],
                scalar2=ryc[:, tt:tt + 1], op0=ALU.subtract, op1=ALU.mult),
                [ybuf_tok[tt], ry_toks[tt]], tag=f"o1{c}")
            o2 = vop(lambda v, tt=tt: v.tensor_mul(ybuf[tt][:, :], ybuf[tt][:, :], ngam_t[:, :]),
                     [o1] + ngam_toks, tag=f"o2{c}")
            o3 = vop(lambda v, tt=tt: v.tensor_add(ybuf[tt][:, :], ybuf[tt][:, :],
                                                   x1rb[0][:, :]), [o2, tok_x1], tag=f"o3{c}")
            x1rb_read[0] = o3
            o4 = vop(lambda v, tt=tt: v.tensor_add(ybuf[tt][:, :], ybuf[tt][:, :], nbet_t[:, :]),
                     [o3] + nbet_toks, tag=f"o4{c}")
            ybuf_tok[tt] = o4
            od = dma("sync", out_d[row:row + 128, :], ybuf[tt][:, :], [o4], tag=f"od{c}")
            ybuf_read[tt] = od
            outdma_toks.append(od)

    dbg_state = {}

    def dbg_dump(i, ap, deps, w=D):
        if dbg_d is None:
            return None
        return dma("gpsimd", dbg_d[i, :, 0:w], ap, deps, tag=f"dbg{i}")

    zrb_read = [None, None]
    p2 = {}

    def p2_m0(ch):
        st = p2.setdefault(ch, {})
        a_toks = a_init(ch)
        ztoks = []
        for tt in range(4):
            ai = 4 * (ch % 2) + tt
            zi = tt % 2
            tl = dma("sync", zrb[zi][:, :],
                     zs_d[512 * ch + 128 * tt: 512 * ch + 128 * tt + 128, :],
                     [zrb_read[zi]] + zdma_by_chunk[ch], tag=f"zl{ch}")
            tadd = vop(lambda v, ai=ai, zi=zi: v.tensor_add(
                ACC[ai][:, :], ACC[ai][:, :], zrb[zi][:, :]),
                [tl, a_toks[tt]], tag=f"zadd{ch}")
            zrb_read[zi] = tadd
            acc_tok[ai] = tadd
            ztoks.append(tadd)
        st["ztoks"] = ztoks

    def p2_m1(ch):
        st = p2[ch]
        stat2 = ln2_x1(ch, st["ztoks"])
        st["r2"] = rstd_visit(ch, stat2, mv2, r2c, r2_tok, "r2")

    def p2_m2(ch):
        st = p2[ch]
        x1_toks = x1_compute(ch, st["r2"])
        stat3 = []
        for tt in range(4):
            ai = 4 * (ch % 2) + tt
            t1 = vop(lambda v, ai=ai: v.bn_stats(st6[:, 0:6], ACC[ai][:, 0:512]),
                     [x1_toks[tt]], tag=f"l3a{ch}")
            t2 = vop(lambda v, ai=ai: v.bn_stats(st6[:, 6:12], ACC[ai][:, 512:1024]),
                     [x1_toks[tt]], tag=f"l3b{ch}")
            t3 = vop(lambda v, tt=tt: v.bn_aggr(mv3[:, 2 * tt:2 * tt + 2], st6[:, :]),
                     [t1, t2, mv3_tok[tt]], tag=f"l3c{ch}")
            mv3_tok[tt] = t3
            stat3.append(t3)
        st["r3"] = rstd_visit(ch, stat3, mv3, r3c, r3_tok, "r3")
        st["x1_toks"] = x1_toks

    def p2_m3(ch):
        st = p2[ch]
        _, st["hT_toks"] = ffn_T1(ch, st["x1_toks"], st["r3"])

    def p2_m3b(ch):
        st = p2[ch]
        ffn_T2(ch, st["hT_toks"])

    def p2_m4(ch):
        st = p2[ch]
        if "ffntail" not in skip:
            compt_stage(ch)
            st["tidx"] = calibrate_argmax(ch, gates_exp(ch))
        else:
            st["tidx"] = [None] * 4

    def p2_m5(ch):
        st = p2[ch]
        if "ffntail" not in skip:
            gth = gathers(ch, st["tidx"])
            st["ysts"] = spline_y(ch, gth)
        else:
            st["ysts"] = [None] * 4

    def p2_m6(ch):
        st = p2[ch]
        st["ry"] = rstd_visit(ch, st["ysts"], mvy, ryc, ry_tok, "ry")

    def p2_m7(ch):
        st = p2[ch]
        out_assembly(ch, st["ry"])
        del p2[ch]

    phase_queue = []

    def enqueue_chunk(ch):
        q = [lambda ch=ch: p2_m0(ch), lambda ch=ch: p2_m1(ch)]
        if ch >= 1:
            q.append(lambda ch=ch: p2_m5(ch - 1))
        q.append(lambda ch=ch: p2_m2(ch))
        if ch >= 1:
            q.append(lambda ch=ch: p2_m6(ch - 1))
        q.append(lambda ch=ch: p2_m3(ch))
        if ch >= 1:
            q.append(lambda ch=ch: p2_m7(ch - 1))
        q.append(lambda ch=ch: p2_m3b(ch))
        q.append(lambda ch=ch: p2_m4(ch))
        phase_queue.extend(q)

    ppc = NPAIR // CH if CH <= NPAIR else 1

    sched_by_c = [tile_schedule(c) for c in range(CH)]
    for p in range(NPAIR):
        if p == 0:
            kvq_load(0)
        for ci, c in enumerate(range(CH)):
            if "pairs" not in skip:
                pv_last, sg_idx = emit_pair(c, p, sched_by_c[c])
                if ci == 0 and p + 1 < NPAIR:
                    kvq_load(p + 1)
                zt = z_evac(c, p, pv_last, sg_idx)
                if c == 0 and p == 0:
                    dbg_dump(1, zsb[:, 0:512], [zs_tok[0]], w=512)
            if phase_queue:
                phase_queue.pop(0)()
                if ci == CH - 1 and len(phase_queue) > 1:
                    phase_queue.pop(0)()
        if (p + 1) % ppc == 0:
            enqueue_chunk((p + 1) // ppc - 1)

    while phase_queue:
        phase_queue.pop(0)()
    p2_m5(CH - 1)
    p2_m6(CH - 1)
    p2_m7(CH - 1)

    fin_deps = [outdma_toks[-1]]
    for sname in DMAPOOL["sync"] + DMAPOOL["gpsimd"]:
        cc = P.count.get(sname, 0)
        if cc:
            fin_deps.append((sname, cc))
    P.op("gpsimd", lambda g: g.nop(), fin_deps, tag="fin")

    import os
    nopify = set(os.environ.get("NOPIFY", "").split(",")) - {""}
    noptags = set(os.environ.get("NOPTAGS", "").split(",")) - {""}
    refd = {}
    for e2 in ENGINES:
        for (fn, waits, sem, amt, tag) in P.stream[e2]:
            for (s, v) in waits:
                refd.setdefault(s, set()).add(v)
    remap = {}
    for e2 in ENGINES:
        if os.environ.get("NO_SEMCOMP", "") == "1":
            break
        s = ESEM[e2]
        total = P.count.get(s, 0)
        if not total:
            continue
        rset = refd.get(s, set())
        m = {}
        nci = 0
        for k in range(1, total + 1):
            if k in rset or k == total:
                nci += 1
                m[k] = (nci, True)
            else:
                m[k] = (nci, False)
        remap[s] = m
    with nc.Block() as block:
        def emit(eng_name):
            def body(eng):
                own = ESEM[eng_name]
                for (fn, waits, sem, amt, tag) in P.stream[eng_name]:
                    for (s, v) in waits:
                        if s in remap:
                            v = remap[s][v][0]
                        eng.wait_ge(semh[s], v)
                    tagnop = any(tag.startswith(tp) for tp in noptags) if noptags else False
                    if (eng_name in nopify or tagnop) and amt == 1:
                        inst = eng.nop()
                    else:
                        inst = fn(eng)
                    if sem in remap:
                        cnt, do_inc = remap[sem][P2_cursor[sem]]
                        P2_cursor[sem] += 1
                        if do_inc:
                            inst.then_inc(semh[sem], amt)
                    else:
                        inst.then_inc(semh[sem], amt)
            return body
        P2_cursor = {s: 1 for s in remap}
        block.sync(emit("sync"))
        block.gpsimd(emit("gpsimd"))
        block.tensor(emit("tensor"))
        block.scalar(emit("scalar"))
        block.vector(emit("vector"))

    es.close()
    return nc


_prog_cache = {}


def _get_prog(S):
    if S not in _prog_cache:
        _prog_cache[S] = build_program(S)
    return _prog_cache[S]


def make_in_map(core_inputs):
    return core_inputs


def kernel(**inputs):
    emb = np.ascontiguousarray(np.asarray(inputs["L0_attn_embedded"], np.float32))
    B, S, _ = emb.shape
    f = lambda k: np.ascontiguousarray(np.asarray(inputs[k], np.float32))
    wqT = np.ascontiguousarray(f("L0_attn_Wq").T)
    wkT = np.ascontiguousarray(f("L0_attn_Wk").T)
    wvT = np.ascontiguousarray(f("L0_attn_Wv").T)
    bqT = np.ascontiguousarray(f("L0_attn_bq").reshape(8, 128).T)
    bkT = np.ascontiguousarray(f("L0_attn_bk").reshape(8, 128).T)
    bvT = np.ascontiguousarray(f("L0_attn_bv").reshape(8, 128).T)
    gamma = f("L0_attn_gamma").reshape(1, D)
    beta = f("L0_attn_beta").reshape(1, D)
    ngamma = f("L0_ffn_norm_gamma").reshape(1, D)
    nbeta = f("L0_ffn_norm_beta").reshape(1, D)
    sigT = np.ascontiguousarray(f("L0_ffn_signatures_raw").T)
    knots = f("L0_ffn_knot_values").reshape(1, 8)
    temp = f("L0_ffn_temperature").reshape(1, 1)
    oscale = f("L0_ffn_output_scale").reshape(1, 1)
    cw1 = f("L0_ffn_compress_w1")
    cb1T = np.ascontiguousarray(f("L0_ffn_compress_b1").reshape(2, 128).T)
    cw2 = np.ascontiguousarray(
        f("L0_ffn_compress_w2").reshape(2, 128, 2).transpose(1, 0, 2).reshape(128, 4))
    cb2 = f("L0_ffn_compress_b2").reshape(2, 1)
    spl = np.ascontiguousarray(np.concatenate([
        f("L0_ffn_spline_w1").reshape(64, 32),
        f("L0_ffn_spline_b1").reshape(64, 16),
        f("L0_ffn_spline_w2").reshape(64, 16),
        f("L0_ffn_spline_b2").reshape(64, 1)], axis=1))
    dirs = f("L0_ffn_directions")

    shared = dict(wqT=wqT, wkT=wkT, wvT=wvT, bqT=bqT, bkT=bkT, bvT=bvT,
                  gamma=gamma, beta=beta, ngamma=ngamma, nbeta=nbeta, sigT=sigT,
                  knots=knots, temp=temp, oscale=oscale, cw1=cw1, cb1T=cb1T,
                  cw2r=cw2, cb2=cb2, spl=spl, dirs=dirs)
    in_maps = []
    for b in range(B):
        m = dict(shared)
        m["x0"] = np.ascontiguousarray(emb[b])
        in_maps.append(m)
    nc = _get_prog(S)
    res = run_bass_kernel_spmd(nc, in_maps, core_ids=list(range(B)))
    out = np.stack([np.asarray(r["out"], np.float32) for r in res.results], axis=0)
    return out


# version 12
# speedup vs baseline: 1.3244x; 1.1904x over previous
import numpy as np
import ml_dtypes
import concourse.bass as bass
import concourse.mybir as mybir
from concourse.bass_utils import run_bass_kernel_spmd
from contextlib import ExitStack

F32 = mybir.dt.float32
BF16 = mybir.dt.bfloat16
AF = mybir.ActivationFunctionType
ALU = mybir.AluOpType
AX = mybir.AxisListType

D = 1024
NPAIR = 8
EPS = 1e-5

ENGINES = ("sync", "gpsimd", "tensor", "scalar", "vector")
ESEM = {"sync": "eS", "gpsimd": "eG", "tensor": "eT", "scalar": "eA", "vector": "eV"}
import os as _os_se
SIM_EDGES = _os_se.environ.get("SIM_EDGES", "") == "1"
NDMASEM = 20
DMAPOOL = {"sync": [f"dS{i}" for i in range(NDMASEM)],
           "gpsimd": [f"dG{i}" for i in range(NDMASEM)]}


class Planner:
    def __init__(self):
        self.stream = {e: [] for e in ENGINES}
        self.count = {}
        self.snap = {}
        self.W = {e: {} for e in ENGINES}
        self.nwaits = 0
        self.dma_rr = {}

    def _merge_into(self, w, tok):
        s, v = tok
        snap = self.snap.get((s, v))
        if snap is None:
            if w.get(s, -1) < v:
                w[s] = v
            return
        for k2, v2 in snap.items():
            if w.get(k2, -1) < v2:
                w[k2] = v2

    def op(self, eng, fn, deps=(), dma=False, tag=""):
        w = self.W[eng]
        deps = list(deps)
        if dma:
            pool = DMAPOOL[eng]
            i = self.dma_rr.get(eng, 0)
            self.dma_rr[eng] = i + 1
            sem = pool[i % NDMASEM]
            prevc = self.count.get(sem, 0)
            if prevc > 0:
                deps.append((sem, prevc))
            amt = 16
        else:
            sem = ESEM[eng]
            amt = 1
            prev = self.count.get(sem, 0)
            if prev > 0:
                deps.insert(0, (sem, prev))
        waits = []
        for tok in deps:
            if tok is None:
                continue
            s, v = tok
            if w.get(s, -1) >= v:
                continue
            waits.append((s, v))
            self._merge_into(w, tok)
        c = self.count.get(sem, 0) + amt
        self.count[sem] = c
        snapd = dict(w)
        snapd[sem] = max(snapd.get(sem, 0), c)
        self.snap[(sem, c)] = snapd
        self.stream[eng].append((fn, waits, sem, amt, tag))
        self.nwaits += len(waits)
        return (sem, c)


def build_program(S, skip=()):
    NT = S // 128
    CH = S // 512
    import os as _os0
    _scr = 16384 if _os0.environ.get("DEVDBG", "") == "1" else 2048
    nc = bass.Bass("TRN2", target_bir_lowering=False, dynamic_dma_scratch_size=_scr)
    P = Planner()

    def din(name, shape):
        return nc.dram_tensor(name, list(shape), F32, kind="ExternalInput").ap()

    x0_d = din("x0", (S, D))
    wqT_d = din("wqT", (D, D)); wkT_d = din("wkT", (D, D)); wvT_d = din("wvT", (D, D))
    bqT_d = din("bqT", (128, 8)); bkT_d = din("bkT", (128, 8)); bvT_d = din("bvT", (128, 8))
    gam_d = din("gamma", (1, D)); bet_d = din("beta", (1, D))
    ngam_d = din("ngamma", (1, D)); nbet_d = din("nbeta", (1, D))
    sigT_d = din("sigT", (D, 64))
    knots_d = din("knots", (1, 8)); temp_d = din("temp", (1, 1)); oscale_d = din("oscale", (1, 1))
    cw1_d = din("cw1", (D, 256)); cb1T_d = din("cb1T", (128, 2))
    cw2_d = din("cw2r", (128, 4)); cb2_d = din("cb2", (2, 1))
    spl_d = din("spl", (64, 65)); dirs_d = din("dirs", (64, D))
    out_d = nc.dram_tensor("out", [S, D], F32, kind="ExternalOutput").ap()
    import os as _os
    _dbg = _os.environ.get("DEVDBG", "") in ("1", "2")
    _dbg1 = _os.environ.get("DEVDBG", "") == "1"
    _ik = "ExternalOutput" if _dbg else "Internal"
    kts_d = nc.dram_tensor("kts", [NPAIR, 128, S], BF16, kind=_ik).ap()
    qts_d = nc.dram_tensor("qts", [NPAIR, 128, S], BF16, kind="Internal").ap()
    aTs_d = nc.dram_tensor("aTs", [8, 128, S], BF16, kind="Internal").ap()
    vs_d = nc.dram_tensor("vs", [NPAIR, NT, 128, 130], BF16, kind=_ik).ap()
    x1s_d = nc.dram_tensor("x1s", [S, D], F32, kind=_ik).ap()
    zs_d = nc.dram_tensor("zs", [S, D], F32, kind=_ik).ap()
    dbg_d = nc.dram_tensor("dbg", [8, 128, D], F32, kind="ExternalOutput").ap() if _dbg1 else None

    es = ExitStack()

    def sb(name, shape, dtype):
        return es.enter_context(nc.sbuf_tensor(name, list(shape), dtype))

    ident = sb("ident", (128, 128), F32)
    diagmask = sb("diagmask", (128, 128), BF16)
    dgf = sb("dgf", (128, 128), F32)
    onescol_b = sb("onescol_b", (128, 1), BF16)
    onesrow = sb("onesrow", (1, 128), F32)
    iota64 = sb("iota64", (64, 1), F32)
    iotaj = sb("iotaj", (128, 72), F32)
    bigmi = sb("bigmi", (128, 8), F32)
    scal = sb("scal", (128, 16), F32)
    scrow = sb("scrow", (1, 16), F32)
    knots_s = sb("knots_s", (1, 8), F32)
    temp_s = sb("temp_s", (1, 4), F32)
    gam_t = sb("gam_t", (128, D), F32); bet_t = sb("bet_t", (128, D), F32)
    ngam_t = sb("ngam_t", (128, D), F32); nbet_t = sb("nbet_t", (128, D), F32)
    bqT = sb("bqT_s", (128, 8), F32); bkT = sb("bkT_s", (128, 8), F32); bvT = sb("bvT_s", (128, 8), F32)
    cb1T = sb("cb1T_s", (128, 2), F32)
    cb2_s = sb("cb2_s", (2, 1), F32)
    mu1 = sb("mu1", (128, NT), F32); r1 = sb("r1", (128, NT), F32)
    mvt = sb("mvt", (128, 2), F32)
    st6 = sb("st6", (128, 12), F32)
    sgt = sb("sgt", (128, 8 * 72), F32)
    x0c = [sb(f"x0c{i}", (128, D), F32) for i in range(8)]
    ACC = [sb(f"ACC{i}", (128, D), F32) for i in range(8)]
    aT = sb("aT", (128, 8 * 512), BF16)
    qT = sb("qT", (128, 8 * 512), BF16)
    kbuf = [sb(f"kbuf{i}", (128, S), BF16) for i in range(2)]
    vbuf = [sb(f"vbuf{i}", (128, NT * 130), BF16) for i in range(2)]
    pTb = [sb(f"pTb{i}", (128, 1536), BF16) for i in range(4)]
    zsb = sb("zsb", (128, 512), F32)
    zsb2 = sb("zsb2", (128, 512), F32)
    dnm = sb("dnm", (128, 64), F32)
    dnr = sb("dnr", (128, 12), F32)
    zst = [sb(f"zst{i}", (128, 64), F32) for i in range(4)]
    wstage = sb("wstage", (128, D), F32)
    zrb = [sb(f"zrb{i}", (128, D), F32) for i in range(2)]
    ybuf = [sb(f"ybuf{i}", (128, D), F32) for i in range(4)]
    x1rb = [sb("x1rb0", (128, D), F32)]
    gates = sb("gates", (128, 4 * 72), F32)
    calv = sb("calv", (128, 72), F32)
    calt = sb("calt", (128, 72), F32)
    cal2 = sb("cal2", (128, 72), F32)
    idxs = sb("idxs", (128, 16), F32)
    tidxr = sb("tidxr", (1, 512), F32)
    ohbufs = [sb(f"ohT{i}", (64, 128), BF16) for i in range(4)]
    compt = sb("compt", (128, 8), F32)
    ctmp = sb("ctmp", (2, 512), F32)
    splg = sb("splg", (128, 65), F32)
    hhs = sb("hhs", (128, 48), F32)
    mags = sb("mags", (128, 4), F32)
    mv2 = sb("mv2", (128, 8), F32)
    mv3 = sb("mv3", (128, 8), F32)
    mvy = sb("mvy", (128, 8), F32)
    r2c = sb("r2c", (128, 4), F32); r3c = sb("r3c", (128, 4), F32); ryc = sb("ryc", (128, 4), F32)
    cw2_b = sb("cw2_b", (128, 4), BF16)
    dirs_b = sb("dirs_b", (64, D), BF16)
    spl_b = sb("spl_b", (64, 65), BF16)

    nsc = sb("nsc", (128, 3 * NT), F32)
    hpre = sb("hpre", (128, 512), F32)
    ps_ring = es.enter_context(nc.psum_tensor("ps_ring", [128, 3072], F32))
    ps_z = es.enter_context(nc.psum_tensor("ps_z", [128, 1024], F32))

    semh = {}
    for sname in (["eS", "eG", "eT", "eA", "eV"] + DMAPOOL["sync"] + DMAPOOL["gpsimd"]):
        semh[sname] = es.enter_context(nc.semaphore(sname))

    def mm(out_ap, lhsT_ap, rhs_ap, deps, start=True, stop=True, tag="mm", tpos=None):
        def f(t, o=out_ap, l=lhsT_ap, r=rhs_ap, s1=start, s2=stop, tp=tpos):
            return t.matmul(o, l, r, start=s1, stop=s2, tile_position=tp)
        return P.op("tensor", f, deps, tag=tag)

    def tp_(out_ap, in_ap, deps, tag="tp", k=128):
        def f(t, o=out_ap, i=in_ap, k=k):
            return t.transpose(o, i, ident[0:k, 0:k])
        return P.op("tensor", f, deps, tag=tag)

    def vop(fn, deps, tag="v"):
        return P.op("vector", fn, deps, tag=tag)

    def sop(fn, deps, tag="s"):
        return P.op("scalar", fn, deps, tag=tag)

    def vcopy(out_ap, in_ap, deps, tag="vcopy"):
        return vop(lambda v, o=out_ap, i=in_ap: v.tensor_copy(o, i), deps, tag=tag)

    def dma(queue, out_ap, in_ap, deps, tag="dma"):
        return P.op(queue, lambda e, o=out_ap, i=in_ap: e.dma_start(out=o, in_=i),
                    deps, dma=True, tag=tag)

    ring_use = [None] * 6
    ring_fill = [None] * 6
    rp = [0]

    def ring_alloc(n=1):
        if (rp[0] % 6) + n > 6:
            rp[0] += 6 - (rp[0] % 6)
        slots = [(rp[0] + i) % 6 for i in range(n)]
        rp[0] += n
        deps = [ring_use[s] for s in slots if ring_use[s] is not None]
        deps += [ring_fill[s] for s in slots if ring_fill[s] is not None]
        return slots, deps

    def ring_filled(slots, tok):
        for s in slots:
            ring_fill[s] = tok

    def ring_used(slots, tok):
        for s in slots:
            ring_use[s] = tok

    def rs(slot, cols=512, off=0):
        return ps_ring[:, 512 * slot + off: 512 * slot + off + cols]

    SC_NINV, SC_K0, SC_D0, SC_OS = 0, 1, 2, 9

    def scap(i):
        return scal[:, i:i + 1]

    def newton_rsqrt(dst_ap, t_ap, w, deps, tag):
        T = nsc[:, 0:w]; Sx = nsc[:, NT:NT + w]; U = nsc[:, 2 * NT:2 * NT + w]
        tk = vcopy(T, t_ap, list(deps) + [newton_busy[0]], tag=tag + "cp")
        tk = vop(lambda v: v.reciprocal(Sx, T), [tk], tag=tag + "rc")
        tk = vop(lambda v: v.tensor_single_scalar(out=Sx, in_=Sx, scalar=1.0, op=ALU.min),
                 [tk], tag=tag + "mn")
        for it_ in range(16):
            tk = vop(lambda v: v.tensor_mul(U, Sx, Sx), [tk], tag=tag + "n1")
            tk = vop(lambda v: v.tensor_mul(U, U, T), [tk], tag=tag + "n2")
            tk = vop(lambda v: v.tensor_scalar(out=U, in0=U, scalar1=-0.5, scalar2=1.5,
                                               op0=ALU.mult, op1=ALU.add), [tk], tag=tag + "n3")
            tk = vop(lambda v: v.tensor_mul(Sx, Sx, U), [tk], tag=tag + "n4")
        tk = vcopy(dst_ap, Sx, [tk], tag=tag + "out")
        newton_busy[0] = tk
        return tk

    newton_busy = [None]
    gelu_busy = [None]

    t_id_i = P.op("gpsimd", lambda g: g.iota(dgf[:, :], pattern=[[1, 128]], base=0,
                                             channel_multiplier=-1,
                                             allow_small_or_imprecise_dtypes=True), (), tag="iota")
    t_ident = vop(lambda v: v.tensor_single_scalar(out=ident[:, :], in_=dgf[:, :],
                                                   scalar=0.0, op=ALU.is_equal), [t_id_i], tag="ident")
    t_dgm = vop(lambda v: v.tensor_single_scalar(out=diagmask[:, :], in_=dgf[:, :],
                                                 scalar=0.0, op=ALU.is_le), [t_id_i], tag="dgm")
    t_io64 = P.op("gpsimd", lambda g: g.iota(iota64[:, :], pattern=[[0, 1]], base=0,
                                             channel_multiplier=1,
                                             allow_small_or_imprecise_dtypes=True),
                  [t_id_i], tag="io64")
    t_ioj = P.op("gpsimd", lambda g: g.iota(iotaj[:, :], pattern=[[1, 72]], base=0,
                                            channel_multiplier=0,
                                            allow_small_or_imprecise_dtypes=True),
                 [t_io64], tag="ioj")
    t_bigmi_i = P.op("gpsimd", lambda g: g.iota(bigmi[:, :], pattern=[[-1, 8]], base=999,
                                                channel_multiplier=0,
                                                allow_small_or_imprecise_dtypes=True),
                     [t_ioj], tag="iobm")
    t_ones_b = vop(lambda v: v.memset(onescol_b[:, :], 1.0), (), tag="onesb")
    t_ones_r = vop(lambda v: v.memset(onesrow[:, :], 1.0), (), tag="onesr")
    t_zsb0 = vop(lambda v: v.memset(zsb[64:128, :], 0.0), (), tag="zsb0")
    t_zsb1 = vop(lambda v: v.memset(zsb2[64:128, :], 0.0), (), tag="zsb1")
    t_zsbp = [t_zsb0, t_zsb1]


    rowbuf_read = [None]

    def bcast_row(dst, src_d, tag):
        tok_l = dma("sync", wstage[0:1, 0:D], src_d[:, :], [rowbuf_read[0]], tag="ld_" + tag)
        outs = []
        for j in range(2):
            slots, rdeps = ring_alloc(1)
            sl = slots[0]
            tmm = mm(rs(sl), onesrow[:, :], wstage[0:1, 512 * j: 512 * j + 512],
                     [tok_l, t_ones_r] + rdeps, tag="bc_" + tag)
            rowbuf_read[0] = tmm
            ring_filled(slots, tmm)
            tcp = vcopy(dst[:, 512 * j: 512 * j + 512], rs(sl), [tmm], tag="bcc_" + tag)
            ring_used(slots, tcp)
            outs.append(tcp)
        return outs, tok_l

    gam_toks, l_g = bcast_row(gam_t, gam_d, "gam")
    bet_toks, l_b = bcast_row(bet_t, bet_d, "bet")
    ngam_toks, l_n = bcast_row(ngam_t, ngam_d, "ngam")
    nbet_toks, l_nb = bcast_row(nbet_t, nbet_d, "nbet")

    tok_bq = dma("sync", bqT[:, :], bqT_d[:, :], (), tag="bq")
    tok_bk = dma("sync", bkT[:, :], bkT_d[:, :], (), tag="bk")
    tok_bv = dma("sync", bvT[:, :], bvT_d[:, :], (), tag="bv")
    tok_cb1 = dma("sync", cb1T[:, :], cb1T_d[:, :], (), tag="cb1")
    tok_cb2 = dma("sync", cb2_s[:, :], cb2_d[:, :], (), tag="cb2")
    tok_kn = dma("sync", knots_s[:, :], knots_d[:, :], (), tag="kn")
    tok_tm = dma("sync", temp_s[:, 0:1], temp_d[:, :], (), tag="tm")
    tok_os = dma("sync", temp_s[:, 3:4], oscale_d[:, :], (), tag="os")
    t_t1 = vop(lambda v: v.tensor_single_scalar(out=temp_s[:, 1:2], in_=temp_s[:, 0:1],
                                                scalar=1e-6, op=ALU.add), [tok_tm], tag="t1")
    t_t2 = vop(lambda v: v.reciprocal(temp_s[:, 2:3], temp_s[:, 1:2]), [t_t1], tag="t2")
    t_t3 = vop(lambda v: v.tensor_single_scalar(out=temp_s[:, 2:3], in_=temp_s[:, 2:3],
                                                scalar=-1.0, op=ALU.mult), [t_t2], tag="t3")
    t_sz = vop(lambda v: v.memset(scrow[:, :], 0.0), (), tag="sz")
    t_s0 = vcopy(scrow[:, 0:1], temp_s[:, 2:3], [t_t3, t_sz], tag="s0")
    t_s1 = vcopy(scrow[:, 1:2], knots_s[:, 0:1], [tok_kn], tag="s1")
    t_s2 = vop(lambda v: v.tensor_sub(scrow[:, 2:9], knots_s[:, 1:8], knots_s[:, 0:7]),
               [tok_kn], tag="s2")
    t_s3 = vcopy(scrow[:, 9:10], temp_s[:, 3:4], [tok_os], tag="s3")
    slots, rdeps = ring_alloc(1)
    sl0 = slots[0]
    t_scm = mm(rs(sl0, 16), onesrow[:, :], scrow[:, 0:16],
               [t_s0, t_s1, t_s2, t_s3, t_ones_r] + rdeps, tag="scm")
    ring_filled(slots, t_scm)
    t_scal = vcopy(scal[:, :], rs(sl0, 16), [t_scm], tag="scal")
    ring_used(slots, t_scal)

    for kk in range(8):
        a_ap = sgt[:, 72 * kk: 72 * kk + 64]
        cs_ap = sgt[:, 72 * kk + 64: 72 * kk + 72]
        tok_l = dma("sync", a_ap, sigT_d[128 * kk:128 * kk + 128, :], (), tag="sig")
        t_a = vop(lambda v, a=a_ap: v.tensor_scalar(out=calv[:, 0:64], in0=a, scalar1=0.3,
                                                    scalar2=None, op0=ALU.is_gt), [tok_l], tag="siga")
        t_b = vop(lambda v, a=a_ap: v.tensor_scalar(out=calt[:, 0:64], in0=a, scalar1=-0.3,
                                                    scalar2=None, op0=ALU.is_lt), [tok_l], tag="sigb")
        t_c = vop(lambda v, a=a_ap: v.tensor_sub(a, calv[:, 0:64], calt[:, 0:64]),
                  [t_a, t_b], tag="sigc")
        t_d = vop(lambda v, kk=kk: v.reduce_sum(
            sgt[:, 72 * kk + 64: 72 * kk + 72],
            sgt[:, 72 * kk: 72 * kk + 64].rearrange("p (c t) -> p c t", t=8), axis=AX.X),
            [t_c], tag="sigd")
        t_e = vop(lambda v, a=cs_ap: v.tensor_scalar(out=calv[:, 0:8], in0=a, scalar1=0.0,
                                                     scalar2=None, op0=ALU.is_gt), [t_d], tag="sige")
        t_f = vop(lambda v, a=cs_ap: v.tensor_scalar(out=calt[:, 0:8], in0=a, scalar1=0.0,
                                                     scalar2=None, op0=ALU.is_lt), [t_d], tag="sigf")
        vop(lambda v, a=cs_ap: v.tensor_sub(a, calv[:, 0:8], calt[:, 0:8]), [t_e, t_f], tag="sigg")

    tok_cw2l = dma("sync", wstage[:, 0:4], cw2_d[:, :], [rowbuf_read[0]], tag="cw2")
    tok_cw2 = vcopy(cw2_b[:, :], wstage[:, 0:4], [tok_cw2l], tag="cw2c")
    tok_dirs0 = dma("sync", wstage[0:64, :], dirs_d[:, :], [tok_cw2], tag="dirs")
    tok_dirs = vcopy(dirs_b[:, :], wstage[0:64, :], [tok_dirs0], tag="dirsc")
    tok_spl0 = dma("sync", wstage[0:64, 0:65], spl_d[:, :], [tok_dirs], tag="spl")
    tok_spl = vcopy(spl_b[:, :], wstage[0:64, 0:65], [tok_spl0], tag="splc")

    x0_free = [None] * 8
    var_toks = []
    for t in range(NT):
        buf = x0c[t % 8]
        tok_l = dma("sync", buf[:, :], x0_d[128 * t:128 * t + 128, :],
                    [x0_free[t % 8]], tag="x0s")
        t1 = vop(lambda v, b=buf: v.bn_stats(st6[:, 0:6], b[:, 0:512]), [tok_l], tag="bns1")
        t2 = vop(lambda v, b=buf: v.bn_stats(st6[:, 6:12], b[:, 512:1024]), [tok_l], tag="bns2")
        t3 = vop(lambda v: v.bn_aggr(mvt[:, :], st6[:, :]), [t1, t2], tag="bnag")
        x0_free[t % 8] = t3
        t4 = vop(lambda v, t=t: v.tensor_single_scalar(out=r1[:, t:t + 1], in_=mvt[:, 1:2],
                                                       scalar=EPS, op=ALU.add), [t3], tag="veps")
        t7 = vcopy(mu1[:, t:t + 1], mvt[:, 0:1], [t3], tag="mu1")
        var_toks.append((t4, t7))
    t_r1 = newton_rsqrt(r1[:, 0:NT], r1[:, 0:NT], NT, [tk for pair in var_toks for tk in pair],
                        "ln1n")
    ln1_tok = [(t_r1, var_toks[t][1]) for t in range(NT)]

    es_kv = ExitStack()
    wkv_b = es_kv.enter_context(nc.sbuf_tensor("wkv_b", [128, 8 * 1024], BF16))
    kstage = es_kv.enter_context(nc.sbuf_tensor("kstage", [128, 512], BF16))
    vstage_f = es_kv.enter_context(nc.sbuf_tensor("vstage_f", [128, 512], F32))
    vstage = es_kv.enter_context(nc.sbuf_tensor("vstage", [128, 4 * 130], BF16))

    acc_tok = [None] * 8
    acc_read = [None] * 8
    aT_read = [None] * 8
    aT_tok = [None] * 8
    aT_store = [None] * 8
    kv_dma_toks = []
    last_wkv_mm = [None]

    def load_wkv(w_d, dep0):
        prev = dep0
        for kk in range(8):
            tok_l = dma("sync", wstage[:, :], w_d[128 * kk:128 * kk + 128, :],
                        [prev], tag="wkv")
            prev = vcopy(wkv_b[:, 1024 * kk: 1024 * kk + 1024], wstage[:, :],
                         [tok_l, last_wkv_mm[0]], tag="wkvc")
        return prev

    kst_free = [None]
    vstf_free = [None]
    vst_free = [None]

    def kv_pass(proj, w_ready, bT, bias_tok, kdst=None, build_aT=False):
        for n in range(CH):
            if build_aT:
                a_toks = []
                for tt in range(4):
                    t = 4 * n + tt
                    buf = x0c[t % 8]
                    tok_l = dma("sync", buf[:, :], x0_d[128 * t:128 * t + 128, :],
                                [x0_free[t % 8]], tag="x0kv")
                    parts = []
                    for q4 in range(4):
                        ta = vop(lambda v, b=buf, t=t, tt=tt, q4=q4: v.tensor_scalar(
                            out=ACC[tt][:, 256 * q4: 256 * q4 + 256],
                            in0=b[:, 256 * q4: 256 * q4 + 256], scalar1=mu1[:, t:t + 1],
                            scalar2=r1[:, t:t + 1], op0=ALU.subtract, op1=ALU.mult),
                            [tok_l, ln1_tok[t][0], ln1_tok[t][1], acc_read[tt],
                             acc_tok[tt]], tag="akv")
                        parts.append(ta)
                    x0_free[t % 8] = parts[-1]
                    acc_tok[tt] = parts[-1]
                    a_toks.append(parts)
                for kk in range(8):
                    slots, rdeps = ring_alloc(1)
                    sl = slots[0]
                    last_t = None
                    for tt in range(4):
                        deps = [a_toks[tt][kk // 2], t_ident] + (rdeps if tt == 0 else [])
                        last_t = tp_(rs(sl, 128, 128 * tt),
                                     ACC[tt][:, 128 * kk:128 * kk + 128],
                                     deps, tag="aTkv")
                        acc_read[tt] = last_t
                    ring_filled(slots, last_t)
                    tc = vop(lambda v, kk=kk, sl=sl: v.tensor_copy(
                        aT[:, 512 * kk: 512 * kk + 512], rs(sl)),
                        [last_t, aT_tok[kk], aT_read[kk], aT_store[kk]], tag="aTcp")
                    ring_used(slots, tc)
                    aT_tok[kk] = tc
                    td = dma("sync", aTs_d[kk, :, 512 * n: 512 * n + 512],
                             aT[:, 512 * kk: 512 * kk + 512], [tc], tag="aTst")
                    aT_store[kk] = td
            else:
                for kk in range(8):
                    tok = dma("sync", aT[:, 512 * kk: 512 * kk + 512],
                              aTs_d[kk, :, 512 * n: 512 * n + 512],
                              [aT_tok[kk], aT_read[kk], aT_store[kk]], tag="aTld")
                    aT_tok[kk] = tok
            for p in range(NPAIR):
                slots, rdeps = ring_alloc(1)
                sl = slots[0]
                mmt = None
                for kk in range(8):
                    deps = [aT_tok[kk], w_ready] + (rdeps if kk == 0 else [])
                    mmt = mm(rs(sl), wkv_b[:, 1024 * kk + 128 * p: 1024 * kk + 128 * p + 128],
                             aT[:, 512 * kk: 512 * kk + 512], deps,
                             start=(kk == 0), stop=(kk == 7), tag=proj + "mm")
                    aT_read[kk] = mmt
                last_wkv_mm[0] = mmt
                ring_filled(slots, mmt)
                if proj in ("k", "q"):
                    tev = vop(lambda v, sl=sl, p=p: v.tensor_scalar(
                        out=kstage[:, :], in0=rs(sl), scalar1=bT[:, p:p + 1], scalar2=None,
                        op0=ALU.add), [mmt, bias_tok, kst_free[0]], tag="kev")
                    ring_used(slots, tev)
                    tdma = dma("sync", kdst[p, :, 512 * n: 512 * n + 512], kstage[:, :],
                               [tev], tag="kst")
                    kst_free[0] = tdma
                    kv_dma_toks.append(tdma)
                else:
                    tevs = []
                    for j in range(4):
                        tev = vop(lambda v, sl=sl, p=p, j=j: v.tensor_scalar(
                            out=vstage_f[:, 128 * j: 128 * j + 128],
                            in0=rs(sl, 128, 128 * j), scalar1=bT[:, p:p + 1], scalar2=None,
                            op0=ALU.add), [mmt, bias_tok, vstf_free[0]], tag="vev")
                        tevs.append(tev)
                    ring_used(slots, tevs[-1])
                    slots2, rdeps2 = ring_alloc(1)
                    sl2 = slots2[0]
                    last_t = None
                    for j in range(4):
                        deps = [tevs[j]] + (rdeps2 if j == 0 else [])
                        last_t = tp_(rs(sl2, 128, 128 * j), vstage_f[:, 128 * j: 128 * j + 128],
                                     deps, tag="vT")
                    vstf_free[0] = last_t
                    ring_filled(slots2, last_t)
                    tcp = vop(lambda v, sl2=sl2: v.tensor_copy(
                        vstage[:, :].rearrange("p (t s c) -> p t s c", t=4, s=2)[:, :, :, 0:64],
                        rs(sl2, 512).rearrange("p (t s c) -> p t s c", t=4, s=2)),
                        [last_t, vst_free[0]], tag="vcp")
                    ring_used(slots2, tcp)
                    tms = vop(lambda v: v.memset(
                        vstage[:, :].rearrange("p (t c) -> p t c", t=4)[:, :, 64:130:65], 1.0),
                        [tcp], tag="vones")
                    tdma = dma("sync", vs_d[p, 4 * n: 4 * n + 4].rearrange("t p c -> p t c"),
                               vstage[:, :].rearrange("p (t c) -> p t c", t=4), [tms], tag="vst")
                    vst_free[0] = tdma
                    kv_dma_toks.append(tdma)

    wq_ready = load_wkv(wqT_d, tok_spl)
    kv_pass("q", wq_ready, bqT, tok_bq, kdst=qts_d, build_aT=True)
    wk_ready = load_wkv(wkT_d, wq_ready)
    kv_pass("k", wk_ready, bkT, tok_bk, kdst=kts_d)
    wv_ready = load_wkv(wvT_d, wk_ready)
    kv_pass("v", wv_ready, bvT, tok_bv)
    zmov = wstage[0:1, 0:512]
    zrow = wstage[0:1, 512:577]
    t_zmov = vop(lambda v: v.memset(wstage[0:1, 0:577], 0.0), [wv_ready], tag="zmov")
    t_zrow = t_zmov
    es_kv.close()
    kv_all = P.op("gpsimd", lambda g: g.nop(), kv_dma_toks, tag="kvbar")

    cw1_f = sb("cw1_f", (128, 8 * 256), F32)
    x2T_f = sb("x2T_f", (128, 8 * 512), F32)
    hTb = sb("hTb", (128, 2 * 512), BF16)
    cw1_toks = []
    for kk in range(8):
        tok_l = dma("sync", cw1_f[:, 256 * kk: 256 * kk + 256],
                    cw1_d[128 * kk: 128 * kk + 128, :], [last_wkv_mm[0]], tag="cw1")
        cw1_toks.append(tok_l)

    kbuf_tok = [None, None]; kbuf_read = [None, None]
    vbuf_tok = [None, None]; vbuf_read = [None, None]
    pT_tok = [None] * 4; pT_read = [None] * 4
    zs_tok = [None, None]; zs_read = [None, None]
    ucnt = [0]
    psz_read = [None, None]; zs_close = [None, None]
    qbuf = [qT, aT]
    qbuf_tok = [None, None]
    qbuf_read = [[], [tk for tk in aT_tok if tk] + [tk for tk in aT_read if tk] +
                 [tk for tk in aT_store if tk]]
    dnm_tok = {}; dnm_read = [None, None]
    ybuf_tok = [None] * 4; ybuf_read = [None] * 4
    x1rb_tok = [None, None]; x1rb_read = [None, None]
    x1dma_tok = [None] * 8
    outdma_toks = []
    gates_tok = [None] * 4
    gexp_tok = [None] * 4
    cal_state = {}
    mags_tok = [None] * 4
    compt_tok = [None] * 4
    oh_tok = [None] * 4; oh_read = [None] * 4
    tidxr_read = [None] * 4; idx_read = [None] * 4
    gather_tok = {}
    x2T_tok = [None] * 8; x2T_read = [None] * 8
    hT_tok = [None] * 2; hT_read = [None] * 2
    mv2_tok = [None] * 4; mv3_tok = [None] * 4; mvy_tok = [None] * 4
    ctmp_read = [None]
    ctv4_hold = [None]
    r2_tok = [None] * 4; r3_tok = [None] * 4; ry_tok = [None] * 4
    ysts_read = [None] * 4
    spl_ps = {}

    def a_init(c, free_x0=False):
        toks = []
        for tt in range(4):
            t = 4 * c + tt
            ai = 4 * (c % 2) + tt
            buf = x0c[(c % 2) * 4 + tt]
            tok_l = dma("sync", buf[:, :], x0_d[128 * t:128 * t + 128, :],
                        [x0_free[(c % 2) * 4 + tt]], tag=f"x0c{c}")
            ta = vop(lambda v, b=buf, t=t, ai=ai: v.tensor_scalar(
                out=ACC[ai][:, :], in0=b[:, :], scalar1=mu1[:, t:t + 1],
                scalar2=r1[:, t:t + 1], op0=ALU.subtract, op1=ALU.mult),
                [tok_l, ln1_tok[t][0], ln1_tok[t][1], acc_read[ai], acc_tok[ai],
                 x1dma_tok[ai]], tag=f"a{c}")
            acc_tok[ai] = ta
            if free_x0:
                x0_free[(c % 2) * 4 + tt] = ta
            toks.append(ta)
        return toks

    def kvq_load(p):
        pb = p % 2
        deps = [kbuf_read[pb], kv_all]
        tk = dma("sync", kbuf[pb][:, :], kts_d[p, :, :], deps, tag=f"kld{p}")
        deps = [vbuf_read[pb], kv_all]
        tvs = []
        for half in range(2):
            tv = dma("sync",
                     vbuf[pb][:, half * (NT // 2) * 130:(half + 1) * (NT // 2) * 130].rearrange(
                         "p (t c) -> p t c", c=130),
                     vs_d[p, half * (NT // 2):(half + 1) * (NT // 2)].rearrange(
                         "t p c -> p t c"), deps, tag=f"vld{p}")
            tvs.append(tv)
        tq = dma("sync", qbuf[pb][:, 0:S], qts_d[p, :, :],
                 list(qbuf_read[pb]) + [kv_all], tag=f"qld{p}")
        kbuf_tok[pb] = tk
        vbuf_tok[pb] = tvs
        qbuf_tok[pb] = tq
        return tk, tvs

    def tile_schedule(c):
        groups = []
        if "sside" in skip:
            CHsk = True
        else:
            CHsk = False
        for hh in range(2):
            if CHsk:
                break
            for it in range(4):
                i = 4 * c + it
                chunks = []
                rem = 128 * (i + 1)
                j = 0
                while rem > 0:
                    nj = min(512, rem)
                    chunks.append((j, nj))
                    rem -= nj
                    j += 1
                g = []
                for (j, nj) in chunks:
                    if nj < 512:
                        if g:
                            groups.append(("S", g))
                            g = []
                        groups.append(("S", [(hh, it, j, nj)]))
                        continue
                    g.append((hh, it, j, nj))
                    if len(g) == 3:
                        groups.append(("S", g))
                        g = []
                if g:
                    groups.append(("S", g))
        stg = []
        for kp in range(4 * c + 4, NT):
            for hh in range(2):
                stg.append((hh, kp, 512))
                if len(stg) == 2:
                    groups.append(("T", stg))
                    stg = []
        if stg:
            groups.append(("T", stg))
        for kp in range(4 * c, min(4 * c + 4, NT)):
            jj = kp - 4 * c
            n = 128 * (jj + 1)
            for hh in range(2):
                groups.append(("T", [(hh, kp, n)]))
        return groups

    def emit_pair(c, p, sched):
        pb = p % 2
        ktok = kbuf_tok[pb]
        vtoks = vbuf_tok[pb]
        qb = qbuf[pb]
        qtok = qbuf_tok[pb]
        ucnt[0] += 1
        dpar = ucnt[0] % 2
        dbase = 32 * dpar
        pv_last = [None, None]
        zopen = [None, None]
        have_full = 4 * c + 4 < NT
        if not have_full:
            for hh in range(2):
                zopen[hh] = mm(ps_z[0:65, 512 * hh: 512 * hh + 512], zrow[0:1, 0:65],
                               zmov[0:1, :],
                               [t_zrow, t_zmov, psz_read[hh], zs_close[hh]], start=True,
                               stop=False, tag=f"zop{c}_{p}")
                pv_last[hh] = zopen[hh]
        first_pv = [have_full, have_full]
        pT_parity = [0]
        pend_consume = []
        kread = []
        vread = []
        qread = []
        sg_idx = {}
        for (kind, g) in sched:
            L = len(g)
            slots, rdeps = ring_alloc(L)
            qk_toks = []
            for ii, item in enumerate(g):
                sl = slots[ii]
                if "qk" in skip:
                    continue
                if "qk1" in skip and not (p == 0 and c == 0 and ii == 0 and kind == "T"):
                    continue
                if kind == "T":
                    hh, kp, n = item
                    lhs = kbuf[pb][64 * hh:64 * hh + 64, 128 * kp:128 * kp + 128]
                    rhsap = qb[64 * hh:64 * hh + 64, 512 * c: 512 * c + n]
                    deps = [ktok, qtok] + (rdeps if ii == 0 else [])
                    tokm = mm(rs(sl, n), lhs, rhsap, deps, tag=f"qkT{c}_{p}")
                else:
                    hh, it, j, n = item
                    lhs = qb[64 * hh:64 * hh + 64, 512 * c + 128 * it: 512 * c + 128 * it + 128]
                    rhsap = kbuf[pb][64 * hh:64 * hh + 64, 512 * j: 512 * j + n]
                    deps = [ktok, qtok] + (rdeps if ii == 0 else [])
                    tokm = mm(rs(sl, n), lhs, rhsap, deps, tag=f"qkS{c}_{p}")
                qk_toks.append(tokm)
                ring_fill[sl] = tokm
                kread.append(tokm)
                qread.append(tokm)
            pp = pT_parity[0]
            pT_parity[0] = (pp + 1) % 4
            ns = [item[-1] for item in g]
            sameN = all(x == ns[0] for x in ns)
            exp_toks = []
            if "exp" in skip:
                for ii2, sl_ in enumerate(slots):
                    if ii2 < len(qk_toks):
                        ring_used([sl_], qk_toks[ii2])
                pT_parity[0] = pp
                continue
            if kind == "S":
                hh, it, _, _ = g[0]
                base_col = dbase + 4 * (4 * hh + it)
                gi = sg_idx.get((hh, it), 0)
                sg_idx[(hh, it)] = gi + 1
                totN = sum(ns)
                acap = dnm[:, base_col + gi: base_col + gi + 1]
                din_ap = rs(slots[0], totN) if (sameN and ns[0] == 512) or L == 1 else None
                if din_ap is None:
                    for ii, item in enumerate(g):
                        _, _, _, n = item
                        gi2 = sg_idx[(hh, it)] - 1 + ii
                        pass
                    raise AssertionError("S group must be uniform or single")
                texp = sop(lambda s, i_=din_ap, o=pTb[pp][:, 0:totN], a=acap:
                           s.activation(out=o, in_=i_, func=AF.Exp, scale=1.0 / 64.0,
                                        accum_out=a),
                           qk_toks + [pT_read[pp], pT_tok[pp], dnm_read[dpar]],
                           tag=f"expS{c}_{p}")
                dnm_tok[(hh, it, gi)] = texp
                pT_tok[pp] = texp
                exp_toks.append(texp)
                ring_used(slots, texp)
            else:
                if sameN:
                    totN = ns[0] * L
                    texp = sop(lambda s, i_=rs(slots[0], totN), o=pTb[pp][:, 0:totN]:
                               s.activation(out=o, in_=i_, func=AF.Exp, scale=1.0 / 64.0),
                               qk_toks + [pT_read[pp], pT_tok[pp]], tag=f"expT{c}_{p}")
                    exp_toks = [texp] * L
                    pT_tok[pp] = texp
                    ring_used(slots, texp)
                else:
                    off = 0
                    for ii, item in enumerate(g):
                        n = item[-1]
                        texp = sop(lambda s, i_=rs(slots[ii], n), o=pTb[pp][:, off:off + n]:
                                   s.activation(out=o, in_=i_, func=AF.Exp, scale=1.0 / 64.0),
                                   [qk_toks[ii], pT_read[pp], pT_tok[pp]], tag=f"expT{c}_{p}")
                        exp_toks.append(texp)
                        pT_tok[pp] = texp
                        ring_used([slots[ii]], texp)
                        off += n
            if kind == "S":
                continue
            if dbg_d is not None and c == 0 and p == 0 and not dbg_state.get(5):
                dbg_state[5] = True
                _dtok = dbg_dump(5, pTb[pp][:, 0:512], [exp_toks[0]], w=512)
                pT_tok[pp] = _dtok
                _d6 = dbg_dump(6, qb[:, 0:512], [qtok], w=512)
                qbuf_read[pb] = [_d6]
                _d7 = dbg_dump(7, kbuf[pb][:, 0:512], [ktok], w=512)
                kbuf_read[pb] = _d7

            def consume(g=g, pp=pp, exp_toks=exp_toks, sameN=sameN):
                off = 0
                creads = []
                for ii, item in enumerate(g):
                    hh, kp, n = item
                    jj = kp - 4 * c
                    et = exp_toks[ii] if not sameN else exp_toks[0]
                    pslice = lambda a, b_: pTb[pp][:, off + a: off + b_]
                    zoff = 512 * hh
                    if jj >= 4:
                        if "pvones" not in skip:
                            opener = first_pv[hh]
                            deps = [et] + vtoks
                            if opener:
                                deps += [psz_read[hh], zs_close[hh]]
                            elif zopen[hh] is not None:
                                deps.append(zopen[hh])
                            st = mm(ps_z[0:65, zoff:zoff + 512],
                                    vbuf[pb][:, 130 * kp + 65 * hh: 130 * kp + 65 * hh + 65],
                                    pslice(0, 512), deps,
                                    start=opener, stop=False, tag=f"pv{c}_{p}")
                            if opener:
                                first_pv[hh] = False
                                zopen[hh] = st
                            pv_last[hh] = st
                            creads.append(st)
                            vread.append(st)
                    else:
                        if jj > 0 and "pvones" not in skip:
                            st = mm(ps_z[0:65, zoff:zoff + 128 * jj],
                                    vbuf[pb][:, 130 * kp + 65 * hh: 130 * kp + 65 * hh + 65],
                                    pslice(0, 128 * jj),
                                    [et, zopen[hh]] + vtoks,
                                    start=False, stop=False, tag=f"pvp{c}_{p}")
                            pv_last[hh] = st
                            creads.append(st)
                            vread.append(st)
                        if "dpv" not in skip:
                            tmk = vop(lambda v, a=pslice(128 * jj, 128 * jj + 128):
                                      v.tensor_mul(a, a, diagmask[:, :]), [et, t_dgm],
                                      tag=f"dmask{c}_{p}")
                            st = mm(ps_z[0:64, zoff + 128 * jj: zoff + 128 * jj + 128],
                                    vbuf[pb][:, 130 * kp + 65 * hh: 130 * kp + 65 * hh + 64],
                                    pslice(128 * jj, 128 * jj + 128),
                                    [tmk, zopen[hh]] + vtoks,
                                    start=False, stop=False, tag=f"dpv{c}_{p}")
                            pv_last[hh] = st
                            creads.append(st)
                            vread.append(st)
                    off += n
                pT_read[pp] = creads[-1] if creads else pT_read[pp]

            pend_consume.append(consume)
            if len(pend_consume) > 3:
                pend_consume.pop(0)()
        while pend_consume:
            pend_consume.pop(0)()
        kbuf_read[pb] = kread[-1] if kread else kbuf_read[pb]
        vbuf_read[pb] = vread[-1] if vread else vbuf_read[pb]
        if qread:
            qbuf_read[pb] = [qread[-1]]
        return pv_last, sg_idx

    zst_read = [None] * 8
    zst_idx = [0]
    zdma_toks = []
    zdma_by_chunk = [[] for _ in range(CH)]
    RH = S // 16

    def z_evac(c, p, pv_last, sg_idx):
        toks = []
        dpar = ucnt[0] % 2
        dbase = 32 * dpar
        for hh in range(2):
            zoff = 512 * hh
            zsbh = zsb if hh == 0 else zsb2
            tclose = mm(ps_z[0:65, zoff: zoff + 1], zrow[0:1, 0:65], zmov[0:1, 0:1],
                        [pv_last[hh], t_zrow], start=False, stop=True, tag=f"zcl{c}_{p}")
            zs_close[hh] = tclose
            tcps = []
            for j in range(4):
                tcp = vcopy(zsbh[0:65, 128 * j: 128 * j + 128],
                            ps_z[0:65, zoff + 128 * j: zoff + 128 * j + 128],
                            [tclose, zs_tok[hh], zs_read[hh], t_zsbp[hh]], tag=f"zc{c}_{p}")
                tcps.append(tcp)
            zs_tok[hh] = tcps[-1]
            psz_read[hh] = tcps[-1]
            slots, rdeps = ring_alloc(1)
            sl = slots[0]
            tts = []
            for j in range(4):
                deps = [tcps[j], t_ident] + (rdeps if j == 0 else [])
                tt_ = tp_(rs(sl, 128, 128 * j), zsbh[:, 128 * j: 128 * j + 128],
                          deps, tag=f"zT{c}_{p}")
                tts.append(tt_)
            zs_read[hh] = tts[-1]
            ring_filled(slots, tts[-1])
            h_glob = 2 * p + hh
            tBs = []
            for j in range(4):
                m = sg_idx.get((hh, j), 0)
                if m > 0:
                    bc = dbase + 4 * (4 * hh + j)
                    tA = vop(lambda v, bc=bc, m=m, j=j: v.reduce_sum(
                        dnr[:, j:j + 1], dnm[:, bc:bc + m], axis=AX.X),
                        [dnm_tok[(hh, j, gg)] for gg in range(m)], tag=f"dn{c}_{p}")
                    tB = vop(lambda v, sl=sl, j=j: v.tensor_add(
                        dnr[:, 4 + j:5 + j], dnr[:, j:j + 1], rs(sl, 1, 128 * j + 64)),
                        [tA, tts[-1]], tag=f"dt{c}_{p}")
                else:
                    tB = vcopy(dnr[:, 4 + j:5 + j], rs(sl, 1, 128 * j + 64), [tts[-1]],
                               tag=f"dt0{c}_{p}")
                tBs.append(tB)
            trc = vop(lambda v: v.reciprocal(dnr[:, 8:12], dnr[:, 4:8]), tBs,
                      tag=f"dr{c}_{p}")
            vlast = None
            for j in range(4):
                zi = zst_idx[0] % 4
                zst_idx[0] += 1
                zb = zst[zi]
                tz = vop(lambda v, sl=sl, j=j, zb=zb: v.tensor_scalar(
                    out=zb[:, :], in0=rs(sl, 64, 128 * j), scalar1=dnr[:, 8 + j:9 + j],
                    scalar2=None, op0=ALU.mult),
                    [trc, zst_read[zi], tts[-1]], tag=f"za{c}_{p}")
                r0 = RH * h_glob + 32 * c + 8 * j
                td = dma("sync",
                         zs_d[r0:r0 + 8, :].rearrange("r (t h) -> (r t) h", h=64),
                         zb[:, :], [tz], tag=f"zd{c}_{p}")
                zst_read[zi] = td
                zdma_toks.append(td)
                zdma_by_chunk[(RH * h_glob) // 512].append(td)
                vlast = tz
            ring_used(slots, vlast)
            dnm_read[dpar] = vlast
            toks.append(vlast)
        return toks

    def ln2_x1(c, z_toks):
        stat_toks = []
        for tt in range(4):
            ai = 4 * (c % 2) + tt
            t1 = vop(lambda v, ai=ai: v.bn_stats(st6[:, 0:6], ACC[ai][:, 0:512]),
                     [acc_tok[ai]] + z_toks, tag=f"l2a{c}")
            t2 = vop(lambda v, ai=ai: v.bn_stats(st6[:, 6:12], ACC[ai][:, 512:1024]),
                     [acc_tok[ai]], tag=f"l2b{c}")
            t3 = vop(lambda v, tt=tt: v.bn_aggr(mv2[:, 2 * tt:2 * tt + 2], st6[:, :]),
                     [t1, t2, mv2_tok[tt]], tag=f"l2c{c}")
            mv2_tok[tt] = t3
            stat_toks.append(t3)
        return stat_toks

    def rstd_visit(c, stat_toks, mvx, rxc, rx_tok, tagp):
        deps = [tk for tk in stat_toks if tk] + [tk for tk in rx_tok if tk]
        t4 = vop(lambda v, mvx=mvx: v.tensor_scalar(
            out=rxc[:, 0:4],
            in0=mvx[:, 0:8].rearrange("p (t c) -> p t c", c=2)[:, :, 1],
            scalar1=EPS, scalar2=None, op0=ALU.add), deps, tag=f"{tagp}e{c}")
        t6 = newton_rsqrt(rxc[:, 0:4], rxc[:, 0:4], 4, [t4], f"{tagp}n{c}")
        for tt in range(4):
            rx_tok[tt] = t6
        return [t6] * 4

    def x1_compute(c, r2_toks):
        toks = []
        for tt in range(4):
            ai = 4 * (c % 2) + tt
            buf = x0c[(c % 2) * 4 + tt]
            t1 = vop(lambda v, ai=ai, tt=tt: v.tensor_scalar(
                out=ACC[ai][:, :], in0=ACC[ai][:, :], scalar1=mv2[:, 2 * tt:2 * tt + 1],
                scalar2=r2c[:, tt:tt + 1], op0=ALU.subtract, op1=ALU.mult),
                [acc_tok[ai], r2_toks[tt], acc_read[ai]], tag=f"x1n{c}")
            t2 = vop(lambda v, ai=ai: v.tensor_mul(ACC[ai][:, :], ACC[ai][:, :], gam_t[:, :]),
                     [t1] + gam_toks, tag=f"x1g{c}")
            t3 = vop(lambda v, ai=ai, b=buf: v.tensor_add(ACC[ai][:, :], ACC[ai][:, :], b[:, :]),
                     [t2], tag=f"x1x{c}")
            t4 = vop(lambda v, ai=ai: v.tensor_add(ACC[ai][:, :], ACC[ai][:, :], bet_t[:, :]),
                     [t3] + bet_toks, tag=f"x1f{c}")
            acc_tok[ai] = t4
            x0_free[(c % 2) * 4 + tt] = t3
            td = dma("sync", x1s_d[128 * (4 * c + tt):128 * (4 * c + tt) + 128, :],
                     ACC[ai][:, :], [t4], tag=f"x1d{c}")
            x1dma_tok[ai] = td
            toks.append(t4)
        return toks

    def ffn_T1(c, x1_toks, r3_toks):
        xn_toks = []
        for tt in range(4):
            ai = 4 * (c % 2) + tt
            buf = x0c[(c % 2) * 4 + tt]
            t1 = vop(lambda v, ai=ai, tt=tt, b=buf: v.tensor_scalar(
                out=b[:, :], in0=ACC[ai][:, :], scalar1=mv3[:, 2 * tt:2 * tt + 1],
                scalar2=r3c[:, tt:tt + 1], op0=ALU.subtract, op1=ALU.mult),
                [acc_tok[ai], r3_toks[tt], x0_free[(c % 2) * 4 + tt]], tag=f"xn{c}")
            x0_free[(c % 2) * 4 + tt] = t1
            xn_toks.append(t1)
        for kk in range(8):
            slots, rdeps = ring_alloc(1)
            sl = slots[0]
            last_t = None
            for tt in range(4):
                buf = x0c[(c % 2) * 4 + tt]
                deps = [xn_toks[tt], t_ident, x2T_read[kk], x2T_tok[kk]] + (rdeps if tt == 0 else [])
                last_t = tp_(rs(sl, 128, 128 * tt), buf[:, 128 * kk:128 * kk + 128],
                             deps, tag=f"x2T{c}")
                x0_free[(c % 2) * 4 + tt] = last_t
            ring_filled(slots, last_t)
            tc = vcopy(x2T_f[:, 512 * kk: 512 * kk + 512], rs(sl), [last_t], tag=f"x2Tc{c}")
            ring_used(slots, tc)
            x2T_tok[kk] = tc
        gelu_in = []
        for jo in range(2):
            slots, rdeps = ring_alloc(1)
            sl = slots[0]
            mmt = None
            for kk in range(8):
                deps = [x2T_tok[kk], cw1_toks[kk]] + (rdeps if kk == 0 else [])
                mmt = mm(rs(sl), cw1_f[:, 256 * kk + 128 * jo: 256 * kk + 128 * jo + 128],
                         x2T_f[:, 512 * kk: 512 * kk + 512], deps,
                         start=(kk == 0), stop=(kk == 7), tag=f"hmm{c}")
                x2T_read[kk] = mmt
            ring_filled(slots, mmt)
            gelu_in.append((sl, mmt, slots))
        A1, A2, A3, A4, A5 = (0.254829592, -0.284496736, 1.421413741,
                              -1.453152027, 1.061405429)
        PP = 0.3275911
        ISQ2 = 0.7071067811865476
        hT_toks = []
        for jo, (sl, mmt, slots) in enumerate(gelu_in):
            hp = hpre[:, :]
            tb = vop(lambda v, sl=sl, jo=jo: v.tensor_scalar(
                out=hp, in0=rs(sl), scalar1=cb1T[:, jo:jo + 1], scalar2=None,
                op0=ALU.add), [mmt, tok_cb1, gelu_busy[0]], tag=f"gb{c}")
            ring_used(slots, tb)
            Z = zsb[:, 0:512]
            tz = vop(lambda v: v.tensor_scalar(out=Z, in0=hp, scalar1=-1.0, scalar2=None,
                                               op0=ALU.mult),
                     [tb, zs_tok[0], zs_read[0]], tag=f"gz{c}")
            tz = vop(lambda v: v.tensor_tensor(out=Z, in0=Z, in1=hp, op=ALU.max),
                     [tz], tag=f"gz2{c}")
            tz = vop(lambda v: v.tensor_scalar(out=Z, in0=Z, scalar1=ISQ2, scalar2=None,
                                               op0=ALU.mult), [tz], tag=f"gz3{c}")
            DN = x1rb[0][:, 0:512]
            td = vop(lambda v: v.tensor_scalar(out=DN, in0=Z, scalar1=PP, scalar2=1.0,
                                               op0=ALU.mult, op1=ALU.add),
                     [tz, x1rb_read[0], x1rb_tok[0]], tag=f"gd{c}")
            td = vop(lambda v: v.reciprocal(DN, DN), [td], tag=f"gd2{c}")
            E = x1rb[0][:, 512:1024]
            te = vop(lambda v: v.tensor_mul(E, Z, Z), [tz, td], tag=f"gsq{c}")
            te = sop(lambda s: s.activation(out=E, in_=E, func=AF.Exp, scale=-1.0),
                     [te], tag=f"gex{c}")
            tq = vop(lambda v: v.tensor_scalar(out=Z, in0=DN, scalar1=0.5 * A5,
                                               scalar2=0.5 * A4, op0=ALU.mult, op1=ALU.add),
                     [te], tag=f"gp1{c}")
            for aa in (A3, A2, A1):
                tq = vop(lambda v: v.tensor_mul(Z, Z, DN), [tq], tag=f"gp2{c}")
                tq = vop(lambda v, aa=aa: v.tensor_single_scalar(out=Z, in_=Z,
                                                                 scalar=0.5 * aa, op=ALU.add),
                         [tq], tag=f"gp3{c}")
            tq = vop(lambda v: v.tensor_mul(Z, Z, DN), [tq], tag=f"gp4{c}")
            tq = vop(lambda v: v.tensor_mul(Z, Z, E), [tq], tag=f"gp5{c}")
            tm = vop(lambda v: v.tensor_scalar(out=DN, in0=hp, scalar1=0.0, scalar2=None,
                                               op0=ALU.is_ge), [tq], tag=f"gm{c}")
            tm2 = vop(lambda v: v.tensor_scalar(out=E, in0=DN, scalar1=-2.0, scalar2=1.0,
                                                op0=ALU.mult, op1=ALU.add), [tm], tag=f"gm2{c}")
            tphi = vop(lambda v: v.tensor_mul(Z, Z, E), [tm2], tag=f"gf1{c}")
            tphi = vop(lambda v: v.tensor_add(Z, Z, DN), [tphi], tag=f"gf2{c}")
            tg = vop(lambda v, jo=jo: v.tensor_mul(hTb[:, 512 * jo: 512 * jo + 512], hp, Z),
                     [tphi, hT_tok[jo], hT_read[jo]], tag=f"gelu{c}")
            gelu_busy[0] = tg
            zs_tok[0] = tg
            x1rb_tok[0] = tg
            x1rb_read[0] = tg
            hT_tok[jo] = tg
            hT_toks.append(tg)
        return xn_toks, hT_toks

    def ffn_T2(c, hT_toks):
        slots, rdeps = ring_alloc(1)
        slc = slots[0]
        mmt = None
        for kk in range(2):
            deps = [hT_toks[kk], tok_cw2] + (rdeps if kk == 0 else [])
            mmt = mm(ps_ring[0:2, 512 * slc: 512 * slc + 512],
                     cw2_b[:, 2 * kk: 2 * kk + 2], hTb[:, 512 * kk: 512 * kk + 512],
                     deps, start=(kk == 0), stop=(kk == 1), tag=f"cmm{c}")
            hT_read[kk] = mmt
        ring_filled(slots, mmt)
        tv1 = vop(lambda v: v.tensor_scalar(out=ctmp[:, :], in0=ps_ring[0:2, 512 * slc: 512 * slc + 512],
                                            scalar1=cb2_s[:, 0:1], scalar2=None, op0=ALU.add),
                  [mmt, tok_cb2, ctmp_read[0]], tag=f"cb{c}")
        ring_used(slots, tv1)
        te = sop(lambda s: s.activation(out=ctmp[:, :], in_=ctmp[:, :], func=AF.Exp, scale=2.0),
                 [tv1], tag=f"ctanh{c}")
        tv2 = vop(lambda v: v.tensor_single_scalar(out=ctmp[:, :], in_=ctmp[:, :], scalar=1.0,
                                                   op=ALU.add), [te], tag=f"ct1{c}")
        tv3 = vop(lambda v: v.reciprocal(ctmp[:, :], ctmp[:, :]), [tv2], tag=f"ct2{c}")
        tv4 = vop(lambda v: v.tensor_scalar(out=ctmp[:, :], in0=ctmp[:, :], scalar1=-2.0,
                                            scalar2=1.0, op0=ALU.mult, op1=ALU.add),
                  [tv3], tag=f"ct3{c}")
        ctv4_hold[0] = tv4
        for tt in range(4):
            slots, rdeps = ring_alloc(1)
            sl = slots[0]
            mmt = None
            for kk in range(8):
                deps = [x2T_tok[kk]] + (rdeps if kk == 0 else [])
                mmt = mm(rs(sl, 72), x2T_f[:, 512 * kk + 128 * tt: 512 * kk + 128 * tt + 128],
                         sgt[:, 72 * kk: 72 * kk + 72], deps,
                         start=(kk == 0), stop=(kk == 7), tag=f"gmm{c}")
                x2T_read[kk] = mmt
            ring_filled(slots, mmt)
            tev = vcopy(gates[:, 72 * tt: 72 * tt + 72], rs(sl, 72), [mmt, gates_tok[tt]],
                        tag=f"gev{c}")
            ring_used(slots, tev)
            gates_tok[tt] = tev

    def compt_stage(c):
        tv4 = ctv4_hold[0]
        slots, rdeps = ring_alloc(1)
        slt = slots[0]
        ctoks = []
        for tt in range(4):
            deps = [tv4, t_ident] + (rdeps if tt == 0 else [])
            ttk = tp_(rs(slt, 2, 128 * tt), ctmp[:, 128 * tt: 128 * tt + 128], deps,
                      tag=f"cT{c}", k=2)
            ctoks.append(ttk)
            ctmp_read[0] = ttk
        ring_filled(slots, ctoks[-1])
        cc = vop(lambda v, slt=slt: v.tensor_copy(
            compt[:, :].rearrange("p (t c) -> p t c", t=4),
            rs(slt, 512).rearrange("p (t c) -> p t c", c=128)[:, :, 0:2]),
            [ctoks[-1]] + [compt_tok[t] for t in range(4)], tag=f"ccp{c}")
        ring_used(slots, cc)
        for tt in range(4):
            compt_tok[tt] = cc

    def gates_exp(c):
        toks = []
        for tt in range(4):
            te = sop(lambda s, tt=tt: s.activation(
                out=gates[:, 72 * tt: 72 * tt + 72], in_=gates[:, 72 * tt: 72 * tt + 72],
                func=AF.Exp, scale=scap(SC_NINV)), [gates_tok[tt], t_scal], tag=f"gex{c}")
            gates_tok[tt] = te
            toks.append(te)
        return toks

    def calibrate_argmax(c, ge_toks):
        tidx_toks = []
        for tt in range(4):
            g_ap = gates[:, 72 * tt: 72 * tt + 72]
            t1 = vop(lambda v, g=g_ap: v.tensor_single_scalar(out=calv[:, :], in_=g,
                                                              scalar=1.0, op=ALU.add),
                     [ge_toks[tt]], tag=f"ca1{c}")
            t2 = vop(lambda v: v.reciprocal(calv[:, :], calv[:, :]), [t1], tag=f"ca2{c}")
            t3 = vop(lambda v: v.tensor_scalar(out=calv[:, :], in0=calv[:, :], scalar1=7.0,
                                               scalar2=None, op0=ALU.mult), [t2], tag=f"ca3{c}")
            tacc = vop(lambda v: v.memset(cal2[:, :], 0.0), [t3], tag=f"ca4{c}")
            for j in range(7):
                ta = vop(lambda v, j=j: v.tensor_scalar(out=calt[:, :], in0=calv[:, :],
                                                        scalar1=float(j), scalar2=0.0,
                                                        op0=ALU.subtract, op1=ALU.max),
                         [tacc if j == 0 else tb2], tag=f"cj1{c}")
                tb = vop(lambda v, j=j: v.tensor_scalar(out=calt[:, :], in0=calt[:, :],
                                                        scalar1=1.0, scalar2=scap(SC_D0 + j),
                                                        op0=ALU.min, op1=ALU.mult),
                         [ta, t_scal], tag=f"cj2{c}")
                tb2 = vop(lambda v: v.tensor_add(cal2[:, :], cal2[:, :], calt[:, :]),
                          [tb], tag=f"cj3{c}")
            tk0 = vop(lambda v: v.tensor_scalar(out=cal2[:, :], in0=cal2[:, :],
                                                scalar1=scap(SC_K0), scalar2=None, op0=ALU.add),
                      [tb2, t_scal], tag=f"ck0{c}")
            m1 = vop(lambda v: v.reduce_max(idxs[:, 0:1], cal2[:, 64:72], axis=AX.X),
                     [tk0], tag=f"am1{c}")
            m2 = vop(lambda v: v.tensor_scalar(out=calt[:, 0:8], in0=cal2[:, 64:72],
                                               scalar1=idxs[:, 0:1], scalar2=None,
                                               op0=ALU.is_ge), [m1], tag=f"am2{c}")
            m3 = vop(lambda v: v.tensor_mul(calt[:, 8:16], calt[:, 0:8], bigmi[:, :]),
                     [m2, t_bigmi_i], tag=f"am3{c}")
            m4 = vop(lambda v: v.tensor_scalar(out=calt[:, 8:16], in0=calt[:, 8:16],
                                               scalar1=-1.0, scalar2=999.0,
                                               op0=ALU.mult, op1=ALU.add), [m3], tag=f"am4{c}")
            m5 = vop(lambda v: v.tensor_reduce(idxs[:, 1:2], calt[:, 8:16], axis=AX.X,
                                               op=ALU.min), [m4], tag=f"am5{c}")
            w1 = vop(lambda v: v.reduce_max(calt[:, 0:8],
                                            cal2[:, 0:64].rearrange("p (c t) -> p c t", t=8),
                                            axis=AX.X), [tk0], tag=f"aw1{c}")
            w2 = vop(lambda v: v.tensor_tensor(
                out=calv[:, 0:64].rearrange("p (c t) -> p c t", t=8),
                in0=cal2[:, 0:64].rearrange("p (c t) -> p c t", t=8),
                in1=calt[:, 0:8].rearrange("p c -> p c 1" if False else "p (c o) -> p c o", o=1).to_broadcast([128, 8, 8]),
                op=ALU.is_ge), [w1], tag=f"aw2{c}")
            w3 = vop(lambda v: v.tensor_tensor(
                out=calv[:, 0:64].rearrange("p (c t) -> p c t", t=8),
                in0=calv[:, 0:64].rearrange("p (c t) -> p c t", t=8),
                in1=bigmi[:, :].rearrange("p (c o) -> p c o" if False else "p t -> p 1 t" if False else "p (o t) -> p o t", o=1).to_broadcast([128, 8, 8]),
                op=ALU.mult), [w2, t_bigmi_i], tag=f"aw3{c}")
            w4 = vop(lambda v: v.tensor_scalar(out=calv[:, 0:64], in0=calv[:, 0:64],
                                               scalar1=-1.0, scalar2=999.0,
                                               op0=ALU.mult, op1=ALU.add), [w3], tag=f"aw4{c}")
            w5 = vop(lambda v: v.tensor_reduce(calt[:, 16:24],
                                               calv[:, 0:64].rearrange("p (c t) -> p c t", t=8),
                                               axis=AX.X, op=ALU.min), [w4], tag=f"aw5{c}")
            s1 = vop(lambda v: v.tensor_scalar(out=calt[:, 24:32], in0=iotaj[:, 0:8],
                                               scalar1=idxs[:, 1:2], scalar2=None,
                                               op0=ALU.is_equal), [m5, t_ioj], tag=f"as1{c}")
            s2 = vop(lambda v: v.tensor_mul(calt[:, 24:32], calt[:, 24:32], calt[:, 16:24]),
                     [s1, w5], tag=f"as2{c}")
            s3 = vop(lambda v: v.reduce_sum(idxs[:, 2:3], calt[:, 24:32], axis=AX.X),
                     [s2], tag=f"as3{c}")
            s4 = vop(lambda v, tt=tt: v.tensor_scalar(out=idxs[:, 3 + tt:4 + tt], in0=idxs[:, 1:2],
                                                      scalar1=8.0, scalar2=idxs[:, 2:3],
                                                      op0=ALU.mult, op1=ALU.add),
                     [s3, idx_read[tt]], tag=f"as4{c}")
            tidx_toks.append(s4)
        return tidx_toks

    def gathers(c, tidx_toks):
        t2s = []
        for tt in range(4):
            slots, rdeps = ring_alloc(1)
            sl = slots[0]
            t1 = tp_(ps_ring[0:1, 512 * sl: 512 * sl + 128], idxs[:, 3 + tt:4 + tt],
                     [tidx_toks[tt], t_ident] + rdeps, tag=f"tiT{c}")
            idx_read[tt] = t1
            ring_filled([sl], t1)
            t2 = vcopy(tidxr[:, 128 * tt: 128 * tt + 128],
                       ps_ring[0:1, 512 * sl: 512 * sl + 128],
                       [t1, tidxr_read[tt]], tag=f"tir{c}")
            ring_used([sl], t2)
            t2s.append(t2)
        t4s = []
        for tt in range(4):
            ohb = ohbufs[tt]
            slots2, rdeps2 = ring_alloc(1)
            sl2 = slots2[0]
            t3 = mm(ps_ring[0:64, 512 * sl2: 512 * sl2 + 128], onesrow[:, 0:64],
                    tidxr[:, 128 * tt: 128 * tt + 128],
                    [t2s[tt], t_ones_r] + rdeps2, tag=f"tibc{c}")
            tidxr_read[tt] = t3
            ring_filled(slots2, t3)
            t4 = vop(lambda v, sl2=sl2, ohb=ohb: v.tensor_scalar(
                out=ohb[:, :], in0=ps_ring[0:64, 512 * sl2: 512 * sl2 + 128],
                scalar1=iota64[:, :], scalar2=None, op0=ALU.is_equal),
                [t3, t_io64, oh_tok[tt], oh_read[tt]], tag=f"ioh{c}")
            ring_used(slots2, t4)
            oh_tok[tt] = t4
            t4s.append(t4)

        def emit_tile(tt):
            ohb = ohbufs[tt]
            slots3, rdeps3 = ring_alloc(3)
            g1 = mm(rs(slots3[0], 65), ohb[:, :], spl_b[:, :], [t4s[tt], tok_spl] + rdeps3,
                    tag=f"gsp{c}")
            g2 = mm(rs(slots3[1]), ohb[:, :], dirs_b[:, 0:512], [t4s[tt], tok_dirs],
                    tag=f"gd1{c}")
            g3 = mm(rs(slots3[2]), ohb[:, :], dirs_b[:, 512:1024], [t4s[tt], tok_dirs],
                    tag=f"gd2{c}")
            ring_filled(slots3, g3)
            oh_read[tt] = g3
            return (slots3, g1, g2, g3)
        return emit_tile

    def spline_y(c, gth_fn):
        ysts = []
        for tt in range(4):
            slots3, g1, g2, g3 = gth_fn(tt)
            ai = 4 * (c % 2) + tt
            e1 = vcopy(splg[:, :], rs(slots3[0], 65), [g1], tag=f"spc{c}")
            h1 = vop(lambda v, tt=tt: v.tensor_scalar(out=hhs[:, 0:16], in0=splg[:, 0:16],
                                               scalar1=compt[:, 2 * tt: 2 * tt + 1],
                                               scalar2=None, op0=ALU.mult),
                     [e1, compt_tok[tt]], tag=f"sp1{c}")
            h2 = vop(lambda v, tt=tt: v.scalar_tensor_tensor(out=hhs[:, 16:32], in0=splg[:, 16:32],
                                                      scalar=compt[:, 2 * tt + 1: 2 * tt + 2],
                                                      in1=hhs[:, 0:16], op0=ALU.mult,
                                                      op1=ALU.add), [h1], tag=f"sp2{c}")
            h3 = vop(lambda v: v.tensor_add(hhs[:, 32:48], hhs[:, 16:32], splg[:, 32:48]),
                     [h2], tag=f"sp3{c}")
            h4 = vop(lambda v: v.tensor_scalar(out=hhs[:, 32:48], in0=hhs[:, 32:48],
                                               scalar1=0.0, scalar2=None, op0=ALU.max),
                     [h3], tag=f"sp4{c}")
            h5 = vop(lambda v: v.tensor_mul(hhs[:, 0:16], hhs[:, 32:48], splg[:, 48:64]),
                     [h4], tag=f"sp5{c}")
            h6 = vop(lambda v, tt=tt: v.reduce_sum(mags[:, tt:tt + 1], hhs[:, 0:16], axis=AX.X),
                     [h5, mags_tok[tt]], tag=f"sp6{c}")
            h7 = vop(lambda v, tt=tt: v.tensor_add(mags[:, tt:tt + 1], mags[:, tt:tt + 1],
                                            splg[:, 64:65]), [h6], tag=f"sp7{c}")
            h8 = vop(lambda v, tt=tt: v.tensor_scalar(out=mags[:, tt:tt + 1], in0=mags[:, tt:tt + 1],
                                               scalar1=scap(SC_OS), scalar2=None,
                                               op0=ALU.mult), [h7, t_scal], tag=f"sp8{c}")
            mags_tok[tt] = h8
            y1 = vop(lambda v, ai=ai, tt=tt: v.tensor_scalar(
                out=ybuf[tt][:, :], in0=ACC[ai][:, :], scalar1=mv3[:, 2 * tt:2 * tt + 1],
                scalar2=r3c[:, tt:tt + 1], op0=ALU.subtract, op1=ALU.mult),
                [acc_tok[ai], p2[c]["r3"][tt], ybuf_read[tt], ybuf_tok[tt]], tag=f"y1{c}")
            y2 = vop(lambda v, tt=tt, s=slots3[1]: v.scalar_tensor_tensor(
                out=ybuf[tt][:, 0:512], in0=rs(s), scalar=mags[:, tt:tt + 1],
                in1=ybuf[tt][:, 0:512], op0=ALU.mult, op1=ALU.add), [y1, h8, g2], tag=f"y2{c}")
            y3 = vop(lambda v, tt=tt, s=slots3[2]: v.scalar_tensor_tensor(
                out=ybuf[tt][:, 512:1024], in0=rs(s), scalar=mags[:, tt:tt + 1],
                in1=ybuf[tt][:, 512:1024], op0=ALU.mult, op1=ALU.add), [y2, g3], tag=f"y3{c}")
            ring_used(slots3, y3)
            ybuf_tok[tt] = y3
            acc_read[ai] = y1
            s1 = vop(lambda v, tt=tt: v.bn_stats(st6[:, 0:6], ybuf[tt][:, 0:512]), [y3],
                     tag=f"ys1{c}")
            s2 = vop(lambda v, tt=tt: v.bn_stats(st6[:, 6:12], ybuf[tt][:, 512:1024]), [y3],
                     tag=f"ys2{c}")
            s3 = vop(lambda v, tt=tt: v.bn_aggr(mvy[:, 2 * tt:2 * tt + 2], st6[:, :]),
                     [s1, s2, mvy_tok[tt]], tag=f"ys3{c}")
            mvy_tok[tt] = s3
            ysts.append(s3)
        return ysts

    def out_assembly(c, ry_toks):
        for tt in range(4):
            row = 128 * (4 * c + tt)
            tok_x1 = dma("sync", x1rb[0][:, :], x1s_d[row:row + 128, :],
                         [x1rb_read[0], x1dma_tok[4 * (c % 2) + tt]], tag=f"x1r{c}")
            x1rb_tok[0] = tok_x1
            o1 = vop(lambda v, tt=tt: v.tensor_scalar(
                out=ybuf[tt][:, :], in0=ybuf[tt][:, :], scalar1=mvy[:, 2 * tt:2 * tt + 1],
                scalar2=ryc[:, tt:tt + 1], op0=ALU.subtract, op1=ALU.mult),
                [ybuf_tok[tt], ry_toks[tt]], tag=f"o1{c}")
            o2 = vop(lambda v, tt=tt: v.tensor_mul(ybuf[tt][:, :], ybuf[tt][:, :], ngam_t[:, :]),
                     [o1] + ngam_toks, tag=f"o2{c}")
            o3 = vop(lambda v, tt=tt: v.tensor_add(ybuf[tt][:, :], ybuf[tt][:, :],
                                                   x1rb[0][:, :]), [o2, tok_x1], tag=f"o3{c}")
            x1rb_read[0] = o3
            o4 = vop(lambda v, tt=tt: v.tensor_add(ybuf[tt][:, :], ybuf[tt][:, :], nbet_t[:, :]),
                     [o3] + nbet_toks, tag=f"o4{c}")
            ybuf_tok[tt] = o4
            od = dma("sync", out_d[row:row + 128, :], ybuf[tt][:, :], [o4], tag=f"od{c}")
            ybuf_read[tt] = od
            outdma_toks.append(od)

    dbg_state = {}

    def dbg_dump(i, ap, deps, w=D):
        if dbg_d is None:
            return None
        return dma("gpsimd", dbg_d[i, :, 0:w], ap, deps, tag=f"dbg{i}")

    zrb_read = [None, None]
    p2 = {}

    def p2_m0(ch):
        st = p2.setdefault(ch, {})
        a_toks = a_init(ch)
        ztoks = []
        for tt in range(4):
            ai = 4 * (ch % 2) + tt
            zi = tt % 2
            tl = dma("sync", zrb[zi][:, :],
                     zs_d[512 * ch + 128 * tt: 512 * ch + 128 * tt + 128, :],
                     [zrb_read[zi]] + zdma_by_chunk[ch], tag=f"zl{ch}")
            tadd = vop(lambda v, ai=ai, zi=zi: v.tensor_add(
                ACC[ai][:, :], ACC[ai][:, :], zrb[zi][:, :]),
                [tl, a_toks[tt]], tag=f"zadd{ch}")
            zrb_read[zi] = tadd
            acc_tok[ai] = tadd
            ztoks.append(tadd)
        st["ztoks"] = ztoks

    def p2_m1(ch):
        st = p2[ch]
        stat2 = ln2_x1(ch, st["ztoks"])
        st["r2"] = rstd_visit(ch, stat2, mv2, r2c, r2_tok, "r2")

    def p2_m2(ch):
        st = p2[ch]
        x1_toks = x1_compute(ch, st["r2"])
        stat3 = []
        for tt in range(4):
            ai = 4 * (ch % 2) + tt
            t1 = vop(lambda v, ai=ai: v.bn_stats(st6[:, 0:6], ACC[ai][:, 0:512]),
                     [x1_toks[tt]], tag=f"l3a{ch}")
            t2 = vop(lambda v, ai=ai: v.bn_stats(st6[:, 6:12], ACC[ai][:, 512:1024]),
                     [x1_toks[tt]], tag=f"l3b{ch}")
            t3 = vop(lambda v, tt=tt: v.bn_aggr(mv3[:, 2 * tt:2 * tt + 2], st6[:, :]),
                     [t1, t2, mv3_tok[tt]], tag=f"l3c{ch}")
            mv3_tok[tt] = t3
            stat3.append(t3)
        st["r3"] = rstd_visit(ch, stat3, mv3, r3c, r3_tok, "r3")
        st["x1_toks"] = x1_toks

    def p2_m3(ch):
        st = p2[ch]
        _, st["hT_toks"] = ffn_T1(ch, st["x1_toks"], st["r3"])

    def p2_m3b(ch):
        st = p2[ch]
        ffn_T2(ch, st["hT_toks"])

    def p2_m4(ch):
        st = p2[ch]
        if "ffntail" not in skip:
            compt_stage(ch)
            st["tidx"] = calibrate_argmax(ch, gates_exp(ch))
        else:
            st["tidx"] = [None] * 4

    def p2_m5(ch):
        st = p2[ch]
        if "ffntail" not in skip:
            gth = gathers(ch, st["tidx"])
            st["ysts"] = spline_y(ch, gth)
        else:
            st["ysts"] = [None] * 4

    def p2_m6(ch):
        st = p2[ch]
        st["ry"] = rstd_visit(ch, st["ysts"], mvy, ryc, ry_tok, "ry")

    def p2_m7(ch):
        st = p2[ch]
        out_assembly(ch, st["ry"])
        del p2[ch]

    phase_queue = []

    def enqueue_chunk(ch):
        q = [lambda ch=ch: p2_m0(ch), lambda ch=ch: p2_m1(ch)]
        if ch >= 1:
            q.append(lambda ch=ch: p2_m5(ch - 1))
        q.append(lambda ch=ch: p2_m2(ch))
        if ch >= 1:
            q.append(lambda ch=ch: p2_m6(ch - 1))
        q.append(lambda ch=ch: p2_m3(ch))
        if ch >= 1:
            q.append(lambda ch=ch: p2_m7(ch - 1))
        q.append(lambda ch=ch: p2_m3b(ch))
        q.append(lambda ch=ch: p2_m4(ch))
        phase_queue.extend(q)

    ppc = NPAIR // CH if CH <= NPAIR else 1

    sched_by_c = [tile_schedule(c) for c in range(CH)]
    for p in range(NPAIR):
        if p == 0:
            kvq_load(0)
        for ci, c in enumerate(range(CH)):
            if "pairs" not in skip:
                pv_last, sg_idx = emit_pair(c, p, sched_by_c[c])
                if ci == 0 and p + 1 < NPAIR:
                    kvq_load(p + 1)
                zt = z_evac(c, p, pv_last, sg_idx)
                if c == 0 and p == 0:
                    dbg_dump(1, zsb[:, 0:512], [zs_tok[0]], w=512)
            if phase_queue:
                phase_queue.pop(0)()
                if ci == CH - 1 and len(phase_queue) > 1:
                    phase_queue.pop(0)()
        if (p + 1) % ppc == 0:
            enqueue_chunk((p + 1) // ppc - 1)

    while phase_queue:
        phase_queue.pop(0)()
    p2_m5(CH - 1)
    p2_m6(CH - 1)
    p2_m7(CH - 1)

    fin_deps = [outdma_toks[-1]]
    for sname in DMAPOOL["sync"] + DMAPOOL["gpsimd"]:
        cc = P.count.get(sname, 0)
        if cc:
            fin_deps.append((sname, cc))
    P.op("gpsimd", lambda g: g.nop(), fin_deps, tag="fin")

    import os
    nopify = set(os.environ.get("NOPIFY", "").split(",")) - {""}
    noptags = set(os.environ.get("NOPTAGS", "").split(",")) - {""}
    refd = {}
    for e2 in ENGINES:
        for (fn, waits, sem, amt, tag) in P.stream[e2]:
            for (s, v) in waits:
                refd.setdefault(s, set()).add(v)
    remap = {}
    for e2 in ENGINES:
        if os.environ.get("NO_SEMCOMP", "") == "1":
            break
        s = ESEM[e2]
        total = P.count.get(s, 0)
        if not total:
            continue
        rset = refd.get(s, set())
        m = {}
        nci = 0
        for k in range(1, total + 1):
            if k in rset or k == total:
                nci += 1
                m[k] = (nci, True)
            else:
                m[k] = (nci, False)
        remap[s] = m
    with nc.Block() as block:
        def emit(eng_name):
            def body(eng):
                own = ESEM[eng_name]
                for (fn, waits, sem, amt, tag) in P.stream[eng_name]:
                    for (s, v) in waits:
                        if s in remap:
                            v = remap[s][v][0]
                        eng.wait_ge(semh[s], v)
                    tagnop = any(tag.startswith(tp) for tp in noptags) if noptags else False
                    if (eng_name in nopify or tagnop) and amt == 1:
                        inst = eng.nop()
                    else:
                        inst = fn(eng)
                    if sem in remap:
                        cnt, do_inc = remap[sem][P2_cursor[sem]]
                        P2_cursor[sem] += 1
                        if do_inc:
                            inst.then_inc(semh[sem], amt)
                    else:
                        inst.then_inc(semh[sem], amt)
            return body
        P2_cursor = {s: 1 for s in remap}
        block.sync(emit("sync"))
        block.gpsimd(emit("gpsimd"))
        block.tensor(emit("tensor"))
        block.scalar(emit("scalar"))
        block.vector(emit("vector"))

    es.close()
    return nc


_prog_cache = {}


def _get_prog(S):
    if S not in _prog_cache:
        _prog_cache[S] = build_program(S)
    return _prog_cache[S]


def make_in_map(core_inputs):
    return core_inputs


def kernel(**inputs):
    emb = np.ascontiguousarray(np.asarray(inputs["L0_attn_embedded"], np.float32))
    B, S, _ = emb.shape
    f = lambda k: np.ascontiguousarray(np.asarray(inputs[k], np.float32))
    wqT = np.ascontiguousarray(f("L0_attn_Wq").T)
    wkT = np.ascontiguousarray(f("L0_attn_Wk").T)
    wvT = np.ascontiguousarray(f("L0_attn_Wv").T)
    bqT = np.ascontiguousarray(f("L0_attn_bq").reshape(8, 128).T)
    bkT = np.ascontiguousarray(f("L0_attn_bk").reshape(8, 128).T)
    bvT = np.ascontiguousarray(f("L0_attn_bv").reshape(8, 128).T)
    gamma = f("L0_attn_gamma").reshape(1, D)
    beta = f("L0_attn_beta").reshape(1, D)
    ngamma = f("L0_ffn_norm_gamma").reshape(1, D)
    nbeta = f("L0_ffn_norm_beta").reshape(1, D)
    sigT = np.ascontiguousarray(f("L0_ffn_signatures_raw").T)
    knots = f("L0_ffn_knot_values").reshape(1, 8)
    temp = f("L0_ffn_temperature").reshape(1, 1)
    oscale = f("L0_ffn_output_scale").reshape(1, 1)
    cw1 = f("L0_ffn_compress_w1")
    cb1T = np.ascontiguousarray(f("L0_ffn_compress_b1").reshape(2, 128).T)
    cw2 = np.ascontiguousarray(
        f("L0_ffn_compress_w2").reshape(2, 128, 2).transpose(1, 0, 2).reshape(128, 4))
    cb2 = f("L0_ffn_compress_b2").reshape(2, 1)
    spl = np.ascontiguousarray(np.concatenate([
        f("L0_ffn_spline_w1").reshape(64, 32),
        f("L0_ffn_spline_b1").reshape(64, 16),
        f("L0_ffn_spline_w2").reshape(64, 16),
        f("L0_ffn_spline_b2").reshape(64, 1)], axis=1))
    dirs = f("L0_ffn_directions")

    shared = dict(wqT=wqT, wkT=wkT, wvT=wvT, bqT=bqT, bkT=bkT, bvT=bvT,
                  gamma=gamma, beta=beta, ngamma=ngamma, nbeta=nbeta, sigT=sigT,
                  knots=knots, temp=temp, oscale=oscale, cw1=cw1, cb1T=cb1T,
                  cw2r=cw2, cb2=cb2, spl=spl, dirs=dirs)
    in_maps = []
    for b in range(B):
        m = dict(shared)
        m["x0"] = np.ascontiguousarray(emb[b])
        in_maps.append(m)
    nc = _get_prog(S)
    res = run_bass_kernel_spmd(nc, in_maps, core_ids=list(range(B)))
    out = np.stack([np.asarray(r["out"], np.float32) for r in res.results], axis=0)
    return out


# version 14 (final)
# speedup vs baseline: 1.3335x; 1.0069x over previous
import numpy as np
import ml_dtypes
import concourse.bass as bass
import concourse.mybir as mybir
from concourse.bass_utils import run_bass_kernel_spmd
from contextlib import ExitStack

F32 = mybir.dt.float32
BF16 = mybir.dt.bfloat16
AF = mybir.ActivationFunctionType
ALU = mybir.AluOpType
AX = mybir.AxisListType

D = 1024
NPAIR = 8
EPS = 1e-5

ENGINES = ("sync", "gpsimd", "tensor", "scalar", "vector")
ESEM = {"sync": "eS", "gpsimd": "eG", "tensor": "eT", "scalar": "eA", "vector": "eV"}
import os as _os_se
SIM_EDGES = _os_se.environ.get("SIM_EDGES", "") == "1"
NDMASEM = 20
DMAPOOL = {"sync": [f"dS{i}" for i in range(NDMASEM)],
           "gpsimd": [f"dG{i}" for i in range(NDMASEM)]}


class Planner:
    def __init__(self):
        self.stream = {e: [] for e in ENGINES}
        self.count = {}
        self.snap = {}
        self.W = {e: {} for e in ENGINES}
        self.nwaits = 0
        self.dma_rr = {}

    def _merge_into(self, w, tok):
        s, v = tok
        snap = self.snap.get((s, v))
        if snap is None:
            if w.get(s, -1) < v:
                w[s] = v
            return
        for k2, v2 in snap.items():
            if w.get(k2, -1) < v2:
                w[k2] = v2

    def op(self, eng, fn, deps=(), dma=False, tag=""):
        w = self.W[eng]
        deps = list(deps)
        if dma:
            pool = DMAPOOL[eng]
            i = self.dma_rr.get(eng, 0)
            self.dma_rr[eng] = i + 1
            sem = pool[i % NDMASEM]
            prevc = self.count.get(sem, 0)
            if prevc > 0:
                deps.append((sem, prevc))
            amt = 16
        else:
            sem = ESEM[eng]
            amt = 1
            prev = self.count.get(sem, 0)
            if prev > 0:
                deps.insert(0, (sem, prev))
        waits = []
        for tok in deps:
            if tok is None:
                continue
            s, v = tok
            if w.get(s, -1) >= v:
                continue
            waits.append((s, v))
            self._merge_into(w, tok)
        c = self.count.get(sem, 0) + amt
        self.count[sem] = c
        snapd = dict(w)
        snapd[sem] = max(snapd.get(sem, 0), c)
        self.snap[(sem, c)] = snapd
        self.stream[eng].append((fn, waits, sem, amt, tag))
        self.nwaits += len(waits)
        return (sem, c)


def build_program(S, skip=()):
    NT = S // 128
    CH = S // 512
    import os as _os0
    _scr = 16384 if _os0.environ.get("DEVDBG", "") == "1" else 2048
    nc = bass.Bass("TRN2", target_bir_lowering=False, dynamic_dma_scratch_size=_scr)
    P = Planner()

    def din(name, shape):
        return nc.dram_tensor(name, list(shape), F32, kind="ExternalInput").ap()

    x0_d = din("x0", (S, D))
    wqT_d = din("wqT", (D, D)); wkT_d = din("wkT", (D, D)); wvT_d = din("wvT", (D, D))
    bqT_d = din("bqT", (128, 8)); bkT_d = din("bkT", (128, 8)); bvT_d = din("bvT", (128, 8))
    gam_d = din("gamma", (1, D)); bet_d = din("beta", (1, D))
    ngam_d = din("ngamma", (1, D)); nbet_d = din("nbeta", (1, D))
    sigT_d = din("sigT", (D, 64))
    knots_d = din("knots", (1, 8)); temp_d = din("temp", (1, 1)); oscale_d = din("oscale", (1, 1))
    cw1_d = din("cw1", (D, 256)); cb1T_d = din("cb1T", (128, 2))
    cw2_d = din("cw2r", (128, 4)); cb2_d = din("cb2", (2, 1))
    spl_d = din("spl", (64, 65)); dirs_d = din("dirs", (64, D))
    out_d = nc.dram_tensor("out", [S, D], F32, kind="ExternalOutput").ap()
    import os as _os
    _dbg = _os.environ.get("DEVDBG", "") in ("1", "2")
    _dbg1 = _os.environ.get("DEVDBG", "") == "1"
    _ik = "ExternalOutput" if _dbg else "Internal"
    kts_d = nc.dram_tensor("kts", [NPAIR, 128, S], BF16, kind=_ik).ap()
    qts_d = nc.dram_tensor("qts", [NPAIR, 128, S], BF16, kind="Internal").ap()
    aTs_d = nc.dram_tensor("aTs", [8, 128, S], BF16, kind="Internal").ap()
    vs_d = nc.dram_tensor("vs", [NPAIR, NT, 128, 130], BF16, kind=_ik).ap()
    x1s_d = nc.dram_tensor("x1s", [S, D], F32, kind=_ik).ap()
    zs_d = nc.dram_tensor("zs", [S, D], F32, kind=_ik).ap()
    dbg_d = nc.dram_tensor("dbg", [8, 128, D], F32, kind="ExternalOutput").ap() if _dbg1 else None

    es = ExitStack()

    def sb(name, shape, dtype):
        return es.enter_context(nc.sbuf_tensor(name, list(shape), dtype))

    ident = sb("ident", (128, 128), F32)
    diagmask = sb("diagmask", (128, 128), BF16)
    dgf = sb("dgf", (128, 128), F32)
    onescol_b = sb("onescol_b", (128, 1), BF16)
    onesrow = sb("onesrow", (1, 128), F32)
    iota64 = sb("iota64", (64, 1), F32)
    iotaj = sb("iotaj", (128, 72), F32)
    bigmi = sb("bigmi", (128, 8), F32)
    scal = sb("scal", (128, 16), F32)
    scrow = sb("scrow", (1, 16), F32)
    knots_s = sb("knots_s", (1, 8), F32)
    temp_s = sb("temp_s", (1, 4), F32)
    gam_t = sb("gam_t", (128, D), F32); bet_t = sb("bet_t", (128, D), F32)
    ngam_t = sb("ngam_t", (128, D), F32); nbet_t = sb("nbet_t", (128, D), F32)
    bqT = sb("bqT_s", (128, 8), F32); bkT = sb("bkT_s", (128, 8), F32); bvT = sb("bvT_s", (128, 8), F32)
    cb1T = sb("cb1T_s", (128, 2), F32)
    cb2_s = sb("cb2_s", (2, 1), F32)
    mu1 = sb("mu1", (128, NT), F32); r1 = sb("r1", (128, NT), F32)
    mvt = sb("mvt", (128, 2), F32)
    st6 = sb("st6", (128, 12), F32)
    sgt = sb("sgt", (128, 8 * 72), F32)
    x0c = [sb(f"x0c{i}", (128, D), F32) for i in range(8)]
    ACC = [sb(f"ACC{i}", (128, D), F32) for i in range(8)]
    aT = sb("aT", (128, 8 * 512), BF16)
    qT = sb("qT", (128, 8 * 512), BF16)
    kbuf = [sb(f"kbuf{i}", (128, S), BF16) for i in range(2)]
    vbuf = [sb(f"vbuf{i}", (128, NT * 130), BF16) for i in range(2)]
    pTb = [sb(f"pTb{i}", (128, 1536), BF16) for i in range(4)]
    zsb = sb("zsb", (128, 512), F32)
    zsb2 = sb("zsb2", (128, 512), F32)
    dnm = sb("dnm", (128, 64), F32)
    dnr = sb("dnr", (128, 12), F32)
    zst = [sb(f"zst{i}", (128, 64), F32) for i in range(4)]
    wstage = sb("wstage", (128, D), F32)
    zrb = [sb(f"zrb{i}", (128, D), F32) for i in range(2)]
    ybuf = [sb(f"ybuf{i}", (128, D), F32) for i in range(4)]
    x1rb = [sb("x1rb0", (128, D), F32)]
    gates = sb("gates", (128, 4 * 72), F32)
    calv = sb("calv", (128, 72), F32)
    calt = sb("calt", (128, 72), F32)
    cal2 = sb("cal2", (128, 72), F32)
    idxs = sb("idxs", (128, 16), F32)
    tidxr = sb("tidxr", (1, 512), F32)
    ohbufs = [sb(f"ohT{i}", (64, 128), BF16) for i in range(4)]
    compt = sb("compt", (128, 8), F32)
    ctmp = sb("ctmp", (2, 512), F32)
    splg = sb("splg", (128, 65), F32)
    hhs = sb("hhs", (128, 48), F32)
    mags = sb("mags", (128, 4), F32)
    mv2 = sb("mv2", (128, 8), F32)
    mv3 = sb("mv3", (128, 8), F32)
    mvy = sb("mvy", (128, 8), F32)
    r2c = sb("r2c", (128, 4), F32); r3c = sb("r3c", (128, 4), F32); ryc = sb("ryc", (128, 4), F32)
    cw2_b = sb("cw2_b", (128, 4), BF16)
    dirs_b = sb("dirs_b", (64, D), BF16)
    spl_b = sb("spl_b", (64, 65), BF16)

    nsc = sb("nsc", (128, 3 * NT), F32)
    hpre = sb("hpre", (128, 512), F32)
    ps_ring = es.enter_context(nc.psum_tensor("ps_ring", [128, 3072], F32))
    ps_z = es.enter_context(nc.psum_tensor("ps_z", [128, 1024], F32))

    semh = {}
    for sname in (["eS", "eG", "eT", "eA", "eV"] + DMAPOOL["sync"] + DMAPOOL["gpsimd"]):
        semh[sname] = es.enter_context(nc.semaphore(sname))

    def mm(out_ap, lhsT_ap, rhs_ap, deps, start=True, stop=True, tag="mm", tpos=None):
        def f(t, o=out_ap, l=lhsT_ap, r=rhs_ap, s1=start, s2=stop, tp=tpos):
            return t.matmul(o, l, r, start=s1, stop=s2, tile_position=tp)
        return P.op("tensor", f, deps, tag=tag)

    def tp_(out_ap, in_ap, deps, tag="tp", k=128):
        def f(t, o=out_ap, i=in_ap, k=k):
            return t.transpose(o, i, ident[0:k, 0:k])
        return P.op("tensor", f, deps, tag=tag)

    def vop(fn, deps, tag="v"):
        return P.op("vector", fn, deps, tag=tag)

    def sop(fn, deps, tag="s"):
        return P.op("scalar", fn, deps, tag=tag)

    def vcopy(out_ap, in_ap, deps, tag="vcopy"):
        return vop(lambda v, o=out_ap, i=in_ap: v.tensor_copy(o, i), deps, tag=tag)

    def dma(queue, out_ap, in_ap, deps, tag="dma"):
        return P.op(queue, lambda e, o=out_ap, i=in_ap: e.dma_start(out=o, in_=i),
                    deps, dma=True, tag=tag)

    ring_use = [None] * 6
    ring_fill = [None] * 6
    rp = [0]

    def ring_alloc(n=1):
        if (rp[0] % 6) + n > 6:
            rp[0] += 6 - (rp[0] % 6)
        slots = [(rp[0] + i) % 6 for i in range(n)]
        rp[0] += n
        deps = [ring_use[s] for s in slots if ring_use[s] is not None]
        deps += [ring_fill[s] for s in slots if ring_fill[s] is not None]
        return slots, deps

    def ring_filled(slots, tok):
        for s in slots:
            ring_fill[s] = tok

    def ring_used(slots, tok):
        for s in slots:
            ring_use[s] = tok

    def rs(slot, cols=512, off=0):
        return ps_ring[:, 512 * slot + off: 512 * slot + off + cols]

    SC_NINV, SC_K0, SC_D0, SC_OS = 0, 1, 2, 9

    def scap(i):
        return scal[:, i:i + 1]

    def newton_rsqrt(dst_ap, t_ap, w, deps, tag):
        T = nsc[:, 0:w]; Sx = nsc[:, NT:NT + w]; U = nsc[:, 2 * NT:2 * NT + w]
        tk = vcopy(T, t_ap, list(deps) + [newton_busy[0]], tag=tag + "cp")
        tk = vop(lambda v: v.reciprocal(Sx, T), [tk], tag=tag + "rc")
        tk = vop(lambda v: v.tensor_single_scalar(out=Sx, in_=Sx, scalar=1.0, op=ALU.min),
                 [tk], tag=tag + "mn")
        for it_ in range(16):
            tk = vop(lambda v: v.tensor_mul(U, Sx, Sx), [tk], tag=tag + "n1")
            tk = vop(lambda v: v.tensor_mul(U, U, T), [tk], tag=tag + "n2")
            tk = vop(lambda v: v.tensor_scalar(out=U, in0=U, scalar1=-0.5, scalar2=1.5,
                                               op0=ALU.mult, op1=ALU.add), [tk], tag=tag + "n3")
            tk = vop(lambda v: v.tensor_mul(Sx, Sx, U), [tk], tag=tag + "n4")
        tk = vcopy(dst_ap, Sx, [tk], tag=tag + "out")
        newton_busy[0] = tk
        return tk

    newton_busy = [None]
    gelu_busy = [None]

    t_id_i = P.op("gpsimd", lambda g: g.iota(dgf[:, :], pattern=[[1, 128]], base=0,
                                             channel_multiplier=-1,
                                             allow_small_or_imprecise_dtypes=True), (), tag="iota")
    t_ident = vop(lambda v: v.tensor_single_scalar(out=ident[:, :], in_=dgf[:, :],
                                                   scalar=0.0, op=ALU.is_equal), [t_id_i], tag="ident")
    t_dgm = vop(lambda v: v.tensor_single_scalar(out=diagmask[:, :], in_=dgf[:, :],
                                                 scalar=0.0, op=ALU.is_le), [t_id_i], tag="dgm")
    t_io64 = P.op("gpsimd", lambda g: g.iota(iota64[:, :], pattern=[[0, 1]], base=0,
                                             channel_multiplier=1,
                                             allow_small_or_imprecise_dtypes=True),
                  [t_id_i], tag="io64")
    t_ioj = P.op("gpsimd", lambda g: g.iota(iotaj[:, :], pattern=[[1, 72]], base=0,
                                            channel_multiplier=0,
                                            allow_small_or_imprecise_dtypes=True),
                 [t_io64], tag="ioj")
    t_bigmi_i = P.op("gpsimd", lambda g: g.iota(bigmi[:, :], pattern=[[-1, 8]], base=999,
                                                channel_multiplier=0,
                                                allow_small_or_imprecise_dtypes=True),
                     [t_ioj], tag="iobm")
    t_ones_b = vop(lambda v: v.memset(onescol_b[:, :], 1.0), (), tag="onesb")
    t_ones_r = vop(lambda v: v.memset(onesrow[:, :], 1.0), (), tag="onesr")
    t_zsb0 = vop(lambda v: v.memset(zsb[64:128, :], 0.0), (), tag="zsb0")
    t_zsb1 = vop(lambda v: v.memset(zsb2[64:128, :], 0.0), (), tag="zsb1")
    t_zsbp = [t_zsb0, t_zsb1]


    rowbuf_read = [None]

    def bcast_row(dst, src_d, tag):
        tok_l = dma("sync", wstage[0:1, 0:D], src_d[:, :], [rowbuf_read[0]], tag="ld_" + tag)
        outs = []
        for j in range(2):
            slots, rdeps = ring_alloc(1)
            sl = slots[0]
            tmm = mm(rs(sl), onesrow[:, :], wstage[0:1, 512 * j: 512 * j + 512],
                     [tok_l, t_ones_r] + rdeps, tag="bc_" + tag)
            rowbuf_read[0] = tmm
            ring_filled(slots, tmm)
            tcp = vcopy(dst[:, 512 * j: 512 * j + 512], rs(sl), [tmm], tag="bcc_" + tag)
            ring_used(slots, tcp)
            outs.append(tcp)
        return outs, tok_l

    gam_toks, l_g = bcast_row(gam_t, gam_d, "gam")
    bet_toks, l_b = bcast_row(bet_t, bet_d, "bet")
    ngam_toks, l_n = bcast_row(ngam_t, ngam_d, "ngam")
    nbet_toks, l_nb = bcast_row(nbet_t, nbet_d, "nbet")

    tok_bq = dma("sync", bqT[:, :], bqT_d[:, :], (), tag="bq")
    tok_bk = dma("sync", bkT[:, :], bkT_d[:, :], (), tag="bk")
    tok_bv = dma("sync", bvT[:, :], bvT_d[:, :], (), tag="bv")
    tok_cb1 = dma("sync", cb1T[:, :], cb1T_d[:, :], (), tag="cb1")
    tok_cb2 = dma("sync", cb2_s[:, :], cb2_d[:, :], (), tag="cb2")
    tok_kn = dma("sync", knots_s[:, :], knots_d[:, :], (), tag="kn")
    tok_tm = dma("sync", temp_s[:, 0:1], temp_d[:, :], (), tag="tm")
    tok_os = dma("sync", temp_s[:, 3:4], oscale_d[:, :], (), tag="os")
    t_t1 = vop(lambda v: v.tensor_single_scalar(out=temp_s[:, 1:2], in_=temp_s[:, 0:1],
                                                scalar=1e-6, op=ALU.add), [tok_tm], tag="t1")
    t_t2 = vop(lambda v: v.reciprocal(temp_s[:, 2:3], temp_s[:, 1:2]), [t_t1], tag="t2")
    t_t3 = vop(lambda v: v.tensor_single_scalar(out=temp_s[:, 2:3], in_=temp_s[:, 2:3],
                                                scalar=-1.0, op=ALU.mult), [t_t2], tag="t3")
    t_sz = vop(lambda v: v.memset(scrow[:, :], 0.0), (), tag="sz")
    t_s0 = vcopy(scrow[:, 0:1], temp_s[:, 2:3], [t_t3, t_sz], tag="s0")
    t_s1 = vcopy(scrow[:, 1:2], knots_s[:, 0:1], [tok_kn], tag="s1")
    t_s2 = vop(lambda v: v.tensor_sub(scrow[:, 2:9], knots_s[:, 1:8], knots_s[:, 0:7]),
               [tok_kn], tag="s2")
    t_s3 = vcopy(scrow[:, 9:10], temp_s[:, 3:4], [tok_os], tag="s3")
    slots, rdeps = ring_alloc(1)
    sl0 = slots[0]
    t_scm = mm(rs(sl0, 16), onesrow[:, :], scrow[:, 0:16],
               [t_s0, t_s1, t_s2, t_s3, t_ones_r] + rdeps, tag="scm")
    ring_filled(slots, t_scm)
    t_scal = vcopy(scal[:, :], rs(sl0, 16), [t_scm], tag="scal")
    ring_used(slots, t_scal)

    for kk in range(8):
        a_ap = sgt[:, 72 * kk: 72 * kk + 64]
        cs_ap = sgt[:, 72 * kk + 64: 72 * kk + 72]
        tok_l = dma("sync", a_ap, sigT_d[128 * kk:128 * kk + 128, :], (), tag="sig")
        t_a = vop(lambda v, a=a_ap: v.tensor_scalar(out=calv[:, 0:64], in0=a, scalar1=0.3,
                                                    scalar2=None, op0=ALU.is_gt), [tok_l], tag="siga")
        t_b = vop(lambda v, a=a_ap: v.tensor_scalar(out=calt[:, 0:64], in0=a, scalar1=-0.3,
                                                    scalar2=None, op0=ALU.is_lt), [tok_l], tag="sigb")
        t_c = vop(lambda v, a=a_ap: v.tensor_sub(a, calv[:, 0:64], calt[:, 0:64]),
                  [t_a, t_b], tag="sigc")
        t_d = vop(lambda v, kk=kk: v.reduce_sum(
            sgt[:, 72 * kk + 64: 72 * kk + 72],
            sgt[:, 72 * kk: 72 * kk + 64].rearrange("p (c t) -> p c t", t=8), axis=AX.X),
            [t_c], tag="sigd")
        t_e = vop(lambda v, a=cs_ap: v.tensor_scalar(out=calv[:, 0:8], in0=a, scalar1=0.0,
                                                     scalar2=None, op0=ALU.is_gt), [t_d], tag="sige")
        t_f = vop(lambda v, a=cs_ap: v.tensor_scalar(out=calt[:, 0:8], in0=a, scalar1=0.0,
                                                     scalar2=None, op0=ALU.is_lt), [t_d], tag="sigf")
        vop(lambda v, a=cs_ap: v.tensor_sub(a, calv[:, 0:8], calt[:, 0:8]), [t_e, t_f], tag="sigg")

    tok_cw2l = dma("sync", wstage[:, 0:4], cw2_d[:, :], [rowbuf_read[0]], tag="cw2")
    tok_cw2 = vcopy(cw2_b[:, :], wstage[:, 0:4], [tok_cw2l], tag="cw2c")
    tok_dirs0 = dma("sync", wstage[0:64, :], dirs_d[:, :], [tok_cw2], tag="dirs")
    tok_dirs = vcopy(dirs_b[:, :], wstage[0:64, :], [tok_dirs0], tag="dirsc")
    tok_spl0 = dma("sync", wstage[0:64, 0:65], spl_d[:, :], [tok_dirs], tag="spl")
    tok_spl = vcopy(spl_b[:, :], wstage[0:64, 0:65], [tok_spl0], tag="splc")

    x0_free = [None] * 8
    var_toks = []
    for t in range(NT):
        buf = x0c[t % 8]
        tok_l = dma("sync", buf[:, :], x0_d[128 * t:128 * t + 128, :],
                    [x0_free[t % 8]], tag="x0s")
        t1 = vop(lambda v, b=buf: v.bn_stats(st6[:, 0:6], b[:, 0:512]), [tok_l], tag="bns1")
        t2 = vop(lambda v, b=buf: v.bn_stats(st6[:, 6:12], b[:, 512:1024]), [tok_l], tag="bns2")
        t3 = vop(lambda v: v.bn_aggr(mvt[:, :], st6[:, :]), [t1, t2], tag="bnag")
        x0_free[t % 8] = t3
        t4 = vop(lambda v, t=t: v.tensor_single_scalar(out=r1[:, t:t + 1], in_=mvt[:, 1:2],
                                                       scalar=EPS, op=ALU.add), [t3], tag="veps")
        t7 = vcopy(mu1[:, t:t + 1], mvt[:, 0:1], [t3], tag="mu1")
        var_toks.append((t4, t7))
    t_r1 = newton_rsqrt(r1[:, 0:NT], r1[:, 0:NT], NT, [tk for pair in var_toks for tk in pair],
                        "ln1n")
    ln1_tok = [(t_r1, var_toks[t][1]) for t in range(NT)]

    es_kv = ExitStack()
    wkv_b = es_kv.enter_context(nc.sbuf_tensor("wkv_b", [128, 8 * 1024], BF16))
    kstage = es_kv.enter_context(nc.sbuf_tensor("kstage", [128, 512], BF16))
    vstage_f = es_kv.enter_context(nc.sbuf_tensor("vstage_f", [128, 512], F32))
    vstage = es_kv.enter_context(nc.sbuf_tensor("vstage", [128, 4 * 130], BF16))

    acc_tok = [None] * 8
    acc_read = [None] * 8
    aT_read = [None] * 8
    aT_tok = [None] * 8
    aT_store = [None] * 8
    kv_dma_toks = []
    last_wkv_mm = [None]

    def load_wkv(w_d, dep0):
        prev = dep0
        for kk in range(8):
            tok_l = dma("sync", wstage[:, :], w_d[128 * kk:128 * kk + 128, :],
                        [prev], tag="wkv")
            prev = vcopy(wkv_b[:, 1024 * kk: 1024 * kk + 1024], wstage[:, :],
                         [tok_l, last_wkv_mm[0]], tag="wkvc")
        return prev

    kst_free = [None]
    vstf_free = [None]
    vst_free = [None]

    def kv_pass(proj, w_ready, bT, bias_tok, kdst=None, build_aT=False):
        for n in range(CH):
            if build_aT:
                a_toks = []
                for tt in range(4):
                    t = 4 * n + tt
                    buf = x0c[t % 8]
                    tok_l = dma("sync", buf[:, :], x0_d[128 * t:128 * t + 128, :],
                                [x0_free[t % 8]], tag="x0kv")
                    ta = vop(lambda v, b=buf, t=t, tt=tt: v.tensor_scalar(
                        out=ACC[tt][:, :], in0=b[:, :], scalar1=mu1[:, t:t + 1],
                        scalar2=r1[:, t:t + 1], op0=ALU.subtract, op1=ALU.mult),
                        [tok_l, ln1_tok[t][0], ln1_tok[t][1], acc_read[tt], acc_tok[tt]],
                        tag="akv")
                    x0_free[t % 8] = ta
                    acc_tok[tt] = ta
                    a_toks.append(ta)
                for kk in range(8):
                    slots, rdeps = ring_alloc(1)
                    sl = slots[0]
                    last_t = None
                    for tt in range(4):
                        deps = [a_toks[tt], t_ident] + (rdeps if tt == 0 else [])
                        last_t = tp_(rs(sl, 128, 128 * tt),
                                     ACC[tt][:, 128 * kk:128 * kk + 128],
                                     deps, tag="aTkv")
                        acc_read[tt] = last_t
                    ring_filled(slots, last_t)
                    tc = vop(lambda v, kk=kk, sl=sl: v.tensor_copy(
                        aT[:, 512 * kk: 512 * kk + 512], rs(sl)),
                        [last_t, aT_tok[kk], aT_read[kk], aT_store[kk]], tag="aTcp")
                    ring_used(slots, tc)
                    aT_tok[kk] = tc
                    td = dma("sync", aTs_d[kk, :, 512 * n: 512 * n + 512],
                             aT[:, 512 * kk: 512 * kk + 512], [tc], tag="aTst")
                    aT_store[kk] = td
            else:
                for kk in range(8):
                    tok = dma("sync", aT[:, 512 * kk: 512 * kk + 512],
                              aTs_d[kk, :, 512 * n: 512 * n + 512],
                              [aT_tok[kk], aT_read[kk], aT_store[kk]], tag="aTld")
                    aT_tok[kk] = tok
            for p in range(NPAIR):
                slots, rdeps = ring_alloc(1)
                sl = slots[0]
                mmt = None
                for kk in range(8):
                    deps = [aT_tok[kk], w_ready] + (rdeps if kk == 0 else [])
                    mmt = mm(rs(sl), wkv_b[:, 1024 * kk + 128 * p: 1024 * kk + 128 * p + 128],
                             aT[:, 512 * kk: 512 * kk + 512], deps,
                             start=(kk == 0), stop=(kk == 7), tag=proj + "mm")
                    aT_read[kk] = mmt
                last_wkv_mm[0] = mmt
                ring_filled(slots, mmt)
                if proj in ("k", "q"):
                    tev = vop(lambda v, sl=sl, p=p: v.tensor_scalar(
                        out=kstage[:, :], in0=rs(sl), scalar1=bT[:, p:p + 1], scalar2=None,
                        op0=ALU.add), [mmt, bias_tok, kst_free[0]], tag="kev")
                    ring_used(slots, tev)
                    tdma = dma("sync", kdst[p, :, 512 * n: 512 * n + 512], kstage[:, :],
                               [tev], tag="kst")
                    kst_free[0] = tdma
                    kv_dma_toks.append(tdma)
                else:
                    tevs = []
                    for j in range(4):
                        tev = vop(lambda v, sl=sl, p=p, j=j: v.tensor_scalar(
                            out=vstage_f[:, 128 * j: 128 * j + 128],
                            in0=rs(sl, 128, 128 * j), scalar1=bT[:, p:p + 1], scalar2=None,
                            op0=ALU.add), [mmt, bias_tok, vstf_free[0]], tag="vev")
                        tevs.append(tev)
                    ring_used(slots, tevs[-1])
                    slots2, rdeps2 = ring_alloc(1)
                    sl2 = slots2[0]
                    last_t = None
                    for j in range(4):
                        deps = [tevs[j]] + (rdeps2 if j == 0 else [])
                        last_t = tp_(rs(sl2, 128, 128 * j), vstage_f[:, 128 * j: 128 * j + 128],
                                     deps, tag="vT")
                    vstf_free[0] = last_t
                    ring_filled(slots2, last_t)
                    tcp = vop(lambda v, sl2=sl2: v.tensor_copy(
                        vstage[:, :].rearrange("p (t s c) -> p t s c", t=4, s=2)[:, :, :, 0:64],
                        rs(sl2, 512).rearrange("p (t s c) -> p t s c", t=4, s=2)),
                        [last_t, vst_free[0]], tag="vcp")
                    ring_used(slots2, tcp)
                    tms = vop(lambda v: v.memset(
                        vstage[:, :].rearrange("p (t c) -> p t c", t=4)[:, :, 64:130:65], 1.0),
                        [tcp], tag="vones")
                    tdma = dma("sync", vs_d[p, 4 * n: 4 * n + 4].rearrange("t p c -> p t c"),
                               vstage[:, :].rearrange("p (t c) -> p t c", t=4), [tms], tag="vst")
                    vst_free[0] = tdma
                    kv_dma_toks.append(tdma)

    wq_ready = load_wkv(wqT_d, tok_spl)
    kv_pass("q", wq_ready, bqT, tok_bq, kdst=qts_d, build_aT=True)
    wk_ready = load_wkv(wkT_d, wq_ready)
    kv_pass("k", wk_ready, bkT, tok_bk, kdst=kts_d)
    wv_ready = load_wkv(wvT_d, wk_ready)
    kv_pass("v", wv_ready, bvT, tok_bv)
    zmov = wstage[0:1, 0:512]
    zrow = wstage[0:1, 512:577]
    t_zmov = vop(lambda v: v.memset(wstage[0:1, 0:577], 0.0), [wv_ready], tag="zmov")
    t_zrow = t_zmov
    es_kv.close()
    kv_all = P.op("gpsimd", lambda g: g.nop(), kv_dma_toks, tag="kvbar")

    cw1_f = sb("cw1_f", (128, 8 * 256), F32)
    x2T_f = sb("x2T_f", (128, 8 * 512), F32)
    hTb = sb("hTb", (128, 2 * 512), BF16)
    cw1_toks = []
    for kk in range(8):
        tok_l = dma("sync", cw1_f[:, 256 * kk: 256 * kk + 256],
                    cw1_d[128 * kk: 128 * kk + 128, :], [last_wkv_mm[0]], tag="cw1")
        cw1_toks.append(tok_l)

    kbuf_tok = [None, None]; kbuf_read = [None, None]
    vbuf_tok = [None, None]; vbuf_read = [None, None]
    pT_tok = [None] * 4; pT_read = [None] * 4
    zs_tok = [None, None]; zs_read = [None, None]
    ucnt = [0]
    psz_read = [None, None]; zs_close = [None, None]
    qbuf = [qT, aT]
    qbuf_tok = [None, None]
    qbuf_read = [[], [tk for tk in aT_tok if tk] + [tk for tk in aT_read if tk] +
                 [tk for tk in aT_store if tk]]
    dnm_tok = {}; dnm_read = [None, None]
    ybuf_tok = [None] * 4; ybuf_read = [None] * 4
    x1rb_tok = [None, None]; x1rb_read = [None, None]
    x1dma_tok = [None] * 8
    outdma_toks = []
    gates_tok = [None] * 4
    gexp_tok = [None] * 4
    cal_state = {}
    mags_tok = [None] * 4
    compt_tok = [None] * 4
    oh_tok = [None] * 4; oh_read = [None] * 4
    tidxr_read = [None] * 4; idx_read = [None] * 4
    gather_tok = {}
    x2T_tok = [None] * 8; x2T_read = [None] * 8
    hT_tok = [None] * 2; hT_read = [None] * 2
    mv2_tok = [None] * 4; mv3_tok = [None] * 4; mvy_tok = [None] * 4
    ctmp_read = [None]
    ctv4_hold = [None]
    r2_tok = [None] * 4; r3_tok = [None] * 4; ry_tok = [None] * 4
    ysts_read = [None] * 4
    spl_ps = {}

    def a_init(c, free_x0=False):
        toks = []
        for tt in range(4):
            t = 4 * c + tt
            ai = 4 * (c % 2) + tt
            buf = x0c[(c % 2) * 4 + tt]
            tok_l = dma("sync", buf[:, :], x0_d[128 * t:128 * t + 128, :],
                        [x0_free[(c % 2) * 4 + tt]], tag=f"x0c{c}")
            ta = vop(lambda v, b=buf, t=t, ai=ai: v.tensor_scalar(
                out=ACC[ai][:, :], in0=b[:, :], scalar1=mu1[:, t:t + 1],
                scalar2=r1[:, t:t + 1], op0=ALU.subtract, op1=ALU.mult),
                [tok_l, ln1_tok[t][0], ln1_tok[t][1], acc_read[ai], acc_tok[ai],
                 x1dma_tok[ai]], tag=f"a{c}")
            acc_tok[ai] = ta
            if free_x0:
                x0_free[(c % 2) * 4 + tt] = ta
            toks.append(ta)
        return toks

    def kvq_load(p):
        pb = p % 2
        deps = [kbuf_read[pb], kv_all]
        tk = dma("sync", kbuf[pb][:, :], kts_d[p, :, :], deps, tag=f"kld{p}")
        deps = [vbuf_read[pb], kv_all]
        tvs = []
        for half in range(2):
            tv = dma("sync",
                     vbuf[pb][:, half * (NT // 2) * 130:(half + 1) * (NT // 2) * 130].rearrange(
                         "p (t c) -> p t c", c=130),
                     vs_d[p, half * (NT // 2):(half + 1) * (NT // 2)].rearrange(
                         "t p c -> p t c"), deps, tag=f"vld{p}")
            tvs.append(tv)
        tq = dma("sync", qbuf[pb][:, 0:S], qts_d[p, :, :],
                 list(qbuf_read[pb]) + [kv_all], tag=f"qld{p}")
        kbuf_tok[pb] = tk
        vbuf_tok[pb] = tvs
        qbuf_tok[pb] = tq
        return tk, tvs

    def tile_schedule(c):
        groups = []
        if "sside" in skip:
            CHsk = True
        else:
            CHsk = False
        for hh in range(2):
            if CHsk:
                break
            for it in range(4):
                i = 4 * c + it
                chunks = []
                rem = 128 * (i + 1)
                j = 0
                while rem > 0:
                    nj = min(512, rem)
                    chunks.append((j, nj))
                    rem -= nj
                    j += 1
                g = []
                for (j, nj) in chunks:
                    if nj < 512:
                        if g:
                            groups.append(("S", g))
                            g = []
                        groups.append(("S", [(hh, it, j, nj)]))
                        continue
                    g.append((hh, it, j, nj))
                    if len(g) == 3:
                        groups.append(("S", g))
                        g = []
                if g:
                    groups.append(("S", g))
        stg = []
        for kp in range(4 * c + 4, NT):
            for hh in range(2):
                stg.append((hh, kp, 512))
                if len(stg) == 2:
                    groups.append(("T", stg))
                    stg = []
        if stg:
            groups.append(("T", stg))
        for kp in range(4 * c, min(4 * c + 4, NT)):
            jj = kp - 4 * c
            n = 128 * (jj + 1)
            for hh in range(2):
                groups.append(("T", [(hh, kp, n)]))
        return groups

    def emit_pair(c, p, sched):
        pb = p % 2
        ktok = kbuf_tok[pb]
        vtoks = vbuf_tok[pb]
        qb = qbuf[pb]
        qtok = qbuf_tok[pb]
        ucnt[0] += 1
        dpar = ucnt[0] % 2
        dbase = 32 * dpar
        pv_last = [None, None]
        zopen = [None, None]
        have_full = 4 * c + 4 < NT
        if not have_full:
            for hh in range(2):
                zopen[hh] = mm(ps_z[0:65, 512 * hh: 512 * hh + 512], zrow[0:1, 0:65],
                               zmov[0:1, :],
                               [t_zrow, t_zmov, psz_read[hh], zs_close[hh]], start=True,
                               stop=False, tag=f"zop{c}_{p}")
                pv_last[hh] = zopen[hh]
        first_pv = [have_full, have_full]
        pT_parity = [0]
        pend_consume = []
        kread = []
        vread = []
        qread = []
        sg_idx = {}
        for (kind, g) in sched:
            L = len(g)
            slots, rdeps = ring_alloc(L)
            qk_toks = []
            for ii, item in enumerate(g):
                sl = slots[ii]
                if "qk" in skip:
                    continue
                if "qk1" in skip and not (p == 0 and c == 0 and ii == 0 and kind == "T"):
                    continue
                if kind == "T":
                    hh, kp, n = item
                    lhs = kbuf[pb][64 * hh:64 * hh + 64, 128 * kp:128 * kp + 128]
                    rhsap = qb[64 * hh:64 * hh + 64, 512 * c: 512 * c + n]
                    deps = [ktok, qtok] + (rdeps if ii == 0 else [])
                    tokm = mm(rs(sl, n), lhs, rhsap, deps, tag=f"qkT{c}_{p}")
                else:
                    hh, it, j, n = item
                    lhs = qb[64 * hh:64 * hh + 64, 512 * c + 128 * it: 512 * c + 128 * it + 128]
                    rhsap = kbuf[pb][64 * hh:64 * hh + 64, 512 * j: 512 * j + n]
                    deps = [ktok, qtok] + (rdeps if ii == 0 else [])
                    tokm = mm(rs(sl, n), lhs, rhsap, deps, tag=f"qkS{c}_{p}")
                qk_toks.append(tokm)
                ring_fill[sl] = tokm
                kread.append(tokm)
                qread.append(tokm)
            pp = pT_parity[0]
            pT_parity[0] = (pp + 1) % 4
            ns = [item[-1] for item in g]
            sameN = all(x == ns[0] for x in ns)
            exp_toks = []
            if "exp" in skip:
                for ii2, sl_ in enumerate(slots):
                    if ii2 < len(qk_toks):
                        ring_used([sl_], qk_toks[ii2])
                pT_parity[0] = pp
                continue
            if kind == "S":
                hh, it, _, _ = g[0]
                base_col = dbase + 4 * (4 * hh + it)
                gi = sg_idx.get((hh, it), 0)
                sg_idx[(hh, it)] = gi + 1
                totN = sum(ns)
                acap = dnm[:, base_col + gi: base_col + gi + 1]
                din_ap = rs(slots[0], totN) if (sameN and ns[0] == 512) or L == 1 else None
                if din_ap is None:
                    for ii, item in enumerate(g):
                        _, _, _, n = item
                        gi2 = sg_idx[(hh, it)] - 1 + ii
                        pass
                    raise AssertionError("S group must be uniform or single")
                texp = sop(lambda s, i_=din_ap, o=pTb[pp][:, 0:totN], a=acap:
                           s.activation(out=o, in_=i_, func=AF.Exp, scale=1.0 / 64.0,
                                        accum_out=a),
                           qk_toks + [pT_read[pp], pT_tok[pp], dnm_read[dpar]],
                           tag=f"expS{c}_{p}")
                dnm_tok[(hh, it, gi)] = texp
                pT_tok[pp] = texp
                exp_toks.append(texp)
                ring_used(slots, texp)
            else:
                if sameN:
                    totN = ns[0] * L
                    texp = sop(lambda s, i_=rs(slots[0], totN), o=pTb[pp][:, 0:totN]:
                               s.activation(out=o, in_=i_, func=AF.Exp, scale=1.0 / 64.0),
                               qk_toks + [pT_read[pp], pT_tok[pp]], tag=f"expT{c}_{p}")
                    exp_toks = [texp] * L
                    pT_tok[pp] = texp
                    ring_used(slots, texp)
                else:
                    off = 0
                    for ii, item in enumerate(g):
                        n = item[-1]
                        texp = sop(lambda s, i_=rs(slots[ii], n), o=pTb[pp][:, off:off + n]:
                                   s.activation(out=o, in_=i_, func=AF.Exp, scale=1.0 / 64.0),
                                   [qk_toks[ii], pT_read[pp], pT_tok[pp]], tag=f"expT{c}_{p}")
                        exp_toks.append(texp)
                        pT_tok[pp] = texp
                        ring_used([slots[ii]], texp)
                        off += n
            if kind == "S":
                continue
            if dbg_d is not None and c == 0 and p == 0 and not dbg_state.get(5):
                dbg_state[5] = True
                _dtok = dbg_dump(5, pTb[pp][:, 0:512], [exp_toks[0]], w=512)
                pT_tok[pp] = _dtok
                _d6 = dbg_dump(6, qb[:, 0:512], [qtok], w=512)
                qbuf_read[pb] = [_d6]
                _d7 = dbg_dump(7, kbuf[pb][:, 0:512], [ktok], w=512)
                kbuf_read[pb] = _d7

            def consume(g=g, pp=pp, exp_toks=exp_toks, sameN=sameN):
                off = 0
                creads = []
                for ii, item in enumerate(g):
                    hh, kp, n = item
                    jj = kp - 4 * c
                    et = exp_toks[ii] if not sameN else exp_toks[0]
                    pslice = lambda a, b_: pTb[pp][:, off + a: off + b_]
                    zoff = 512 * hh
                    if jj >= 4:
                        if "pvones" not in skip:
                            opener = first_pv[hh]
                            deps = [et] + vtoks
                            if opener:
                                deps += [psz_read[hh], zs_close[hh]]
                            elif zopen[hh] is not None:
                                deps.append(zopen[hh])
                            st = mm(ps_z[0:65, zoff:zoff + 512],
                                    vbuf[pb][:, 130 * kp + 65 * hh: 130 * kp + 65 * hh + 65],
                                    pslice(0, 512), deps,
                                    start=opener, stop=False, tag=f"pv{c}_{p}")
                            if opener:
                                first_pv[hh] = False
                                zopen[hh] = st
                            pv_last[hh] = st
                            creads.append(st)
                            vread.append(st)
                    else:
                        if jj > 0 and "pvones" not in skip:
                            st = mm(ps_z[0:65, zoff:zoff + 128 * jj],
                                    vbuf[pb][:, 130 * kp + 65 * hh: 130 * kp + 65 * hh + 65],
                                    pslice(0, 128 * jj),
                                    [et, zopen[hh]] + vtoks,
                                    start=False, stop=False, tag=f"pvp{c}_{p}")
                            pv_last[hh] = st
                            creads.append(st)
                            vread.append(st)
                        if "dpv" not in skip:
                            tmk = vop(lambda v, a=pslice(128 * jj, 128 * jj + 128):
                                      v.tensor_mul(a, a, diagmask[:, :]), [et, t_dgm],
                                      tag=f"dmask{c}_{p}")
                            st = mm(ps_z[0:64, zoff + 128 * jj: zoff + 128 * jj + 128],
                                    vbuf[pb][:, 130 * kp + 65 * hh: 130 * kp + 65 * hh + 64],
                                    pslice(128 * jj, 128 * jj + 128),
                                    [tmk, zopen[hh]] + vtoks,
                                    start=False, stop=False, tag=f"dpv{c}_{p}")
                            pv_last[hh] = st
                            creads.append(st)
                            vread.append(st)
                    off += n
                pT_read[pp] = creads[-1] if creads else pT_read[pp]

            pend_consume.append(consume)
            if len(pend_consume) > 3:
                pend_consume.pop(0)()
        while pend_consume:
            pend_consume.pop(0)()
        kbuf_read[pb] = kread[-1] if kread else kbuf_read[pb]
        vbuf_read[pb] = vread[-1] if vread else vbuf_read[pb]
        if qread:
            qbuf_read[pb] = [qread[-1]]
        return pv_last, sg_idx

    zst_read = [None] * 8
    zst_idx = [0]
    zdma_toks = []
    zdma_by_chunk = [[] for _ in range(CH)]
    RH = S // 16

    def z_evac(c, p, pv_last, sg_idx):
        toks = []
        dpar = ucnt[0] % 2
        dbase = 32 * dpar
        for hh in range(2):
            zoff = 512 * hh
            zsbh = zsb if hh == 0 else zsb2
            tclose = mm(ps_z[0:65, zoff: zoff + 1], zrow[0:1, 0:65], zmov[0:1, 0:1],
                        [pv_last[hh], t_zrow], start=False, stop=True, tag=f"zcl{c}_{p}")
            zs_close[hh] = tclose
            tcps = []
            for j in range(4):
                tcp = vcopy(zsbh[0:65, 128 * j: 128 * j + 128],
                            ps_z[0:65, zoff + 128 * j: zoff + 128 * j + 128],
                            [tclose, zs_tok[hh], zs_read[hh], t_zsbp[hh]], tag=f"zc{c}_{p}")
                tcps.append(tcp)
            zs_tok[hh] = tcps[-1]
            psz_read[hh] = tcps[-1]
            slots, rdeps = ring_alloc(1)
            sl = slots[0]
            tts = []
            for j in range(4):
                deps = [tcps[j], t_ident] + (rdeps if j == 0 else [])
                tt_ = tp_(rs(sl, 128, 128 * j), zsbh[:, 128 * j: 128 * j + 128],
                          deps, tag=f"zT{c}_{p}")
                tts.append(tt_)
            zs_read[hh] = tts[-1]
            ring_filled(slots, tts[-1])
            h_glob = 2 * p + hh
            tBs = []
            for j in range(4):
                m = sg_idx.get((hh, j), 0)
                if m > 0:
                    bc = dbase + 4 * (4 * hh + j)
                    tA = vop(lambda v, bc=bc, m=m, j=j: v.reduce_sum(
                        dnr[:, j:j + 1], dnm[:, bc:bc + m], axis=AX.X),
                        [dnm_tok[(hh, j, gg)] for gg in range(m)], tag=f"dn{c}_{p}")
                    tB = vop(lambda v, sl=sl, j=j: v.tensor_add(
                        dnr[:, 4 + j:5 + j], dnr[:, j:j + 1], rs(sl, 1, 128 * j + 64)),
                        [tA, tts[-1]], tag=f"dt{c}_{p}")
                else:
                    tB = vcopy(dnr[:, 4 + j:5 + j], rs(sl, 1, 128 * j + 64), [tts[-1]],
                               tag=f"dt0{c}_{p}")
                tBs.append(tB)
            trc = vop(lambda v: v.reciprocal(dnr[:, 8:12], dnr[:, 4:8]), tBs,
                      tag=f"dr{c}_{p}")
            vlast = None
            for j in range(4):
                zi = zst_idx[0] % 4
                zst_idx[0] += 1
                zb = zst[zi]
                tz = vop(lambda v, sl=sl, j=j, zb=zb: v.tensor_scalar(
                    out=zb[:, :], in0=rs(sl, 64, 128 * j), scalar1=dnr[:, 8 + j:9 + j],
                    scalar2=None, op0=ALU.mult),
                    [trc, zst_read[zi], tts[-1]], tag=f"za{c}_{p}")
                r0 = RH * h_glob + 32 * c + 8 * j
                td = dma("sync",
                         zs_d[r0:r0 + 8, :].rearrange("r (t h) -> (r t) h", h=64),
                         zb[:, :], [tz], tag=f"zd{c}_{p}")
                zst_read[zi] = td
                zdma_toks.append(td)
                zdma_by_chunk[(RH * h_glob) // 512].append(td)
                vlast = tz
            ring_used(slots, vlast)
            dnm_read[dpar] = vlast
            toks.append(vlast)
        return toks

    def ln2_x1(c, z_toks):
        stat_toks = []
        for tt in range(4):
            ai = 4 * (c % 2) + tt
            t1 = vop(lambda v, ai=ai: v.bn_stats(st6[:, 0:6], ACC[ai][:, 0:512]),
                     [acc_tok[ai]] + z_toks, tag=f"l2a{c}")
            t2 = vop(lambda v, ai=ai: v.bn_stats(st6[:, 6:12], ACC[ai][:, 512:1024]),
                     [acc_tok[ai]], tag=f"l2b{c}")
            t3 = vop(lambda v, tt=tt: v.bn_aggr(mv2[:, 2 * tt:2 * tt + 2], st6[:, :]),
                     [t1, t2, mv2_tok[tt]], tag=f"l2c{c}")
            mv2_tok[tt] = t3
            stat_toks.append(t3)
        return stat_toks

    def rstd_visit(c, stat_toks, mvx, rxc, rx_tok, tagp):
        deps = [tk for tk in stat_toks if tk] + [tk for tk in rx_tok if tk]
        t4 = vop(lambda v, mvx=mvx: v.tensor_scalar(
            out=rxc[:, 0:4],
            in0=mvx[:, 0:8].rearrange("p (t c) -> p t c", c=2)[:, :, 1],
            scalar1=EPS, scalar2=None, op0=ALU.add), deps, tag=f"{tagp}e{c}")
        t6 = newton_rsqrt(rxc[:, 0:4], rxc[:, 0:4], 4, [t4], f"{tagp}n{c}")
        for tt in range(4):
            rx_tok[tt] = t6
        return [t6] * 4

    def x1_compute(c, r2_toks):
        toks = []
        for tt in range(4):
            ai = 4 * (c % 2) + tt
            buf = x0c[(c % 2) * 4 + tt]
            t1 = vop(lambda v, ai=ai, tt=tt: v.tensor_scalar(
                out=ACC[ai][:, :], in0=ACC[ai][:, :], scalar1=mv2[:, 2 * tt:2 * tt + 1],
                scalar2=r2c[:, tt:tt + 1], op0=ALU.subtract, op1=ALU.mult),
                [acc_tok[ai], r2_toks[tt], acc_read[ai]], tag=f"x1n{c}")
            t2 = vop(lambda v, ai=ai: v.tensor_mul(ACC[ai][:, :], ACC[ai][:, :], gam_t[:, :]),
                     [t1] + gam_toks, tag=f"x1g{c}")
            t3 = vop(lambda v, ai=ai, b=buf: v.tensor_add(ACC[ai][:, :], ACC[ai][:, :], b[:, :]),
                     [t2], tag=f"x1x{c}")
            t4 = vop(lambda v, ai=ai: v.tensor_add(ACC[ai][:, :], ACC[ai][:, :], bet_t[:, :]),
                     [t3] + bet_toks, tag=f"x1f{c}")
            acc_tok[ai] = t4
            x0_free[(c % 2) * 4 + tt] = t3
            td = dma("sync", x1s_d[128 * (4 * c + tt):128 * (4 * c + tt) + 128, :],
                     ACC[ai][:, :], [t4], tag=f"x1d{c}")
            x1dma_tok[ai] = td
            toks.append(t4)
        return toks

    def ffn_T1(c, x1_toks, r3_toks):
        xn_toks = []
        for tt in range(4):
            ai = 4 * (c % 2) + tt
            buf = x0c[(c % 2) * 4 + tt]
            t1 = vop(lambda v, ai=ai, tt=tt, b=buf: v.tensor_scalar(
                out=b[:, :], in0=ACC[ai][:, :], scalar1=mv3[:, 2 * tt:2 * tt + 1],
                scalar2=r3c[:, tt:tt + 1], op0=ALU.subtract, op1=ALU.mult),
                [acc_tok[ai], r3_toks[tt], x0_free[(c % 2) * 4 + tt]], tag=f"xn{c}")
            x0_free[(c % 2) * 4 + tt] = t1
            xn_toks.append(t1)
        for kk in range(8):
            slots, rdeps = ring_alloc(1)
            sl = slots[0]
            last_t = None
            for tt in range(4):
                buf = x0c[(c % 2) * 4 + tt]
                deps = [xn_toks[tt], t_ident, x2T_read[kk], x2T_tok[kk]] + (rdeps if tt == 0 else [])
                last_t = tp_(rs(sl, 128, 128 * tt), buf[:, 128 * kk:128 * kk + 128],
                             deps, tag=f"x2T{c}")
                x0_free[(c % 2) * 4 + tt] = last_t
            ring_filled(slots, last_t)
            tc = vcopy(x2T_f[:, 512 * kk: 512 * kk + 512], rs(sl), [last_t], tag=f"x2Tc{c}")
            ring_used(slots, tc)
            x2T_tok[kk] = tc
        gelu_in = []
        for jo in range(2):
            slots, rdeps = ring_alloc(1)
            sl = slots[0]
            mmt = None
            for kk in range(8):
                deps = [x2T_tok[kk], cw1_toks[kk]] + (rdeps if kk == 0 else [])
                mmt = mm(rs(sl), cw1_f[:, 256 * kk + 128 * jo: 256 * kk + 128 * jo + 128],
                         x2T_f[:, 512 * kk: 512 * kk + 512], deps,
                         start=(kk == 0), stop=(kk == 7), tag=f"hmm{c}")
                x2T_read[kk] = mmt
            ring_filled(slots, mmt)
            gelu_in.append((sl, mmt, slots))
        A1, A2, A3, A4, A5 = (0.254829592, -0.284496736, 1.421413741,
                              -1.453152027, 1.061405429)
        PP = 0.3275911
        ISQ2 = 0.7071067811865476
        hT_toks = []
        for jo, (sl, mmt, slots) in enumerate(gelu_in):
            hp = hpre[:, :]
            tb = vop(lambda v, sl=sl, jo=jo: v.tensor_scalar(
                out=hp, in0=rs(sl), scalar1=cb1T[:, jo:jo + 1], scalar2=None,
                op0=ALU.add), [mmt, tok_cb1, gelu_busy[0]], tag=f"gb{c}")
            ring_used(slots, tb)
            Z = zsb[:, 0:512]
            tz = vop(lambda v: v.tensor_scalar(out=Z, in0=hp, scalar1=-1.0, scalar2=None,
                                               op0=ALU.mult),
                     [tb, zs_tok[0], zs_read[0]], tag=f"gz{c}")
            tz = vop(lambda v: v.tensor_tensor(out=Z, in0=Z, in1=hp, op=ALU.max),
                     [tz], tag=f"gz2{c}")
            tz = vop(lambda v: v.tensor_scalar(out=Z, in0=Z, scalar1=ISQ2, scalar2=None,
                                               op0=ALU.mult), [tz], tag=f"gz3{c}")
            DN = x1rb[0][:, 0:512]
            td = vop(lambda v: v.tensor_scalar(out=DN, in0=Z, scalar1=PP, scalar2=1.0,
                                               op0=ALU.mult, op1=ALU.add),
                     [tz, x1rb_read[0], x1rb_tok[0]], tag=f"gd{c}")
            td = vop(lambda v: v.reciprocal(DN, DN), [td], tag=f"gd2{c}")
            E = x1rb[0][:, 512:1024]
            te = vop(lambda v: v.tensor_mul(E, Z, Z), [tz, td], tag=f"gsq{c}")
            te = sop(lambda s: s.activation(out=E, in_=E, func=AF.Exp, scale=-1.0),
                     [te], tag=f"gex{c}")
            tq = vop(lambda v: v.tensor_scalar(out=Z, in0=DN, scalar1=0.5 * A5,
                                               scalar2=0.5 * A4, op0=ALU.mult, op1=ALU.add),
                     [te], tag=f"gp1{c}")
            for aa in (A3, A2, A1):
                tq = vop(lambda v: v.tensor_mul(Z, Z, DN), [tq], tag=f"gp2{c}")
                tq = vop(lambda v, aa=aa: v.tensor_single_scalar(out=Z, in_=Z,
                                                                 scalar=0.5 * aa, op=ALU.add),
                         [tq], tag=f"gp3{c}")
            tq = vop(lambda v: v.tensor_mul(Z, Z, DN), [tq], tag=f"gp4{c}")
            tq = vop(lambda v: v.tensor_mul(Z, Z, E), [tq], tag=f"gp5{c}")
            tm = vop(lambda v: v.tensor_scalar(out=DN, in0=hp, scalar1=0.0, scalar2=None,
                                               op0=ALU.is_ge), [tq], tag=f"gm{c}")
            tm2 = vop(lambda v: v.tensor_scalar(out=E, in0=DN, scalar1=-2.0, scalar2=1.0,
                                                op0=ALU.mult, op1=ALU.add), [tm], tag=f"gm2{c}")
            tphi = vop(lambda v: v.tensor_mul(Z, Z, E), [tm2], tag=f"gf1{c}")
            tphi = vop(lambda v: v.tensor_add(Z, Z, DN), [tphi], tag=f"gf2{c}")
            tg = vop(lambda v, jo=jo: v.tensor_mul(hTb[:, 512 * jo: 512 * jo + 512], hp, Z),
                     [tphi, hT_tok[jo], hT_read[jo]], tag=f"gelu{c}")
            gelu_busy[0] = tg
            zs_tok[0] = tg
            x1rb_tok[0] = tg
            x1rb_read[0] = tg
            hT_tok[jo] = tg
            hT_toks.append(tg)
        return xn_toks, hT_toks

    def ffn_T2(c, hT_toks):
        slots, rdeps = ring_alloc(1)
        slc = slots[0]
        mmt = None
        for kk in range(2):
            deps = [hT_toks[kk], tok_cw2] + (rdeps if kk == 0 else [])
            mmt = mm(ps_ring[0:2, 512 * slc: 512 * slc + 512],
                     cw2_b[:, 2 * kk: 2 * kk + 2], hTb[:, 512 * kk: 512 * kk + 512],
                     deps, start=(kk == 0), stop=(kk == 1), tag=f"cmm{c}")
            hT_read[kk] = mmt
        ring_filled(slots, mmt)
        tv1 = vop(lambda v: v.tensor_scalar(out=ctmp[:, :], in0=ps_ring[0:2, 512 * slc: 512 * slc + 512],
                                            scalar1=cb2_s[:, 0:1], scalar2=None, op0=ALU.add),
                  [mmt, tok_cb2, ctmp_read[0]], tag=f"cb{c}")
        ring_used(slots, tv1)
        te = sop(lambda s: s.activation(out=ctmp[:, :], in_=ctmp[:, :], func=AF.Exp, scale=2.0),
                 [tv1], tag=f"ctanh{c}")
        tv2 = vop(lambda v: v.tensor_single_scalar(out=ctmp[:, :], in_=ctmp[:, :], scalar=1.0,
                                                   op=ALU.add), [te], tag=f"ct1{c}")
        tv3 = vop(lambda v: v.reciprocal(ctmp[:, :], ctmp[:, :]), [tv2], tag=f"ct2{c}")
        tv4 = vop(lambda v: v.tensor_scalar(out=ctmp[:, :], in0=ctmp[:, :], scalar1=-2.0,
                                            scalar2=1.0, op0=ALU.mult, op1=ALU.add),
                  [tv3], tag=f"ct3{c}")
        ctv4_hold[0] = tv4
        for tt in range(4):
            slots, rdeps = ring_alloc(1)
            sl = slots[0]
            mmt = None
            for kk in range(8):
                deps = [x2T_tok[kk]] + (rdeps if kk == 0 else [])
                mmt = mm(rs(sl, 72), x2T_f[:, 512 * kk + 128 * tt: 512 * kk + 128 * tt + 128],
                         sgt[:, 72 * kk: 72 * kk + 72], deps,
                         start=(kk == 0), stop=(kk == 7), tag=f"gmm{c}")
                x2T_read[kk] = mmt
            ring_filled(slots, mmt)
            tev = vcopy(gates[:, 72 * tt: 72 * tt + 72], rs(sl, 72), [mmt, gates_tok[tt]],
                        tag=f"gev{c}")
            ring_used(slots, tev)
            gates_tok[tt] = tev

    def compt_stage(c):
        tv4 = ctv4_hold[0]
        slots, rdeps = ring_alloc(1)
        slt = slots[0]
        ctoks = []
        for tt in range(4):
            deps = [tv4, t_ident] + (rdeps if tt == 0 else [])
            ttk = tp_(rs(slt, 2, 128 * tt), ctmp[:, 128 * tt: 128 * tt + 128], deps,
                      tag=f"cT{c}", k=2)
            ctoks.append(ttk)
            ctmp_read[0] = ttk
        ring_filled(slots, ctoks[-1])
        cc = vop(lambda v, slt=slt: v.tensor_copy(
            compt[:, :].rearrange("p (t c) -> p t c", t=4),
            rs(slt, 512).rearrange("p (t c) -> p t c", c=128)[:, :, 0:2]),
            [ctoks[-1]] + [compt_tok[t] for t in range(4)], tag=f"ccp{c}")
        ring_used(slots, cc)
        for tt in range(4):
            compt_tok[tt] = cc

    def gates_exp(c):
        toks = []
        for tt in range(4):
            te = sop(lambda s, tt=tt: s.activation(
                out=gates[:, 72 * tt: 72 * tt + 72], in_=gates[:, 72 * tt: 72 * tt + 72],
                func=AF.Exp, scale=scap(SC_NINV)), [gates_tok[tt], t_scal], tag=f"gex{c}")
            gates_tok[tt] = te
            toks.append(te)
        return toks

    def calibrate_argmax(c, ge_toks):
        tidx_toks = []
        for tt in range(4):
            g_ap = gates[:, 72 * tt: 72 * tt + 72]
            t1 = vop(lambda v, g=g_ap: v.tensor_single_scalar(out=calv[:, :], in_=g,
                                                              scalar=1.0, op=ALU.add),
                     [ge_toks[tt]], tag=f"ca1{c}")
            t2 = vop(lambda v: v.reciprocal(calv[:, :], calv[:, :]), [t1], tag=f"ca2{c}")
            t3 = vop(lambda v: v.tensor_scalar(out=calv[:, :], in0=calv[:, :], scalar1=7.0,
                                               scalar2=None, op0=ALU.mult), [t2], tag=f"ca3{c}")
            tacc = vop(lambda v: v.memset(cal2[:, :], 0.0), [t3], tag=f"ca4{c}")
            for j in range(7):
                ta = vop(lambda v, j=j: v.tensor_scalar(out=calt[:, :], in0=calv[:, :],
                                                        scalar1=float(j), scalar2=0.0,
                                                        op0=ALU.subtract, op1=ALU.max),
                         [tacc if j == 0 else tb2], tag=f"cj1{c}")
                tb = vop(lambda v, j=j: v.tensor_scalar(out=calt[:, :], in0=calt[:, :],
                                                        scalar1=1.0, scalar2=scap(SC_D0 + j),
                                                        op0=ALU.min, op1=ALU.mult),
                         [ta, t_scal], tag=f"cj2{c}")
                tb2 = vop(lambda v: v.tensor_add(cal2[:, :], cal2[:, :], calt[:, :]),
                          [tb], tag=f"cj3{c}")
            tk0 = vop(lambda v: v.tensor_scalar(out=cal2[:, :], in0=cal2[:, :],
                                                scalar1=scap(SC_K0), scalar2=None, op0=ALU.add),
                      [tb2, t_scal], tag=f"ck0{c}")
            m1 = vop(lambda v: v.reduce_max(idxs[:, 0:1], cal2[:, 64:72], axis=AX.X),
                     [tk0], tag=f"am1{c}")
            m2 = vop(lambda v: v.tensor_scalar(out=calt[:, 0:8], in0=cal2[:, 64:72],
                                               scalar1=idxs[:, 0:1], scalar2=None,
                                               op0=ALU.is_ge), [m1], tag=f"am2{c}")
            m3 = vop(lambda v: v.tensor_mul(calt[:, 8:16], calt[:, 0:8], bigmi[:, :]),
                     [m2, t_bigmi_i], tag=f"am3{c}")
            m4 = vop(lambda v: v.tensor_scalar(out=calt[:, 8:16], in0=calt[:, 8:16],
                                               scalar1=-1.0, scalar2=999.0,
                                               op0=ALU.mult, op1=ALU.add), [m3], tag=f"am4{c}")
            m5 = vop(lambda v: v.tensor_reduce(idxs[:, 1:2], calt[:, 8:16], axis=AX.X,
                                               op=ALU.min), [m4], tag=f"am5{c}")
            w1 = vop(lambda v: v.reduce_max(calt[:, 0:8],
                                            cal2[:, 0:64].rearrange("p (c t) -> p c t", t=8),
                                            axis=AX.X), [tk0], tag=f"aw1{c}")
            w2 = vop(lambda v: v.tensor_tensor(
                out=calv[:, 0:64].rearrange("p (c t) -> p c t", t=8),
                in0=cal2[:, 0:64].rearrange("p (c t) -> p c t", t=8),
                in1=calt[:, 0:8].rearrange("p c -> p c 1" if False else "p (c o) -> p c o", o=1).to_broadcast([128, 8, 8]),
                op=ALU.is_ge), [w1], tag=f"aw2{c}")
            w3 = vop(lambda v: v.tensor_tensor(
                out=calv[:, 0:64].rearrange("p (c t) -> p c t", t=8),
                in0=calv[:, 0:64].rearrange("p (c t) -> p c t", t=8),
                in1=bigmi[:, :].rearrange("p (c o) -> p c o" if False else "p t -> p 1 t" if False else "p (o t) -> p o t", o=1).to_broadcast([128, 8, 8]),
                op=ALU.mult), [w2, t_bigmi_i], tag=f"aw3{c}")
            w4 = vop(lambda v: v.tensor_scalar(out=calv[:, 0:64], in0=calv[:, 0:64],
                                               scalar1=-1.0, scalar2=999.0,
                                               op0=ALU.mult, op1=ALU.add), [w3], tag=f"aw4{c}")
            w5 = vop(lambda v: v.tensor_reduce(calt[:, 16:24],
                                               calv[:, 0:64].rearrange("p (c t) -> p c t", t=8),
                                               axis=AX.X, op=ALU.min), [w4], tag=f"aw5{c}")
            s1 = vop(lambda v: v.tensor_scalar(out=calt[:, 24:32], in0=iotaj[:, 0:8],
                                               scalar1=idxs[:, 1:2], scalar2=None,
                                               op0=ALU.is_equal), [m5, t_ioj], tag=f"as1{c}")
            s2 = vop(lambda v: v.tensor_mul(calt[:, 24:32], calt[:, 24:32], calt[:, 16:24]),
                     [s1, w5], tag=f"as2{c}")
            s3 = vop(lambda v: v.reduce_sum(idxs[:, 2:3], calt[:, 24:32], axis=AX.X),
                     [s2], tag=f"as3{c}")
            s4 = vop(lambda v, tt=tt: v.tensor_scalar(out=idxs[:, 3 + tt:4 + tt], in0=idxs[:, 1:2],
                                                      scalar1=8.0, scalar2=idxs[:, 2:3],
                                                      op0=ALU.mult, op1=ALU.add),
                     [s3, idx_read[tt]], tag=f"as4{c}")
            tidx_toks.append(s4)
        return tidx_toks

    def gathers(c, tidx_toks):
        t2s = []
        for tt in range(4):
            slots, rdeps = ring_alloc(1)
            sl = slots[0]
            t1 = tp_(ps_ring[0:1, 512 * sl: 512 * sl + 128], idxs[:, 3 + tt:4 + tt],
                     [tidx_toks[tt], t_ident] + rdeps, tag=f"tiT{c}")
            idx_read[tt] = t1
            ring_filled([sl], t1)
            t2 = vcopy(tidxr[:, 128 * tt: 128 * tt + 128],
                       ps_ring[0:1, 512 * sl: 512 * sl + 128],
                       [t1, tidxr_read[tt]], tag=f"tir{c}")
            ring_used([sl], t2)
            t2s.append(t2)
        t4s = []
        for tt in range(4):
            ohb = ohbufs[tt]
            slots2, rdeps2 = ring_alloc(1)
            sl2 = slots2[0]
            t3 = mm(ps_ring[0:64, 512 * sl2: 512 * sl2 + 128], onesrow[:, 0:64],
                    tidxr[:, 128 * tt: 128 * tt + 128],
                    [t2s[tt], t_ones_r] + rdeps2, tag=f"tibc{c}")
            tidxr_read[tt] = t3
            ring_filled(slots2, t3)
            t4 = vop(lambda v, sl2=sl2, ohb=ohb: v.tensor_scalar(
                out=ohb[:, :], in0=ps_ring[0:64, 512 * sl2: 512 * sl2 + 128],
                scalar1=iota64[:, :], scalar2=None, op0=ALU.is_equal),
                [t3, t_io64, oh_tok[tt], oh_read[tt]], tag=f"ioh{c}")
            ring_used(slots2, t4)
            oh_tok[tt] = t4
            t4s.append(t4)

        def emit_tile(tt):
            ohb = ohbufs[tt]
            slots3, rdeps3 = ring_alloc(3)
            g1 = mm(rs(slots3[0], 65), ohb[:, :], spl_b[:, :], [t4s[tt], tok_spl] + rdeps3,
                    tag=f"gsp{c}")
            g2 = mm(rs(slots3[1]), ohb[:, :], dirs_b[:, 0:512], [t4s[tt], tok_dirs],
                    tag=f"gd1{c}")
            g3 = mm(rs(slots3[2]), ohb[:, :], dirs_b[:, 512:1024], [t4s[tt], tok_dirs],
                    tag=f"gd2{c}")
            ring_filled(slots3, g3)
            oh_read[tt] = g3
            return (slots3, g1, g2, g3)
        return emit_tile

    def spline_y(c, gth_fn):
        ysts = []
        for tt in range(4):
            slots3, g1, g2, g3 = gth_fn(tt)
            ai = 4 * (c % 2) + tt
            e1 = vcopy(splg[:, :], rs(slots3[0], 65), [g1], tag=f"spc{c}")
            h1 = vop(lambda v, tt=tt: v.tensor_scalar(out=hhs[:, 0:16], in0=splg[:, 0:16],
                                               scalar1=compt[:, 2 * tt: 2 * tt + 1],
                                               scalar2=None, op0=ALU.mult),
                     [e1, compt_tok[tt]], tag=f"sp1{c}")
            h2 = vop(lambda v, tt=tt: v.scalar_tensor_tensor(out=hhs[:, 16:32], in0=splg[:, 16:32],
                                                      scalar=compt[:, 2 * tt + 1: 2 * tt + 2],
                                                      in1=hhs[:, 0:16], op0=ALU.mult,
                                                      op1=ALU.add), [h1], tag=f"sp2{c}")
            h3 = vop(lambda v: v.tensor_add(hhs[:, 32:48], hhs[:, 16:32], splg[:, 32:48]),
                     [h2], tag=f"sp3{c}")
            h4 = vop(lambda v: v.tensor_scalar(out=hhs[:, 32:48], in0=hhs[:, 32:48],
                                               scalar1=0.0, scalar2=None, op0=ALU.max),
                     [h3], tag=f"sp4{c}")
            h5 = vop(lambda v: v.tensor_mul(hhs[:, 0:16], hhs[:, 32:48], splg[:, 48:64]),
                     [h4], tag=f"sp5{c}")
            h6 = vop(lambda v, tt=tt: v.reduce_sum(mags[:, tt:tt + 1], hhs[:, 0:16], axis=AX.X),
                     [h5, mags_tok[tt]], tag=f"sp6{c}")
            h7 = vop(lambda v, tt=tt: v.tensor_add(mags[:, tt:tt + 1], mags[:, tt:tt + 1],
                                            splg[:, 64:65]), [h6], tag=f"sp7{c}")
            h8 = vop(lambda v, tt=tt: v.tensor_scalar(out=mags[:, tt:tt + 1], in0=mags[:, tt:tt + 1],
                                               scalar1=scap(SC_OS), scalar2=None,
                                               op0=ALU.mult), [h7, t_scal], tag=f"sp8{c}")
            mags_tok[tt] = h8
            y1 = vop(lambda v, ai=ai, tt=tt: v.tensor_scalar(
                out=ybuf[tt][:, :], in0=ACC[ai][:, :], scalar1=mv3[:, 2 * tt:2 * tt + 1],
                scalar2=r3c[:, tt:tt + 1], op0=ALU.subtract, op1=ALU.mult),
                [acc_tok[ai], p2[c]["r3"][tt], ybuf_read[tt], ybuf_tok[tt]], tag=f"y1{c}")
            y2 = vop(lambda v, tt=tt, s=slots3[1]: v.scalar_tensor_tensor(
                out=ybuf[tt][:, 0:512], in0=rs(s), scalar=mags[:, tt:tt + 1],
                in1=ybuf[tt][:, 0:512], op0=ALU.mult, op1=ALU.add), [y1, h8, g2], tag=f"y2{c}")
            y3 = vop(lambda v, tt=tt, s=slots3[2]: v.scalar_tensor_tensor(
                out=ybuf[tt][:, 512:1024], in0=rs(s), scalar=mags[:, tt:tt + 1],
                in1=ybuf[tt][:, 512:1024], op0=ALU.mult, op1=ALU.add), [y2, g3], tag=f"y3{c}")
            ring_used(slots3, y3)
            ybuf_tok[tt] = y3
            acc_read[ai] = y1
            s1 = vop(lambda v, tt=tt: v.bn_stats(st6[:, 0:6], ybuf[tt][:, 0:512]), [y3],
                     tag=f"ys1{c}")
            s2 = vop(lambda v, tt=tt: v.bn_stats(st6[:, 6:12], ybuf[tt][:, 512:1024]), [y3],
                     tag=f"ys2{c}")
            s3 = vop(lambda v, tt=tt: v.bn_aggr(mvy[:, 2 * tt:2 * tt + 2], st6[:, :]),
                     [s1, s2, mvy_tok[tt]], tag=f"ys3{c}")
            mvy_tok[tt] = s3
            ysts.append(s3)
        return ysts

    def out_assembly(c, ry_toks):
        for tt in range(4):
            row = 128 * (4 * c + tt)
            ai = 4 * (c % 2) + tt
            o1 = vop(lambda v, tt=tt: v.tensor_scalar(
                out=ybuf[tt][:, :], in0=ybuf[tt][:, :], scalar1=mvy[:, 2 * tt:2 * tt + 1],
                scalar2=ryc[:, tt:tt + 1], op0=ALU.subtract, op1=ALU.mult),
                [ybuf_tok[tt], ry_toks[tt]], tag=f"o1{c}")
            o2 = vop(lambda v, tt=tt: v.tensor_mul(ybuf[tt][:, :], ybuf[tt][:, :], ngam_t[:, :]),
                     [o1] + ngam_toks, tag=f"o2{c}")
            o3 = vop(lambda v, tt=tt, ai=ai: v.tensor_add(ybuf[tt][:, :], ybuf[tt][:, :],
                                                          ACC[ai][:, :]),
                     [o2, acc_tok[ai]], tag=f"o3{c}")
            acc_read[ai] = o3
            o4 = vop(lambda v, tt=tt: v.tensor_add(ybuf[tt][:, :], ybuf[tt][:, :], nbet_t[:, :]),
                     [o3] + nbet_toks, tag=f"o4{c}")
            ybuf_tok[tt] = o4
            od = dma("sync", out_d[row:row + 128, :], ybuf[tt][:, :], [o4], tag=f"od{c}")
            ybuf_read[tt] = od
            outdma_toks.append(od)

    dbg_state = {}

    def dbg_dump(i, ap, deps, w=D):
        if dbg_d is None:
            return None
        return dma("gpsimd", dbg_d[i, :, 0:w], ap, deps, tag=f"dbg{i}")

    zrb_read = [None, None]
    p2 = {}

    def p2_m0(ch):
        st = p2.setdefault(ch, {})
        a_toks = a_init(ch)
        ztoks = []
        for tt in range(4):
            ai = 4 * (ch % 2) + tt
            zi = tt % 2
            tl = dma("sync", zrb[zi][:, :],
                     zs_d[512 * ch + 128 * tt: 512 * ch + 128 * tt + 128, :],
                     [zrb_read[zi]] + zdma_by_chunk[ch], tag=f"zl{ch}")
            tadd = vop(lambda v, ai=ai, zi=zi: v.tensor_add(
                ACC[ai][:, :], ACC[ai][:, :], zrb[zi][:, :]),
                [tl, a_toks[tt]], tag=f"zadd{ch}")
            zrb_read[zi] = tadd
            acc_tok[ai] = tadd
            ztoks.append(tadd)
        st["ztoks"] = ztoks

    def p2_m1(ch):
        st = p2[ch]
        stat2 = ln2_x1(ch, st["ztoks"])
        st["r2"] = rstd_visit(ch, stat2, mv2, r2c, r2_tok, "r2")

    def p2_m2(ch):
        st = p2[ch]
        x1_toks = x1_compute(ch, st["r2"])
        stat3 = []
        for tt in range(4):
            ai = 4 * (ch % 2) + tt
            t1 = vop(lambda v, ai=ai: v.bn_stats(st6[:, 0:6], ACC[ai][:, 0:512]),
                     [x1_toks[tt]], tag=f"l3a{ch}")
            t2 = vop(lambda v, ai=ai: v.bn_stats(st6[:, 6:12], ACC[ai][:, 512:1024]),
                     [x1_toks[tt]], tag=f"l3b{ch}")
            t3 = vop(lambda v, tt=tt: v.bn_aggr(mv3[:, 2 * tt:2 * tt + 2], st6[:, :]),
                     [t1, t2, mv3_tok[tt]], tag=f"l3c{ch}")
            mv3_tok[tt] = t3
            stat3.append(t3)
        st["r3"] = rstd_visit(ch, stat3, mv3, r3c, r3_tok, "r3")
        st["x1_toks"] = x1_toks

    def p2_m3(ch):
        st = p2[ch]
        _, st["hT_toks"] = ffn_T1(ch, st["x1_toks"], st["r3"])

    def p2_m3b(ch):
        st = p2[ch]
        ffn_T2(ch, st["hT_toks"])

    def p2_m4(ch):
        st = p2[ch]
        if "ffntail" not in skip:
            compt_stage(ch)
            st["tidx"] = calibrate_argmax(ch, gates_exp(ch))
        else:
            st["tidx"] = [None] * 4

    def p2_m5(ch):
        st = p2[ch]
        if "ffntail" not in skip:
            gth = gathers(ch, st["tidx"])
            st["ysts"] = spline_y(ch, gth)
        else:
            st["ysts"] = [None] * 4

    def p2_m6(ch):
        st = p2[ch]
        st["ry"] = rstd_visit(ch, st["ysts"], mvy, ryc, ry_tok, "ry")

    def p2_m7(ch):
        st = p2[ch]
        out_assembly(ch, st["ry"])
        del p2[ch]

    phase_queue = []

    def enqueue_chunk(ch):
        q = [lambda ch=ch: p2_m0(ch), lambda ch=ch: p2_m1(ch)]
        if ch >= 1:
            q.append(lambda ch=ch: p2_m5(ch - 1))
        q.append(lambda ch=ch: p2_m2(ch))
        if ch >= 1:
            q.append(lambda ch=ch: p2_m6(ch - 1))
        q.append(lambda ch=ch: p2_m3(ch))
        if ch >= 1:
            q.append(lambda ch=ch: p2_m7(ch - 1))
        q.append(lambda ch=ch: p2_m3b(ch))
        q.append(lambda ch=ch: p2_m4(ch))
        phase_queue.extend(q)

    ppc = NPAIR // CH if CH <= NPAIR else 1

    sched_by_c = [tile_schedule(c) for c in range(CH)]
    for p in range(NPAIR):
        if p == 0:
            kvq_load(0)
        for ci, c in enumerate(range(CH)):
            if "pairs" not in skip:
                pv_last, sg_idx = emit_pair(c, p, sched_by_c[c])
                if ci == 0 and p + 1 < NPAIR:
                    kvq_load(p + 1)
                zt = z_evac(c, p, pv_last, sg_idx)
                if c == 0 and p == 0:
                    dbg_dump(1, zsb[:, 0:512], [zs_tok[0]], w=512)
            if phase_queue:
                phase_queue.pop(0)()
                if ci == CH - 1 and len(phase_queue) > 1:
                    phase_queue.pop(0)()
        if (p + 1) % ppc == 0:
            enqueue_chunk((p + 1) // ppc - 1)

    while phase_queue:
        phase_queue.pop(0)()
    p2_m5(CH - 1)
    p2_m6(CH - 1)
    p2_m7(CH - 1)

    fin_deps = [outdma_toks[-1]]
    for sname in DMAPOOL["sync"] + DMAPOOL["gpsimd"]:
        cc = P.count.get(sname, 0)
        if cc:
            fin_deps.append((sname, cc))
    P.op("gpsimd", lambda g: g.nop(), fin_deps, tag="fin")

    import os
    nopify = set(os.environ.get("NOPIFY", "").split(",")) - {""}
    noptags = set(os.environ.get("NOPTAGS", "").split(",")) - {""}
    refd = {}
    for e2 in ENGINES:
        for (fn, waits, sem, amt, tag) in P.stream[e2]:
            for (s, v) in waits:
                refd.setdefault(s, set()).add(v)
    remap = {}
    for e2 in ENGINES:
        if os.environ.get("NO_SEMCOMP", "") == "1":
            break
        s = ESEM[e2]
        total = P.count.get(s, 0)
        if not total:
            continue
        rset = refd.get(s, set())
        m = {}
        nci = 0
        for k in range(1, total + 1):
            if k in rset or k == total:
                nci += 1
                m[k] = (nci, True)
            else:
                m[k] = (nci, False)
        remap[s] = m
    with nc.Block() as block:
        def emit(eng_name):
            def body(eng):
                own = ESEM[eng_name]
                for (fn, waits, sem, amt, tag) in P.stream[eng_name]:
                    for (s, v) in waits:
                        if s in remap:
                            v = remap[s][v][0]
                        eng.wait_ge(semh[s], v)
                    tagnop = any(tag.startswith(tp) for tp in noptags) if noptags else False
                    if (eng_name in nopify or tagnop) and amt == 1:
                        inst = eng.nop()
                    else:
                        inst = fn(eng)
                    if sem in remap:
                        cnt, do_inc = remap[sem][P2_cursor[sem]]
                        P2_cursor[sem] += 1
                        if do_inc:
                            inst.then_inc(semh[sem], amt)
                    else:
                        inst.then_inc(semh[sem], amt)
            return body
        P2_cursor = {s: 1 for s in remap}
        block.sync(emit("sync"))
        block.gpsimd(emit("gpsimd"))
        block.tensor(emit("tensor"))
        block.scalar(emit("scalar"))
        block.vector(emit("vector"))

    es.close()
    return nc


_prog_cache = {}


def _get_prog(S):
    if S not in _prog_cache:
        _prog_cache[S] = build_program(S)
    return _prog_cache[S]


def make_in_map(core_inputs):
    return core_inputs


def kernel(**inputs):
    emb = np.ascontiguousarray(np.asarray(inputs["L0_attn_embedded"], np.float32))
    B, S, _ = emb.shape
    f = lambda k: np.ascontiguousarray(np.asarray(inputs[k], np.float32))
    wqT = np.ascontiguousarray(f("L0_attn_Wq").T)
    wkT = np.ascontiguousarray(f("L0_attn_Wk").T)
    wvT = np.ascontiguousarray(f("L0_attn_Wv").T)
    bqT = np.ascontiguousarray(f("L0_attn_bq").reshape(8, 128).T)
    bkT = np.ascontiguousarray(f("L0_attn_bk").reshape(8, 128).T)
    bvT = np.ascontiguousarray(f("L0_attn_bv").reshape(8, 128).T)
    gamma = f("L0_attn_gamma").reshape(1, D)
    beta = f("L0_attn_beta").reshape(1, D)
    ngamma = f("L0_ffn_norm_gamma").reshape(1, D)
    nbeta = f("L0_ffn_norm_beta").reshape(1, D)
    sigT = np.ascontiguousarray(f("L0_ffn_signatures_raw").T)
    knots = f("L0_ffn_knot_values").reshape(1, 8)
    temp = f("L0_ffn_temperature").reshape(1, 1)
    oscale = f("L0_ffn_output_scale").reshape(1, 1)
    cw1 = f("L0_ffn_compress_w1")
    cb1T = np.ascontiguousarray(f("L0_ffn_compress_b1").reshape(2, 128).T)
    cw2 = np.ascontiguousarray(
        f("L0_ffn_compress_w2").reshape(2, 128, 2).transpose(1, 0, 2).reshape(128, 4))
    cb2 = f("L0_ffn_compress_b2").reshape(2, 1)
    spl = np.ascontiguousarray(np.concatenate([
        f("L0_ffn_spline_w1").reshape(64, 32),
        f("L0_ffn_spline_b1").reshape(64, 16),
        f("L0_ffn_spline_w2").reshape(64, 16),
        f("L0_ffn_spline_b2").reshape(64, 1)], axis=1))
    dirs = f("L0_ffn_directions")

    shared = dict(wqT=wqT, wkT=wkT, wvT=wvT, bqT=bqT, bkT=bkT, bvT=bvT,
                  gamma=gamma, beta=beta, ngamma=ngamma, nbeta=nbeta, sigT=sigT,
                  knots=knots, temp=temp, oscale=oscale, cw1=cw1, cb1T=cb1T,
                  cw2r=cw2, cb2=cb2, spl=spl, dirs=dirs)
    in_maps = []
    for b in range(B):
        m = dict(shared)
        m["x0"] = np.ascontiguousarray(emb[b])
        in_maps.append(m)
    nc = _get_prog(S)
    res = run_bass_kernel_spmd(nc, in_maps, core_ids=list(range(B)))
    out = np.stack([np.asarray(r["out"], np.float32) for r in res.results], axis=0)
    return out


# version 15
# speedup vs baseline: 1.3363x; 1.0021x over previous
import numpy as np
import ml_dtypes
import concourse.bass as bass
import concourse.mybir as mybir
from concourse.bass_utils import run_bass_kernel_spmd
from contextlib import ExitStack

F32 = mybir.dt.float32
BF16 = mybir.dt.bfloat16
AF = mybir.ActivationFunctionType
ALU = mybir.AluOpType
AX = mybir.AxisListType

D = 1024
NPAIR = 8
EPS = 1e-5

ENGINES = ("sync", "gpsimd", "tensor", "scalar", "vector")
ESEM = {"sync": "eS", "gpsimd": "eG", "tensor": "eT", "scalar": "eA", "vector": "eV"}
import os as _os_se
SIM_EDGES = _os_se.environ.get("SIM_EDGES", "") == "1"
NDMASEM = 20
DMAPOOL = {"sync": [f"dS{i}" for i in range(NDMASEM)],
           "gpsimd": [f"dG{i}" for i in range(NDMASEM)]}


class Planner:
    def __init__(self):
        self.stream = {e: [] for e in ENGINES}
        self.count = {}
        self.snap = {}
        self.W = {e: {} for e in ENGINES}
        self.nwaits = 0
        self.dma_rr = {}

    def _merge_into(self, w, tok):
        s, v = tok
        snap = self.snap.get((s, v))
        if snap is None:
            if w.get(s, -1) < v:
                w[s] = v
            return
        for k2, v2 in snap.items():
            if w.get(k2, -1) < v2:
                w[k2] = v2

    def op(self, eng, fn, deps=(), dma=False, tag=""):
        w = self.W[eng]
        deps = list(deps)
        if dma:
            pool = DMAPOOL[eng]
            i = self.dma_rr.get(eng, 0)
            self.dma_rr[eng] = i + 1
            sem = pool[i % NDMASEM]
            prevc = self.count.get(sem, 0)
            if prevc > 0:
                deps.append((sem, prevc))
            amt = 16
        else:
            sem = ESEM[eng]
            amt = 1
            prev = self.count.get(sem, 0)
            if prev > 0:
                deps.insert(0, (sem, prev))
        waits = []
        for tok in deps:
            if tok is None:
                continue
            s, v = tok
            if w.get(s, -1) >= v:
                continue
            waits.append((s, v))
            self._merge_into(w, tok)
        c = self.count.get(sem, 0) + amt
        self.count[sem] = c
        snapd = dict(w)
        snapd[sem] = max(snapd.get(sem, 0), c)
        self.snap[(sem, c)] = snapd
        self.stream[eng].append((fn, waits, sem, amt, tag))
        self.nwaits += len(waits)
        return (sem, c)


def build_program(S, skip=()):
    NT = S // 128
    CH = S // 512
    import os as _os0
    _scr = 16384 if _os0.environ.get("DEVDBG", "") == "1" else 2048
    nc = bass.Bass("TRN2", target_bir_lowering=False, dynamic_dma_scratch_size=_scr)
    P = Planner()

    def din(name, shape):
        return nc.dram_tensor(name, list(shape), F32, kind="ExternalInput").ap()

    x0_d = din("x0", (S, D))
    wqT_d = din("wqT", (D, D)); wkT_d = din("wkT", (D, D)); wvT_d = din("wvT", (D, D))
    bqT_d = din("bqT", (128, 8)); bkT_d = din("bkT", (128, 8)); bvT_d = din("bvT", (128, 8))
    gam_d = din("gamma", (1, D)); bet_d = din("beta", (1, D))
    ngam_d = din("ngamma", (1, D)); nbet_d = din("nbeta", (1, D))
    sigT_d = din("sigT", (D, 64))
    knots_d = din("knots", (1, 8)); temp_d = din("temp", (1, 1)); oscale_d = din("oscale", (1, 1))
    cw1_d = din("cw1", (D, 256)); cb1T_d = din("cb1T", (128, 2))
    cw2_d = din("cw2r", (128, 4)); cb2_d = din("cb2", (2, 1))
    spl_d = din("spl", (64, 65)); dirs_d = din("dirs", (64, D))
    out_d = nc.dram_tensor("out", [S, D], F32, kind="ExternalOutput").ap()
    import os as _os
    _dbg = _os.environ.get("DEVDBG", "") in ("1", "2")
    _dbg1 = _os.environ.get("DEVDBG", "") == "1"
    _ik = "ExternalOutput" if _dbg else "Internal"
    kts_d = nc.dram_tensor("kts", [NPAIR, 128, S], BF16, kind=_ik).ap()
    qts_d = nc.dram_tensor("qts", [NPAIR, 128, S], BF16, kind="Internal").ap()
    aTs_d = nc.dram_tensor("aTs", [8, 128, S], BF16, kind="Internal").ap()
    vs_d = nc.dram_tensor("vs", [NPAIR, NT, 128, 130], BF16, kind=_ik).ap()
    x1s_d = nc.dram_tensor("x1s", [S, D], F32, kind=_ik).ap()
    zs_d = nc.dram_tensor("zs", [S, D], F32, kind=_ik).ap()
    dbg_d = nc.dram_tensor("dbg", [8, 128, D], F32, kind="ExternalOutput").ap() if _dbg1 else None

    es = ExitStack()

    def sb(name, shape, dtype):
        return es.enter_context(nc.sbuf_tensor(name, list(shape), dtype))

    ident = sb("ident", (128, 128), F32)
    diagmask = sb("diagmask", (128, 128), BF16)
    dgf = sb("dgf", (128, 128), F32)
    onescol_b = sb("onescol_b", (128, 1), BF16)
    onesrow = sb("onesrow", (1, 128), F32)
    iota64 = sb("iota64", (64, 1), F32)
    iotaj = sb("iotaj", (128, 72), F32)
    bigmi = sb("bigmi", (128, 8), F32)
    scal = sb("scal", (128, 16), F32)
    scrow = sb("scrow", (1, 16), F32)
    knots_s = sb("knots_s", (1, 8), F32)
    temp_s = sb("temp_s", (1, 4), F32)
    gam_t = sb("gam_t", (128, D), F32); bet_t = sb("bet_t", (128, D), F32)
    ngam_t = sb("ngam_t", (128, D), F32); nbet_t = sb("nbet_t", (128, D), F32)
    bqT = sb("bqT_s", (128, 8), F32); bkT = sb("bkT_s", (128, 8), F32); bvT = sb("bvT_s", (128, 8), F32)
    cb1T = sb("cb1T_s", (128, 2), F32)
    cb2_s = sb("cb2_s", (2, 1), F32)
    mu1 = sb("mu1", (128, NT), F32); r1 = sb("r1", (128, NT), F32)
    mvt = sb("mvt", (128, 2), F32)
    st6 = sb("st6", (128, 12), F32)
    sgt = sb("sgt", (128, 8 * 72), F32)
    x0c = [sb(f"x0c{i}", (128, D), F32) for i in range(8)]
    ACC = [sb(f"ACC{i}", (128, D), F32) for i in range(8)]
    aT = sb("aT", (128, 8 * 512), BF16)
    qT = sb("qT", (128, 8 * 512), BF16)
    kbuf = [sb(f"kbuf{i}", (128, S), BF16) for i in range(2)]
    vbuf = [sb(f"vbuf{i}", (128, NT * 130), BF16) for i in range(2)]
    pTb = [sb(f"pTb{i}", (128, 1536), BF16) for i in range(4)]
    zsb = sb("zsb", (128, 512), F32)
    zsb2 = sb("zsb2", (128, 512), F32)
    dnm = sb("dnm", (128, 64), F32)
    dnr = sb("dnr", (128, 12), F32)
    zst = [sb(f"zst{i}", (128, 64), F32) for i in range(4)]
    wstage = sb("wstage", (128, D), F32)
    zrb = [sb(f"zrb{i}", (128, D), F32) for i in range(2)]
    ybuf = [sb(f"ybuf{i}", (128, D), F32) for i in range(4)]
    x1rb = [sb("x1rb0", (128, D), F32)]
    gates = sb("gates", (128, 4 * 72), F32)
    calv = sb("calv", (128, 72), F32)
    calt = sb("calt", (128, 72), F32)
    cal2 = sb("cal2", (128, 72), F32)
    idxs = sb("idxs", (128, 16), F32)
    tidxr = sb("tidxr", (1, 512), F32)
    ohbufs = [sb(f"ohT{i}", (64, 128), BF16) for i in range(4)]
    compt = sb("compt", (128, 8), F32)
    ctmp = sb("ctmp", (2, 512), F32)
    splg = sb("splg", (128, 65), F32)
    hhs = sb("hhs", (128, 48), F32)
    mags = sb("mags", (128, 4), F32)
    mv2 = sb("mv2", (128, 8), F32)
    mv3 = sb("mv3", (128, 8), F32)
    mvy = sb("mvy", (128, 8), F32)
    r2c = sb("r2c", (128, 4), F32); r3c = sb("r3c", (128, 4), F32); ryc = sb("ryc", (128, 4), F32)
    cw2_b = sb("cw2_b", (128, 4), BF16)
    dirs_b = sb("dirs_b", (64, D), BF16)
    spl_b = sb("spl_b", (64, 65), BF16)

    nsc = sb("nsc", (128, 3 * NT), F32)
    hpre = sb("hpre", (128, 512), F32)
    ps_ring = es.enter_context(nc.psum_tensor("ps_ring", [128, 3072], F32))
    ps_z = es.enter_context(nc.psum_tensor("ps_z", [128, 1024], F32))

    semh = {}
    for sname in (["eS", "eG", "eT", "eA", "eV"] + DMAPOOL["sync"] + DMAPOOL["gpsimd"]):
        semh[sname] = es.enter_context(nc.semaphore(sname))

    def mm(out_ap, lhsT_ap, rhs_ap, deps, start=True, stop=True, tag="mm", tpos=None):
        def f(t, o=out_ap, l=lhsT_ap, r=rhs_ap, s1=start, s2=stop, tp=tpos):
            return t.matmul(o, l, r, start=s1, stop=s2, tile_position=tp)
        return P.op("tensor", f, deps, tag=tag)

    def tp_(out_ap, in_ap, deps, tag="tp", k=128):
        def f(t, o=out_ap, i=in_ap, k=k):
            return t.transpose(o, i, ident[0:k, 0:k])
        return P.op("tensor", f, deps, tag=tag)

    def vop(fn, deps, tag="v"):
        return P.op("vector", fn, deps, tag=tag)

    def sop(fn, deps, tag="s"):
        return P.op("scalar", fn, deps, tag=tag)

    def vcopy(out_ap, in_ap, deps, tag="vcopy"):
        return vop(lambda v, o=out_ap, i=in_ap: v.tensor_copy(o, i), deps, tag=tag)

    def dma(queue, out_ap, in_ap, deps, tag="dma"):
        return P.op(queue, lambda e, o=out_ap, i=in_ap: e.dma_start(out=o, in_=i),
                    deps, dma=True, tag=tag)

    ring_use = [None] * 6
    ring_fill = [None] * 6
    rp = [0]

    def ring_alloc(n=1):
        if (rp[0] % 6) + n > 6:
            rp[0] += 6 - (rp[0] % 6)
        slots = [(rp[0] + i) % 6 for i in range(n)]
        rp[0] += n
        deps = [ring_use[s] for s in slots if ring_use[s] is not None]
        deps += [ring_fill[s] for s in slots if ring_fill[s] is not None]
        return slots, deps

    def ring_filled(slots, tok):
        for s in slots:
            ring_fill[s] = tok

    def ring_used(slots, tok):
        for s in slots:
            ring_use[s] = tok

    def rs(slot, cols=512, off=0):
        return ps_ring[:, 512 * slot + off: 512 * slot + off + cols]

    SC_NINV, SC_K0, SC_D0, SC_OS = 0, 1, 2, 9

    def scap(i):
        return scal[:, i:i + 1]

    def newton_rsqrt(dst_ap, t_ap, w, deps, tag):
        T = nsc[:, 0:w]; Sx = nsc[:, NT:NT + w]; U = nsc[:, 2 * NT:2 * NT + w]
        tk = vcopy(T, t_ap, list(deps) + [newton_busy[0]], tag=tag + "cp")
        tk = vop(lambda v: v.reciprocal(Sx, T), [tk], tag=tag + "rc")
        tk = vop(lambda v: v.tensor_single_scalar(out=Sx, in_=Sx, scalar=1.0, op=ALU.min),
                 [tk], tag=tag + "mn")
        for it_ in range(16):
            tk = vop(lambda v: v.tensor_mul(U, Sx, Sx), [tk], tag=tag + "n1")
            tk = vop(lambda v: v.tensor_mul(U, U, T), [tk], tag=tag + "n2")
            tk = vop(lambda v: v.tensor_scalar(out=U, in0=U, scalar1=-0.5, scalar2=1.5,
                                               op0=ALU.mult, op1=ALU.add), [tk], tag=tag + "n3")
            tk = vop(lambda v: v.tensor_mul(Sx, Sx, U), [tk], tag=tag + "n4")
        tk = vcopy(dst_ap, Sx, [tk], tag=tag + "out")
        newton_busy[0] = tk
        return tk

    newton_busy = [None]
    gelu_busy = [None]

    t_id_i = P.op("gpsimd", lambda g: g.iota(dgf[:, :], pattern=[[1, 128]], base=0,
                                             channel_multiplier=-1,
                                             allow_small_or_imprecise_dtypes=True), (), tag="iota")
    t_ident = vop(lambda v: v.tensor_single_scalar(out=ident[:, :], in_=dgf[:, :],
                                                   scalar=0.0, op=ALU.is_equal), [t_id_i], tag="ident")
    t_dgm = vop(lambda v: v.tensor_single_scalar(out=diagmask[:, :], in_=dgf[:, :],
                                                 scalar=0.0, op=ALU.is_le), [t_id_i], tag="dgm")
    t_io64 = P.op("gpsimd", lambda g: g.iota(iota64[:, :], pattern=[[0, 1]], base=0,
                                             channel_multiplier=1,
                                             allow_small_or_imprecise_dtypes=True),
                  [t_id_i], tag="io64")
    t_ioj = P.op("gpsimd", lambda g: g.iota(iotaj[:, :], pattern=[[1, 72]], base=0,
                                            channel_multiplier=0,
                                            allow_small_or_imprecise_dtypes=True),
                 [t_io64], tag="ioj")
    t_bigmi_i = P.op("gpsimd", lambda g: g.iota(bigmi[:, :], pattern=[[-1, 8]], base=999,
                                                channel_multiplier=0,
                                                allow_small_or_imprecise_dtypes=True),
                     [t_ioj], tag="iobm")
    t_ones_b = vop(lambda v: v.memset(onescol_b[:, :], 1.0), (), tag="onesb")
    t_ones_r = vop(lambda v: v.memset(onesrow[:, :], 1.0), (), tag="onesr")
    t_zsb0 = vop(lambda v: v.memset(zsb[64:128, :], 0.0), (), tag="zsb0")
    t_zsb1 = vop(lambda v: v.memset(zsb2[64:128, :], 0.0), (), tag="zsb1")
    t_zsbp = [t_zsb0, t_zsb1]


    rowbuf_read = [None]

    def bcast_row(dst, src_d, tag):
        tok_l = dma("sync", wstage[0:1, 0:D], src_d[:, :], [rowbuf_read[0]], tag="ld_" + tag)
        outs = []
        for j in range(2):
            slots, rdeps = ring_alloc(1)
            sl = slots[0]
            tmm = mm(rs(sl), onesrow[:, :], wstage[0:1, 512 * j: 512 * j + 512],
                     [tok_l, t_ones_r] + rdeps, tag="bc_" + tag)
            rowbuf_read[0] = tmm
            ring_filled(slots, tmm)
            tcp = vcopy(dst[:, 512 * j: 512 * j + 512], rs(sl), [tmm], tag="bcc_" + tag)
            ring_used(slots, tcp)
            outs.append(tcp)
        return outs, tok_l

    gam_toks, l_g = bcast_row(gam_t, gam_d, "gam")
    bet_toks, l_b = bcast_row(bet_t, bet_d, "bet")
    ngam_toks, l_n = bcast_row(ngam_t, ngam_d, "ngam")
    nbet_toks, l_nb = bcast_row(nbet_t, nbet_d, "nbet")

    tok_bq = dma("sync", bqT[:, :], bqT_d[:, :], (), tag="bq")
    tok_bk = dma("sync", bkT[:, :], bkT_d[:, :], (), tag="bk")
    tok_bv = dma("sync", bvT[:, :], bvT_d[:, :], (), tag="bv")
    tok_cb1 = dma("sync", cb1T[:, :], cb1T_d[:, :], (), tag="cb1")
    tok_cb2 = dma("sync", cb2_s[:, :], cb2_d[:, :], (), tag="cb2")
    tok_kn = dma("sync", knots_s[:, :], knots_d[:, :], (), tag="kn")
    tok_tm = dma("sync", temp_s[:, 0:1], temp_d[:, :], (), tag="tm")
    tok_os = dma("sync", temp_s[:, 3:4], oscale_d[:, :], (), tag="os")
    t_t1 = vop(lambda v: v.tensor_single_scalar(out=temp_s[:, 1:2], in_=temp_s[:, 0:1],
                                                scalar=1e-6, op=ALU.add), [tok_tm], tag="t1")
    t_t2 = vop(lambda v: v.reciprocal(temp_s[:, 2:3], temp_s[:, 1:2]), [t_t1], tag="t2")
    t_t3 = vop(lambda v: v.tensor_single_scalar(out=temp_s[:, 2:3], in_=temp_s[:, 2:3],
                                                scalar=-1.0, op=ALU.mult), [t_t2], tag="t3")
    t_sz = vop(lambda v: v.memset(scrow[:, :], 0.0), (), tag="sz")
    t_s0 = vcopy(scrow[:, 0:1], temp_s[:, 2:3], [t_t3, t_sz], tag="s0")
    t_s1 = vcopy(scrow[:, 1:2], knots_s[:, 0:1], [tok_kn], tag="s1")
    t_s2 = vop(lambda v: v.tensor_sub(scrow[:, 2:9], knots_s[:, 1:8], knots_s[:, 0:7]),
               [tok_kn], tag="s2")
    t_s3 = vcopy(scrow[:, 9:10], temp_s[:, 3:4], [tok_os], tag="s3")
    slots, rdeps = ring_alloc(1)
    sl0 = slots[0]
    t_scm = mm(rs(sl0, 16), onesrow[:, :], scrow[:, 0:16],
               [t_s0, t_s1, t_s2, t_s3, t_ones_r] + rdeps, tag="scm")
    ring_filled(slots, t_scm)
    t_scal = vcopy(scal[:, :], rs(sl0, 16), [t_scm], tag="scal")
    ring_used(slots, t_scal)

    for kk in range(8):
        a_ap = sgt[:, 72 * kk: 72 * kk + 64]
        cs_ap = sgt[:, 72 * kk + 64: 72 * kk + 72]
        tok_l = dma("sync", a_ap, sigT_d[128 * kk:128 * kk + 128, :], (), tag="sig")
        t_a = vop(lambda v, a=a_ap: v.tensor_scalar(out=calv[:, 0:64], in0=a, scalar1=0.3,
                                                    scalar2=None, op0=ALU.is_gt), [tok_l], tag="siga")
        t_b = vop(lambda v, a=a_ap: v.tensor_scalar(out=calt[:, 0:64], in0=a, scalar1=-0.3,
                                                    scalar2=None, op0=ALU.is_lt), [tok_l], tag="sigb")
        t_c = vop(lambda v, a=a_ap: v.tensor_sub(a, calv[:, 0:64], calt[:, 0:64]),
                  [t_a, t_b], tag="sigc")
        t_d = vop(lambda v, kk=kk: v.reduce_sum(
            sgt[:, 72 * kk + 64: 72 * kk + 72],
            sgt[:, 72 * kk: 72 * kk + 64].rearrange("p (c t) -> p c t", t=8), axis=AX.X),
            [t_c], tag="sigd")
        t_e = vop(lambda v, a=cs_ap: v.tensor_scalar(out=calv[:, 0:8], in0=a, scalar1=0.0,
                                                     scalar2=None, op0=ALU.is_gt), [t_d], tag="sige")
        t_f = vop(lambda v, a=cs_ap: v.tensor_scalar(out=calt[:, 0:8], in0=a, scalar1=0.0,
                                                     scalar2=None, op0=ALU.is_lt), [t_d], tag="sigf")
        vop(lambda v, a=cs_ap: v.tensor_sub(a, calv[:, 0:8], calt[:, 0:8]), [t_e, t_f], tag="sigg")

    tok_cw2l = dma("sync", wstage[:, 0:4], cw2_d[:, :], [rowbuf_read[0]], tag="cw2")
    tok_cw2 = vcopy(cw2_b[:, :], wstage[:, 0:4], [tok_cw2l], tag="cw2c")
    tok_dirs0 = dma("sync", wstage[0:64, :], dirs_d[:, :], [tok_cw2], tag="dirs")
    tok_dirs = vcopy(dirs_b[:, :], wstage[0:64, :], [tok_dirs0], tag="dirsc")
    tok_spl0 = dma("sync", wstage[0:64, 0:65], spl_d[:, :], [tok_dirs], tag="spl")
    tok_spl = vcopy(spl_b[:, :], wstage[0:64, 0:65], [tok_spl0], tag="splc")

    x0_free = [None] * 8
    var_toks = []
    for t in range(NT):
        buf = x0c[t % 8]
        tok_l = dma("sync", buf[:, :], x0_d[128 * t:128 * t + 128, :],
                    [x0_free[t % 8]], tag="x0s")
        t1 = vop(lambda v, b=buf: v.bn_stats(st6[:, 0:6], b[:, 0:512]), [tok_l], tag="bns1")
        t2 = vop(lambda v, b=buf: v.bn_stats(st6[:, 6:12], b[:, 512:1024]), [tok_l], tag="bns2")
        t3 = vop(lambda v: v.bn_aggr(mvt[:, :], st6[:, :]), [t1, t2], tag="bnag")
        x0_free[t % 8] = t3
        t4 = vop(lambda v, t=t: v.tensor_single_scalar(out=r1[:, t:t + 1], in_=mvt[:, 1:2],
                                                       scalar=EPS, op=ALU.add), [t3], tag="veps")
        t7 = vcopy(mu1[:, t:t + 1], mvt[:, 0:1], [t3], tag="mu1")
        var_toks.append((t4, t7))
    t_r1 = newton_rsqrt(r1[:, 0:NT], r1[:, 0:NT], NT, [tk for pair in var_toks for tk in pair],
                        "ln1n")
    ln1_tok = [(t_r1, var_toks[t][1]) for t in range(NT)]

    es_kv = ExitStack()
    wkv_b = es_kv.enter_context(nc.sbuf_tensor("wkv_b", [128, 8 * 1024], BF16))
    kstage = es_kv.enter_context(nc.sbuf_tensor("kstage", [128, 512], BF16))
    vstage_f = es_kv.enter_context(nc.sbuf_tensor("vstage_f", [128, 512], F32))
    vstage = es_kv.enter_context(nc.sbuf_tensor("vstage", [128, 4 * 130], BF16))

    acc_tok = [None] * 8
    acc_read = [None] * 8
    aT_read = [None] * 8
    aT_tok = [None] * 8
    aT_store = [None] * 8
    kv_dma_toks = []
    last_wkv_mm = [None]

    def load_wkv(w_d, dep0):
        prev = dep0
        for kk in range(8):
            tok_l = dma("sync", wstage[:, :], w_d[128 * kk:128 * kk + 128, :],
                        [prev], tag="wkv")
            prev = vcopy(wkv_b[:, 1024 * kk: 1024 * kk + 1024], wstage[:, :],
                         [tok_l, last_wkv_mm[0]], tag="wkvc")
        return prev

    kst_free = [None]
    vstf_free = [None]
    vst_free = [None]

    def kv_pass(proj, w_ready, bT, bias_tok, kdst=None, build_aT=False):
        for n in range(CH):
            if build_aT:
                a_toks = []
                for tt in range(4):
                    t = 4 * n + tt
                    buf = x0c[t % 8]
                    tok_l = dma("sync", buf[:, :], x0_d[128 * t:128 * t + 128, :],
                                [x0_free[t % 8]], tag="x0kv")
                    ta = vop(lambda v, b=buf, t=t, tt=tt: v.tensor_scalar(
                        out=ACC[tt][:, :], in0=b[:, :], scalar1=mu1[:, t:t + 1],
                        scalar2=r1[:, t:t + 1], op0=ALU.subtract, op1=ALU.mult),
                        [tok_l, ln1_tok[t][0], ln1_tok[t][1], acc_read[tt], acc_tok[tt]],
                        tag="akv")
                    x0_free[t % 8] = ta
                    acc_tok[tt] = ta
                    a_toks.append(ta)
                for kk in range(8):
                    slots, rdeps = ring_alloc(1)
                    sl = slots[0]
                    last_t = None
                    for tt in range(4):
                        deps = [a_toks[tt], t_ident] + (rdeps if tt == 0 else [])
                        last_t = tp_(rs(sl, 128, 128 * tt),
                                     ACC[tt][:, 128 * kk:128 * kk + 128],
                                     deps, tag="aTkv")
                        acc_read[tt] = last_t
                    ring_filled(slots, last_t)
                    tc = vop(lambda v, kk=kk, sl=sl: v.tensor_copy(
                        aT[:, 512 * kk: 512 * kk + 512], rs(sl)),
                        [last_t, aT_tok[kk], aT_read[kk], aT_store[kk]], tag="aTcp")
                    ring_used(slots, tc)
                    aT_tok[kk] = tc
                    td = dma("sync", aTs_d[kk, :, 512 * n: 512 * n + 512],
                             aT[:, 512 * kk: 512 * kk + 512], [tc], tag="aTst")
                    aT_store[kk] = td
            else:
                for kk in range(8):
                    tok = dma("sync", aT[:, 512 * kk: 512 * kk + 512],
                              aTs_d[kk, :, 512 * n: 512 * n + 512],
                              [aT_tok[kk], aT_read[kk], aT_store[kk]], tag="aTld")
                    aT_tok[kk] = tok
            for p in range(NPAIR):
                slots, rdeps = ring_alloc(1)
                sl = slots[0]
                mmt = None
                for kk in range(8):
                    deps = [aT_tok[kk], w_ready] + (rdeps if kk == 0 else [])
                    mmt = mm(rs(sl), wkv_b[:, 1024 * kk + 128 * p: 1024 * kk + 128 * p + 128],
                             aT[:, 512 * kk: 512 * kk + 512], deps,
                             start=(kk == 0), stop=(kk == 7), tag=proj + "mm")
                    aT_read[kk] = mmt
                last_wkv_mm[0] = mmt
                ring_filled(slots, mmt)
                if proj in ("k", "q"):
                    tev = vop(lambda v, sl=sl, p=p: v.tensor_scalar(
                        out=kstage[:, :], in0=rs(sl), scalar1=bT[:, p:p + 1], scalar2=None,
                        op0=ALU.add), [mmt, bias_tok, kst_free[0]], tag="kev")
                    ring_used(slots, tev)
                    tdma = dma("sync", kdst[p, :, 512 * n: 512 * n + 512], kstage[:, :],
                               [tev], tag="kst")
                    kst_free[0] = tdma
                    kv_dma_toks.append(tdma)
                else:
                    tevs = []
                    for j in range(4):
                        tev = vop(lambda v, sl=sl, p=p, j=j: v.tensor_scalar(
                            out=vstage_f[:, 128 * j: 128 * j + 128],
                            in0=rs(sl, 128, 128 * j), scalar1=bT[:, p:p + 1], scalar2=None,
                            op0=ALU.add), [mmt, bias_tok, vstf_free[0]], tag="vev")
                        tevs.append(tev)
                    ring_used(slots, tevs[-1])
                    slots2, rdeps2 = ring_alloc(1)
                    sl2 = slots2[0]
                    last_t = None
                    for j in range(4):
                        deps = [tevs[j]] + (rdeps2 if j == 0 else [])
                        last_t = tp_(rs(sl2, 128, 128 * j), vstage_f[:, 128 * j: 128 * j + 128],
                                     deps, tag="vT")
                    vstf_free[0] = last_t
                    ring_filled(slots2, last_t)
                    tcp = vop(lambda v, sl2=sl2: v.tensor_copy(
                        vstage[:, :].rearrange("p (t s c) -> p t s c", t=4, s=2)[:, :, :, 0:64],
                        rs(sl2, 512).rearrange("p (t s c) -> p t s c", t=4, s=2)),
                        [last_t, vst_free[0]], tag="vcp")
                    ring_used(slots2, tcp)
                    tms = vop(lambda v: v.memset(
                        vstage[:, :].rearrange("p (t c) -> p t c", t=4)[:, :, 64:130:65], 1.0),
                        [tcp], tag="vones")
                    tdma = dma("sync", vs_d[p, 4 * n: 4 * n + 4].rearrange("t p c -> p t c"),
                               vstage[:, :].rearrange("p (t c) -> p t c", t=4), [tms], tag="vst")
                    vst_free[0] = tdma
                    kv_dma_toks.append(tdma)

    wq_ready = load_wkv(wqT_d, tok_spl)
    kv_pass("q", wq_ready, bqT, tok_bq, kdst=qts_d, build_aT=True)
    wk_ready = load_wkv(wkT_d, wq_ready)
    kv_pass("k", wk_ready, bkT, tok_bk, kdst=kts_d)
    wv_ready = load_wkv(wvT_d, wk_ready)
    kv_pass("v", wv_ready, bvT, tok_bv)
    zmov = wstage[0:1, 0:512]
    zrow = wstage[0:1, 512:577]
    t_zmov = vop(lambda v: v.memset(wstage[0:1, 0:577], 0.0), [wv_ready], tag="zmov")
    t_zrow = t_zmov
    es_kv.close()
    kv_all = P.op("gpsimd", lambda g: g.nop(), kv_dma_toks, tag="kvbar")

    cw1_f = sb("cw1_f", (128, 8 * 256), F32)
    x2T_f = sb("x2T_f", (128, 8 * 512), F32)
    hTb = sb("hTb", (128, 2 * 512), BF16)
    cw1_toks = []
    for kk in range(8):
        tok_l = dma("sync", cw1_f[:, 256 * kk: 256 * kk + 256],
                    cw1_d[128 * kk: 128 * kk + 128, :], [last_wkv_mm[0]], tag="cw1")
        cw1_toks.append(tok_l)

    kbuf_tok = [None, None]; kbuf_read = [None, None]
    vbuf_tok = [None, None]; vbuf_read = [None, None]
    pT_tok = [None] * 4; pT_read = [None] * 4
    zs_tok = [None, None]; zs_read = [None, None]
    ucnt = [0]
    psz_read = [None, None]; zs_close = [None, None]
    qbuf = [qT, aT]
    qbuf_tok = [None, None]
    qbuf_read = [[], [tk for tk in aT_tok if tk] + [tk for tk in aT_read if tk] +
                 [tk for tk in aT_store if tk]]
    dnm_tok = {}; dnm_read = [None, None]
    ybuf_tok = [None] * 4; ybuf_read = [None] * 4
    x1rb_tok = [None, None]; x1rb_read = [None, None]
    x1dma_tok = [None] * 8
    outdma_toks = []
    gates_tok = [None] * 4
    gexp_tok = [None] * 4
    cal_state = {}
    mags_tok = [None] * 4
    compt_tok = [None] * 4
    oh_tok = [None] * 4; oh_read = [None] * 4
    tidxr_read = [None] * 4; idx_read = [None] * 4
    gather_tok = {}
    x2T_tok = [None] * 8; x2T_read = [None] * 8
    hT_tok = [None] * 2; hT_read = [None] * 2
    mv2_tok = [None] * 4; mv3_tok = [None] * 4; mvy_tok = [None] * 4
    ctmp_read = [None]
    ctv4_hold = [None]
    r2_tok = [None] * 4; r3_tok = [None] * 4; ry_tok = [None] * 4
    ysts_read = [None] * 4
    spl_ps = {}

    def a_init(c, free_x0=False):
        toks = []
        for tt in range(4):
            t = 4 * c + tt
            ai = 4 * (c % 2) + tt
            buf = x0c[(c % 2) * 4 + tt]
            tok_l = dma("sync", buf[:, :], x0_d[128 * t:128 * t + 128, :],
                        [x0_free[(c % 2) * 4 + tt]], tag=f"x0c{c}")
            ta = vop(lambda v, b=buf, t=t, ai=ai: v.tensor_scalar(
                out=ACC[ai][:, :], in0=b[:, :], scalar1=mu1[:, t:t + 1],
                scalar2=r1[:, t:t + 1], op0=ALU.subtract, op1=ALU.mult),
                [tok_l, ln1_tok[t][0], ln1_tok[t][1], acc_read[ai], acc_tok[ai],
                 x1dma_tok[ai]], tag=f"a{c}")
            acc_tok[ai] = ta
            if free_x0:
                x0_free[(c % 2) * 4 + tt] = ta
            toks.append(ta)
        return toks

    def kvq_load(p):
        pb = p % 2
        deps = [kbuf_read[pb], kv_all]
        tk = dma("sync", kbuf[pb][:, :], kts_d[p, :, :], deps, tag=f"kld{p}")
        deps = [vbuf_read[pb], kv_all]
        tvs = []
        for half in range(2):
            tv = dma("sync",
                     vbuf[pb][:, half * (NT // 2) * 130:(half + 1) * (NT // 2) * 130].rearrange(
                         "p (t c) -> p t c", c=130),
                     vs_d[p, half * (NT // 2):(half + 1) * (NT // 2)].rearrange(
                         "t p c -> p t c"), deps, tag=f"vld{p}")
            tvs.append(tv)
        tq = dma("sync", qbuf[pb][:, 0:S], qts_d[p, :, :],
                 list(qbuf_read[pb]) + [kv_all], tag=f"qld{p}")
        kbuf_tok[pb] = tk
        vbuf_tok[pb] = tvs
        qbuf_tok[pb] = tq
        return tk, tvs

    def tile_schedule(c):
        groups = []
        if "sside" in skip:
            CHsk = True
        else:
            CHsk = False
        for hh in range(2):
            if CHsk:
                break
            for it in range(4):
                i = 4 * c + it
                chunks = []
                rem = 128 * (i + 1)
                j = 0
                while rem > 0:
                    nj = min(512, rem)
                    chunks.append((j, nj))
                    rem -= nj
                    j += 1
                g = []
                for (j, nj) in chunks:
                    if nj < 512:
                        if g:
                            groups.append(("S", g))
                            g = []
                        groups.append(("S", [(hh, it, j, nj)]))
                        continue
                    g.append((hh, it, j, nj))
                    if len(g) == 3:
                        groups.append(("S", g))
                        g = []
                if g:
                    groups.append(("S", g))
        stg = []
        for kp in range(4 * c + 4, NT):
            for hh in range(2):
                stg.append((hh, kp, 512))
                if len(stg) == 2:
                    groups.append(("T", stg))
                    stg = []
        if stg:
            groups.append(("T", stg))
        for kp in range(4 * c, min(4 * c + 4, NT)):
            jj = kp - 4 * c
            n = 128 * (jj + 1)
            for hh in range(2):
                groups.append(("T", [(hh, kp, n)]))
        return groups

    def emit_pair(c, p, sched):
        pb = p % 2
        ktok = kbuf_tok[pb]
        vtoks = vbuf_tok[pb]
        qb = qbuf[pb]
        qtok = qbuf_tok[pb]
        ucnt[0] += 1
        dpar = ucnt[0] % 2
        dbase = 32 * dpar
        pv_last = [None, None]
        zopen = [None, None]
        have_full = 4 * c + 4 < NT
        if not have_full:
            for hh in range(2):
                zopen[hh] = mm(ps_z[0:65, 512 * hh: 512 * hh + 512], zrow[0:1, 0:65],
                               zmov[0:1, :],
                               [t_zrow, t_zmov, psz_read[hh], zs_close[hh]], start=True,
                               stop=False, tag=f"zop{c}_{p}")
                pv_last[hh] = zopen[hh]
        first_pv = [have_full, have_full]
        pT_parity = [0]
        pend_consume = []
        kread = []
        vread = []
        qread = []
        sg_idx = {}
        for (kind, g) in sched:
            L = len(g)
            slots, rdeps = ring_alloc(L)
            qk_toks = []
            for ii, item in enumerate(g):
                sl = slots[ii]
                if "qk" in skip:
                    continue
                if "qk1" in skip and not (p == 0 and c == 0 and ii == 0 and kind == "T"):
                    continue
                if kind == "T":
                    hh, kp, n = item
                    lhs = kbuf[pb][64 * hh:64 * hh + 64, 128 * kp:128 * kp + 128]
                    rhsap = qb[64 * hh:64 * hh + 64, 512 * c: 512 * c + n]
                    deps = [ktok, qtok] + (rdeps if ii == 0 else [])
                    tokm = mm(rs(sl, n), lhs, rhsap, deps, tag=f"qkT{c}_{p}")
                else:
                    hh, it, j, n = item
                    lhs = qb[64 * hh:64 * hh + 64, 512 * c + 128 * it: 512 * c + 128 * it + 128]
                    rhsap = kbuf[pb][64 * hh:64 * hh + 64, 512 * j: 512 * j + n]
                    deps = [ktok, qtok] + (rdeps if ii == 0 else [])
                    tokm = mm(rs(sl, n), lhs, rhsap, deps, tag=f"qkS{c}_{p}")
                qk_toks.append(tokm)
                ring_fill[sl] = tokm
                kread.append(tokm)
                qread.append(tokm)
            pp = pT_parity[0]
            pT_parity[0] = (pp + 1) % 4
            ns = [item[-1] for item in g]
            sameN = all(x == ns[0] for x in ns)
            exp_toks = []
            if "exp" in skip:
                for ii2, sl_ in enumerate(slots):
                    if ii2 < len(qk_toks):
                        ring_used([sl_], qk_toks[ii2])
                pT_parity[0] = pp
                continue
            if kind == "S":
                hh, it, _, _ = g[0]
                base_col = dbase + 4 * (4 * hh + it)
                gi = sg_idx.get((hh, it), 0)
                sg_idx[(hh, it)] = gi + 1
                totN = sum(ns)
                acap = dnm[:, base_col + gi: base_col + gi + 1]
                din_ap = rs(slots[0], totN) if (sameN and ns[0] == 512) or L == 1 else None
                if din_ap is None:
                    for ii, item in enumerate(g):
                        _, _, _, n = item
                        gi2 = sg_idx[(hh, it)] - 1 + ii
                        pass
                    raise AssertionError("S group must be uniform or single")
                texp = sop(lambda s, i_=din_ap, o=pTb[pp][:, 0:totN], a=acap:
                           s.activation(out=o, in_=i_, func=AF.Exp, scale=1.0 / 64.0,
                                        accum_out=a),
                           qk_toks + [pT_read[pp], pT_tok[pp], dnm_read[dpar]],
                           tag=f"expS{c}_{p}")
                dnm_tok[(hh, it, gi)] = texp
                pT_tok[pp] = texp
                exp_toks.append(texp)
                ring_used(slots, texp)
            else:
                if sameN:
                    totN = ns[0] * L
                    texp = sop(lambda s, i_=rs(slots[0], totN), o=pTb[pp][:, 0:totN]:
                               s.activation(out=o, in_=i_, func=AF.Exp, scale=1.0 / 64.0),
                               qk_toks + [pT_read[pp], pT_tok[pp]], tag=f"expT{c}_{p}")
                    exp_toks = [texp] * L
                    pT_tok[pp] = texp
                    ring_used(slots, texp)
                else:
                    off = 0
                    for ii, item in enumerate(g):
                        n = item[-1]
                        texp = sop(lambda s, i_=rs(slots[ii], n), o=pTb[pp][:, off:off + n]:
                                   s.activation(out=o, in_=i_, func=AF.Exp, scale=1.0 / 64.0),
                                   [qk_toks[ii], pT_read[pp], pT_tok[pp]], tag=f"expT{c}_{p}")
                        exp_toks.append(texp)
                        pT_tok[pp] = texp
                        ring_used([slots[ii]], texp)
                        off += n
            if kind == "S":
                continue
            if dbg_d is not None and c == 0 and p == 0 and not dbg_state.get(5):
                dbg_state[5] = True
                _dtok = dbg_dump(5, pTb[pp][:, 0:512], [exp_toks[0]], w=512)
                pT_tok[pp] = _dtok
                _d6 = dbg_dump(6, qb[:, 0:512], [qtok], w=512)
                qbuf_read[pb] = [_d6]
                _d7 = dbg_dump(7, kbuf[pb][:, 0:512], [ktok], w=512)
                kbuf_read[pb] = _d7

            def consume(g=g, pp=pp, exp_toks=exp_toks, sameN=sameN):
                off = 0
                creads = []
                for ii, item in enumerate(g):
                    hh, kp, n = item
                    jj = kp - 4 * c
                    et = exp_toks[ii] if not sameN else exp_toks[0]
                    pslice = lambda a, b_: pTb[pp][:, off + a: off + b_]
                    zoff = 512 * hh
                    if jj >= 4:
                        if "pvones" not in skip:
                            opener = first_pv[hh]
                            deps = [et] + vtoks
                            if opener:
                                deps += [psz_read[hh], zs_close[hh]]
                            elif zopen[hh] is not None:
                                deps.append(zopen[hh])
                            st = mm(ps_z[0:65, zoff:zoff + 512],
                                    vbuf[pb][:, 130 * kp + 65 * hh: 130 * kp + 65 * hh + 65],
                                    pslice(0, 512), deps,
                                    start=opener, stop=False, tag=f"pv{c}_{p}")
                            if opener:
                                first_pv[hh] = False
                                zopen[hh] = st
                            pv_last[hh] = st
                            creads.append(st)
                            vread.append(st)
                    else:
                        if jj > 0 and "pvones" not in skip:
                            st = mm(ps_z[0:65, zoff:zoff + 128 * jj],
                                    vbuf[pb][:, 130 * kp + 65 * hh: 130 * kp + 65 * hh + 65],
                                    pslice(0, 128 * jj),
                                    [et, zopen[hh]] + vtoks,
                                    start=False, stop=False, tag=f"pvp{c}_{p}")
                            pv_last[hh] = st
                            creads.append(st)
                            vread.append(st)
                        if "dpv" not in skip:
                            tmk = vop(lambda v, a=pslice(128 * jj, 128 * jj + 128):
                                      v.tensor_mul(a, a, diagmask[:, :]), [et, t_dgm],
                                      tag=f"dmask{c}_{p}")
                            st = mm(ps_z[0:64, zoff + 128 * jj: zoff + 128 * jj + 128],
                                    vbuf[pb][:, 130 * kp + 65 * hh: 130 * kp + 65 * hh + 64],
                                    pslice(128 * jj, 128 * jj + 128),
                                    [tmk, zopen[hh]] + vtoks,
                                    start=False, stop=False, tag=f"dpv{c}_{p}")
                            pv_last[hh] = st
                            creads.append(st)
                            vread.append(st)
                    off += n
                pT_read[pp] = creads[-1] if creads else pT_read[pp]

            pend_consume.append(consume)
            if len(pend_consume) > 3:
                pend_consume.pop(0)()
        while pend_consume:
            pend_consume.pop(0)()
        kbuf_read[pb] = kread[-1] if kread else kbuf_read[pb]
        vbuf_read[pb] = vread[-1] if vread else vbuf_read[pb]
        if qread:
            qbuf_read[pb] = [qread[-1]]
        return pv_last, sg_idx

    zst_read = [None] * 8
    zst_idx = [0]
    zdma_toks = []
    zdma_by_chunk = [[] for _ in range(CH)]
    RH = S // 16

    def z_evac(c, p, pv_last, sg_idx):
        toks = []
        dpar = ucnt[0] % 2
        dbase = 32 * dpar
        for hh in range(2):
            zoff = 512 * hh
            zsbh = zsb if hh == 0 else zsb2
            tclose = mm(ps_z[0:65, zoff: zoff + 1], zrow[0:1, 0:65], zmov[0:1, 0:1],
                        [pv_last[hh], t_zrow], start=False, stop=True, tag=f"zcl{c}_{p}")
            zs_close[hh] = tclose
            tcps = []
            for j in range(4):
                tcp = vcopy(zsbh[0:65, 128 * j: 128 * j + 128],
                            ps_z[0:65, zoff + 128 * j: zoff + 128 * j + 128],
                            [tclose, zs_tok[hh], zs_read[hh], t_zsbp[hh]], tag=f"zc{c}_{p}")
                tcps.append(tcp)
            zs_tok[hh] = tcps[-1]
            psz_read[hh] = tcps[-1]
            slots, rdeps = ring_alloc(1)
            sl = slots[0]
            tts = []
            for j in range(4):
                deps = [tcps[j], t_ident] + (rdeps if j == 0 else [])
                tt_ = tp_(rs(sl, 128, 128 * j), zsbh[:, 128 * j: 128 * j + 128],
                          deps, tag=f"zT{c}_{p}")
                tts.append(tt_)
            zs_read[hh] = tts[-1]
            ring_filled(slots, tts[-1])
            h_glob = 2 * p + hh
            tBs = []
            for j in range(4):
                m = sg_idx.get((hh, j), 0)
                if m > 0:
                    bc = dbase + 4 * (4 * hh + j)
                    tA = vop(lambda v, bc=bc, m=m, j=j: v.reduce_sum(
                        dnr[:, j:j + 1], dnm[:, bc:bc + m], axis=AX.X),
                        [dnm_tok[(hh, j, gg)] for gg in range(m)], tag=f"dn{c}_{p}")
                    tB = vop(lambda v, sl=sl, j=j: v.tensor_add(
                        dnr[:, 4 + j:5 + j], dnr[:, j:j + 1], rs(sl, 1, 128 * j + 64)),
                        [tA, tts[-1]], tag=f"dt{c}_{p}")
                else:
                    tB = vcopy(dnr[:, 4 + j:5 + j], rs(sl, 1, 128 * j + 64), [tts[-1]],
                               tag=f"dt0{c}_{p}")
                tBs.append(tB)
            trc = vop(lambda v: v.reciprocal(dnr[:, 8:12], dnr[:, 4:8]), tBs,
                      tag=f"dr{c}_{p}")
            vlast = None
            for j in range(4):
                zi = zst_idx[0] % 4
                zst_idx[0] += 1
                zb = zst[zi]
                tz = vop(lambda v, sl=sl, j=j, zb=zb: v.tensor_scalar(
                    out=zb[:, :], in0=rs(sl, 64, 128 * j), scalar1=dnr[:, 8 + j:9 + j],
                    scalar2=None, op0=ALU.mult),
                    [trc, zst_read[zi], tts[-1]], tag=f"za{c}_{p}")
                r0 = RH * h_glob + 32 * c + 8 * j
                td = dma("sync",
                         zs_d[r0:r0 + 8, :].rearrange("r (t h) -> (r t) h", h=64),
                         zb[:, :], [tz], tag=f"zd{c}_{p}")
                zst_read[zi] = td
                zdma_toks.append(td)
                zdma_by_chunk[(RH * h_glob) // 512].append(td)
                vlast = tz
            ring_used(slots, vlast)
            dnm_read[dpar] = vlast
            toks.append(vlast)
        return toks

    def ln2_x1(c, z_toks):
        stat_toks = []
        for tt in range(4):
            ai = 4 * (c % 2) + tt
            t1 = vop(lambda v, ai=ai: v.bn_stats(st6[:, 0:6], ACC[ai][:, 0:512]),
                     [acc_tok[ai]] + z_toks, tag=f"l2a{c}")
            t2 = vop(lambda v, ai=ai: v.bn_stats(st6[:, 6:12], ACC[ai][:, 512:1024]),
                     [acc_tok[ai]], tag=f"l2b{c}")
            t3 = vop(lambda v, tt=tt: v.bn_aggr(mv2[:, 2 * tt:2 * tt + 2], st6[:, :]),
                     [t1, t2, mv2_tok[tt]], tag=f"l2c{c}")
            mv2_tok[tt] = t3
            stat_toks.append(t3)
        return stat_toks

    def rstd_visit(c, stat_toks, mvx, rxc, rx_tok, tagp):
        deps = [tk for tk in stat_toks if tk] + [tk for tk in rx_tok if tk]
        t4 = vop(lambda v, mvx=mvx: v.tensor_scalar(
            out=rxc[:, 0:4],
            in0=mvx[:, 0:8].rearrange("p (t c) -> p t c", c=2)[:, :, 1],
            scalar1=EPS, scalar2=None, op0=ALU.add), deps, tag=f"{tagp}e{c}")
        t6 = newton_rsqrt(rxc[:, 0:4], rxc[:, 0:4], 4, [t4], f"{tagp}n{c}")
        for tt in range(4):
            rx_tok[tt] = t6
        return [t6] * 4

    def x1_compute(c, r2_toks):
        toks = []
        for tt in range(4):
            ai = 4 * (c % 2) + tt
            buf = x0c[(c % 2) * 4 + tt]
            t1 = vop(lambda v, ai=ai, tt=tt: v.tensor_scalar(
                out=ACC[ai][:, :], in0=ACC[ai][:, :], scalar1=mv2[:, 2 * tt:2 * tt + 1],
                scalar2=r2c[:, tt:tt + 1], op0=ALU.subtract, op1=ALU.mult),
                [acc_tok[ai], r2_toks[tt], acc_read[ai]], tag=f"x1n{c}")
            t2 = vop(lambda v, ai=ai: v.tensor_mul(ACC[ai][:, :], ACC[ai][:, :], gam_t[:, :]),
                     [t1] + gam_toks, tag=f"x1g{c}")
            t3 = vop(lambda v, ai=ai, b=buf: v.tensor_add(ACC[ai][:, :], ACC[ai][:, :], b[:, :]),
                     [t2], tag=f"x1x{c}")
            t4 = vop(lambda v, ai=ai: v.tensor_add(ACC[ai][:, :], ACC[ai][:, :], bet_t[:, :]),
                     [t3] + bet_toks, tag=f"x1f{c}")
            acc_tok[ai] = t4
            x0_free[(c % 2) * 4 + tt] = t3
            if _dbg:
                td = dma("sync", x1s_d[128 * (4 * c + tt):128 * (4 * c + tt) + 128, :],
                         ACC[ai][:, :], [t4], tag=f"x1d{c}")
                x1dma_tok[ai] = td
            toks.append(t4)
        return toks

    def ffn_T1(c, x1_toks, r3_toks):
        xn_toks = []
        for tt in range(4):
            ai = 4 * (c % 2) + tt
            buf = x0c[(c % 2) * 4 + tt]
            t1 = vop(lambda v, ai=ai, tt=tt, b=buf: v.tensor_scalar(
                out=b[:, :], in0=ACC[ai][:, :], scalar1=mv3[:, 2 * tt:2 * tt + 1],
                scalar2=r3c[:, tt:tt + 1], op0=ALU.subtract, op1=ALU.mult),
                [acc_tok[ai], r3_toks[tt], x0_free[(c % 2) * 4 + tt]], tag=f"xn{c}")
            x0_free[(c % 2) * 4 + tt] = t1
            xn_toks.append(t1)
        for kk in range(8):
            slots, rdeps = ring_alloc(1)
            sl = slots[0]
            last_t = None
            for tt in range(4):
                buf = x0c[(c % 2) * 4 + tt]
                deps = [xn_toks[tt], t_ident, x2T_read[kk], x2T_tok[kk]] + (rdeps if tt == 0 else [])
                last_t = tp_(rs(sl, 128, 128 * tt), buf[:, 128 * kk:128 * kk + 128],
                             deps, tag=f"x2T{c}")
                x0_free[(c % 2) * 4 + tt] = last_t
            ring_filled(slots, last_t)
            tc = vcopy(x2T_f[:, 512 * kk: 512 * kk + 512], rs(sl), [last_t], tag=f"x2Tc{c}")
            ring_used(slots, tc)
            x2T_tok[kk] = tc
        gelu_in = []
        for jo in range(2):
            slots, rdeps = ring_alloc(1)
            sl = slots[0]
            mmt = None
            for kk in range(8):
                deps = [x2T_tok[kk], cw1_toks[kk]] + (rdeps if kk == 0 else [])
                mmt = mm(rs(sl), cw1_f[:, 256 * kk + 128 * jo: 256 * kk + 128 * jo + 128],
                         x2T_f[:, 512 * kk: 512 * kk + 512], deps,
                         start=(kk == 0), stop=(kk == 7), tag=f"hmm{c}")
                x2T_read[kk] = mmt
            ring_filled(slots, mmt)
            gelu_in.append((sl, mmt, slots))
        A1, A2, A3, A4, A5 = (0.254829592, -0.284496736, 1.421413741,
                              -1.453152027, 1.061405429)
        PP = 0.3275911
        ISQ2 = 0.7071067811865476
        hT_toks = []
        for jo, (sl, mmt, slots) in enumerate(gelu_in):
            hp = hpre[:, :]
            tb = vop(lambda v, sl=sl, jo=jo: v.tensor_scalar(
                out=hp, in0=rs(sl), scalar1=cb1T[:, jo:jo + 1], scalar2=None,
                op0=ALU.add), [mmt, tok_cb1, gelu_busy[0]], tag=f"gb{c}")
            ring_used(slots, tb)
            Z = zsb[:, 0:512]
            tz = vop(lambda v: v.tensor_scalar(out=Z, in0=hp, scalar1=-1.0, scalar2=None,
                                               op0=ALU.mult),
                     [tb, zs_tok[0], zs_read[0]], tag=f"gz{c}")
            tz = vop(lambda v: v.tensor_tensor(out=Z, in0=Z, in1=hp, op=ALU.max),
                     [tz], tag=f"gz2{c}")
            tz = vop(lambda v: v.tensor_scalar(out=Z, in0=Z, scalar1=ISQ2, scalar2=None,
                                               op0=ALU.mult), [tz], tag=f"gz3{c}")
            DN = x1rb[0][:, 0:512]
            td = vop(lambda v: v.tensor_scalar(out=DN, in0=Z, scalar1=PP, scalar2=1.0,
                                               op0=ALU.mult, op1=ALU.add),
                     [tz, x1rb_read[0], x1rb_tok[0]], tag=f"gd{c}")
            td = vop(lambda v: v.reciprocal(DN, DN), [td], tag=f"gd2{c}")
            E = x1rb[0][:, 512:1024]
            te = vop(lambda v: v.tensor_mul(E, Z, Z), [tz, td], tag=f"gsq{c}")
            te = sop(lambda s: s.activation(out=E, in_=E, func=AF.Exp, scale=-1.0),
                     [te], tag=f"gex{c}")
            tq = vop(lambda v: v.tensor_scalar(out=Z, in0=DN, scalar1=0.5 * A5,
                                               scalar2=0.5 * A4, op0=ALU.mult, op1=ALU.add),
                     [te], tag=f"gp1{c}")
            for aa in (A3, A2, A1):
                tq = vop(lambda v: v.tensor_mul(Z, Z, DN), [tq], tag=f"gp2{c}")
                tq = vop(lambda v, aa=aa: v.tensor_single_scalar(out=Z, in_=Z,
                                                                 scalar=0.5 * aa, op=ALU.add),
                         [tq], tag=f"gp3{c}")
            tq = vop(lambda v: v.tensor_mul(Z, Z, DN), [tq], tag=f"gp4{c}")
            tq = vop(lambda v: v.tensor_mul(Z, Z, E), [tq], tag=f"gp5{c}")
            tm = vop(lambda v: v.tensor_scalar(out=DN, in0=hp, scalar1=0.0, scalar2=None,
                                               op0=ALU.is_ge), [tq], tag=f"gm{c}")
            tm2 = vop(lambda v: v.tensor_scalar(out=E, in0=DN, scalar1=-2.0, scalar2=1.0,
                                                op0=ALU.mult, op1=ALU.add), [tm], tag=f"gm2{c}")
            tphi = vop(lambda v: v.tensor_mul(Z, Z, E), [tm2], tag=f"gf1{c}")
            tphi = vop(lambda v: v.tensor_add(Z, Z, DN), [tphi], tag=f"gf2{c}")
            tg = vop(lambda v, jo=jo: v.tensor_mul(hTb[:, 512 * jo: 512 * jo + 512], hp, Z),
                     [tphi, hT_tok[jo], hT_read[jo]], tag=f"gelu{c}")
            gelu_busy[0] = tg
            zs_tok[0] = tg
            x1rb_tok[0] = tg
            x1rb_read[0] = tg
            hT_tok[jo] = tg
            hT_toks.append(tg)
        return xn_toks, hT_toks

    def ffn_T2(c, hT_toks):
        slots, rdeps = ring_alloc(1)
        slc = slots[0]
        mmt = None
        for kk in range(2):
            deps = [hT_toks[kk], tok_cw2] + (rdeps if kk == 0 else [])
            mmt = mm(ps_ring[0:2, 512 * slc: 512 * slc + 512],
                     cw2_b[:, 2 * kk: 2 * kk + 2], hTb[:, 512 * kk: 512 * kk + 512],
                     deps, start=(kk == 0), stop=(kk == 1), tag=f"cmm{c}")
            hT_read[kk] = mmt
        ring_filled(slots, mmt)
        tv1 = vop(lambda v: v.tensor_scalar(out=ctmp[:, :], in0=ps_ring[0:2, 512 * slc: 512 * slc + 512],
                                            scalar1=cb2_s[:, 0:1], scalar2=None, op0=ALU.add),
                  [mmt, tok_cb2, ctmp_read[0]], tag=f"cb{c}")
        ring_used(slots, tv1)
        te = sop(lambda s: s.activation(out=ctmp[:, :], in_=ctmp[:, :], func=AF.Exp, scale=2.0),
                 [tv1], tag=f"ctanh{c}")
        tv2 = vop(lambda v: v.tensor_single_scalar(out=ctmp[:, :], in_=ctmp[:, :], scalar=1.0,
                                                   op=ALU.add), [te], tag=f"ct1{c}")
        tv3 = vop(lambda v: v.reciprocal(ctmp[:, :], ctmp[:, :]), [tv2], tag=f"ct2{c}")
        tv4 = vop(lambda v: v.tensor_scalar(out=ctmp[:, :], in0=ctmp[:, :], scalar1=-2.0,
                                            scalar2=1.0, op0=ALU.mult, op1=ALU.add),
                  [tv3], tag=f"ct3{c}")
        ctv4_hold[0] = tv4
        for tt in range(4):
            slots, rdeps = ring_alloc(1)
            sl = slots[0]
            mmt = None
            for kk in range(8):
                deps = [x2T_tok[kk]] + (rdeps if kk == 0 else [])
                mmt = mm(rs(sl, 72), x2T_f[:, 512 * kk + 128 * tt: 512 * kk + 128 * tt + 128],
                         sgt[:, 72 * kk: 72 * kk + 72], deps,
                         start=(kk == 0), stop=(kk == 7), tag=f"gmm{c}")
                x2T_read[kk] = mmt
            ring_filled(slots, mmt)
            tev = vcopy(gates[:, 72 * tt: 72 * tt + 72], rs(sl, 72), [mmt, gates_tok[tt]],
                        tag=f"gev{c}")
            ring_used(slots, tev)
            gates_tok[tt] = tev

    def compt_stage(c):
        tv4 = ctv4_hold[0]
        slots, rdeps = ring_alloc(1)
        slt = slots[0]
        ctoks = []
        for tt in range(4):
            deps = [tv4, t_ident] + (rdeps if tt == 0 else [])
            ttk = tp_(rs(slt, 2, 128 * tt), ctmp[:, 128 * tt: 128 * tt + 128], deps,
                      tag=f"cT{c}", k=2)
            ctoks.append(ttk)
            ctmp_read[0] = ttk
        ring_filled(slots, ctoks[-1])
        cc = vop(lambda v, slt=slt: v.tensor_copy(
            compt[:, :].rearrange("p (t c) -> p t c", t=4),
            rs(slt, 512).rearrange("p (t c) -> p t c", c=128)[:, :, 0:2]),
            [ctoks[-1]] + [compt_tok[t] for t in range(4)], tag=f"ccp{c}")
        ring_used(slots, cc)
        for tt in range(4):
            compt_tok[tt] = cc

    def gates_exp(c):
        toks = []
        for tt in range(4):
            te = sop(lambda s, tt=tt: s.activation(
                out=gates[:, 72 * tt: 72 * tt + 72], in_=gates[:, 72 * tt: 72 * tt + 72],
                func=AF.Exp, scale=scap(SC_NINV)), [gates_tok[tt], t_scal], tag=f"gex{c}")
            gates_tok[tt] = te
            toks.append(te)
        return toks

    def calibrate_argmax(c, ge_toks):
        tidx_toks = []
        for tt in range(4):
            g_ap = gates[:, 72 * tt: 72 * tt + 72]
            t1 = vop(lambda v, g=g_ap: v.tensor_single_scalar(out=calv[:, :], in_=g,
                                                              scalar=1.0, op=ALU.add),
                     [ge_toks[tt]], tag=f"ca1{c}")
            t2 = vop(lambda v: v.reciprocal(calv[:, :], calv[:, :]), [t1], tag=f"ca2{c}")
            t3 = vop(lambda v: v.tensor_scalar(out=calv[:, :], in0=calv[:, :], scalar1=7.0,
                                               scalar2=None, op0=ALU.mult), [t2], tag=f"ca3{c}")
            tacc = vop(lambda v: v.memset(cal2[:, :], 0.0), [t3], tag=f"ca4{c}")
            for j in range(7):
                ta = vop(lambda v, j=j: v.tensor_scalar(out=calt[:, :], in0=calv[:, :],
                                                        scalar1=float(j), scalar2=0.0,
                                                        op0=ALU.subtract, op1=ALU.max),
                         [tacc if j == 0 else tb2], tag=f"cj1{c}")
                tb = vop(lambda v, j=j: v.tensor_scalar(out=calt[:, :], in0=calt[:, :],
                                                        scalar1=1.0, scalar2=scap(SC_D0 + j),
                                                        op0=ALU.min, op1=ALU.mult),
                         [ta, t_scal], tag=f"cj2{c}")
                tb2 = vop(lambda v: v.tensor_add(cal2[:, :], cal2[:, :], calt[:, :]),
                          [tb], tag=f"cj3{c}")
            tk0 = vop(lambda v: v.tensor_scalar(out=cal2[:, :], in0=cal2[:, :],
                                                scalar1=scap(SC_K0), scalar2=None, op0=ALU.add),
                      [tb2, t_scal], tag=f"ck0{c}")
            m1 = vop(lambda v: v.reduce_max(idxs[:, 0:1], cal2[:, 64:72], axis=AX.X),
                     [tk0], tag=f"am1{c}")
            m2 = vop(lambda v: v.tensor_scalar(out=calt[:, 0:8], in0=cal2[:, 64:72],
                                               scalar1=idxs[:, 0:1], scalar2=None,
                                               op0=ALU.is_ge), [m1], tag=f"am2{c}")
            m3 = vop(lambda v: v.tensor_mul(calt[:, 8:16], calt[:, 0:8], bigmi[:, :]),
                     [m2, t_bigmi_i], tag=f"am3{c}")
            m4 = vop(lambda v: v.tensor_scalar(out=calt[:, 8:16], in0=calt[:, 8:16],
                                               scalar1=-1.0, scalar2=999.0,
                                               op0=ALU.mult, op1=ALU.add), [m3], tag=f"am4{c}")
            m5 = vop(lambda v: v.tensor_reduce(idxs[:, 1:2], calt[:, 8:16], axis=AX.X,
                                               op=ALU.min), [m4], tag=f"am5{c}")
            w1 = vop(lambda v: v.reduce_max(calt[:, 0:8],
                                            cal2[:, 0:64].rearrange("p (c t) -> p c t", t=8),
                                            axis=AX.X), [tk0], tag=f"aw1{c}")
            w2 = vop(lambda v: v.tensor_tensor(
                out=calv[:, 0:64].rearrange("p (c t) -> p c t", t=8),
                in0=cal2[:, 0:64].rearrange("p (c t) -> p c t", t=8),
                in1=calt[:, 0:8].rearrange("p c -> p c 1" if False else "p (c o) -> p c o", o=1).to_broadcast([128, 8, 8]),
                op=ALU.is_ge), [w1], tag=f"aw2{c}")
            w3 = vop(lambda v: v.tensor_tensor(
                out=calv[:, 0:64].rearrange("p (c t) -> p c t", t=8),
                in0=calv[:, 0:64].rearrange("p (c t) -> p c t", t=8),
                in1=bigmi[:, :].rearrange("p (c o) -> p c o" if False else "p t -> p 1 t" if False else "p (o t) -> p o t", o=1).to_broadcast([128, 8, 8]),
                op=ALU.mult), [w2, t_bigmi_i], tag=f"aw3{c}")
            w4 = vop(lambda v: v.tensor_scalar(out=calv[:, 0:64], in0=calv[:, 0:64],
                                               scalar1=-1.0, scalar2=999.0,
                                               op0=ALU.mult, op1=ALU.add), [w3], tag=f"aw4{c}")
            w5 = vop(lambda v: v.tensor_reduce(calt[:, 16:24],
                                               calv[:, 0:64].rearrange("p (c t) -> p c t", t=8),
                                               axis=AX.X, op=ALU.min), [w4], tag=f"aw5{c}")
            s1 = vop(lambda v: v.tensor_scalar(out=calt[:, 24:32], in0=iotaj[:, 0:8],
                                               scalar1=idxs[:, 1:2], scalar2=None,
                                               op0=ALU.is_equal), [m5, t_ioj], tag=f"as1{c}")
            s2 = vop(lambda v: v.tensor_mul(calt[:, 24:32], calt[:, 24:32], calt[:, 16:24]),
                     [s1, w5], tag=f"as2{c}")
            s3 = vop(lambda v: v.reduce_sum(idxs[:, 2:3], calt[:, 24:32], axis=AX.X),
                     [s2], tag=f"as3{c}")
            s4 = vop(lambda v, tt=tt: v.tensor_scalar(out=idxs[:, 3 + tt:4 + tt], in0=idxs[:, 1:2],
                                                      scalar1=8.0, scalar2=idxs[:, 2:3],
                                                      op0=ALU.mult, op1=ALU.add),
                     [s3, idx_read[tt]], tag=f"as4{c}")
            tidx_toks.append(s4)
        return tidx_toks

    def gathers(c, tidx_toks):
        t2s = []
        for tt in range(4):
            slots, rdeps = ring_alloc(1)
            sl = slots[0]
            t1 = tp_(ps_ring[0:1, 512 * sl: 512 * sl + 128], idxs[:, 3 + tt:4 + tt],
                     [tidx_toks[tt], t_ident] + rdeps, tag=f"tiT{c}")
            idx_read[tt] = t1
            ring_filled([sl], t1)
            t2 = vcopy(tidxr[:, 128 * tt: 128 * tt + 128],
                       ps_ring[0:1, 512 * sl: 512 * sl + 128],
                       [t1, tidxr_read[tt]], tag=f"tir{c}")
            ring_used([sl], t2)
            t2s.append(t2)
        t4s = []
        for tt in range(4):
            ohb = ohbufs[tt]
            slots2, rdeps2 = ring_alloc(1)
            sl2 = slots2[0]
            t3 = mm(ps_ring[0:64, 512 * sl2: 512 * sl2 + 128], onesrow[:, 0:64],
                    tidxr[:, 128 * tt: 128 * tt + 128],
                    [t2s[tt], t_ones_r] + rdeps2, tag=f"tibc{c}")
            tidxr_read[tt] = t3
            ring_filled(slots2, t3)
            t4 = vop(lambda v, sl2=sl2, ohb=ohb: v.tensor_scalar(
                out=ohb[:, :], in0=ps_ring[0:64, 512 * sl2: 512 * sl2 + 128],
                scalar1=iota64[:, :], scalar2=None, op0=ALU.is_equal),
                [t3, t_io64, oh_tok[tt], oh_read[tt]], tag=f"ioh{c}")
            ring_used(slots2, t4)
            oh_tok[tt] = t4
            t4s.append(t4)

        def emit_tile(tt):
            ohb = ohbufs[tt]
            slots3, rdeps3 = ring_alloc(3)
            g1 = mm(rs(slots3[0], 65), ohb[:, :], spl_b[:, :], [t4s[tt], tok_spl] + rdeps3,
                    tag=f"gsp{c}")
            g2 = mm(rs(slots3[1]), ohb[:, :], dirs_b[:, 0:512], [t4s[tt], tok_dirs],
                    tag=f"gd1{c}")
            g3 = mm(rs(slots3[2]), ohb[:, :], dirs_b[:, 512:1024], [t4s[tt], tok_dirs],
                    tag=f"gd2{c}")
            ring_filled(slots3, g3)
            oh_read[tt] = g3
            return (slots3, g1, g2, g3)
        return emit_tile

    def spline_y(c, gth_fn):
        ysts = []
        for tt in range(4):
            slots3, g1, g2, g3 = gth_fn(tt)
            ai = 4 * (c % 2) + tt
            e1 = vcopy(splg[:, :], rs(slots3[0], 65), [g1], tag=f"spc{c}")
            h1 = vop(lambda v, tt=tt: v.tensor_scalar(out=hhs[:, 0:16], in0=splg[:, 0:16],
                                               scalar1=compt[:, 2 * tt: 2 * tt + 1],
                                               scalar2=None, op0=ALU.mult),
                     [e1, compt_tok[tt]], tag=f"sp1{c}")
            h2 = vop(lambda v, tt=tt: v.scalar_tensor_tensor(out=hhs[:, 16:32], in0=splg[:, 16:32],
                                                      scalar=compt[:, 2 * tt + 1: 2 * tt + 2],
                                                      in1=hhs[:, 0:16], op0=ALU.mult,
                                                      op1=ALU.add), [h1], tag=f"sp2{c}")
            h3 = vop(lambda v: v.tensor_add(hhs[:, 32:48], hhs[:, 16:32], splg[:, 32:48]),
                     [h2], tag=f"sp3{c}")
            h4 = vop(lambda v: v.tensor_scalar(out=hhs[:, 32:48], in0=hhs[:, 32:48],
                                               scalar1=0.0, scalar2=None, op0=ALU.max),
                     [h3], tag=f"sp4{c}")
            h5 = vop(lambda v: v.tensor_mul(hhs[:, 0:16], hhs[:, 32:48], splg[:, 48:64]),
                     [h4], tag=f"sp5{c}")
            h6 = vop(lambda v, tt=tt: v.reduce_sum(mags[:, tt:tt + 1], hhs[:, 0:16], axis=AX.X),
                     [h5, mags_tok[tt]], tag=f"sp6{c}")
            h7 = vop(lambda v, tt=tt: v.tensor_add(mags[:, tt:tt + 1], mags[:, tt:tt + 1],
                                            splg[:, 64:65]), [h6], tag=f"sp7{c}")
            h8 = vop(lambda v, tt=tt: v.tensor_scalar(out=mags[:, tt:tt + 1], in0=mags[:, tt:tt + 1],
                                               scalar1=scap(SC_OS), scalar2=None,
                                               op0=ALU.mult), [h7, t_scal], tag=f"sp8{c}")
            mags_tok[tt] = h8
            y1 = vop(lambda v, ai=ai, tt=tt: v.tensor_scalar(
                out=ybuf[tt][:, :], in0=ACC[ai][:, :], scalar1=mv3[:, 2 * tt:2 * tt + 1],
                scalar2=r3c[:, tt:tt + 1], op0=ALU.subtract, op1=ALU.mult),
                [acc_tok[ai], p2[c]["r3"][tt], ybuf_read[tt], ybuf_tok[tt]], tag=f"y1{c}")
            y2 = vop(lambda v, tt=tt, s=slots3[1]: v.scalar_tensor_tensor(
                out=ybuf[tt][:, 0:512], in0=rs(s), scalar=mags[:, tt:tt + 1],
                in1=ybuf[tt][:, 0:512], op0=ALU.mult, op1=ALU.add), [y1, h8, g2], tag=f"y2{c}")
            y3 = vop(lambda v, tt=tt, s=slots3[2]: v.scalar_tensor_tensor(
                out=ybuf[tt][:, 512:1024], in0=rs(s), scalar=mags[:, tt:tt + 1],
                in1=ybuf[tt][:, 512:1024], op0=ALU.mult, op1=ALU.add), [y2, g3], tag=f"y3{c}")
            ring_used(slots3, y3)
            ybuf_tok[tt] = y3
            acc_read[ai] = y1
            s1 = vop(lambda v, tt=tt: v.bn_stats(st6[:, 0:6], ybuf[tt][:, 0:512]), [y3],
                     tag=f"ys1{c}")
            s2 = vop(lambda v, tt=tt: v.bn_stats(st6[:, 6:12], ybuf[tt][:, 512:1024]), [y3],
                     tag=f"ys2{c}")
            s3 = vop(lambda v, tt=tt: v.bn_aggr(mvy[:, 2 * tt:2 * tt + 2], st6[:, :]),
                     [s1, s2, mvy_tok[tt]], tag=f"ys3{c}")
            mvy_tok[tt] = s3
            ysts.append(s3)
        return ysts

    def out_assembly(c, ry_toks):
        for tt in range(4):
            row = 128 * (4 * c + tt)
            ai = 4 * (c % 2) + tt
            o1 = vop(lambda v, tt=tt: v.tensor_scalar(
                out=ybuf[tt][:, :], in0=ybuf[tt][:, :], scalar1=mvy[:, 2 * tt:2 * tt + 1],
                scalar2=ryc[:, tt:tt + 1], op0=ALU.subtract, op1=ALU.mult),
                [ybuf_tok[tt], ry_toks[tt]], tag=f"o1{c}")
            o2 = vop(lambda v, tt=tt: v.tensor_mul(ybuf[tt][:, :], ybuf[tt][:, :], ngam_t[:, :]),
                     [o1] + ngam_toks, tag=f"o2{c}")
            o3 = vop(lambda v, tt=tt, ai=ai: v.tensor_add(ybuf[tt][:, :], ybuf[tt][:, :],
                                                          ACC[ai][:, :]),
                     [o2, acc_tok[ai]], tag=f"o3{c}")
            acc_read[ai] = o3
            o4 = vop(lambda v, tt=tt: v.tensor_add(ybuf[tt][:, :], ybuf[tt][:, :], nbet_t[:, :]),
                     [o3] + nbet_toks, tag=f"o4{c}")
            ybuf_tok[tt] = o4
            od = dma("sync", out_d[row:row + 128, :], ybuf[tt][:, :], [o4], tag=f"od{c}")
            ybuf_read[tt] = od
            outdma_toks.append(od)

    dbg_state = {}

    def dbg_dump(i, ap, deps, w=D):
        if dbg_d is None:
            return None
        return dma("gpsimd", dbg_d[i, :, 0:w], ap, deps, tag=f"dbg{i}")

    zrb_read = [None, None]
    p2 = {}

    def p2_m0(ch):
        st = p2.setdefault(ch, {})
        a_toks = a_init(ch)
        ztoks = []
        for tt in range(4):
            ai = 4 * (ch % 2) + tt
            zi = tt % 2
            tl = dma("sync", zrb[zi][:, :],
                     zs_d[512 * ch + 128 * tt: 512 * ch + 128 * tt + 128, :],
                     [zrb_read[zi]] + zdma_by_chunk[ch], tag=f"zl{ch}")
            tadd = vop(lambda v, ai=ai, zi=zi: v.tensor_add(
                ACC[ai][:, :], ACC[ai][:, :], zrb[zi][:, :]),
                [tl, a_toks[tt]], tag=f"zadd{ch}")
            zrb_read[zi] = tadd
            acc_tok[ai] = tadd
            ztoks.append(tadd)
        st["ztoks"] = ztoks

    def p2_m1(ch):
        st = p2[ch]
        stat2 = ln2_x1(ch, st["ztoks"])
        st["r2"] = rstd_visit(ch, stat2, mv2, r2c, r2_tok, "r2")

    def p2_m2(ch):
        st = p2[ch]
        x1_toks = x1_compute(ch, st["r2"])
        stat3 = []
        for tt in range(4):
            ai = 4 * (ch % 2) + tt
            t1 = vop(lambda v, ai=ai: v.bn_stats(st6[:, 0:6], ACC[ai][:, 0:512]),
                     [x1_toks[tt]], tag=f"l3a{ch}")
            t2 = vop(lambda v, ai=ai: v.bn_stats(st6[:, 6:12], ACC[ai][:, 512:1024]),
                     [x1_toks[tt]], tag=f"l3b{ch}")
            t3 = vop(lambda v, tt=tt: v.bn_aggr(mv3[:, 2 * tt:2 * tt + 2], st6[:, :]),
                     [t1, t2, mv3_tok[tt]], tag=f"l3c{ch}")
            mv3_tok[tt] = t3
            stat3.append(t3)
        st["r3"] = rstd_visit(ch, stat3, mv3, r3c, r3_tok, "r3")
        st["x1_toks"] = x1_toks

    def p2_m3(ch):
        st = p2[ch]
        _, st["hT_toks"] = ffn_T1(ch, st["x1_toks"], st["r3"])

    def p2_m3b(ch):
        st = p2[ch]
        ffn_T2(ch, st["hT_toks"])

    def p2_m4(ch):
        st = p2[ch]
        if "ffntail" not in skip:
            compt_stage(ch)
            st["tidx"] = calibrate_argmax(ch, gates_exp(ch))
        else:
            st["tidx"] = [None] * 4

    def p2_m5(ch):
        st = p2[ch]
        if "ffntail" not in skip:
            gth = gathers(ch, st["tidx"])
            st["ysts"] = spline_y(ch, gth)
        else:
            st["ysts"] = [None] * 4

    def p2_m6(ch):
        st = p2[ch]
        st["ry"] = rstd_visit(ch, st["ysts"], mvy, ryc, ry_tok, "ry")

    def p2_m7(ch):
        st = p2[ch]
        out_assembly(ch, st["ry"])
        del p2[ch]

    phase_queue = []

    def enqueue_chunk(ch):
        q = [lambda ch=ch: p2_m0(ch), lambda ch=ch: p2_m1(ch)]
        if ch >= 1:
            q.append(lambda ch=ch: p2_m5(ch - 1))
        q.append(lambda ch=ch: p2_m2(ch))
        if ch >= 1:
            q.append(lambda ch=ch: p2_m6(ch - 1))
        q.append(lambda ch=ch: p2_m3(ch))
        if ch >= 1:
            q.append(lambda ch=ch: p2_m7(ch - 1))
        q.append(lambda ch=ch: p2_m3b(ch))
        q.append(lambda ch=ch: p2_m4(ch))
        phase_queue.extend(q)

    ppc = NPAIR // CH if CH <= NPAIR else 1

    sched_by_c = [tile_schedule(c) for c in range(CH)]
    for p in range(NPAIR):
        if p == 0:
            kvq_load(0)
        for ci, c in enumerate(range(CH)):
            if "pairs" not in skip:
                pv_last, sg_idx = emit_pair(c, p, sched_by_c[c])
                if ci == 0 and p + 1 < NPAIR:
                    kvq_load(p + 1)
                zt = z_evac(c, p, pv_last, sg_idx)
                if c == 0 and p == 0:
                    dbg_dump(1, zsb[:, 0:512], [zs_tok[0]], w=512)
            if phase_queue:
                phase_queue.pop(0)()
                if ci == CH - 1 and len(phase_queue) > 1:
                    phase_queue.pop(0)()
        if (p + 1) % ppc == 0:
            enqueue_chunk((p + 1) // ppc - 1)

    while phase_queue:
        phase_queue.pop(0)()
    p2_m5(CH - 1)
    p2_m6(CH - 1)
    p2_m7(CH - 1)

    fin_deps = [outdma_toks[-1]]
    for sname in DMAPOOL["sync"] + DMAPOOL["gpsimd"]:
        cc = P.count.get(sname, 0)
        if cc:
            fin_deps.append((sname, cc))
    P.op("gpsimd", lambda g: g.nop(), fin_deps, tag="fin")

    import os
    nopify = set(os.environ.get("NOPIFY", "").split(",")) - {""}
    noptags = set(os.environ.get("NOPTAGS", "").split(",")) - {""}
    refd = {}
    for e2 in ENGINES:
        for (fn, waits, sem, amt, tag) in P.stream[e2]:
            for (s, v) in waits:
                refd.setdefault(s, set()).add(v)
    remap = {}
    for e2 in ENGINES:
        if os.environ.get("NO_SEMCOMP", "") == "1":
            break
        s = ESEM[e2]
        total = P.count.get(s, 0)
        if not total:
            continue
        rset = refd.get(s, set())
        m = {}
        nci = 0
        for k in range(1, total + 1):
            if k in rset or k == total:
                nci += 1
                m[k] = (nci, True)
            else:
                m[k] = (nci, False)
        remap[s] = m
    with nc.Block() as block:
        def emit(eng_name):
            def body(eng):
                own = ESEM[eng_name]
                for (fn, waits, sem, amt, tag) in P.stream[eng_name]:
                    for (s, v) in waits:
                        if s in remap:
                            v = remap[s][v][0]
                        eng.wait_ge(semh[s], v)
                    tagnop = any(tag.startswith(tp) for tp in noptags) if noptags else False
                    if (eng_name in nopify or tagnop) and amt == 1:
                        inst = eng.nop()
                    else:
                        inst = fn(eng)
                    if sem in remap:
                        cnt, do_inc = remap[sem][P2_cursor[sem]]
                        P2_cursor[sem] += 1
                        if do_inc:
                            inst.then_inc(semh[sem], amt)
                    else:
                        inst.then_inc(semh[sem], amt)
            return body
        P2_cursor = {s: 1 for s in remap}
        block.sync(emit("sync"))
        block.gpsimd(emit("gpsimd"))
        block.tensor(emit("tensor"))
        block.scalar(emit("scalar"))
        block.vector(emit("vector"))

    es.close()
    return nc


_prog_cache = {}


def _get_prog(S):
    if S not in _prog_cache:
        _prog_cache[S] = build_program(S)
    return _prog_cache[S]


def make_in_map(core_inputs):
    return core_inputs


def kernel(**inputs):
    emb = np.ascontiguousarray(np.asarray(inputs["L0_attn_embedded"], np.float32))
    B, S, _ = emb.shape
    f = lambda k: np.ascontiguousarray(np.asarray(inputs[k], np.float32))
    wqT = np.ascontiguousarray(f("L0_attn_Wq").T)
    wkT = np.ascontiguousarray(f("L0_attn_Wk").T)
    wvT = np.ascontiguousarray(f("L0_attn_Wv").T)
    bqT = np.ascontiguousarray(f("L0_attn_bq").reshape(8, 128).T)
    bkT = np.ascontiguousarray(f("L0_attn_bk").reshape(8, 128).T)
    bvT = np.ascontiguousarray(f("L0_attn_bv").reshape(8, 128).T)
    gamma = f("L0_attn_gamma").reshape(1, D)
    beta = f("L0_attn_beta").reshape(1, D)
    ngamma = f("L0_ffn_norm_gamma").reshape(1, D)
    nbeta = f("L0_ffn_norm_beta").reshape(1, D)
    sigT = np.ascontiguousarray(f("L0_ffn_signatures_raw").T)
    knots = f("L0_ffn_knot_values").reshape(1, 8)
    temp = f("L0_ffn_temperature").reshape(1, 1)
    oscale = f("L0_ffn_output_scale").reshape(1, 1)
    cw1 = f("L0_ffn_compress_w1")
    cb1T = np.ascontiguousarray(f("L0_ffn_compress_b1").reshape(2, 128).T)
    cw2 = np.ascontiguousarray(
        f("L0_ffn_compress_w2").reshape(2, 128, 2).transpose(1, 0, 2).reshape(128, 4))
    cb2 = f("L0_ffn_compress_b2").reshape(2, 1)
    spl = np.ascontiguousarray(np.concatenate([
        f("L0_ffn_spline_w1").reshape(64, 32),
        f("L0_ffn_spline_b1").reshape(64, 16),
        f("L0_ffn_spline_w2").reshape(64, 16),
        f("L0_ffn_spline_b2").reshape(64, 1)], axis=1))
    dirs = f("L0_ffn_directions")

    shared = dict(wqT=wqT, wkT=wkT, wvT=wvT, bqT=bqT, bkT=bkT, bvT=bvT,
                  gamma=gamma, beta=beta, ngamma=ngamma, nbeta=nbeta, sigT=sigT,
                  knots=knots, temp=temp, oscale=oscale, cw1=cw1, cb1T=cb1T,
                  cw2r=cw2, cb2=cb2, spl=spl, dirs=dirs)
    in_maps = []
    for b in range(B):
        m = dict(shared)
        m["x0"] = np.ascontiguousarray(emb[b])
        in_maps.append(m)
    nc = _get_prog(S)
    res = run_bass_kernel_spmd(nc, in_maps, core_ids=list(range(B)))
    out = np.stack([np.asarray(r["out"], np.float32) for r in res.results], axis=0)
    return out
